# version 05
# baseline (speedup 1.0000x reference)
import numpy as np
from contextlib import ExitStack
import concourse.bass as bass
import concourse.mybir as mybir
from concourse.bass_utils import run_bass_kernel_spmd
import ml_dtypes

BF = ml_dtypes.bfloat16
F32 = mybir.dt.float32
BF16 = mybir.dt.bfloat16
I32 = mybir.dt.int32
AF = mybir.ActivationFunctionType
ALU = mybir.AluOpType
AX = mybir.AxisListType.X

D = 1024
B = 2
N = 8192
NTOK = 2048
NT = NTOK // 128
NC = NTOK // 512
DC = D // 128


class Ctx:

    def __init__(self, name):
        self.name = name
        self.nc = bass.Bass("TRN2", target_bir_lowering=False, debug=False)
        self.es = ExitStack()
        self.sems = {}
        self.cnt = {}
        self.block = None

    def __enter__(self):
        self.es.__enter__()
        for s in ("dma", "gd", "te", "ve", "se", "st"):
            self.sems[s] = self.es.enter_context(self.nc.semaphore(s))
            self.cnt[s] = 0
        self.block = self.es.enter_context(self.nc.Block())
        return self

    def __exit__(self, *a):
        return self.es.__exit__(*a)

    def sb(self, name, shape, dtype):
        return self.es.enter_context(self.nc.sbuf_tensor(name, shape, dtype))

    def ps(self, name, cols=512, dtype=F32):
        return self.es.enter_context(self.nc.psum_tensor(name, [128, cols], dtype))

    def inc(self, inst, s, k=1):
        inst.then_inc(self.sems[s], k)
        self.cnt[s] += k

    def wait(self, eng, s, val=None):
        eng.wait_ge(self.sems[s], self.cnt[s] if val is None else val)

    def on(self, engname):
        def deco(fn):
            getattr(self.block, engname)(fn)
        return deco


def ln_tile(c, src, dst, sq, stats, Dfree, res=None, res_out=None):
    mu, m2, var, rstd, nmr = (stats[:, i:i + 1] for i in range(5))

    @c.on("vector")
    def _(v):
        c.inc(v.reduce_sum(mu, src, axis=AX), "ve")

    @c.on("scalar")
    def _(s):
        c.wait(s, "ve")
        c.inc(s.activation(sq[:, 0:src.shape[1]], src, AF.Square, accum_out=m2), "se")

    @c.on("vector")
    def _(v):
        c.wait(v, "se")
        v.tensor_scalar_mul(mu, mu, 1.0 / Dfree)
        c.inc(v.tensor_scalar_mul(m2, m2, 1.0 / Dfree), "ve")
        c.wait(v, "ve")
        c.inc(v.tensor_tensor(var, mu, mu, op=ALU.mult), "ve")
        c.wait(v, "ve")
        c.inc(v.tensor_tensor(var, m2, var, op=ALU.subtract), "ve")
        c.wait(v, "ve")
        c.inc(v.tensor_scalar_add(var, var, 1e-5), "ve")

    @c.on("scalar")
    def _(s):
        c.wait(s, "ve")
        c.inc(s.activation(rstd, var, AF.Sqrt), "se")

    @c.on("vector")
    def _(v):
        c.wait(v, "se")
        c.inc(v.reciprocal(rstd, rstd), "ve")
        c.wait(v, "ve")
        c.inc(v.scalar_tensor_tensor(nmr, mu, -1.0, rstd, op0=ALU.mult, op1=ALU.mult), "ve")

    @c.on("scalar")
    def _(s):
        c.wait(s, "ve")
        c.wait(s, "te")
        c.inc(s.activation(dst, src, AF.Identity, bias=nmr, scale=rstd), "se")

    if res is not None:
        @c.on("vector")
        def _(v):
            c.wait(v, "se")
            c.inc(v.tensor_tensor(res_out, res, dst, op=ALU.add), "ve")


def build_l0pre():
    c = Ctx("l0pre")
    nc = c.nc
    xr = nc.dram_tensor("xr", [NTOK, D], F32, kind="ExternalInput").ap()
    wq = nc.dram_tensor("wq", [D, D], F32, kind="ExternalInput").ap()
    wk = nc.dram_tensor("wk", [D, D], F32, kind="ExternalInput").ap()
    wv = nc.dram_tensor("wv", [D, D], F32, kind="ExternalInput").ap()
    ident = nc.dram_tensor("ident", [128, 128], BF16, kind="ExternalInput").ap()
    identf = nc.dram_tensor("identf", [128, 128], F32, kind="ExternalInput").ap()
    q_fm = nc.dram_tensor("q_fm", [D, NTOK], BF16, kind="ExternalOutput").ap()
    k_fm = nc.dram_tensor("k_fm", [D, NTOK], BF16, kind="ExternalOutput").ap()
    v_fm = nc.dram_tensor("v_fm", [D, NTOK], BF16, kind="ExternalOutput").ap()
    v_tm = nc.dram_tensor("v_tm", [NTOK, D], BF16, kind="ExternalOutput").ap()
    SCQ = float(N) ** -0.5
    with c:
        idb = c.sb("idb", [128, 128], BF16)
        idf = c.sb("idf", [128, 128], F32)
        xs = c.sb("xs", [128, D], F32)
        sq = c.sb("sq", [128, D], F32)
        st8 = c.sb("st8", [128, 8], F32)
        ytmf = c.sb("ytmf", [128, D], F32)
        ytm = c.sb("ytm", [128, D], BF16)
        yfm = c.sb("yfm", [128, DC * NTOK], BF16)
        ypfm = c.sb("ypfm", [128, DC * NTOK], BF16)
        wqb = c.sb("wqb", [128, DC * D], BF16)
        wkb = c.sb("wkb", [128, DC * D], BF16)
        ebf = c.sb("ebf", [128, DC * D], F32)
        eb = c.sb("eb", [128, DC * D], BF16)
        et = c.sb("et", [128, DC * D], BF16)
        csum = c.sb("csum", [128, 8], F32)
        rec = c.sb("rec", [128, 8], F32)
        recrow = c.sb("recrow", [1, D], F32)
        recbc = c.sb("recbc", [128, D], F32)
        srow = c.sb("srow", [1, NTOK], F32)
        srowb = c.sb("srowb", [1, NTOK], BF16)
        scol = c.sb("scol", [128, 1], F32)
        tmpf = c.sb("tmpf", [128, D], F32)
        negone = c.sb("negone", [128, 1], F32)
        onesb = c.sb("onesb", [1, 128], BF16)
        onesf = c.sb("onesf", [1, 128], F32)
        outst = c.sb("outst", [128, 8 * 512], BF16)
        ptr = c.ps("ptr", 1024, BF16)
        prow = c.ps("prow", 512)
        prb = c.ps("prb", 1024)
        pj = [c.ps(f"pj{i}", 512) for i in range(4)]

        @c.on("gpsimd")
        def _(g):
            c.inc(g.dma_start(out=idb[:], in_=ident), "gd", 16)
            c.inc(g.dma_start(out=idf[:], in_=identf), "gd", 16)
            for dc in range(DC):
                c.inc(g.dma_start(out=wqb[:, dc * D:(dc + 1) * D],
                                  in_=wq[128 * dc:128 * (dc + 1), :]), "gd", 16)
            for dc in range(DC):
                c.inc(g.dma_start(out=wkb[:, dc * D:(dc + 1) * D],
                                  in_=wk[128 * dc:128 * (dc + 1), :]), "gd", 16)
            for kb in range(DC):
                c.inc(g.dma_start(out=ebf[:, kb * D:(kb + 1) * D],
                                  in_=wv[128 * kb:128 * (kb + 1), :]), "gd", 16)

        @c.on("vector")
        def _(v):
            v.memset(onesb[:], 1.0)
            v.memset(onesf[:], 1.0)
            c.inc(v.memset(negone[:], -1.0), "ve")

        @c.on("scalar")
        def _(s):
            c.wait(s, "gd")
            c.wait(s, "ve")
            for kb in range(DC):
                c.inc(s.activation(ebf[:, kb * D:(kb + 1) * D],
                                   ebf[:, kb * D:(kb + 1) * D], AF.Exp), "se")
                c.wait(s, "se")
                c.inc(s.activation(eb[:, kb * D:(kb + 1) * D],
                                   ebf[:, kb * D:(kb + 1) * D], AF.Identity,
                                   bias=negone[:]), "se")

        for jb in range(8):
            @c.on("tensor")
            def _(t, jb=jb):
                c.wait(t, "se")
                c.wait(t, "ve")
                for kb in range(DC):
                    last = t.transpose(ptr[:, kb * 128:(kb + 1) * 128],
                                       eb[:, kb * D + jb * 128:kb * D + (jb + 1) * 128],
                                       idb[:])
                c.inc(last, "te")

            @c.on("vector")
            def _(v, jb=jb):
                c.wait(v, "te")
                c.inc(v.tensor_copy(et[:, jb * D:(jb + 1) * D], ptr[:, 0:1024]), "ve")

            @c.on("scalar")
            def _(s, jb=jb):
                c.wait(s, "ve")
                c.wait(s, "se")
                c.inc(s.activation(et[:, jb * D:(jb + 1) * D], et[:, jb * D:(jb + 1) * D],
                                   AF.Identity, accum_out=csum[:, jb:jb + 1]), "se")

        @c.on("vector")
        def _(v):
            c.wait(v, "se")
            c.inc(v.tensor_scalar_add(csum[:], csum[:], float(D)), "ve")
            c.wait(v, "ve")
            c.inc(v.reciprocal(rec[:], csum[:]), "ve")
            c.wait(v, "ve")
            c.inc(v.tensor_scalar_mul(rec[:], rec[:], 1.0 / D), "ve")

        for half in range(2):
            @c.on("tensor")
            def _(t, half=half):
                c.wait(t, "ve")
                for j in range(4):
                    jb = 4 * half + j
                    last = t.matmul(prow[0:1, 128 * j:128 * (j + 1)], rec[:, jb:jb + 1],
                                    idf[:], start=(j == 0), stop=(j == 3))
                c.inc(last, "te")

            @c.on("vector")
            def _(v, half=half):
                c.wait(v, "te")
                c.inc(v.tensor_copy(recrow[:, 512 * half:512 * (half + 1)],
                                    prow[0:1, 0:512]), "ve")

        @c.on("tensor")
        def _(t):
            c.wait(t, "ve")
            t.matmul(prb[:, 0:512], onesf[:], recrow[:, 0:512], start=True, stop=True)
            last = t.matmul(prb[:, 512:1024], onesf[:], recrow[:, 512:1024],
                            start=True, stop=True)
            c.inc(last, "te")

        @c.on("vector")
        def _(v):
            c.wait(v, "te")
            c.inc(v.tensor_copy(recbc[:], prb[:, 0:1024]), "ve")

        for tt in range(NT):
            @c.on("sync")
            def _(sy, tt=tt):
                if tt > 0:
                    c.wait(sy, "se")
                c.inc(sy.dma_start(out=xs[:], in_=xr[128 * tt:128 * (tt + 1), :]), "dma", 16)

            @c.on("vector")
            def _(v):
                c.wait(v, "dma")

            ln_tile(c, xs[:], ytmf[:], sq, st8, D)

            @c.on("vector")
            def _(v, tt=tt):
                c.wait(v, "se")
                c.inc(v.tensor_copy(ytm[:], ytmf[:]), "ve")
                c.inc(v.tensor_tensor(tmpf[:], ytmf[:], recbc[:], op=ALU.mult), "ve")
                c.wait(v, "ve")
                c.inc(v.reduce_sum(scol[:], tmpf[:], axis=AX), "ve")

            @c.on("tensor")
            def _(t, tt=tt):
                c.wait(t, "ve")
                for dc2 in range(DC):
                    last = t.transpose(ptr[:, dc2 * 128:(dc2 + 1) * 128],
                                       ytm[:, dc2 * 128:(dc2 + 1) * 128], idb[:])
                c.inc(last, "te")
                last = t.matmul(prow[0:1, 128 * (tt % 4):128 * (tt % 4 + 1)], scol[:],
                                idf[:], start=(tt % 4 == 0), stop=(tt % 4 == 3))
                c.inc(last, "te")

            @c.on("vector")
            def _(v, tt=tt):
                c.wait(v, "te")
                for dc2 in range(DC):
                    last = v.tensor_copy(yfm[:, dc2 * NTOK + 128 * tt:dc2 * NTOK + 128 * (tt + 1)],
                                         ptr[:, dc2 * 128:(dc2 + 1) * 128])
                c.inc(last, "ve")
                if tt % 4 == 3:
                    c.inc(v.tensor_copy(srow[:, 512 * (tt // 4):512 * (tt // 4 + 1)],
                                        prow[0:1, 0:512]), "ve")

        @c.on("vector")
        def _(v):
            c.wait(v, "ve")
            c.inc(v.tensor_copy(srowb[:], srow[:]), "ve")
            for jb in range(DC):
                last = v.tensor_scalar(ypfm[:, jb * NTOK:(jb + 1) * NTOK],
                                       yfm[:, jb * NTOK:(jb + 1) * NTOK],
                                       rec[:, jb:jb + 1], None, op0=ALU.mult)
            c.inc(last, "ve")

        def project(wtiles, rhs, out_dram, scale):
            for ncx in range(NC):
                for dg in range(2):
                    @c.on("tensor")
                    def _(t, ncx=ncx, dg=dg):
                        c.wait(t, "ve")
                        c.wait(t, "se")
                        for dl in range(4):
                            do = 4 * dg + dl
                            for di in range(DC):
                                last = t.matmul(pj[dl][:, 0:512], wtiles(di, do),
                                                rhs(di, ncx), start=(di == 0), stop=(di == DC - 1))
                        c.inc(last, "te")

                    @c.on("scalar")
                    def _(s, ncx=ncx, dg=dg):
                        c.wait(s, "te")
                        c.wait(s, "st")
                        for dl in range(4):
                            do = 4 * dg + dl
                            if scale == 1.0:
                                last = s.copy(outst[:, do * 512:(do + 1) * 512], pj[dl][:, 0:512])
                            else:
                                last = s.activation(outst[:, do * 512:(do + 1) * 512],
                                                    pj[dl][:, 0:512], AF.Copy, scale=scale)
                        c.inc(last, "se")

                    @c.on("sync")
                    def _(sy, ncx=ncx, dg=dg):
                        c.wait(sy, "se")
                        for dl in range(4):
                            do = 4 * dg + dl
                            c.inc(sy.dma_start(out=out_dram[do * 128:(do + 1) * 128, 512 * ncx:512 * (ncx + 1)],
                                               in_=outst[:, do * 512:(do + 1) * 512]), "st", 16)

        project(lambda di, do: wqb[:, di * D + do * 128:di * D + (do + 1) * 128],
                lambda di, ncx: yfm[:, di * NTOK + 512 * ncx:di * NTOK + 512 * (ncx + 1)],
                q_fm, SCQ)
        project(lambda di, do: wkb[:, di * D + do * 128:di * D + (do + 1) * 128],
                lambda di, ncx: yfm[:, di * NTOK + 512 * ncx:di * NTOK + 512 * (ncx + 1)],
                k_fm, 1.0)

        for ncx in range(NC):
            for dg in range(2):
                @c.on("tensor")
                def _(t, ncx=ncx, dg=dg):
                    c.wait(t, "ve")
                    c.wait(t, "se")
                    for dl in range(4):
                        do = 4 * dg + dl
                        for di in range(DC):
                            t.matmul(pj[dl][:, 0:512],
                                     et[:, di * D + do * 128:di * D + (do + 1) * 128],
                                     ypfm[:, di * NTOK + 512 * ncx:di * NTOK + 512 * (ncx + 1)],
                                     start=(di == 0), stop=False)
                        last = t.matmul(pj[dl][:, 0:512], onesb[:],
                                        srowb[:, 512 * ncx:512 * (ncx + 1)],
                                        start=False, stop=True)
                    c.inc(last, "te")

                @c.on("scalar")
                def _(s, ncx=ncx, dg=dg):
                    c.wait(s, "te")
                    c.wait(s, "st")
                    for dl in range(4):
                        do = 4 * dg + dl
                        last = s.copy(outst[:, do * 512:(do + 1) * 512], pj[dl][:, 0:512])
                    c.inc(last, "se")

                @c.on("sync")
                def _(sy, ncx=ncx, dg=dg):
                    c.wait(sy, "se")
                    for dl in range(4):
                        do = 4 * dg + dl
                        c.inc(sy.dma_start(out=v_fm[do * 128:(do + 1) * 128, 512 * ncx:512 * (ncx + 1)],
                                           in_=outst[:, do * 512:(do + 1) * 512]), "st", 16)

            for sub in range(4):
                @c.on("tensor")
                def _(t, ncx=ncx, sub=sub):
                    c.wait(t, "se")
                    c.wait(t, "ve")
                    for do in range(DC):
                        last = t.transpose(ptr[:, do * 128:(do + 1) * 128],
                                           outst[:, do * 512 + sub * 128:do * 512 + (sub + 1) * 128],
                                           idb[:])
                    c.inc(last, "te")

                @c.on("vector")
                def _(v, ncx=ncx, sub=sub):
                    c.wait(v, "te")
                    c.wait(v, "st")
                    c.inc(v.tensor_copy(ytm[:, 0:1024], ptr[:, 0:1024]), "ve")

                @c.on("sync")
                def _(sy, ncx=ncx, sub=sub):
                    c.wait(sy, "ve")
                    c.inc(sy.dma_start(out=v_tm[512 * ncx + 128 * sub:512 * ncx + 128 * (sub + 1), :],
                                       in_=ytm[:, 0:1024]), "st", 16)

        @c.on("sync")
        def _(sy):
            c.wait(sy, "st")
    return c


def _pstg2(pstg, stg, gi):
    buf = pstg if (gi // 2) % 2 == 0 else stg
    return buf[:, (gi % 2) * 512:(gi % 2 + 1) * 512]


def build_l0attn():
    c = Ctx("l0attn")
    nc = c.nc
    qf = nc.dram_tensor("qf", [D, N], BF16, kind="ExternalInput").ap()
    kf = nc.dram_tensor("kf", [D, NTOK], BF16, kind="ExternalInput").ap()
    vt = nc.dram_tensor("vt", [NTOK, D], BF16, kind="ExternalInput").ap()
    part = nc.dram_tensor("part", [D, N], BF16, kind="ExternalOutput").ap()
    pT = nc.dram_tensor("pT", [NTOK, N], BF16)
    NQ = N // 512
    with c:
        kb = c.sb("kb", [128, DC * NTOK], BF16)
        vtm = c.sb("vtm", [128, NT * D], BF16)
        vp = c.sb("vp", [128, NT * D], BF16)
        qs = c.sb("qs", [128, DC * 512], BF16)
        stg = c.sb("stg", [128, 2 * 512], BF16)
        pstg = c.sb("pstg", [128, 2 * 512], BF16)
        zparts = c.sb("zparts", [128, NT * NQ], F32)
        z1 = c.sb("z1", [128, 1], F32)
        outst = c.sb("outst", [128, 8 * 512], BF16)
        pj = [c.ps(f"pj{i}", 512) for i in range(8)]

        @c.on("gpsimd")
        def _(g):
            for di in range(DC):
                c.inc(g.dma_start(out=kb[:, di * NTOK:(di + 1) * NTOK],
                                  in_=kf[128 * di:128 * (di + 1), :]), "gd", 16)
            for mt in range(NT):
                c.inc(g.dma_start(out=vtm[:, mt * D:(mt + 1) * D],
                                  in_=vt[128 * mt:128 * (mt + 1), :]), "gd", 16)

        qs2 = c.sb("qs2", [128, DC * 512], BF16)
        stg2 = c.sb("stg2", [128, 2 * 512], BF16)
        sS_te = {}
        sS_se = {}
        sS_st = {}

        def qs_of(ncx):
            return qs if ncx % 2 == 0 else qs2

        def stgb(i):
            return stg[:, (i % 2) * 512:(i % 2 + 1) * 512] if (i // 2) % 2 == 0 \
                else stg2[:, (i % 2) * 512:(i % 2 + 1) * 512]

        for ncx in range(NQ):
            @c.on("sync")
            def _(sy, ncx=ncx):
                if ncx >= 2:
                    c.wait(sy, "te", sS_te[(ncx - 2, NT - 1)])
                for di in range(DC):
                    c.inc(sy.dma_start(out=qs_of(ncx)[:, di * 512:(di + 1) * 512],
                                       in_=qf[128 * di:128 * (di + 1), 512 * ncx:512 * (ncx + 1)]),
                          "dma", 16)
                sS_dma = c.cnt["dma"]
                sS_st[("qld", ncx)] = sS_dma

            for mt in range(NT):
                gi = ncx * NT + mt
                @c.on("tensor")
                def _(t, ncx=ncx, mt=mt, gi=gi):
                    if mt == 0:
                        c.wait(t, "dma", sS_st[("qld", ncx)])
                        if ncx == 0:
                            c.wait(t, "gd")
                    if gi >= 4:
                        c.wait(t, "se", sS_se[gi - 4])
                    for di in range(DC):
                        last = t.matmul(pj[gi % 4][:, 0:512],
                                        kb[:, di * NTOK + 128 * mt:di * NTOK + 128 * (mt + 1)],
                                        qs_of(ncx)[:, di * 512:(di + 1) * 512],
                                        start=(di == 0), stop=(di == DC - 1))
                    c.inc(last, "te")
                    sS_te[(ncx, mt)] = c.cnt["te"]

                @c.on("scalar")
                def _(s, ncx=ncx, mt=mt, gi=gi):
                    c.wait(s, "te", sS_te[(ncx, mt)])
                    if gi >= 4:
                        c.wait(s, "st", sS_st[gi - 4])
                    c.inc(s.activation(stgb(gi),
                                       pj[gi % 4][:, 0:512], AF.Exp,
                                       accum_out=zparts[:, mt * NQ + ncx:mt * NQ + ncx + 1]), "se")
                    sS_se[gi] = c.cnt["se"]

                @c.on("sync")
                def _(sy, ncx=ncx, mt=mt, gi=gi):
                    c.wait(sy, "se", sS_se[gi])
                    c.inc(sy.dma_start(out=pT[128 * mt:128 * (mt + 1), 512 * ncx:512 * (ncx + 1)],
                                       in_=stgb(gi)), "st", 16)
                    sS_st[gi] = c.cnt["st"]

        for mt in range(NT):
            @c.on("vector")
            def _(v, mt=mt):
                c.wait(v, "se")
                c.inc(v.reduce_sum(z1[:], zparts[:, mt * NQ:(mt + 1) * NQ], axis=AX), "ve")
                c.wait(v, "ve")
                c.inc(v.reciprocal(z1[:], z1[:]), "ve")
                c.wait(v, "ve")
                c.inc(v.tensor_scalar(vp[:, mt * D:(mt + 1) * D], vtm[:, mt * D:(mt + 1) * D],
                                      z1[:], None, op0=ALU.mult), "ve")

        pB_dma = {}
        pB_te = {}
        pB_se = {}
        stB0 = None
        for half in range(2):
            for ncx in range(NQ):
                for mt in range(NT):
                    @c.on("sync")
                    def _(sy, half=half, ncx=ncx, mt=mt):
                        gi = (half * NQ + ncx) * NT + mt
                        if gi < 4:
                            c.wait(sy, "st")
                        else:
                            c.wait(sy, "te", pB_te[gi - 4])
                        c.inc(sy.dma_start(out=_pstg2(pstg, stg, gi),
                                           in_=pT[128 * mt:128 * (mt + 1), 512 * ncx:512 * (ncx + 1)]),
                              "dma", 16)
                        pB_dma[gi] = c.cnt["dma"]

                    @c.on("tensor")
                    def _(t, half=half, ncx=ncx, mt=mt):
                        gi = (half * NQ + ncx) * NT + mt
                        c.wait(t, "dma", pB_dma[gi])
                        if mt == 0:
                            if ncx == 0 and half == 0:
                                c.wait(t, "ve")
                            c.wait(t, "se")
                        for dl in range(4):
                            do = 4 * half + dl
                            last = t.matmul(pj[dl][:, 0:512],
                                            vp[:, mt * D + do * 128:mt * D + (do + 1) * 128],
                                            _pstg2(pstg, stg, gi),
                                            start=(mt == 0), stop=(mt == NT - 1))
                        c.inc(last, "te")
                        pB_te[gi] = c.cnt["te"]

                @c.on("scalar")
                def _(s, half=half, ncx=ncx):
                    c.wait(s, "te")
                    c.wait(s, "st", max(0, c.cnt["st"] - 16 * 64))
                    for dl in range(4):
                        do = 4 * half + dl
                        last = s.copy(outst[:, do * 512:(do + 1) * 512], pj[dl][:, 0:512])
                    c.inc(last, "se")

                @c.on("sync")
                def _(sy, half=half, ncx=ncx):
                    c.wait(sy, "se")
                    for dl in range(4):
                        do = 4 * half + dl
                        c.inc(sy.dma_start(out=part[do * 128:(do + 1) * 128, 512 * ncx:512 * (ncx + 1)],
                                           in_=outst[:, do * 512:(do + 1) * 512]), "st", 16)

        @c.on("sync")
        def _(sy):
            c.wait(sy, "st")
    return c


def build_l0red_ffn():
    c = Ctx("l0red")
    nc = c.nc
    xr = nc.dram_tensor("xr", [NTOK, D], F32, kind="ExternalInput").ap()
    p0 = nc.dram_tensor("p0", [D, NTOK], BF16, kind="ExternalInput").ap()
    p1 = nc.dram_tensor("p1", [D, NTOK], BF16, kind="ExternalInput").ap()
    p2 = nc.dram_tensor("p2", [D, NTOK], BF16, kind="ExternalInput").ap()
    p3 = nc.dram_tensor("p3", [D, NTOK], BF16, kind="ExternalInput").ap()
    vfm = nc.dram_tensor("vfm", [D, NTOK], BF16, kind="ExternalInput").ap()
    wg_i = nc.dram_tensor("wg_i", [3 * D, D], I32, kind="ExternalInput").ap()
    wu_i = nc.dram_tensor("wu_i", [3 * D, D], I32, kind="ExternalInput").ap()
    wd_i = nc.dram_tensor("wd_i", [D, 3 * D], I32, kind="ExternalInput").ap()
    ident = nc.dram_tensor("ident", [128, 128], BF16, kind="ExternalInput").ap()
    x2o = nc.dram_tensor("x2o", [NTOK, D], F32, kind="ExternalOutput").ap()
    x1_d = nc.dram_tensor("x1_d", [NTOK, D], F32)
    wgT_d = nc.dram_tensor("wgT_d", [D, 3 * D], BF16)
    wuT_d = nc.dram_tensor("wuT_d", [D, 3 * D], BF16)
    wdT_d = nc.dram_tensor("wdT_d", [3 * D, D], BF16)
    H3 = 3 * D
    HC = H3 // 128
    with c:
        idb = c.sb("idb", [128, 128], BF16)
        buf0 = c.sb("buf0", [128, DC * 512], BF16)
        buf1 = c.sb("buf1", [128, DC * 512], BF16)
        afm = c.sb("afm", [128, DC * 512], BF16)
        xs = c.sb("xs", [128, D], F32)
        sq = c.sb("sq", [128, D], F32)
        st8 = c.sb("st8", [128, 8], F32)
        lnt = c.sb("lnt", [128, D], F32)
        x1t = c.sb("x1t", [128, D], F32)
        atm = c.sb("atm", [128, D], F32)
        ytm = c.sb("ytm", [128, D], BF16)
        yfm = c.sb("yfm", [128, DC * NTOK], BF16)
        wi = c.sb("wi", [128, D], I32)
        wb = c.sb("wb", [128, D], BF16)
        ws = c.sb("ws", [128, 24 * 1024], BF16)
        gb = c.sb("gb", [128, HC * 512], BF16)
        ub = c.sb("ub", [128, HC * 512], BF16)
        outst = c.sb("outst", [128, 8 * 512], BF16)
        ptr = c.ps("ptr", 1024, BF16)
        pj = [c.ps(f"pj{i}", 512) for i in range(4)]

        @c.on("gpsimd")
        def _(g):
            c.inc(g.dma_start(out=idb[:], in_=ident), "gd", 16)

        for ncx in range(NC):
            @c.on("sync")
            def _(sy, ncx=ncx):
                c.wait(sy, "ve")
                for di in range(DC):
                    c.inc(sy.dma_start(out=buf0[:, di * 512:(di + 1) * 512],
                                       in_=p0[128 * di:128 * (di + 1), 512 * ncx:512 * (ncx + 1)]), "dma", 16)
                    c.inc(sy.dma_start(out=buf1[:, di * 512:(di + 1) * 512],
                                       in_=p1[128 * di:128 * (di + 1), 512 * ncx:512 * (ncx + 1)]), "dma", 16)

            @c.on("vector")
            def _(v, ncx=ncx):
                c.wait(v, "dma")
                c.inc(v.tensor_tensor(afm[:], buf0[:], buf1[:], op=ALU.add), "ve")

            @c.on("sync")
            def _(sy, ncx=ncx):
                c.wait(sy, "ve")
                for di in range(DC):
                    c.inc(sy.dma_start(out=buf0[:, di * 512:(di + 1) * 512],
                                       in_=p2[128 * di:128 * (di + 1), 512 * ncx:512 * (ncx + 1)]), "dma", 16)
                    c.inc(sy.dma_start(out=buf1[:, di * 512:(di + 1) * 512],
                                       in_=p3[128 * di:128 * (di + 1), 512 * ncx:512 * (ncx + 1)]), "dma", 16)

            @c.on("vector")
            def _(v, ncx=ncx):
                c.wait(v, "dma")
                c.inc(v.tensor_tensor(buf0[:], buf0[:], buf1[:], op=ALU.add), "ve")
                c.wait(v, "ve")
                c.inc(v.tensor_tensor(afm[:], afm[:], buf0[:], op=ALU.add), "ve")

            @c.on("sync")
            def _(sy, ncx=ncx):
                c.wait(sy, "ve")
                for di in range(DC):
                    c.inc(sy.dma_start(out=buf1[:, di * 512:(di + 1) * 512],
                                       in_=vfm[128 * di:128 * (di + 1), 512 * ncx:512 * (ncx + 1)]), "dma", 16)

            @c.on("vector")
            def _(v, ncx=ncx):
                c.wait(v, "dma")
                c.inc(v.tensor_tensor(afm[:], afm[:], buf1[:], op=ALU.subtract), "ve")

            for sub in range(4):
                tt = 4 * ncx + sub
                @c.on("tensor")
                def _(t, ncx=ncx, sub=sub):
                    c.wait(t, "ve")
                    for di in range(DC):
                        last = t.transpose(ptr[:, di * 128:(di + 1) * 128],
                                           afm[:, di * 512 + sub * 128:di * 512 + (sub + 1) * 128],
                                           idb[:])
                    c.inc(last, "te")

                @c.on("vector")
                def _(v, ncx=ncx, sub=sub):
                    c.wait(v, "te")
                    c.inc(v.tensor_copy(atm[:], ptr[:, 0:1024]), "ve")

                @c.on("sync")
                def _(sy, tt=tt):
                    c.wait(sy, "se")
                    c.inc(sy.dma_start(out=xs[:], in_=xr[128 * tt:128 * (tt + 1), :]), "dma", 16)

                @c.on("vector")
                def _(v):
                    c.wait(v, "dma")

                ln_tile(c, atm[:], lnt[:], sq, st8, D, res=xs[:], res_out=x1t[:])

                @c.on("sync")
                def _(sy, tt=tt):
                    c.wait(sy, "ve")
                    c.inc(sy.dma_start(out=x1_d[128 * tt:128 * (tt + 1), :], in_=x1t[:]), "st", 16)

        def prep_weight(w_i, rows, cols, wT_d):
            for rt in range(rows // 128):
                for ct in range(cols // 1024):
                    @c.on("sync")
                    def _(sy, rt=rt, ct=ct):
                        c.wait(sy, "ve")
                        c.inc(sy.dma_start(out=wi[:],
                                           in_=w_i[128 * rt:128 * (rt + 1), 1024 * ct:1024 * (ct + 1)]),
                              "dma", 16)

                    @c.on("vector")
                    def _(v):
                        c.wait(v, "dma")
                        c.wait(v, "te")
                        c.inc(v.tensor_scalar(wb[:], wi[:], 1.0 / 127.0, None, op0=ALU.mult), "ve")

                    @c.on("tensor")
                    def _(t):
                        c.wait(t, "ve")
                        c.wait(t, "st")
                        for dc2 in range(DC):
                            last = t.transpose(ptr[:, dc2 * 128:(dc2 + 1) * 128],
                                               wb[:, dc2 * 128:(dc2 + 1) * 128], idb[:])
                        c.inc(last, "te")

                    @c.on("vector")
                    def _(v):
                        c.wait(v, "te")
                        c.inc(v.tensor_copy(ytm[:, 0:1024], ptr[:, 0:1024]), "ve")

                    @c.on("sync")
                    def _(sy, rt=rt, ct=ct):
                        c.wait(sy, "ve")
                        for dc2 in range(DC):
                            c.inc(sy.dma_start(
                                out=wT_d[1024 * ct + 128 * dc2:1024 * ct + 128 * (dc2 + 1),
                                         128 * rt:128 * (rt + 1)],
                                in_=ytm[:, dc2 * 128:(dc2 + 1) * 128]), "st", 16)

        prep_weight(wg_i, H3, D, wgT_d)
        prep_weight(wu_i, H3, D, wuT_d)
        prep_weight(wd_i, D, H3, wdT_d)

        for tt in range(NT):
            @c.on("sync")
            def _(sy, tt=tt):
                if tt > 0:
                    c.wait(sy, "se")
                c.inc(sy.dma_start(out=xs[:], in_=x1_d[128 * tt:128 * (tt + 1), :]), "dma", 16)

            @c.on("vector")
            def _(v):
                c.wait(v, "dma")

            ln_tile(c, xs[:], ytm[:], sq, st8, D)

            @c.on("tensor")
            def _(t, tt=tt):
                c.wait(t, "se")
                c.wait(t, "ve")
                for dc2 in range(DC):
                    last = t.transpose(ptr[:, dc2 * 128:(dc2 + 1) * 128],
                                       ytm[:, dc2 * 128:(dc2 + 1) * 128], idb[:])
                c.inc(last, "te")

            @c.on("vector")
            def _(v, tt=tt):
                c.wait(v, "te")
                for dc2 in range(DC):
                    last = v.tensor_copy(yfm[:, dc2 * NTOK + 128 * tt:dc2 * NTOK + 128 * (tt + 1)],
                                         ptr[:, dc2 * 128:(dc2 + 1) * 128])
                c.inc(last, "ve")

        for ncx in range(NC):
            @c.on("sync")
            def _(sy, ncx=ncx):
                c.wait(sy, "te")
                for di in range(DC):
                    c.inc(sy.dma_start(out=ws[:, di * H3:(di + 1) * H3],
                                       in_=wgT_d[128 * di:128 * (di + 1), :]), "dma", 16)

            g_te = {}
            g_se = {}

            def gu_mm(t, ncx, g, dest):
                if g >= 2:
                    c.wait(t, "se", g_se[g - 2])
                for hl in range(2):
                    hh = 2 * g + hl
                    bank = pj[(g % 2) * 2 + hl]
                    for di in range(DC):
                        last = t.matmul(bank[:, 0:512],
                                        ws[:, di * H3 + hh * 128:di * H3 + (hh + 1) * 128],
                                        yfm[:, di * NTOK + 512 * ncx:di * NTOK + 512 * (ncx + 1)],
                                        start=(di == 0), stop=(di == DC - 1))
                c.inc(last, "te")
                g_te[g] = c.cnt["te"]

            def gu_ev(s, ncx, g, dest, silu):
                c.wait(s, "te", g_te[g])
                for hl in range(2):
                    hh = 2 * g + hl
                    bank = pj[(g % 2) * 2 + hl]
                    if silu:
                        last = s.activation(dest[:, hh * 512:(hh + 1) * 512], bank[:, 0:512],
                                            AF.Silu)
                    else:
                        last = s.copy(dest[:, hh * 512:(hh + 1) * 512], bank[:, 0:512])
                c.inc(last, "se")
                g_se[g] = c.cnt["se"]

            for g in range(12):
                @c.on("tensor")
                def _(t, ncx=ncx, g=g):
                    if g == 0:
                        c.wait(t, "dma")
                        c.wait(t, "ve")
                        c.wait(t, "se")
                    gu_mm(t, ncx, g, gb)

                @c.on("scalar")
                def _(s, ncx=ncx, g=g):
                    gu_ev(s, ncx, g, gb, True)

            @c.on("sync")
            def _(sy, ncx=ncx):
                c.wait(sy, "te")
                for di in range(DC):
                    c.inc(sy.dma_start(out=ws[:, di * H3:(di + 1) * H3],
                                       in_=wuT_d[128 * di:128 * (di + 1), :]), "dma", 16)

            g_te = {}
            g_se = {}
            for g in range(12):
                @c.on("tensor")
                def _(t, ncx=ncx, g=g):
                    if g == 0:
                        c.wait(t, "dma")
                        c.wait(t, "se")
                    gu_mm(t, ncx, g, ub)

                @c.on("scalar")
                def _(s, ncx=ncx, g=g):
                    gu_ev(s, ncx, g, ub, False)

            @c.on("vector")
            def _(v, ncx=ncx):
                c.wait(v, "se")
                for hh in range(HC):
                    last = v.tensor_tensor(gb[:, hh * 512:(hh + 1) * 512],
                                           gb[:, hh * 512:(hh + 1) * 512],
                                           ub[:, hh * 512:(hh + 1) * 512], op=ALU.mult)
                c.inc(last, "ve")

            @c.on("sync")
            def _(sy, ncx=ncx):
                c.wait(sy, "te")
                for hh in range(HC):
                    c.inc(sy.dma_start(out=ws[:, hh * 1024:(hh + 1) * 1024],
                                       in_=wdT_d[128 * hh:128 * (hh + 1), :]), "dma", 16)

            d_te = {}
            d_se = {}

            def dn_mm(t, ncx, g):
                if g >= 2:
                    c.wait(t, "se", d_se[g - 2])
                for dl in range(2):
                    do = 2 * g + dl
                    bank = pj[(g % 2) * 2 + dl]
                    for hh in range(HC):
                        last = t.matmul(bank[:, 0:512],
                                        ws[:, hh * 1024 + do * 128:hh * 1024 + (do + 1) * 128],
                                        gb[:, hh * 512:(hh + 1) * 512],
                                        start=(hh == 0), stop=(hh == HC - 1))
                c.inc(last, "te")
                d_te[g] = c.cnt["te"]

            def dn_ev(s, ncx, g):
                c.wait(s, "te", d_te[g])
                for dl in range(2):
                    do = 2 * g + dl
                    bank = pj[(g % 2) * 2 + dl]
                    last = s.copy(outst[:, do * 512:(do + 1) * 512], bank[:, 0:512])
                c.inc(last, "se")
                d_se[g] = c.cnt["se"]

            for g in range(4):
                @c.on("tensor")
                def _(t, ncx=ncx, g=g):
                    if g == 0:
                        c.wait(t, "dma")
                        c.wait(t, "ve")
                        c.wait(t, "se")
                    dn_mm(t, ncx, g)

                @c.on("scalar")
                def _(s, ncx=ncx, g=g):
                    dn_ev(s, ncx, g)

            for sub in range(4):
                tt = 4 * ncx + sub
                @c.on("tensor")
                def _(t, ncx=ncx, sub=sub):
                    c.wait(t, "se")
                    c.wait(t, "ve")
                    for di in range(DC):
                        last = t.transpose(ptr[:, di * 128:(di + 1) * 128],
                                           outst[:, di * 512 + sub * 128:di * 512 + (sub + 1) * 128],
                                           idb[:])
                    c.inc(last, "te")

                @c.on("vector")
                def _(v, ncx=ncx, sub=sub):
                    c.wait(v, "te")
                    c.inc(v.tensor_copy(atm[:], ptr[:, 0:1024]), "ve")

                @c.on("sync")
                def _(sy, tt=tt):
                    c.wait(sy, "se")
                    c.inc(sy.dma_start(out=xs[:], in_=x1_d[128 * tt:128 * (tt + 1), :]), "dma", 16)

                @c.on("vector")
                def _(v):
                    c.wait(v, "dma")

                ln_tile(c, atm[:], lnt[:], sq, st8, D, res=xs[:], res_out=x1t[:])

                @c.on("sync")
                def _(sy, tt=tt):
                    c.wait(sy, "ve")
                    c.inc(sy.dma_start(out=x2o[128 * tt:128 * (tt + 1), :], in_=x1t[:]), "st", 16)

        @c.on("sync")
        def _(sy):
            c.wait(sy, "st")
    return c


def build_l1pre():
    c = Ctx("l1pre")
    nc = c.nc
    xr = nc.dram_tensor("xr", [NTOK, D], F32, kind="ExternalInput").ap()
    Wdq = nc.dram_tensor("Wdq", [128, D], F32, kind="ExternalInput").ap()
    Wuq = nc.dram_tensor("Wuq", [D, 128], F32, kind="ExternalInput").ap()
    Wdkv = nc.dram_tensor("Wdkv", [128, D], F32, kind="ExternalInput").ap()
    Wuk = nc.dram_tensor("Wuk", [D, 128], F32, kind="ExternalInput").ap()
    ident = nc.dram_tensor("ident", [128, 128], BF16, kind="ExternalInput").ap()
    ql_o = nc.dram_tensor("ql_o", [128, 16 * NTOK], BF16, kind="ExternalOutput").ap()
    ckv_o = nc.dram_tensor("ckv_o", [NTOK, 128], BF16, kind="ExternalOutput").ap()
    with c:
        idb = c.sb("idb", [128, 128], BF16)
        xs = c.sb("xs", [128, D], F32)
        sq = c.sb("sq", [128, D], F32)
        st8 = c.sb("st8", [128, 8], F32)
        ytm = c.sb("ytm", [128, D], BF16)
        yfm = c.sb("yfm", [128, DC * NTOK], BF16)
        wdq = c.sb("wdq", [128, D], BF16)
        wdqT = c.sb("wdqT", [128, DC * 128], BF16)
        wuq = c.sb("wuq", [128, DC * 128], BF16)
        wuqT = c.sb("wuqT", [128, DC * 128], BF16)
        wdkv = c.sb("wdkv", [128, D], BF16)
        wdkvT = c.sb("wdkvT", [128, DC * 128], BF16)
        wuk = c.sb("wuk", [128, DC * 128], BF16)
        at = c.sb("at", [128, DC * D], BF16)
        keffs = c.sb("keffs", [128, DC * 128], BF16)
        t1 = c.sb("t1", [128, NTOK], BF16)
        qfm = c.sb("qfm", [128, DC * NTOK], BF16)
        qlb = c.sb("qlb", [128, 512], BF16)
        ckfm = c.sb("ckfm", [128, NTOK], BF16)
        outst = c.sb("outst", [128, 1024], BF16)
        ptr = c.ps("ptr", 1024, BF16)
        pj = [c.ps(f"pj{i}", 512) for i in range(4)]

        @c.on("gpsimd")
        def _(g):
            c.inc(g.dma_start(out=idb[:], in_=ident), "gd", 16)
            c.inc(g.dma_start(out=wdq[:], in_=Wdq), "gd", 16)
            c.inc(g.dma_start(out=wdkv[:], in_=Wdkv), "gd", 16)
            for cc in range(DC):
                c.inc(g.dma_start(out=wuq[:, cc * 128:(cc + 1) * 128],
                                  in_=Wuq[128 * cc:128 * (cc + 1), :]), "gd", 16)
                c.inc(g.dma_start(out=wuk[:, cc * 128:(cc + 1) * 128],
                                  in_=Wuk[128 * cc:128 * (cc + 1), :]), "gd", 16)

        @c.on("tensor")
        def _(t):
            c.wait(t, "gd")
            for cc in range(DC):
                last = t.transpose(ptr[:, cc * 128:(cc + 1) * 128],
                                   wdq[:, cc * 128:(cc + 1) * 128], idb[:])
            c.inc(last, "te")

        @c.on("vector")
        def _(v):
            c.wait(v, "te")
            c.inc(v.tensor_copy(wdqT[:], ptr[:, 0:1024]), "ve")

        @c.on("tensor")
        def _(t):
            c.wait(t, "ve")
            for cc in range(DC):
                last = t.transpose(ptr[:, cc * 128:(cc + 1) * 128],
                                   wuq[:, cc * 128:(cc + 1) * 128], idb[:])
            c.inc(last, "te")

        @c.on("vector")
        def _(v):
            c.wait(v, "te")
            c.inc(v.tensor_copy(wuqT[:], ptr[:, 0:1024]), "ve")

        @c.on("tensor")
        def _(t):
            c.wait(t, "ve")
            for cc in range(DC):
                last = t.transpose(ptr[:, cc * 128:(cc + 1) * 128],
                                   wdkv[:, cc * 128:(cc + 1) * 128], idb[:])
            c.inc(last, "te")

        @c.on("vector")
        def _(v):
            c.wait(v, "te")
            c.inc(v.tensor_copy(wdkvT[:], ptr[:, 0:1024]), "ve")

        for c2 in range(DC):
            @c.on("tensor")
            def _(t, c2=c2):
                c.wait(t, "ve")
                c.wait(t, "se")
                for half in range(2):
                    last = t.matmul(pj[half][:, 0:512], wuqT[:, c2 * 128:(c2 + 1) * 128],
                                    wdq[:, half * 512:(half + 1) * 512],
                                    start=True, stop=True)
                c.inc(last, "te")

            @c.on("scalar")
            def _(s, c2=c2):
                c.wait(s, "te")
                for half in range(2):
                    last = s.copy(at[:, c2 * D + half * 512:c2 * D + (half + 1) * 512],
                                  pj[half][:, 0:512])
                c.inc(last, "se")

        for cg in range(2):
            @c.on("tensor")
            def _(t, cg=cg):
                c.wait(t, "se")
                for cl in range(4):
                    cc = 4 * cg + cl
                    for c2 in range(DC):
                        last = t.matmul(pj[cl][0:128, 0:128],
                                        at[:, c2 * D + cc * 128:c2 * D + (cc + 1) * 128],
                                        wuk[:, c2 * 128:(c2 + 1) * 128],
                                        start=(c2 == 0), stop=(c2 == DC - 1))
                c.inc(last, "te")

            @c.on("scalar")
            def _(s, cg=cg):
                c.wait(s, "te")
                for cl in range(4):
                    cc = 4 * cg + cl
                    last = s.activation(keffs[:, cc * 128:(cc + 1) * 128],
                                        pj[cl][0:128, 0:128], AF.Copy, scale=0.125)
                c.inc(last, "se")

        for tt in range(NT):
            @c.on("sync")
            def _(sy, tt=tt):
                if tt > 0:
                    c.wait(sy, "se")
                c.inc(sy.dma_start(out=xs[:], in_=xr[128 * tt:128 * (tt + 1), :]), "dma", 16)

            @c.on("vector")
            def _(v):
                c.wait(v, "dma")

            ln_tile(c, xs[:], ytm[:], sq, st8, D)

            @c.on("tensor")
            def _(t, tt=tt):
                c.wait(t, "se")
                c.wait(t, "ve")
                for dc2 in range(DC):
                    last = t.transpose(ptr[:, dc2 * 128:(dc2 + 1) * 128],
                                       ytm[:, dc2 * 128:(dc2 + 1) * 128], idb[:])
                c.inc(last, "te")

            @c.on("vector")
            def _(v, tt=tt):
                c.wait(v, "te")
                for dc2 in range(DC):
                    last = v.tensor_copy(yfm[:, dc2 * NTOK + 128 * tt:dc2 * NTOK + 128 * (tt + 1)],
                                         ptr[:, dc2 * 128:(dc2 + 1) * 128])
                c.inc(last, "ve")

        for ncx in range(NC):
            @c.on("tensor")
            def _(t, ncx=ncx):
                c.wait(t, "ve")
                c.wait(t, "se")
                for di in range(DC):
                    t.matmul(pj[0][:, 0:512], wdqT[:, di * 128:(di + 1) * 128],
                             yfm[:, di * NTOK + 512 * ncx:di * NTOK + 512 * (ncx + 1)],
                             start=(di == 0), stop=(di == DC - 1))
                for di in range(DC):
                    last = t.matmul(pj[1][:, 0:512], wdkvT[:, di * 128:(di + 1) * 128],
                                    yfm[:, di * NTOK + 512 * ncx:di * NTOK + 512 * (ncx + 1)],
                                    start=(di == 0), stop=(di == DC - 1))
                c.inc(last, "te")

            @c.on("scalar")
            def _(s, ncx=ncx):
                c.wait(s, "te")
                s.copy(t1[:, 512 * ncx:512 * (ncx + 1)], pj[0][:, 0:512])
                c.inc(s.copy(ckfm[:, 512 * ncx:512 * (ncx + 1)], pj[1][:, 0:512]), "se")

            for cg in range(2):
                @c.on("tensor")
                def _(t, ncx=ncx, cg=cg):
                    c.wait(t, "se")
                    for cl in range(4):
                        cc = 4 * cg + cl
                        last = t.matmul(pj[cl][:, 0:512], wuqT[:, cc * 128:(cc + 1) * 128],
                                        t1[:, 512 * ncx:512 * (ncx + 1)],
                                        start=True, stop=True)
                    c.inc(last, "te")

                @c.on("scalar")
                def _(s, ncx=ncx, cg=cg):
                    c.wait(s, "te")
                    for cl in range(4):
                        cc = 4 * cg + cl
                        last = s.copy(qfm[:, cc * NTOK + 512 * ncx:cc * NTOK + 512 * (ncx + 1)],
                                      pj[cl][:, 0:512])
                    c.inc(last, "se")

        for h in range(16):
            for ncx in range(NC):
                @c.on("tensor")
                def _(t, h=h, ncx=ncx):
                    c.wait(t, "se")
                    c.wait(t, "st")
                    r0 = 64 * (h % 2)
                    cc = h // 2
                    last = t.matmul(pj[ncx % 4][:, 0:512],
                                    keffs[r0:r0 + 64, cc * 128:(cc + 1) * 128],
                                    qfm[r0:r0 + 64, cc * NTOK + 512 * ncx:cc * NTOK + 512 * (ncx + 1)],
                                    start=True, stop=True)
                    c.inc(last, "te")

                @c.on("scalar")
                def _(s, h=h, ncx=ncx):
                    c.wait(s, "te")
                    c.inc(s.copy(qlb[:, 0:512], pj[ncx % 4][:, 0:512]), "se")

                @c.on("sync")
                def _(sy, h=h, ncx=ncx):
                    c.wait(sy, "se")
                    c.inc(sy.dma_start(out=ql_o[0:128, h * NTOK + 512 * ncx:h * NTOK + 512 * (ncx + 1)],
                                       in_=qlb[:, 0:512]), "st", 16)

        for tt in range(NT):
            @c.on("tensor")
            def _(t, tt=tt):
                c.wait(t, "se")
                c.wait(t, "ve")
                c.inc(t.transpose(ptr[:, 0:128], ckfm[:, 128 * tt:128 * (tt + 1)], idb[:]), "te")

            @c.on("vector")
            def _(v, tt=tt):
                c.wait(v, "te")
                c.wait(v, "st")
                c.inc(v.tensor_copy(outst[:, 0:128], ptr[:, 0:128]), "ve")

            @c.on("sync")
            def _(sy, tt=tt):
                c.wait(sy, "ve")
                c.inc(sy.dma_start(out=ckv_o[128 * tt:128 * (tt + 1), :], in_=outst[:, 0:128]), "st", 16)

        @c.on("sync")
        def _(sy):
            c.wait(sy, "st")
    return c


def build_l1attn():
    c = Ctx("l1attn")
    nc = c.nc
    xr = nc.dram_tensor("xr", [NTOK, D], F32, kind="ExternalInput").ap()
    ql_i = nc.dram_tensor("ql_i", [128, 16 * NTOK], BF16, kind="ExternalInput").ap()
    ckvfm = nc.dram_tensor("ckvfm", [128, N], BF16, kind="ExternalInput").ap()
    ckvtm = nc.dram_tensor("ckvtm", [N, 128], BF16, kind="ExternalInput").ap()
    masks = nc.dram_tensor("masks", [NC * 128, 16 * 512], BF16, kind="ExternalInput").ap()
    Wuv = nc.dram_tensor("Wuv", [D, 128], F32, kind="ExternalInput").ap()
    Wo = nc.dram_tensor("Wo", [D, D], F32, kind="ExternalInput").ap()
    x3o = nc.dram_tensor("x3o", [NTOK, D], F32, kind="ExternalOutput").ap()
    wo_d = nc.dram_tensor("wo_d", [D, D], BF16)
    ctx_d = nc.dram_tensor("ctx_d", [NTOK, 16 * 128], BF16)
    y_d = nc.dram_tensor("y_d", [D, NTOK], BF16)
    NST = N // 128
    with c:
        qls = c.sb("qls", [128, 16 * NTOK], BF16)
        ckf = c.sb("ckf", [128, N], BF16)
        cke = c.sb("cke", [128, NST * 130], BF16)
        msk = c.sb("msk", [128, 16 * 512], BF16)
        wuv = c.sb("wuv", [128, DC * 128], BF16)
        wob = c.sb("wob", [128, D], BF16)
        vef = c.sb("vef", [128, D], BF16)
        pe = c.sb("pe", [128, 8 * 512], BF16)
        zc = c.sb("zc", [128, 1], F32)
        cstg = c.sb("cstg", [128, 128], BF16)
        ctxf = c.sb("ctxf", [128, NTOK], BF16)
        yst = c.sb("yst", [128, 512], BF16)
        xs = c.sb("xs", [128, D], F32)
        sq = c.sb("sq", [128, D], F32)
        st8 = c.sb("st8", [128, 8], F32)
        lnt = c.sb("lnt", [128, D], F32)
        x1t = c.sb("x1t", [128, D], F32)
        atm = c.sb("atm", [128, D], F32)
        pS = c.ps("pS", 1024)
        pC = [c.ps(f"pC{i}", 1024) for i in range(2)]

        def cOFF(ts):
            return (ts // 2) * 512 + (ts % 2) * 129

        @c.on("gpsimd")
        def _(g):
            for hq in range(4):
                c.inc(g.dma_start(out=qls[:, hq * 4 * NTOK:(hq + 1) * 4 * NTOK],
                                  in_=ql_i[0:128, hq * 4 * NTOK:(hq + 1) * 4 * NTOK]), "gd", 16)
            c.inc(g.dma_start(out=ckf[:], in_=ckvfm), "gd", 16)
            src = ckvtm.rearrange("(t p) k -> p t k", p=128)
            dst = cke[:].rearrange("p (t k) -> p t k", k=130)[:, :, 0:128]
            c.inc(g.dma_start(out=dst, in_=src), "gd", 16)
            for cc in range(DC):
                c.inc(g.dma_start(out=wuv[:, cc * 128:(cc + 1) * 128],
                                  in_=Wuv[128 * cc:128 * (cc + 1), :]), "gd", 16)

        @c.on("vector")
        def _(v):
            c.wait(v, "gd")
            ones_view = cke[:].rearrange("p (t k) -> p t k", k=130)[:, :, 128:129]
            c.inc(v.memset(ones_view, 1.0), "ve")

        for cc in range(DC):
            @c.on("gpsimd")
            def _(g, cc=cc):
                c.wait(g, "st")
                c.inc(g.dma_start(out=wob[:], in_=Wo[128 * cc:128 * (cc + 1), :]), "gd", 16)

            @c.on("sync")
            def _(sy, cc=cc):
                c.wait(sy, "gd")
                c.inc(sy.dma_start(out=wo_d[128 * cc:128 * (cc + 1), :], in_=wob[:]), "st", 16)

        for c2 in range(DC):
            @c.on("sync")
            def _(sy, c2=c2):
                c.wait(sy, "st")
                c.wait(sy, "te")
                c.inc(sy.dma_start(out=wob[:], in_=wo_d[:, 128 * c2:128 * (c2 + 1)],
                                   transpose=True), "dma", 16)

            @c.on("tensor")
            def _(t, c2=c2):
                c.wait(t, "dma")
                c.wait(t, "ve")
                for half in range(2):
                    last = t.matmul(pS[:, half * 512:(half + 1) * 512],
                                    wuv[:, c2 * 128:(c2 + 1) * 128],
                                    wob[:, half * 512:(half + 1) * 512],
                                    start=(c2 == 0), stop=(c2 == DC - 1))
                c.inc(last, "te")

        @c.on("vector")
        def _(v):
            c.wait(v, "te")
            c.inc(v.tensor_copy(vef[:], pS[:, 0:1024]), "ve")

        pS2 = c.ps("pS2", 1024)
        pe2off = 1024
        for k in range(NC):
            @c.on("sync")
            def _(sy, k=k):
                c.wait(sy, "ve")
                c.wait(sy, "te")
                c.inc(sy.dma_start(out=msk[:], in_=masks[k * 128:(k + 1) * 128, :]), "dma", 16)

            for hg in range(8):
                JMAX = 16 * k + 16
                se_ck = {}
                te_ck = {}
                ve_ck = {}

                def S_of(j):
                    return pS if j % 2 == 0 else pS2

                def emit_S(t, k, hg, j):
                    if j >= 2:
                        c.wait(t, "se", se_ck[j - 2])
                    for i in range(2):
                        h = 2 * hg + i
                        last = t.matmul(S_of(j)[:, i * 512:(i + 1) * 512],
                                        ckf[:, 128 * j:128 * (j + 1)],
                                        qls[:, h * NTOK + 512 * k:h * NTOK + 512 * (k + 1)],
                                        start=True, stop=True)
                    c.inc(last, "te")
                    te_ck[j] = c.cnt["te"]

                def emit_PV(t, k, hg, j, JMAX):
                    if j == 0:
                        c.wait(t, "ve")
                    if j >= 16 * k:
                        c.wait(t, "ve", ve_ck[j])
                    else:
                        c.wait(t, "se", se_ck[j])
                    for i in range(2):
                        for ts in range(4):
                            last = t.matmul(pC[i][:, cOFF(ts):cOFF(ts) + 129],
                                            pe[:, (j % 4) * pe2off + i * 512 + ts * 128:
                                               (j % 4) * pe2off + i * 512 + (ts + 1) * 128],
                                            cke[:, 130 * j:130 * j + 129],
                                            start=(j == 0 and ts % 2 == 0),
                                            stop=(j == JMAX - 1 and ts % 2 == 1))
                    c.inc(last, "te")

                for j in range(JMAX):
                    @c.on("tensor")
                    def _(t, k=k, hg=hg, j=j, JMAX=JMAX):
                        if j == 0:
                            c.wait(t, "dma")
                        if j >= 2:
                            emit_PV(t, k, hg, j - 2, JMAX)
                        emit_S(t, k, hg, j)

                    @c.on("scalar")
                    def _(s, k=k, hg=hg, j=j):
                        c.wait(s, "te", te_ck[j])
                        if j >= 2:
                            pass
                        c.inc(s.activation(pe[:, (j % 4) * pe2off:(j % 4) * pe2off + 1024],
                                           S_of(j)[:, 0:1024], AF.Exp), "se")
                        se_ck[j] = c.cnt["se"]

                    if j >= 16 * k:
                        @c.on("vector")
                        def _(v, k=k, hg=hg, j=j):
                            c.wait(v, "se", se_ck[j])
                            jj = j - 16 * k
                            for i in range(2):
                                last = v.tensor_tensor(pe[:, (j % 4) * pe2off + i * 512:
                                                          (j % 4) * pe2off + (i + 1) * 512],
                                                       pe[:, (j % 4) * pe2off + i * 512:
                                                          (j % 4) * pe2off + (i + 1) * 512],
                                                       msk[:, jj * 512:(jj + 1) * 512],
                                                       op=ALU.mult)
                            c.inc(last, "ve")
                            ve_ck[j] = c.cnt["ve"]

                @c.on("tensor")
                def _(t, k=k, hg=hg, JMAX=JMAX):
                    if JMAX >= 2:
                        emit_PV(t, k, hg, JMAX - 2, JMAX)
                    emit_PV(t, k, hg, JMAX - 1, JMAX)

                for i in range(2):
                    h = 2 * hg + i
                    for ts in range(4):
                        @c.on("vector")
                        def _(v, k=k, i=i, ts=ts, h=h):
                            c.wait(v, "te")
                            c.wait(v, "st")
                            c.inc(v.reciprocal(zc[:], pC[i][:, cOFF(ts) + 128:cOFF(ts) + 129]), "ve")
                            c.wait(v, "ve")
                            c.inc(v.tensor_scalar(cstg[:], pC[i][:, cOFF(ts):cOFF(ts) + 128],
                                                  zc[:], None, op0=ALU.mult), "ve")

                        @c.on("sync")
                        def _(sy, k=k, i=i, ts=ts, h=h):
                            c.wait(sy, "ve")
                            tt = 4 * k + ts
                            c.inc(sy.dma_start(out=ctx_d[128 * tt:128 * (tt + 1), 128 * h:128 * (h + 1)],
                                               in_=cstg[:]), "st", 16)

        for h in range(16):
            @c.on("sync")
            def _(sy, h=h):
                c.wait(sy, "te")
                c.inc(sy.dma_start(out=ctxf[:], in_=ctx_d[:, 128 * h:128 * (h + 1)],
                                   transpose=True), "dma", 16)

            for ncx in range(NC):
                @c.on("tensor")
                def _(t, h=h, ncx=ncx):
                    c.wait(t, "dma")
                    c.wait(t, "se")
                    c.inc(t.matmul(pS[0:64, 0:512], vef[:, 64 * h:64 * (h + 1)],
                                   ctxf[:, 512 * ncx:512 * (ncx + 1)], start=True, stop=True), "te")

                @c.on("scalar")
                def _(s, h=h, ncx=ncx):
                    c.wait(s, "te")
                    c.wait(s, "st")
                    c.inc(s.copy(yst[:, 0:0 + 512][0:64, :], pS[0:64, 0:512]), "se")

                @c.on("sync")
                def _(sy, h=h, ncx=ncx):
                    c.wait(sy, "se")
                    c.inc(sy.dma_start(out=y_d[64 * h:64 * (h + 1), 512 * ncx:512 * (ncx + 1)],
                                       in_=yst[0:64, 0:512]), "st", 16)

        ybt = c.sb("ybt", [128, D], BF16)
        for tt in range(NT):
            @c.on("sync")
            def _(sy, tt=tt):
                c.wait(sy, "ve")
                c.inc(sy.dma_start(out=ybt[:], in_=y_d[:, 128 * tt:128 * (tt + 1)],
                                   transpose=True), "dma", 16)
                c.inc(sy.dma_start(out=xs[:], in_=xr[128 * tt:128 * (tt + 1), :]), "dma", 16)

            @c.on("vector")
            def _(v, tt=tt):
                c.wait(v, "dma")
                c.inc(v.tensor_copy(atm[:], ybt[:]), "ve")

            ln_tile(c, atm[:], lnt[:], sq, st8, D, res=xs[:], res_out=x1t[:])

            @c.on("sync")
            def _(sy, tt=tt):
                c.wait(sy, "ve")
                c.inc(sy.dma_start(out=x3o[128 * tt:128 * (tt + 1), :], in_=x1t[:]), "st", 16)

        @c.on("sync")
        def _(sy):
            c.wait(sy, "st")
    return c


def build_l1fpre():
    c = Ctx("l1fpre")
    nc = c.nc
    xr = nc.dram_tensor("xr", [NTOK, D], F32, kind="ExternalInput").ap()
    qw = nc.dram_tensor("qw", [D, D], F32, kind="ExternalInput").ap()
    qb = nc.dram_tensor("qb", [1, D], F32, kind="ExternalInput").ap()
    kw = nc.dram_tensor("kw", [D, 512], F32, kind="ExternalInput").ap()
    kbi = nc.dram_tensor("kbi", [1, 512], F32, kind="ExternalInput").ap()
    vw = nc.dram_tensor("vw", [D, 512], F32, kind="ExternalInput").ap()
    vbi = nc.dram_tensor("vbi", [1, 512], F32, kind="ExternalInput").ap()
    ident = nc.dram_tensor("ident", [128, 128], BF16, kind="ExternalInput").ap()
    q_o = nc.dram_tensor("q_o", [D, NTOK], BF16, kind="ExternalOutput").ap()
    k_o = nc.dram_tensor("k_o", [512, NTOK], BF16, kind="ExternalOutput").ap()
    vt_o = nc.dram_tensor("vt_o", [NTOK, 512], BF16, kind="ExternalOutput").ap()
    with c:
        idb = c.sb("idb", [128, 128], BF16)
        xs = c.sb("xs", [128, D], F32)
        sq = c.sb("sq", [128, D], F32)
        st8 = c.sb("st8", [128, 8], F32)
        ytm = c.sb("ytm", [128, D], BF16)
        yfm = c.sb("yfm", [128, DC * NTOK], BF16)
        qwb = c.sb("qwb", [128, DC * D], BF16)
        kwb = c.sb("kwb", [128, DC * 512], BF16)
        vwb = c.sb("vwb", [128, DC * 512], BF16)
        brow = c.sb("brow", [1, 2 * D], BF16)
        onesone = c.sb("onesone", [1, 1], BF16)
        bcol = c.sb("bcol", [128, 16], F32)
        outst = c.sb("outst", [128, 8 * 512], BF16)
        ptr = c.ps("ptr", 1024, BF16)
        pb = c.ps("pb", 512)
        pj = [c.ps(f"pj{i}", 512) for i in range(4)]

        @c.on("gpsimd")
        def _(g):
            c.inc(g.dma_start(out=idb[:], in_=ident), "gd", 16)
            for dc in range(DC):
                c.inc(g.dma_start(out=qwb[:, dc * D:(dc + 1) * D],
                                  in_=qw[128 * dc:128 * (dc + 1), :]), "gd", 16)
                c.inc(g.dma_start(out=kwb[:, dc * 512:(dc + 1) * 512],
                                  in_=kw[128 * dc:128 * (dc + 1), :]), "gd", 16)
                c.inc(g.dma_start(out=vwb[:, dc * 512:(dc + 1) * 512],
                                  in_=vw[128 * dc:128 * (dc + 1), :]), "gd", 16)
            c.inc(g.dma_start(out=brow[:, 0:D], in_=qb), "gd", 16)
            c.inc(g.dma_start(out=brow[:, D:D + 512], in_=kbi), "gd", 16)
            c.inc(g.dma_start(out=brow[:, D + 512:D + 1024], in_=vbi), "gd", 16)

        @c.on("vector")
        def _(v):
            c.inc(v.memset(onesone[:], 1.0), "ve")

        @c.on("tensor")
        def _(t):
            c.wait(t, "gd")
            c.wait(t, "ve")
            for i in range(16):
                last = t.matmul(pb[:, i:i + 1], brow[:, 128 * i:128 * (i + 1)],
                                onesone[:], start=(i == 0), stop=(i == 15))
            c.inc(last, "te")

        @c.on("vector")
        def _(v):
            c.wait(v, "te")
            c.inc(v.tensor_copy(bcol[:], pb[:, 0:16]), "ve")

        for tt in range(NT):
            @c.on("sync")
            def _(sy, tt=tt):
                if tt > 0:
                    c.wait(sy, "se")
                c.inc(sy.dma_start(out=xs[:], in_=xr[128 * tt:128 * (tt + 1), :]), "dma", 16)

            @c.on("vector")
            def _(v):
                c.wait(v, "dma")

            ln_tile(c, xs[:], ytm[:], sq, st8, D)

            @c.on("tensor")
            def _(t, tt=tt):
                c.wait(t, "se")
                c.wait(t, "ve")
                for dc2 in range(DC):
                    last = t.transpose(ptr[:, dc2 * 128:(dc2 + 1) * 128],
                                       ytm[:, dc2 * 128:(dc2 + 1) * 128], idb[:])
                c.inc(last, "te")

            @c.on("vector")
            def _(v, tt=tt):
                c.wait(v, "te")
                for dc2 in range(DC):
                    last = v.tensor_copy(yfm[:, dc2 * NTOK + 128 * tt:dc2 * NTOK + 128 * (tt + 1)],
                                         ptr[:, dc2 * 128:(dc2 + 1) * 128])
                c.inc(last, "ve")

        for ncx in range(NC):
            for dg in range(2):
                @c.on("tensor")
                def _(t, ncx=ncx, dg=dg):
                    c.wait(t, "ve")
                    c.wait(t, "se")
                    for dl in range(4):
                        do = 4 * dg + dl
                        for di in range(DC):
                            last = t.matmul(pj[dl][:, 0:512],
                                            qwb[:, di * D + do * 128:di * D + (do + 1) * 128],
                                            yfm[:, di * NTOK + 512 * ncx:di * NTOK + 512 * (ncx + 1)],
                                            start=(di == 0), stop=(di == DC - 1))
                    c.inc(last, "te")

                @c.on("scalar")
                def _(s, ncx=ncx, dg=dg):
                    c.wait(s, "te")
                    c.wait(s, "st")
                    for dl in range(4):
                        do = 4 * dg + dl
                        last = s.activation(outst[:, do * 512:(do + 1) * 512], pj[dl][:, 0:512],
                                            AF.Identity, bias=bcol[:, do:do + 1])
                    c.inc(last, "se")

                @c.on("sync")
                def _(sy, ncx=ncx, dg=dg):
                    c.wait(sy, "se")
                    for dl in range(4):
                        do = 4 * dg + dl
                        c.inc(sy.dma_start(out=q_o[do * 128:(do + 1) * 128, 512 * ncx:512 * (ncx + 1)],
                                           in_=outst[:, do * 512:(do + 1) * 512]), "st", 16)

        for ncx in range(NC):
            @c.on("tensor")
            def _(t, ncx=ncx):
                c.wait(t, "ve")
                c.wait(t, "se")
                for dl in range(4):
                    for di in range(DC):
                        last = t.matmul(pj[dl][:, 0:512],
                                        kwb[:, di * 512 + dl * 128:di * 512 + (dl + 1) * 128],
                                        yfm[:, di * NTOK + 512 * ncx:di * NTOK + 512 * (ncx + 1)],
                                        start=(di == 0), stop=(di == DC - 1))
                c.inc(last, "te")

            @c.on("scalar")
            def _(s, ncx=ncx):
                c.wait(s, "te")
                c.wait(s, "st")
                for dl in range(4):
                    last = s.activation(outst[:, dl * 512:(dl + 1) * 512], pj[dl][:, 0:512],
                                        AF.Identity, bias=bcol[:, 8 + dl:9 + dl])
                c.inc(last, "se")

            @c.on("sync")
            def _(sy, ncx=ncx):
                c.wait(sy, "se")
                for dl in range(4):
                    c.inc(sy.dma_start(out=k_o[dl * 128:(dl + 1) * 128, 512 * ncx:512 * (ncx + 1)],
                                       in_=outst[:, dl * 512:(dl + 1) * 512]), "st", 16)

            @c.on("tensor")
            def _(t, ncx=ncx):
                c.wait(t, "ve")
                c.wait(t, "se")
                for dl in range(4):
                    for di in range(DC):
                        last = t.matmul(pj[dl][:, 0:512],
                                        vwb[:, di * 512 + dl * 128:di * 512 + (dl + 1) * 128],
                                        yfm[:, di * NTOK + 512 * ncx:di * NTOK + 512 * (ncx + 1)],
                                        start=(di == 0), stop=(di == DC - 1))
                c.inc(last, "te")

            @c.on("scalar")
            def _(s, ncx=ncx):
                c.wait(s, "te")
                c.wait(s, "st")
                for dl in range(4):
                    last = s.activation(outst[:, (4 + dl) * 512:(5 + dl) * 512], pj[dl][:, 0:512],
                                        AF.Identity, bias=bcol[:, 12 + dl:13 + dl])
                c.inc(last, "se")

            for sub in range(4):
                @c.on("tensor")
                def _(t, ncx=ncx, sub=sub):
                    c.wait(t, "se")
                    c.wait(t, "ve")
                    for dl in range(4):
                        last = t.transpose(ptr[:, dl * 128:(dl + 1) * 128],
                                           outst[:, (4 + dl) * 512 + sub * 128:(4 + dl) * 512 + (sub + 1) * 128],
                                           idb[:])
                    c.inc(last, "te")

                @c.on("vector")
                def _(v, ncx=ncx, sub=sub):
                    c.wait(v, "te")
                    c.wait(v, "st")
                    c.inc(v.tensor_copy(ytm[:, 0:512], ptr[:, 0:512]), "ve")

                @c.on("sync")
                def _(sy, ncx=ncx, sub=sub):
                    c.wait(sy, "ve")
                    c.inc(sy.dma_start(out=vt_o[512 * ncx + 128 * sub:512 * ncx + 128 * (sub + 1), :],
                                       in_=ytm[:, 0:512]), "st", 16)

        @c.on("sync")
        def _(sy):
            c.wait(sy, "st")
    return c


def build_l1fmain():
    c = Ctx("l1fmain")
    nc = c.nc
    xr = nc.dram_tensor("xr", [NTOK, D], F32, kind="ExternalInput").ap()
    qfm = nc.dram_tensor("qfm", [D, NTOK], BF16, kind="ExternalInput").ap()
    kf = nc.dram_tensor("kf", [512, N], BF16, kind="ExternalInput").ap()
    vtf = nc.dram_tensor("vtf", [N, 512], BF16, kind="ExternalInput").ap()
    ow = nc.dram_tensor("ow", [D, D], F32, kind="ExternalInput").ap()
    ob = nc.dram_tensor("ob", [1, D], F32, kind="ExternalInput").ap()
    rw = nc.dram_tensor("rw", [D, 8], F32, kind="ExternalInput").ap()
    rb = nc.dram_tensor("rb", [1, 8], F32, kind="ExternalInput").ap()
    w1 = nc.dram_tensor("w1", [8 * D, 2 * D], F32, kind="ExternalInput").ap()
    b1 = nc.dram_tensor("b1", [8, 2 * D], F32, kind="ExternalInput").ap()
    w2 = nc.dram_tensor("w2", [8 * 2 * D, D], F32, kind="ExternalInput").ap()
    b2 = nc.dram_tensor("b2", [8, D], F32, kind="ExternalInput").ap()
    identf = nc.dram_tensor("identf", [128, 128], F32, kind="ExternalInput").ap()
    x4o = nc.dram_tensor("x4o", [NTOK, D], F32, kind="ExternalOutput").ap()
    ao_d = nc.dram_tensor("ao_d", [NTOK, 512], BF16)
    NST = N // 128
    SCS = float(32) ** -0.5
    with c:
        idf = c.sb("idf", [128, 128], F32)
        ar1 = c.sb("ar1", [128, 16384], BF16)
        ar2 = c.sb("ar2", [128, 8192], BF16)
        ws1 = c.sb("ws1", [128, 16384], BF16)
        mla = c.sb("mla", [128, DC * NTOK], BF16)
        macc = c.sb("macc", [128, DC * NTOK], F32)
        aost = c.sb("aost", [128, 32], BF16)
        zc = c.sb("zc", [128, 1], F32)
        rws = c.sb("rws", [128, 64], BF16)
        rbrow = c.sb("rbrow", [1, 8], BF16)
        b1row = c.sb("b1row", [1, 2 * D], BF16)
        b2row = c.sb("b2row", [1, D], BF16)
        bcol = c.sb("bcol", [128, 24], F32)
        onesone = c.sb("onesone", [1, 1], BF16)
        onesbf = c.sb("onesbf", [1, 128], BF16)
        lg = c.sb("lg", [128, 128], F32)
        p8 = c.sb("p8", [128, 8], F32)
        s1 = c.sb("s1", [128, 1], F32)
        s2 = c.sb("s2", [128, 1], F32)
        s3 = c.sb("s3", [128, 1], F32)
        m8 = c.sb("m8", [128, 8], F32)
        comb = c.sb("comb", [128, 128], F32)
        cfb = c.sb("cfb", [1, NTOK], BF16)
        bct = c.sb("bct", [128, 512], BF16)
        xs = c.sb("xs", [128, D], F32)
        sq = c.sb("sq", [128, D], F32)
        st8 = c.sb("st8", [128, 8], F32)
        ffnt = c.sb("ffnt", [128, D], F32)
        lnt = c.sb("lnt", [128, D], F32)
        pA = c.ps("pA", 512)
        pjS = c.ps("pjS", 2048)
        pj = [pjS[:, 512 * i:512 * (i + 1)] for i in range(4)]

        KHS = 0
        QHS = 8192
        VES = 10240
        PES = 12416

        @c.on("gpsimd")
        def _(g):
            c.inc(g.dma_start(out=idf[:], in_=identf), "gd", 16)
            c.inc(g.dma_start(out=rbrow[:], in_=rb), "gd", 16)
            for di in range(DC):
                c.inc(g.dma_start(out=rws[:, di * 8:(di + 1) * 8],
                                  in_=rw[128 * di:128 * (di + 1), :]), "gd", 16)

        @c.on("vector")
        def _(v):
            v.memset(onesone[:], 1.0)
            c.inc(v.memset(onesbf[:], 1.0), "ve")

        for h in range(16):
            @c.on("gpsimd")
            def _(g, h=h):
                c.wait(g, "te")
                c.inc(g.dma_start(out=ar1[0:32, KHS:KHS + N], in_=kf[32 * h:32 * (h + 1), :]), "gd", 16)
                c.inc(g.dma_start(out=ar1[0:32, QHS:QHS + NTOK],
                                  in_=qfm[64 * h:64 * h + 32, :]), "gd", 16)
                vsrc = vtf.rearrange("(t p) k -> p t k", p=128)[:, :, 32 * h:32 * (h + 1)]
                vdst = ar1[:, VES:VES + NST * 34].rearrange("p (t k) -> p t k", k=34)[:, :, 0:32]
                c.inc(g.dma_start(out=vdst, in_=vsrc), "gd", 16)

            @c.on("vector")
            def _(v, h=h):
                c.wait(v, "gd")
                ones_view = ar1[:, VES:VES + NST * 34].rearrange("p (t k) -> p t k", k=34)[:, :, 32:33]
                c.inc(v.memset(ones_view, 1.0), "ve")

            for sc in range(NC):
                se_ck = {}
                te_ck = {}

                def emit_S(t, h, sc, tt):
                    if tt >= 4:
                        c.wait(t, "se", se_ck[(tt - 4) // 2 * 2])
                    c.inc(t.matmul(pjS[:, (tt % 4) * 512:(tt % 4) * 512 + 512],
                                   ar1[0:32, KHS + 128 * tt:KHS + 128 * (tt + 1)],
                                   ar1[0:32, QHS + 512 * sc:QHS + 512 * (sc + 1)],
                                   start=True, stop=True), "te")
                    te_ck[tt] = c.cnt["te"]

                def emit_PV(t, h, sc, tt):
                    if tt == 0:
                        c.wait(t, "ve")
                    c.wait(t, "se", se_ck[tt // 2 * 2])
                    for ts in range(4):
                        last = t.matmul(pA[:, ts * 33:ts * 33 + 33],
                                        ar1[:, PES + (tt % 4) * 512 + ts * 128:PES + (tt % 4) * 512 + (ts + 1) * 128],
                                        ar1[:, VES + 34 * tt:VES + 34 * tt + 33],
                                        start=(tt == 0 and ts == 0),
                                        stop=(tt == NST - 1 and ts == 3))
                    c.inc(last, "te")

                def emit_exp2(s, h, sc, tt):
                    c.wait(s, "te", te_ck[tt + 1])
                    base = (tt % 4) * 512
                    c.inc(s.activation(ar1[:, PES + base:PES + base + 1024],
                                       pjS[:, base:base + 1024], AF.Exp, scale=SCS), "se")
                    se_ck[tt] = c.cnt["se"]

                for tt in range(NST):
                    @c.on("tensor")
                    def _(t, h=h, sc=sc, tt=tt):
                        if tt == 0:
                            c.wait(t, "gd")
                            c.wait(t, "ve")
                            c.wait(t, "se")
                        if tt >= 2:
                            emit_PV(t, h, sc, tt - 2)
                        emit_S(t, h, sc, tt)

                    if tt % 2 == 1:
                        @c.on("scalar")
                        def _(s, h=h, sc=sc, tt=tt):
                            emit_exp2(s, h, sc, tt - 1)

                @c.on("tensor")
                def _(t, h=h, sc=sc):
                    emit_PV(t, h, sc, NST - 2)
                    emit_PV(t, h, sc, NST - 1)

                for ts in range(4):
                    @c.on("vector")
                    def _(v, h=h, sc=sc, ts=ts):
                        c.wait(v, "te")
                        c.wait(v, "st")
                        c.inc(v.reciprocal(zc[:], pA[:, ts * 33 + 32:ts * 33 + 33]), "ve")
                        c.wait(v, "ve")
                        c.inc(v.tensor_scalar(aost[:], pA[:, ts * 33:ts * 33 + 32],
                                              zc[:], None, op0=ALU.mult), "ve")

                    @c.on("sync")
                    def _(sy, h=h, sc=sc, ts=ts):
                        c.wait(sy, "ve")
                        tix = 4 * sc + ts
                        c.inc(sy.dma_start(out=ao_d[128 * tix:128 * (tix + 1), 32 * h:32 * (h + 1)],
                                           in_=aost[:]), "st", 16)

        for cc2 in range(4):
            @c.on("sync")
            def _(sy, cc2=cc2):
                c.wait(sy, "st")
                c.wait(sy, "te")
                c.inc(sy.dma_start(out=ar2[:, cc2 * NTOK:(cc2 + 1) * NTOK],
                                   in_=ao_d[:, 128 * cc2:128 * (cc2 + 1)], transpose=True), "dma", 16)

        @c.on("gpsimd")
        def _(g):
            for cc2 in range(4):
                c.inc(g.dma_start(out=ws1[:, cc2 * 1024:(cc2 + 1) * 1024],
                                  in_=ow[128 * cc2:128 * (cc2 + 1), :]), "gd", 16)
            c.inc(g.dma_start(out=b1row[:, 0:D], in_=ob), "gd", 16)

        @c.on("tensor")
        def _(t):
            c.wait(t, "gd")
            c.wait(t, "ve")
            for i in range(8):
                last = t.matmul(pj[3][:, i:i + 1], b1row[:, 128 * i:128 * (i + 1)],
                                onesone[:], start=(i == 0), stop=(i == 7))
            c.inc(last, "te")

        @c.on("vector")
        def _(v):
            c.wait(v, "te")
            c.inc(v.tensor_copy(bcol[:, 0:8], pj[3][:, 0:8]), "ve")

        for ncx in range(NC):
            for dg in range(2):
                @c.on("tensor")
                def _(t, ncx=ncx, dg=dg):
                    c.wait(t, "dma")
                    c.wait(t, "ve")
                    c.wait(t, "se")
                    for dl in range(4):
                        do = 4 * dg + dl
                        for cc2 in range(4):
                            last = t.matmul(pj[dl][:, 0:512],
                                            ws1[:, cc2 * 1024 + do * 128:cc2 * 1024 + (do + 1) * 128],
                                            ar2[:, cc2 * NTOK + 512 * ncx:cc2 * NTOK + 512 * (ncx + 1)],
                                            start=(cc2 == 0), stop=(cc2 == 3))
                    c.inc(last, "te")

                @c.on("scalar")
                def _(s, ncx=ncx, dg=dg):
                    c.wait(s, "te")
                    for dl in range(4):
                        do = 4 * dg + dl
                        last = s.activation(mla[:, do * NTOK + 512 * ncx:do * NTOK + 512 * (ncx + 1)],
                                            pj[dl][:, 0:512], AF.Identity, bias=bcol[:, do:do + 1])
                    c.inc(last, "se")

        for nt in range(NT):
            @c.on("tensor")
            def _(t, nt=nt):
                c.wait(t, "se")
                c.wait(t, "ve")
                for di in range(DC):
                    t.matmul(pj[0][:, nt * 8:(nt + 1) * 8],
                             mla[:, di * NTOK + 128 * nt:di * NTOK + 128 * (nt + 1)],
                             rws[:, di * 8:(di + 1) * 8],
                             start=(nt == 0 and di == 0), stop=False)
                last = t.matmul(pj[0][:, nt * 8:(nt + 1) * 8], onesbf[:], rbrow[:],
                                start=False, stop=(nt == NT - 1))
                c.inc(last, "te")

        @c.on("vector")
        def _(v):
            c.wait(v, "te")
            c.inc(v.tensor_copy(lg[:], pj[0][:, 0:128]), "ve")

        for nt in range(NT):
            sl = slice(nt * 8, (nt + 1) * 8)
            @c.on("vector")
            def _(v, nt=nt, sl=sl):
                c.wait(v, "ve")
                c.inc(v.reduce_max(s1[:], lg[:, sl], axis=AX), "ve")
                c.wait(v, "ve")
                c.inc(v.tensor_scalar(p8[:], lg[:, sl], s1[:], None, op0=ALU.subtract), "ve")

            @c.on("scalar")
            def _(s, nt=nt):
                c.wait(s, "ve")
                c.inc(s.activation(p8[:], p8[:], AF.Exp, accum_out=s2[:]), "se")

            @c.on("vector")
            def _(v, nt=nt):
                c.wait(v, "se")
                c.inc(v.reciprocal(s2[:], s2[:]), "ve")
                c.wait(v, "ve")
                c.inc(v.tensor_scalar(p8[:], p8[:], s2[:], None, op0=ALU.mult), "ve")
                c.wait(v, "ve")
                c.inc(v.reduce_max(s1[:], p8[:], axis=AX), "ve")
                c.wait(v, "ve")
                c.inc(v.tensor_scalar(m8[:], p8[:], s1[:], None, op0=ALU.is_ge), "ve")
                c.wait(v, "ve")
                c.inc(v.scalar_tensor_tensor(m8[:], m8[:], -2.0, p8[:], op0=ALU.mult, op1=ALU.add), "ve")
                c.wait(v, "ve")
                c.inc(v.reduce_max(s2[:], m8[:], axis=AX), "ve")
                c.wait(v, "ve")
                c.inc(v.tensor_scalar(m8[:], p8[:], s2[:], None, op0=ALU.is_ge), "ve")
                c.wait(v, "ve")
                c.inc(v.tensor_tensor(s1[:], s1[:], s2[:], op=ALU.add), "ve")
                c.wait(v, "ve")
                c.inc(v.tensor_scalar_add(s1[:], s1[:], 1e-8), "ve")
                c.wait(v, "ve")
                c.inc(v.reciprocal(s3[:], s1[:]), "ve")
                c.wait(v, "ve")
                c.inc(v.tensor_scalar_mul(s3[:], s3[:], 0.3), "ve")
                c.wait(v, "ve")
                c.inc(v.tensor_tensor(m8[:], m8[:], p8[:], op=ALU.mult), "ve")
                c.wait(v, "ve")
                c.inc(v.tensor_scalar(comb[:, nt * 8:(nt + 1) * 8], m8[:], s3[:], None,
                                      op0=ALU.mult), "ve")

        moe_gd = {}
        for e in range(8):
            @c.on("gpsimd")
            def _(g, e=e):
                c.wait(g, "te")
                for di in range(DC):
                    c.inc(g.dma_start(out=ws1[:, di * 2048:(di + 1) * 2048],
                                      in_=w1[e * D + 128 * di:e * D + 128 * (di + 1), :]), "gd", 16)
                c.inc(g.dma_start(out=b1row[:], in_=b1[e:e + 1, :]), "gd", 16)
                c.inc(g.dma_start(out=b2row[:], in_=b2[e:e + 1, :]), "gd", 16)
                moe_gd[("w1", e)] = c.cnt["gd"]

            @c.on("tensor")
            def _(t, e=e):
                c.wait(t, "gd")
                c.wait(t, "ve")
                for i in range(16):
                    last = t.matmul(pj[3][:, 64 + i:65 + i], b1row[:, 128 * i:128 * (i + 1)],
                                    onesone[:], start=(i == 0), stop=(i == 15))
                c.inc(last, "te")

            @c.on("vector")
            def _(v, e=e):
                c.wait(v, "te")
                c.inc(v.tensor_copy(bcol[:, 8:24], pj[3][:, 64:80]), "ve")

            for q4 in range(4):
                @c.on("tensor")
                def _(t, e=e, q4=q4):
                    c.wait(t, "ve")
                    c.wait(t, "se")
                    for i in range(4):
                        nt = 4 * q4 + i
                        last = t.matmul(pj[3][0:1, 128 * i:128 * (i + 1)],
                                        comb[:, nt * 8 + e:nt * 8 + e + 1], idf[:],
                                        start=(i == 0), stop=(i == 3))
                    c.inc(last, "te")

                @c.on("vector")
                def _(v, e=e, q4=q4):
                    c.wait(v, "te")
                    c.inc(v.tensor_copy(cfb[:, 512 * q4:512 * (q4 + 1)], pj[3][0:1, 0:512]), "ve")

            for ncx in range(NC):
                @c.on("tensor")
                def _(t, e=e, ncx=ncx):
                    c.wait(t, "ve")
                    c.wait(t, "se")
                    c.inc(t.matmul(pj[3][:, 0:512], onesbf[:], cfb[:, 512 * ncx:512 * (ncx + 1)],
                                   start=True, stop=True), "te")

                @c.on("vector")
                def _(v, e=e, ncx=ncx):
                    c.wait(v, "te")
                    c.inc(v.tensor_copy(bct[:], pj[3][:, 0:512]), "ve")

                h1_te = {}
                h1_se = {}

                def h1_mm(t, e, ncx, g):
                    if g >= 2:
                        c.wait(t, "se", h1_se[g - 2])
                    for hl in range(2):
                        hh = 2 * g + hl
                        bank = pj[(g % 2) * 2 + hl]
                        for di in range(DC):
                            last = t.matmul(bank[:, 0:512],
                                            ws1[:, di * 2048 + hh * 128:di * 2048 + (hh + 1) * 128],
                                            mla[:, di * NTOK + 512 * ncx:di * NTOK + 512 * (ncx + 1)],
                                            start=(di == 0), stop=(di == DC - 1))
                    c.inc(last, "te")
                    h1_te[g] = c.cnt["te"]

                def h1_gelu(s, e, ncx, g):
                    c.wait(s, "te", h1_te[g])
                    for hl in range(2):
                        hh = 2 * g + hl
                        bank = pj[(g % 2) * 2 + hl]
                        last = s.activation(ar2[:, hh * 512:(hh + 1) * 512], bank[:, 0:512],
                                            AF.Gelu, bias=bcol[:, 8 + hh:9 + hh])
                    c.inc(last, "se")
                    h1_se[g] = c.cnt["se"]

                for g in range(8):
                    @c.on("tensor")
                    def _(t, e=e, ncx=ncx, g=g):
                        if g == 0:
                            if ncx == 0:
                                c.wait(t, "gd", moe_gd[("w1", e)])
                            c.wait(t, "ve")
                            c.wait(t, "se")
                        h1_mm(t, e, ncx, g)

                    @c.on("scalar")
                    def _(s, e=e, ncx=ncx, g=g):
                        h1_gelu(s, e, ncx, g)

                if ncx == 0:
                    @c.on("gpsimd")
                    def _(g_, e=e):
                        c.wait(g_, "te", h1_te[0])
                        for hh in range(16):
                            c.inc(g_.dma_start(out=ar1[:, hh * 1024:(hh + 1) * 1024],
                                               in_=w2[e * 2048 + 128 * hh:e * 2048 + 128 * (hh + 1), :]),
                                  "gd", 16)
                        moe_gd[("w2", e)] = c.cnt["gd"]

                @c.on("vector")
                def _(v, e=e, ncx=ncx):
                    c.wait(v, "se")
                    for hh in range(16):
                        last = v.tensor_tensor(ar2[:, hh * 512:(hh + 1) * 512],
                                               ar2[:, hh * 512:(hh + 1) * 512], bct[:], op=ALU.mult)
                    c.inc(last, "ve")

                h2_te = {}
                h2_ve = {}

                def h2_mm(t, e, ncx, g):
                    if g >= 2:
                        c.wait(t, "ve", h2_ve[g - 2])
                    for dl in range(2):
                        do = 2 * g + dl
                        bank = pj[(g % 2) * 2 + dl]
                        for hh in range(16):
                            t.matmul(bank[:, 0:512],
                                     ar1[:, hh * 1024 + do * 128:hh * 1024 + (do + 1) * 128],
                                     ar2[:, hh * 512:(hh + 1) * 512],
                                     start=(hh == 0), stop=False)
                        last = t.matmul(bank[:, 0:512], b2row[:, do * 128:(do + 1) * 128],
                                        cfb[:, 512 * ncx:512 * (ncx + 1)],
                                        start=False, stop=True)
                    c.inc(last, "te")
                    h2_te[g] = c.cnt["te"]

                def h2_acc(v, e, ncx, g):
                    c.wait(v, "te", h2_te[g])
                    for dl in range(2):
                        do = 2 * g + dl
                        bank = pj[(g % 2) * 2 + dl]
                        dst = macc[:, do * NTOK + 512 * ncx:do * NTOK + 512 * (ncx + 1)]
                        if e == 0:
                            last = v.tensor_copy(dst, bank[:, 0:512])
                        else:
                            last = v.tensor_tensor(dst, dst, bank[:, 0:512], op=ALU.add)
                    c.inc(last, "ve")
                    h2_ve[g] = c.cnt["ve"]

                for g in range(4):
                    @c.on("tensor")
                    def _(t, e=e, ncx=ncx, g=g):
                        if g == 0:
                            c.wait(t, "gd", moe_gd[("w2", e)])
                            c.wait(t, "ve")
                            c.wait(t, "se")
                        h2_mm(t, e, ncx, g)

                    @c.on("vector")
                    def _(v, e=e, ncx=ncx, g=g):
                        h2_acc(v, e, ncx, g)

        for tt in range(NT):
            @c.on("sync")
            def _(sy, tt=tt):
                if tt > 0:
                    c.wait(sy, "ve")
                c.inc(sy.dma_start(out=xs[:], in_=xr[128 * tt:128 * (tt + 1), :]), "dma", 16)

            @c.on("vector")
            def _(v):
                c.wait(v, "dma")

            ln_tile(c, xs[:], lnt[:], sq, st8, D)

            for half in range(2):
                @c.on("tensor")
                def _(t, tt=tt, half=half):
                    c.wait(t, "ve")
                    c.wait(t, "se")
                    for q in range(4):
                        do = 4 * half + q
                        last = t.transpose(pj[q][:, 0:128],
                                           macc[:, do * NTOK + 128 * tt:do * NTOK + 128 * (tt + 1)],
                                           idf[:])
                    c.inc(last, "te")

                @c.on("vector")
                def _(v, tt=tt, half=half):
                    c.wait(v, "te")
                    for q in range(4):
                        do = 4 * half + q
                        last = v.tensor_copy(ffnt[:, do * 128:(do + 1) * 128], pj[q][:, 0:128])
                    c.inc(last, "ve")

            @c.on("vector")
            def _(v, tt=tt):
                c.wait(v, "se")
                c.wait(v, "ve")
                c.inc(v.tensor_tensor(ffnt[:], ffnt[:], lnt[:], op=ALU.add), "ve")

            ln_tile(c, ffnt[:], lnt[:], sq, st8, D, res=xs[:], res_out=sq[:])

            @c.on("sync")
            def _(sy, tt=tt):
                c.wait(sy, "ve")
                c.inc(sy.dma_start(out=x4o[128 * tt:128 * (tt + 1), :], in_=sq[:]), "st", 16)

        @c.on("sync")
        def _(sy):
            c.wait(sy, "st")
    return c


def rows_of_core(core):
    b, q = core // 4, core % 4
    idx = []
    for k in range(4):
        g = 4 * k + q
        idx.append((b, g * 512, (g + 1) * 512))
    return b, idx


def shard_rows(x):
    out = []
    for core in range(8):
        b, idx = rows_of_core(core)
        out.append(np.ascontiguousarray(
            np.concatenate([x[b, s:e] for (_, s, e) in idx], axis=0)))
    return out


def unshard_rows(shards, dtype=np.float32):
    x = np.empty((B, N, D), dtype)
    for core in range(8):
        b, idx = rows_of_core(core)
        for k, (_, s, e) in enumerate(idx):
            x[b, s:e] = shards[core][k * 512:(k + 1) * 512]
    return x


def assemble_full_cols(pieces):
    d = pieces[0].shape[0]
    full = [np.empty((d, N), pieces[0].dtype) for _ in range(B)]
    for core in range(8):
        b, idx = rows_of_core(core)
        for k, (_, s, e) in enumerate(idx):
            full[b][:, s:e] = pieces[core][:, k * 512:(k + 1) * 512]
    return full


def assemble_full_rows(pieces):
    d = pieces[0].shape[1]
    full = [np.empty((N, d), pieces[0].dtype) for _ in range(B)]
    for core in range(8):
        b, idx = rows_of_core(core)
        for k, (_, s, e) in enumerate(idx):
            full[b][s:e] = pieces[core][k * 512:(k + 1) * 512]
    return full


_IDENT = np.eye(128, dtype=np.float32).astype(BF)
_IDENTF = np.eye(128, dtype=np.float32)


def run_l0pre(x, wq, wk, wv):
    nc = build_l0pre()
    xs = shard_rows(x)
    maps = [{"xr": xs[i], "wq": wq, "wk": wk, "wv": wv, "ident": _IDENT,
             "identf": _IDENTF} for i in range(8)]
    res = run_bass_kernel_spmd(nc.nc, maps, core_ids=list(range(8)))
    return res.results


def my_cols(core):
    b, idx = rows_of_core(core)
    return b, [(s, e) for (_, s, e) in idx]


def run_l0attn(l1res):
    q_full = assemble_full_cols([np.asarray(r["q_fm"]) for r in l1res])
    nc = build_l0attn()
    maps = []
    for core in range(8):
        b = core // 4
        maps.append({"qf": np.ascontiguousarray(q_full[b]),
                     "kf": np.asarray(l1res[core]["k_fm"]),
                     "vt": np.asarray(l1res[core]["v_tm"])})
    res = run_bass_kernel_spmd(nc.nc, maps, core_ids=list(range(8)))
    return res.results


def run_l0red_ffn(x, l1res, l2res, wg, wu, wd):
    xs = shard_rows(x)
    nc = build_l0red_ffn()
    maps = []
    for core in range(8):
        b, cols = my_cols(core)
        ps = []
        for i in range(4):
            pfull = np.asarray(l2res[4 * b + i]["part"])
            ps.append(np.ascontiguousarray(
                np.concatenate([pfull[:, s:e] for (s, e) in cols], axis=1)))
        maps.append({"xr": xs[core], "p0": ps[0], "p1": ps[1], "p2": ps[2],
                     "p3": ps[3], "vfm": np.asarray(l1res[core]["v_fm"]),
                     "wg_i": wg, "wu_i": wu, "wd_i": wd, "ident": _IDENT})
    res = run_bass_kernel_spmd(nc.nc, maps, core_ids=list(range(8)))
    return res.results


def make_masks(core):
    q = core % 4
    out = np.zeros((NC * 128, 16 * 512), np.float32)
    for k in range(NC):
        t = (4 * k + q) * 512 + np.arange(512)[None, :]
        for jj in range(16):
            s = (16 * k + jj) * 128 + np.arange(128)[:, None]
            out[k * 128:(k + 1) * 128, jj * 512:(jj + 1) * 512] = (s <= t)
    return out.astype(BF)


_MASKS = None


def run_l1pre(x2, Wdq, Wuq, Wdkv, Wuk):
    xs = shard_rows(x2)
    nc = build_l1pre()
    maps = [{"xr": xs[i], "Wdq": Wdq, "Wuq": Wuq, "Wdkv": Wdkv, "Wuk": Wuk,
             "ident": _IDENT} for i in range(8)]
    res = run_bass_kernel_spmd(nc.nc, maps, core_ids=list(range(8)))
    return res.results


def run_l1attn(x2, l4res, Wuv, Wo):
    global _MASKS
    if _MASKS is None:
        _MASKS = [make_masks(i) for i in range(8)]
    xs = shard_rows(x2)
    ckv_full = assemble_full_rows([np.asarray(r["ckv_o"]) for r in l4res])
    nc = build_l1attn()
    maps = []
    for core in range(8):
        b = core // 4
        maps.append({"xr": xs[core], "ql_i": np.asarray(l4res[core]["ql_o"]),
                     "ckvfm": np.ascontiguousarray(ckv_full[b].T),
                     "ckvtm": ckv_full[b], "masks": _MASKS[core],
                     "Wuv": Wuv, "Wo": Wo})
    res = run_bass_kernel_spmd(nc.nc, maps, core_ids=list(range(8)))
    return res.results


def run_l1fpre(x3, qw, qb, kw, kb, vw, vb):
    xs = shard_rows(x3)
    nc = build_l1fpre()
    maps = [{"xr": xs[i], "qw": qw, "qb": qb.reshape(1, -1), "kw": kw,
             "kbi": kb.reshape(1, -1), "vw": vw, "vbi": vb.reshape(1, -1),
             "ident": _IDENT} for i in range(8)]
    res = run_bass_kernel_spmd(nc.nc, maps, core_ids=list(range(8)))
    return res.results


def run_l1fmain(x3, l6res, ow, ob, rw, rb, w1, b1, w2, b2):
    xs = shard_rows(x3)
    k_full = assemble_full_cols([np.asarray(r["k_o"]) for r in l6res])
    vt_full = assemble_full_rows([np.asarray(r["vt_o"]) for r in l6res])
    nc = build_l1fmain()
    maps = []
    for core in range(8):
        b = core // 4
        maps.append({"xr": xs[core], "qfm": np.asarray(l6res[core]["q_o"]),
                     "kf": np.ascontiguousarray(k_full[b]), "vtf": vt_full[b],
                     "ow": ow, "ob": ob.reshape(1, -1), "rw": rw, "rb": rb.reshape(1, -1),
                     "w1": w1.reshape(8 * D, 2 * D), "b1": b1,
                     "w2": w2.reshape(8 * 2 * D, D), "b2": b2, "identf": _IDENTF})
    res = run_bass_kernel_spmd(nc.nc, maps, core_ids=list(range(8)))
    return res.results


def kernel(**inputs):
    x = np.asarray(inputs["L0_attn_x"], np.float32)
    l1 = run_l0pre(x, inputs["L0_attn_wq"], inputs["L0_attn_wk"], inputs["L0_attn_wv"])
    l2 = run_l0attn(l1)
    l3 = run_l0red_ffn(x, l1, l2, inputs["L0_ffn_w_gate"], inputs["L0_ffn_w_up"],
                       inputs["L0_ffn_w_down"])
    x2 = unshard_rows([np.asarray(r["x2o"]) for r in l3])
    l4 = run_l1pre(x2, inputs["L1_attn_W_dq"], inputs["L1_attn_W_uq"],
                   inputs["L1_attn_W_dkv"], inputs["L1_attn_W_uk"])
    l5 = run_l1attn(x2, l4, inputs["L1_attn_W_uv"], inputs["L1_attn_W_o"])
    x3 = unshard_rows([np.asarray(r["x3o"]) for r in l5])
    l6 = run_l1fpre(x3, inputs["L1_ffn_q_w"], inputs["L1_ffn_q_b"], inputs["L1_ffn_k_w"],
                    inputs["L1_ffn_k_b"], inputs["L1_ffn_v_w"], inputs["L1_ffn_v_b"])
    l7 = run_l1fmain(x3, l6, inputs["L1_ffn_o_w"], inputs["L1_ffn_o_b"],
                     inputs["L1_ffn_router_w"], inputs["L1_ffn_router_b"],
                     inputs["L1_ffn_w1"], inputs["L1_ffn_b1"],
                     inputs["L1_ffn_w2"], inputs["L1_ffn_b2"])
    return unshard_rows([np.asarray(r["x4o"]) for r in l7])


# version 06
# speedup vs baseline: 1.2116x; 1.2116x over previous
import numpy as np
from contextlib import ExitStack
import concourse.bass as bass
import concourse.mybir as mybir
from concourse.bass_utils import run_bass_kernel_spmd
import ml_dtypes

BF = ml_dtypes.bfloat16
F32 = mybir.dt.float32
BF16 = mybir.dt.bfloat16
I32 = mybir.dt.int32
AF = mybir.ActivationFunctionType
ALU = mybir.AluOpType
AX = mybir.AxisListType.X

D = 1024
B = 2
N = 8192
NTOK = 2048
NT = NTOK // 128
NC = NTOK // 512
DC = D // 128


class Ctx:

    def __init__(self, name):
        self.name = name
        self.nc = bass.Bass("TRN2", target_bir_lowering=False, debug=False)
        self.es = ExitStack()
        self.sems = {}
        self.cnt = {}
        self.block = None

    def __enter__(self):
        self.es.__enter__()
        for s in ("dma", "gd", "te", "ve", "se", "st"):
            self.sems[s] = self.es.enter_context(self.nc.semaphore(s))
            self.cnt[s] = 0
        self.block = self.es.enter_context(self.nc.Block())
        return self

    def __exit__(self, *a):
        return self.es.__exit__(*a)

    def sb(self, name, shape, dtype):
        return self.es.enter_context(self.nc.sbuf_tensor(name, shape, dtype))

    def ps(self, name, cols=512, dtype=F32):
        return self.es.enter_context(self.nc.psum_tensor(name, [128, cols], dtype))

    def inc(self, inst, s, k=1):
        inst.then_inc(self.sems[s], k)
        self.cnt[s] += k

    def wait(self, eng, s, val=None):
        eng.wait_ge(self.sems[s], self.cnt[s] if val is None else val)

    def on(self, engname):
        def deco(fn):
            getattr(self.block, engname)(fn)
        return deco


def ln_tile(c, src, dst, sq, stats, Dfree, res=None, res_out=None):
    mu, m2, var, rstd, nmr = (stats[:, i:i + 1] for i in range(5))

    @c.on("vector")
    def _(v):
        c.inc(v.reduce_sum(mu, src, axis=AX), "ve")

    @c.on("scalar")
    def _(s):
        c.wait(s, "ve")
        c.inc(s.activation(sq[:, 0:src.shape[1]], src, AF.Square, accum_out=m2), "se")

    @c.on("vector")
    def _(v):
        c.wait(v, "se")
        v.tensor_scalar_mul(mu, mu, 1.0 / Dfree)
        c.inc(v.tensor_scalar_mul(m2, m2, 1.0 / Dfree), "ve")
        c.wait(v, "ve")
        c.inc(v.tensor_tensor(var, mu, mu, op=ALU.mult), "ve")
        c.wait(v, "ve")
        c.inc(v.tensor_tensor(var, m2, var, op=ALU.subtract), "ve")
        c.wait(v, "ve")
        c.inc(v.tensor_scalar_add(var, var, 1e-5), "ve")

    @c.on("scalar")
    def _(s):
        c.wait(s, "ve")
        c.inc(s.activation(rstd, var, AF.Sqrt), "se")

    @c.on("vector")
    def _(v):
        c.wait(v, "se")
        c.inc(v.reciprocal(rstd, rstd), "ve")
        c.wait(v, "ve")
        c.inc(v.scalar_tensor_tensor(nmr, mu, -1.0, rstd, op0=ALU.mult, op1=ALU.mult), "ve")

    @c.on("scalar")
    def _(s):
        c.wait(s, "ve")
        c.wait(s, "te")
        c.inc(s.activation(dst, src, AF.Identity, bias=nmr, scale=rstd), "se")

    if res is not None:
        @c.on("vector")
        def _(v):
            c.wait(v, "se")
            c.inc(v.tensor_tensor(res_out, res, dst, op=ALU.add), "ve")


def build_l0pre():
    c = Ctx("l0pre")
    nc = c.nc
    xr = nc.dram_tensor("xr", [NTOK, D], F32, kind="ExternalInput").ap()
    wq = nc.dram_tensor("wq", [D, D], F32, kind="ExternalInput").ap()
    wk = nc.dram_tensor("wk", [D, D], F32, kind="ExternalInput").ap()
    wv = nc.dram_tensor("wv", [D, D], F32, kind="ExternalInput").ap()
    ident = nc.dram_tensor("ident", [128, 128], BF16, kind="ExternalInput").ap()
    identf = nc.dram_tensor("identf", [128, 128], F32, kind="ExternalInput").ap()
    q_fm = nc.dram_tensor("q_fm", [D, NTOK], BF16, kind="ExternalOutput").ap()
    k_fm = nc.dram_tensor("k_fm", [D, NTOK], BF16, kind="ExternalOutput").ap()
    v_fm = nc.dram_tensor("v_fm", [D, NTOK], BF16, kind="ExternalOutput").ap()
    v_tm = nc.dram_tensor("v_tm", [NTOK, D], BF16, kind="ExternalOutput").ap()
    SCQ = float(N) ** -0.5
    with c:
        idb = c.sb("idb", [128, 128], BF16)
        idf = c.sb("idf", [128, 128], F32)
        xs = c.sb("xs", [128, D], F32)
        sq = c.sb("sq", [128, D], F32)
        st8 = c.sb("st8", [128, 8], F32)
        ytmf = c.sb("ytmf", [128, D], F32)
        ytm = c.sb("ytm", [128, D], BF16)
        yfm = c.sb("yfm", [128, DC * NTOK], BF16)
        ypfm = c.sb("ypfm", [128, DC * NTOK], BF16)
        wqb = c.sb("wqb", [128, DC * D], BF16)
        wkb = c.sb("wkb", [128, DC * D], BF16)
        ebf = c.sb("ebf", [128, DC * D], F32)
        eb = c.sb("eb", [128, DC * D], BF16)
        et = c.sb("et", [128, DC * D], BF16)
        csum = c.sb("csum", [128, 8], F32)
        rec = c.sb("rec", [128, 8], F32)
        recrow = c.sb("recrow", [1, D], F32)
        recbc = c.sb("recbc", [128, D], F32)
        srow = c.sb("srow", [1, NTOK], F32)
        srowb = c.sb("srowb", [1, NTOK], BF16)
        scol = c.sb("scol", [128, 1], F32)
        tmpf = c.sb("tmpf", [128, D], F32)
        negone = c.sb("negone", [128, 1], F32)
        onesb = c.sb("onesb", [1, 128], BF16)
        onesf = c.sb("onesf", [1, 128], F32)
        outst = c.sb("outst", [128, 8 * 512], BF16)
        ptr = c.ps("ptr", 1024, BF16)
        prow = c.ps("prow", 512)
        prb = c.ps("prb", 1024)
        pj = [c.ps(f"pj{i}", 512) for i in range(4)]

        @c.on("gpsimd")
        def _(g):
            c.inc(g.dma_start(out=idb[:], in_=ident), "gd", 16)
            c.inc(g.dma_start(out=idf[:], in_=identf), "gd", 16)
            for dc in range(DC):
                c.inc(g.dma_start(out=wqb[:, dc * D:(dc + 1) * D],
                                  in_=wq[128 * dc:128 * (dc + 1), :]), "gd", 16)
            for dc in range(DC):
                c.inc(g.dma_start(out=wkb[:, dc * D:(dc + 1) * D],
                                  in_=wk[128 * dc:128 * (dc + 1), :]), "gd", 16)
            for kb in range(DC):
                c.inc(g.dma_start(out=ebf[:, kb * D:(kb + 1) * D],
                                  in_=wv[128 * kb:128 * (kb + 1), :]), "gd", 16)

        @c.on("vector")
        def _(v):
            v.memset(onesb[:], 1.0)
            v.memset(onesf[:], 1.0)
            c.inc(v.memset(negone[:], -1.0), "ve")

        @c.on("scalar")
        def _(s):
            c.wait(s, "gd")
            c.wait(s, "ve")
            for kb in range(DC):
                c.inc(s.activation(ebf[:, kb * D:(kb + 1) * D],
                                   ebf[:, kb * D:(kb + 1) * D], AF.Exp), "se")
                c.wait(s, "se")
                c.inc(s.activation(eb[:, kb * D:(kb + 1) * D],
                                   ebf[:, kb * D:(kb + 1) * D], AF.Identity,
                                   bias=negone[:]), "se")

        for jb in range(8):
            @c.on("tensor")
            def _(t, jb=jb):
                c.wait(t, "se")
                c.wait(t, "ve")
                for kb in range(DC):
                    last = t.transpose(ptr[:, kb * 128:(kb + 1) * 128],
                                       eb[:, kb * D + jb * 128:kb * D + (jb + 1) * 128],
                                       idb[:])
                c.inc(last, "te")

            @c.on("vector")
            def _(v, jb=jb):
                c.wait(v, "te")
                c.inc(v.tensor_copy(et[:, jb * D:(jb + 1) * D], ptr[:, 0:1024]), "ve")

            @c.on("scalar")
            def _(s, jb=jb):
                c.wait(s, "ve")
                c.wait(s, "se")
                c.inc(s.activation(et[:, jb * D:(jb + 1) * D], et[:, jb * D:(jb + 1) * D],
                                   AF.Identity, accum_out=csum[:, jb:jb + 1]), "se")

        @c.on("vector")
        def _(v):
            c.wait(v, "se")
            c.inc(v.tensor_scalar_add(csum[:], csum[:], float(D)), "ve")
            c.wait(v, "ve")
            c.inc(v.reciprocal(rec[:], csum[:]), "ve")
            c.wait(v, "ve")
            c.inc(v.tensor_scalar_mul(rec[:], rec[:], 1.0 / D), "ve")

        for half in range(2):
            @c.on("tensor")
            def _(t, half=half):
                c.wait(t, "ve")
                for j in range(4):
                    jb = 4 * half + j
                    last = t.matmul(prow[0:1, 128 * j:128 * (j + 1)], rec[:, jb:jb + 1],
                                    idf[:], start=(j == 0), stop=(j == 3))
                c.inc(last, "te")

            @c.on("vector")
            def _(v, half=half):
                c.wait(v, "te")
                c.inc(v.tensor_copy(recrow[:, 512 * half:512 * (half + 1)],
                                    prow[0:1, 0:512]), "ve")

        @c.on("tensor")
        def _(t):
            c.wait(t, "ve")
            t.matmul(prb[:, 0:512], onesf[:], recrow[:, 0:512], start=True, stop=True)
            last = t.matmul(prb[:, 512:1024], onesf[:], recrow[:, 512:1024],
                            start=True, stop=True)
            c.inc(last, "te")

        @c.on("vector")
        def _(v):
            c.wait(v, "te")
            c.inc(v.tensor_copy(recbc[:], prb[:, 0:1024]), "ve")

        for tt in range(NT):
            @c.on("sync")
            def _(sy, tt=tt):
                if tt > 0:
                    c.wait(sy, "se")
                c.inc(sy.dma_start(out=xs[:], in_=xr[128 * tt:128 * (tt + 1), :]), "dma", 16)

            @c.on("vector")
            def _(v):
                c.wait(v, "dma")

            ln_tile(c, xs[:], ytmf[:], sq, st8, D)

            @c.on("vector")
            def _(v, tt=tt):
                c.wait(v, "se")
                c.inc(v.tensor_copy(ytm[:], ytmf[:]), "ve")
                c.inc(v.tensor_tensor(tmpf[:], ytmf[:], recbc[:], op=ALU.mult), "ve")
                c.wait(v, "ve")
                c.inc(v.reduce_sum(scol[:], tmpf[:], axis=AX), "ve")

            @c.on("tensor")
            def _(t, tt=tt):
                c.wait(t, "ve")
                for dc2 in range(DC):
                    last = t.transpose(ptr[:, dc2 * 128:(dc2 + 1) * 128],
                                       ytm[:, dc2 * 128:(dc2 + 1) * 128], idb[:])
                c.inc(last, "te")
                last = t.matmul(prow[0:1, 128 * (tt % 4):128 * (tt % 4 + 1)], scol[:],
                                idf[:], start=(tt % 4 == 0), stop=(tt % 4 == 3))
                c.inc(last, "te")

            @c.on("vector")
            def _(v, tt=tt):
                c.wait(v, "te")
                for dc2 in range(DC):
                    last = v.tensor_copy(yfm[:, dc2 * NTOK + 128 * tt:dc2 * NTOK + 128 * (tt + 1)],
                                         ptr[:, dc2 * 128:(dc2 + 1) * 128])
                c.inc(last, "ve")
                if tt % 4 == 3:
                    c.inc(v.tensor_copy(srow[:, 512 * (tt // 4):512 * (tt // 4 + 1)],
                                        prow[0:1, 0:512]), "ve")

        @c.on("vector")
        def _(v):
            c.wait(v, "ve")
            c.inc(v.tensor_copy(srowb[:], srow[:]), "ve")
            for jb in range(DC):
                last = v.tensor_scalar(ypfm[:, jb * NTOK:(jb + 1) * NTOK],
                                       yfm[:, jb * NTOK:(jb + 1) * NTOK],
                                       rec[:, jb:jb + 1], None, op0=ALU.mult)
            c.inc(last, "ve")

        def project(wtiles, rhs, out_dram, scale):
            for ncx in range(NC):
                for dg in range(2):
                    @c.on("tensor")
                    def _(t, ncx=ncx, dg=dg):
                        c.wait(t, "ve")
                        c.wait(t, "se")
                        for dl in range(4):
                            do = 4 * dg + dl
                            for di in range(DC):
                                last = t.matmul(pj[dl][:, 0:512], wtiles(di, do),
                                                rhs(di, ncx), start=(di == 0), stop=(di == DC - 1))
                        c.inc(last, "te")

                    @c.on("scalar")
                    def _(s, ncx=ncx, dg=dg):
                        c.wait(s, "te")
                        c.wait(s, "st")
                        for dl in range(4):
                            do = 4 * dg + dl
                            if scale == 1.0:
                                last = s.copy(outst[:, do * 512:(do + 1) * 512], pj[dl][:, 0:512])
                            else:
                                last = s.activation(outst[:, do * 512:(do + 1) * 512],
                                                    pj[dl][:, 0:512], AF.Copy, scale=scale)
                        c.inc(last, "se")

                    @c.on("sync")
                    def _(sy, ncx=ncx, dg=dg):
                        c.wait(sy, "se")
                        for dl in range(4):
                            do = 4 * dg + dl
                            c.inc(sy.dma_start(out=out_dram[do * 128:(do + 1) * 128, 512 * ncx:512 * (ncx + 1)],
                                               in_=outst[:, do * 512:(do + 1) * 512]), "st", 16)

        project(lambda di, do: wqb[:, di * D + do * 128:di * D + (do + 1) * 128],
                lambda di, ncx: yfm[:, di * NTOK + 512 * ncx:di * NTOK + 512 * (ncx + 1)],
                q_fm, SCQ)
        project(lambda di, do: wkb[:, di * D + do * 128:di * D + (do + 1) * 128],
                lambda di, ncx: yfm[:, di * NTOK + 512 * ncx:di * NTOK + 512 * (ncx + 1)],
                k_fm, 1.0)

        for ncx in range(NC):
            for dg in range(2):
                @c.on("tensor")
                def _(t, ncx=ncx, dg=dg):
                    c.wait(t, "ve")
                    c.wait(t, "se")
                    for dl in range(4):
                        do = 4 * dg + dl
                        for di in range(DC):
                            t.matmul(pj[dl][:, 0:512],
                                     et[:, di * D + do * 128:di * D + (do + 1) * 128],
                                     ypfm[:, di * NTOK + 512 * ncx:di * NTOK + 512 * (ncx + 1)],
                                     start=(di == 0), stop=False)
                        last = t.matmul(pj[dl][:, 0:512], onesb[:],
                                        srowb[:, 512 * ncx:512 * (ncx + 1)],
                                        start=False, stop=True)
                    c.inc(last, "te")

                @c.on("scalar")
                def _(s, ncx=ncx, dg=dg):
                    c.wait(s, "te")
                    c.wait(s, "st")
                    for dl in range(4):
                        do = 4 * dg + dl
                        last = s.copy(outst[:, do * 512:(do + 1) * 512], pj[dl][:, 0:512])
                    c.inc(last, "se")

                @c.on("sync")
                def _(sy, ncx=ncx, dg=dg):
                    c.wait(sy, "se")
                    for dl in range(4):
                        do = 4 * dg + dl
                        c.inc(sy.dma_start(out=v_fm[do * 128:(do + 1) * 128, 512 * ncx:512 * (ncx + 1)],
                                           in_=outst[:, do * 512:(do + 1) * 512]), "st", 16)

            for sub in range(4):
                @c.on("tensor")
                def _(t, ncx=ncx, sub=sub):
                    c.wait(t, "se")
                    c.wait(t, "ve")
                    for do in range(DC):
                        last = t.transpose(ptr[:, do * 128:(do + 1) * 128],
                                           outst[:, do * 512 + sub * 128:do * 512 + (sub + 1) * 128],
                                           idb[:])
                    c.inc(last, "te")

                @c.on("vector")
                def _(v, ncx=ncx, sub=sub):
                    c.wait(v, "te")
                    c.wait(v, "st")
                    c.inc(v.tensor_copy(ytm[:, 0:1024], ptr[:, 0:1024]), "ve")

                @c.on("sync")
                def _(sy, ncx=ncx, sub=sub):
                    c.wait(sy, "ve")
                    c.inc(sy.dma_start(out=v_tm[512 * ncx + 128 * sub:512 * ncx + 128 * (sub + 1), :],
                                       in_=ytm[:, 0:1024]), "st", 16)

        @c.on("sync")
        def _(sy):
            c.wait(sy, "st")
    return c


def _pstg2(pstg, stg, gi):
    buf = pstg if (gi // 2) % 2 == 0 else stg
    return buf[:, (gi % 2) * 512:(gi % 2 + 1) * 512]


def build_l0attn():
    c = Ctx("l0attn")
    nc = c.nc
    qf = nc.dram_tensor("qf", [D, N], BF16, kind="ExternalInput").ap()
    kf = nc.dram_tensor("kf", [D, NTOK], BF16, kind="ExternalInput").ap()
    vt = nc.dram_tensor("vt", [NTOK, D], BF16, kind="ExternalInput").ap()
    part = nc.dram_tensor("part", [D, N], BF16, kind="ExternalOutput").ap()
    pT = nc.dram_tensor("pT", [NTOK, N], BF16)
    NQ = N // 512
    with c:
        kb = c.sb("kb", [128, DC * NTOK], BF16)
        vtm = c.sb("vtm", [128, NT * D], BF16)
        vp = c.sb("vp", [128, NT * D], BF16)
        qs = c.sb("qs", [128, DC * 512], BF16)
        stg = c.sb("stg", [128, 2 * 512], BF16)
        pstg = c.sb("pstg", [128, 2 * 512], BF16)
        zparts = c.sb("zparts", [128, NT * NQ], F32)
        z1 = c.sb("z1", [128, 1], F32)
        outst = c.sb("outst", [128, 8 * 512], BF16)
        pj = [c.ps(f"pj{i}", 512) for i in range(8)]

        @c.on("gpsimd")
        def _(g):
            for di in range(DC):
                c.inc(g.dma_start(out=kb[:, di * NTOK:(di + 1) * NTOK],
                                  in_=kf[128 * di:128 * (di + 1), :]), "gd", 16)
            for mt in range(NT):
                c.inc(g.dma_start(out=vtm[:, mt * D:(mt + 1) * D],
                                  in_=vt[128 * mt:128 * (mt + 1), :]), "gd", 16)

        qs2 = c.sb("qs2", [128, DC * 512], BF16)
        stg2 = c.sb("stg2", [128, 2 * 512], BF16)
        sS_te = {}
        sS_se = {}
        sS_st = {}

        def qs_of(ncx):
            return qs if ncx % 2 == 0 else qs2

        def stgb(i):
            return stg[:, (i % 2) * 512:(i % 2 + 1) * 512] if (i // 2) % 2 == 0 \
                else stg2[:, (i % 2) * 512:(i % 2 + 1) * 512]

        stS0 = c.cnt["st"]
        for ncx in range(NQ):
            @c.on("sync")
            def _(sy, ncx=ncx):
                if ncx >= 2:
                    c.wait(sy, "te", sS_te[(ncx - 2, NT - 1)])
                for di in range(DC):
                    c.inc(sy.dma_start(out=qs_of(ncx)[:, di * 512:(di + 1) * 512],
                                       in_=qf[128 * di:128 * (di + 1), 512 * ncx:512 * (ncx + 1)]),
                          "dma", 16)
            sS_st[("qld", ncx)] = c.cnt["dma"]
            se0 = c.cnt["se"]

            def se_of(gi, se0=se0, ncx=ncx):
                return se0 + (gi - ncx * NT) + 1

            @c.on("tensor")
            def _(t, ncx=ncx, se0=se0):
                for mt in range(NT):
                    gi = ncx * NT + mt
                    if mt == 0:
                        c.wait(t, "dma", sS_st[("qld", ncx)])
                        if ncx == 0:
                            c.wait(t, "gd")
                    if gi >= 4:
                        if mt < 4:
                            c.wait(t, "se", sS_se[gi - 4])
                        else:
                            c.wait(t, "se", se_of(gi - 4))
                    for di in range(DC):
                        last = t.matmul(pj[gi % 4][:, 0:512],
                                        kb[:, di * NTOK + 128 * mt:di * NTOK + 128 * (mt + 1)],
                                        qs_of(ncx)[:, di * 512:(di + 1) * 512],
                                        start=(di == 0), stop=(di == DC - 1))
                    c.inc(last, "te")
                    sS_te[(ncx, mt)] = c.cnt["te"]

            @c.on("scalar")
            def _(s, ncx=ncx):
                for mt in range(NT):
                    gi = ncx * NT + mt
                    c.wait(s, "te", sS_te[(ncx, mt)])
                    if gi >= 4:
                        c.wait(s, "st", stS0 + 16 * (gi - 4 + 1))
                    c.inc(s.activation(stgb(gi),
                                       pj[gi % 4][:, 0:512], AF.Exp,
                                       accum_out=zparts[:, mt * NQ + ncx:mt * NQ + ncx + 1]), "se")
                    sS_se[gi] = c.cnt["se"]

            @c.on("sync")
            def _(sy, ncx=ncx):
                for mt in range(NT):
                    gi = ncx * NT + mt
                    c.wait(sy, "se", sS_se[gi])
                    c.inc(sy.dma_start(out=pT[128 * mt:128 * (mt + 1), 512 * ncx:512 * (ncx + 1)],
                                       in_=stgb(gi)), "st", 16)
                    sS_st[gi] = c.cnt["st"]

        for mt in range(NT):
            @c.on("vector")
            def _(v, mt=mt):
                c.wait(v, "se")
                c.inc(v.reduce_sum(z1[:], zparts[:, mt * NQ:(mt + 1) * NQ], axis=AX), "ve")
                c.wait(v, "ve")
                c.inc(v.reciprocal(z1[:], z1[:]), "ve")
                c.wait(v, "ve")
                c.inc(v.tensor_scalar(vp[:, mt * D:(mt + 1) * D], vtm[:, mt * D:(mt + 1) * D],
                                      z1[:], None, op0=ALU.mult), "ve")

        pB_dma = {}
        pB_te = {}
        pB_se = {}
        stB0 = None
        for half in range(2):
            for ncx in range(NQ):
                for mt in range(NT):
                    @c.on("sync")
                    def _(sy, half=half, ncx=ncx, mt=mt):
                        gi = (half * NQ + ncx) * NT + mt
                        if gi < 4:
                            c.wait(sy, "st")
                        else:
                            c.wait(sy, "te", pB_te[gi - 4])
                        c.inc(sy.dma_start(out=_pstg2(pstg, stg, gi),
                                           in_=pT[128 * mt:128 * (mt + 1), 512 * ncx:512 * (ncx + 1)]),
                              "dma", 16)
                        pB_dma[gi] = c.cnt["dma"]

                    @c.on("tensor")
                    def _(t, half=half, ncx=ncx, mt=mt):
                        gi = (half * NQ + ncx) * NT + mt
                        c.wait(t, "dma", pB_dma[gi])
                        if mt == 0:
                            if ncx == 0 and half == 0:
                                c.wait(t, "ve")
                            c.wait(t, "se")
                        for dl in range(4):
                            do = 4 * half + dl
                            last = t.matmul(pj[dl][:, 0:512],
                                            vp[:, mt * D + do * 128:mt * D + (do + 1) * 128],
                                            _pstg2(pstg, stg, gi),
                                            start=(mt == 0), stop=(mt == NT - 1))
                        c.inc(last, "te")
                        pB_te[gi] = c.cnt["te"]

                @c.on("scalar")
                def _(s, half=half, ncx=ncx):
                    c.wait(s, "te")
                    c.wait(s, "st", max(0, c.cnt["st"] - 16 * 64))
                    for dl in range(4):
                        do = 4 * half + dl
                        last = s.copy(outst[:, do * 512:(do + 1) * 512], pj[dl][:, 0:512])
                    c.inc(last, "se")

                @c.on("sync")
                def _(sy, half=half, ncx=ncx):
                    c.wait(sy, "se")
                    for dl in range(4):
                        do = 4 * half + dl
                        c.inc(sy.dma_start(out=part[do * 128:(do + 1) * 128, 512 * ncx:512 * (ncx + 1)],
                                           in_=outst[:, do * 512:(do + 1) * 512]), "st", 16)

        @c.on("sync")
        def _(sy):
            c.wait(sy, "st")
    return c


def build_l0red_ffn():
    c = Ctx("l0red")
    nc = c.nc
    xr = nc.dram_tensor("xr", [NTOK, D], F32, kind="ExternalInput").ap()
    p0 = nc.dram_tensor("p0", [D, NTOK], BF16, kind="ExternalInput").ap()
    p1 = nc.dram_tensor("p1", [D, NTOK], BF16, kind="ExternalInput").ap()
    p2 = nc.dram_tensor("p2", [D, NTOK], BF16, kind="ExternalInput").ap()
    p3 = nc.dram_tensor("p3", [D, NTOK], BF16, kind="ExternalInput").ap()
    vfm = nc.dram_tensor("vfm", [D, NTOK], BF16, kind="ExternalInput").ap()
    wg_i = nc.dram_tensor("wg_i", [3 * D, D], I32, kind="ExternalInput").ap()
    wu_i = nc.dram_tensor("wu_i", [3 * D, D], I32, kind="ExternalInput").ap()
    wd_i = nc.dram_tensor("wd_i", [D, 3 * D], I32, kind="ExternalInput").ap()
    ident = nc.dram_tensor("ident", [128, 128], BF16, kind="ExternalInput").ap()
    x2o = nc.dram_tensor("x2o", [NTOK, D], F32, kind="ExternalOutput").ap()
    x1_d = nc.dram_tensor("x1_d", [NTOK, D], F32)
    wgT_d = nc.dram_tensor("wgT_d", [D, 3 * D], BF16)
    wuT_d = nc.dram_tensor("wuT_d", [D, 3 * D], BF16)
    wdT_d = nc.dram_tensor("wdT_d", [3 * D, D], BF16)
    H3 = 3 * D
    HC = H3 // 128
    with c:
        idb = c.sb("idb", [128, 128], BF16)
        buf0 = c.sb("buf0", [128, DC * 512], BF16)
        buf1 = c.sb("buf1", [128, DC * 512], BF16)
        afm = c.sb("afm", [128, DC * 512], BF16)
        xs = c.sb("xs", [128, D], F32)
        sq = c.sb("sq", [128, D], F32)
        st8 = c.sb("st8", [128, 8], F32)
        lnt = c.sb("lnt", [128, D], F32)
        x1t = c.sb("x1t", [128, D], F32)
        atm = c.sb("atm", [128, D], F32)
        ytm = c.sb("ytm", [128, D], BF16)
        yfm = c.sb("yfm", [128, DC * NTOK], BF16)
        wi = c.sb("wi", [128, D], I32)
        wb = c.sb("wb", [128, D], BF16)
        ws = c.sb("ws", [128, 24 * 1024], BF16)
        gb = c.sb("gb", [128, HC * 512], BF16)
        ub = c.sb("ub", [128, HC * 512], BF16)
        outst = c.sb("outst", [128, 8 * 512], BF16)
        ptr = c.ps("ptr", 1024, BF16)
        pj = [c.ps(f"pj{i}", 512) for i in range(4)]

        @c.on("gpsimd")
        def _(g):
            c.inc(g.dma_start(out=idb[:], in_=ident), "gd", 16)

        for ncx in range(NC):
            @c.on("sync")
            def _(sy, ncx=ncx):
                c.wait(sy, "ve")
                for di in range(DC):
                    c.inc(sy.dma_start(out=buf0[:, di * 512:(di + 1) * 512],
                                       in_=p0[128 * di:128 * (di + 1), 512 * ncx:512 * (ncx + 1)]), "dma", 16)
                    c.inc(sy.dma_start(out=buf1[:, di * 512:(di + 1) * 512],
                                       in_=p1[128 * di:128 * (di + 1), 512 * ncx:512 * (ncx + 1)]), "dma", 16)

            @c.on("vector")
            def _(v, ncx=ncx):
                c.wait(v, "dma")
                c.inc(v.tensor_tensor(afm[:], buf0[:], buf1[:], op=ALU.add), "ve")

            @c.on("sync")
            def _(sy, ncx=ncx):
                c.wait(sy, "ve")
                for di in range(DC):
                    c.inc(sy.dma_start(out=buf0[:, di * 512:(di + 1) * 512],
                                       in_=p2[128 * di:128 * (di + 1), 512 * ncx:512 * (ncx + 1)]), "dma", 16)
                    c.inc(sy.dma_start(out=buf1[:, di * 512:(di + 1) * 512],
                                       in_=p3[128 * di:128 * (di + 1), 512 * ncx:512 * (ncx + 1)]), "dma", 16)

            @c.on("vector")
            def _(v, ncx=ncx):
                c.wait(v, "dma")
                c.inc(v.tensor_tensor(buf0[:], buf0[:], buf1[:], op=ALU.add), "ve")
                c.wait(v, "ve")
                c.inc(v.tensor_tensor(afm[:], afm[:], buf0[:], op=ALU.add), "ve")

            @c.on("sync")
            def _(sy, ncx=ncx):
                c.wait(sy, "ve")
                for di in range(DC):
                    c.inc(sy.dma_start(out=buf1[:, di * 512:(di + 1) * 512],
                                       in_=vfm[128 * di:128 * (di + 1), 512 * ncx:512 * (ncx + 1)]), "dma", 16)

            @c.on("vector")
            def _(v, ncx=ncx):
                c.wait(v, "dma")
                c.inc(v.tensor_tensor(afm[:], afm[:], buf1[:], op=ALU.subtract), "ve")

            for sub in range(4):
                tt = 4 * ncx + sub
                @c.on("tensor")
                def _(t, ncx=ncx, sub=sub):
                    c.wait(t, "ve")
                    for di in range(DC):
                        last = t.transpose(ptr[:, di * 128:(di + 1) * 128],
                                           afm[:, di * 512 + sub * 128:di * 512 + (sub + 1) * 128],
                                           idb[:])
                    c.inc(last, "te")

                @c.on("vector")
                def _(v, ncx=ncx, sub=sub):
                    c.wait(v, "te")
                    c.inc(v.tensor_copy(atm[:], ptr[:, 0:1024]), "ve")

                @c.on("sync")
                def _(sy, tt=tt):
                    c.wait(sy, "se")
                    c.inc(sy.dma_start(out=xs[:], in_=xr[128 * tt:128 * (tt + 1), :]), "dma", 16)

                @c.on("vector")
                def _(v):
                    c.wait(v, "dma")

                ln_tile(c, atm[:], lnt[:], sq, st8, D, res=xs[:], res_out=x1t[:])

                @c.on("sync")
                def _(sy, tt=tt):
                    c.wait(sy, "ve")
                    c.inc(sy.dma_start(out=x1_d[128 * tt:128 * (tt + 1), :], in_=x1t[:]), "st", 16)

        def prep_weight(w_i, rows, cols, wT_d):
            for rt in range(rows // 128):
                for ct in range(cols // 1024):
                    @c.on("sync")
                    def _(sy, rt=rt, ct=ct):
                        c.wait(sy, "ve")
                        c.inc(sy.dma_start(out=wi[:],
                                           in_=w_i[128 * rt:128 * (rt + 1), 1024 * ct:1024 * (ct + 1)]),
                              "dma", 16)

                    @c.on("vector")
                    def _(v):
                        c.wait(v, "dma")
                        c.wait(v, "te")
                        c.inc(v.tensor_scalar(wb[:], wi[:], 1.0 / 127.0, None, op0=ALU.mult), "ve")

                    @c.on("tensor")
                    def _(t):
                        c.wait(t, "ve")
                        c.wait(t, "st")
                        for dc2 in range(DC):
                            last = t.transpose(ptr[:, dc2 * 128:(dc2 + 1) * 128],
                                               wb[:, dc2 * 128:(dc2 + 1) * 128], idb[:])
                        c.inc(last, "te")

                    @c.on("vector")
                    def _(v):
                        c.wait(v, "te")
                        c.inc(v.tensor_copy(ytm[:, 0:1024], ptr[:, 0:1024]), "ve")

                    @c.on("sync")
                    def _(sy, rt=rt, ct=ct):
                        c.wait(sy, "ve")
                        for dc2 in range(DC):
                            c.inc(sy.dma_start(
                                out=wT_d[1024 * ct + 128 * dc2:1024 * ct + 128 * (dc2 + 1),
                                         128 * rt:128 * (rt + 1)],
                                in_=ytm[:, dc2 * 128:(dc2 + 1) * 128]), "st", 16)

        prep_weight(wg_i, H3, D, wgT_d)
        prep_weight(wu_i, H3, D, wuT_d)
        prep_weight(wd_i, D, H3, wdT_d)

        for tt in range(NT):
            @c.on("sync")
            def _(sy, tt=tt):
                if tt > 0:
                    c.wait(sy, "se")
                c.inc(sy.dma_start(out=xs[:], in_=x1_d[128 * tt:128 * (tt + 1), :]), "dma", 16)

            @c.on("vector")
            def _(v):
                c.wait(v, "dma")

            ln_tile(c, xs[:], ytm[:], sq, st8, D)

            @c.on("tensor")
            def _(t, tt=tt):
                c.wait(t, "se")
                c.wait(t, "ve")
                for dc2 in range(DC):
                    last = t.transpose(ptr[:, dc2 * 128:(dc2 + 1) * 128],
                                       ytm[:, dc2 * 128:(dc2 + 1) * 128], idb[:])
                c.inc(last, "te")

            @c.on("vector")
            def _(v, tt=tt):
                c.wait(v, "te")
                for dc2 in range(DC):
                    last = v.tensor_copy(yfm[:, dc2 * NTOK + 128 * tt:dc2 * NTOK + 128 * (tt + 1)],
                                         ptr[:, dc2 * 128:(dc2 + 1) * 128])
                c.inc(last, "ve")

        for ncx in range(NC):
            @c.on("sync")
            def _(sy, ncx=ncx):
                c.wait(sy, "te")
                for di in range(DC):
                    c.inc(sy.dma_start(out=ws[:, di * H3:(di + 1) * H3],
                                       in_=wgT_d[128 * di:128 * (di + 1), :]), "dma", 16)

            g_te = {}

            def gu_phase(ncx, dest, silu):
                se0 = c.cnt["se"]

                @c.on("tensor")
                def _(t, ncx=ncx, se0=se0, dest=dest):
                    for g in range(12):
                        if g == 0:
                            c.wait(t, "dma")
                            c.wait(t, "ve")
                            c.wait(t, "se", se0)
                        if g >= 2:
                            c.wait(t, "se", se0 + g - 1)
                        for hl in range(2):
                            hh = 2 * g + hl
                            bank = pj[(g % 2) * 2 + hl]
                            for di in range(DC):
                                last = t.matmul(bank[:, 0:512],
                                                ws[:, di * H3 + hh * 128:di * H3 + (hh + 1) * 128],
                                                yfm[:, di * NTOK + 512 * ncx:di * NTOK + 512 * (ncx + 1)],
                                                start=(di == 0), stop=(di == DC - 1))
                        c.inc(last, "te")
                        g_te[g] = c.cnt["te"]

                @c.on("scalar")
                def _(s, ncx=ncx, dest=dest, silu=silu):
                    for g in range(12):
                        c.wait(s, "te", g_te[g])
                        for hl in range(2):
                            hh = 2 * g + hl
                            bank = pj[(g % 2) * 2 + hl]
                            if silu:
                                last = s.activation(dest[:, hh * 512:(hh + 1) * 512],
                                                    bank[:, 0:512], AF.Silu)
                            else:
                                last = s.copy(dest[:, hh * 512:(hh + 1) * 512], bank[:, 0:512])
                        c.inc(last, "se")

            gu_phase(ncx, gb, True)

            @c.on("sync")
            def _(sy, ncx=ncx):
                c.wait(sy, "te")
                for di in range(DC):
                    c.inc(sy.dma_start(out=ws[:, di * H3:(di + 1) * H3],
                                       in_=wuT_d[128 * di:128 * (di + 1), :]), "dma", 16)

            gu_phase(ncx, ub, False)

            @c.on("vector")
            def _(v, ncx=ncx):
                c.wait(v, "se")
                for hh in range(HC):
                    last = v.tensor_tensor(gb[:, hh * 512:(hh + 1) * 512],
                                           gb[:, hh * 512:(hh + 1) * 512],
                                           ub[:, hh * 512:(hh + 1) * 512], op=ALU.mult)
                c.inc(last, "ve")

            @c.on("sync")
            def _(sy, ncx=ncx):
                c.wait(sy, "te")
                for hh in range(HC):
                    c.inc(sy.dma_start(out=ws[:, hh * 1024:(hh + 1) * 1024],
                                       in_=wdT_d[128 * hh:128 * (hh + 1), :]), "dma", 16)

            d_te = {}
            se0d = c.cnt["se"]

            @c.on("tensor")
            def _(t, ncx=ncx, se0d=se0d):
                for g in range(4):
                    if g == 0:
                        c.wait(t, "dma")
                        c.wait(t, "ve")
                        c.wait(t, "se", se0d)
                    if g >= 2:
                        c.wait(t, "se", se0d + g - 1)
                    for dl in range(2):
                        do = 2 * g + dl
                        bank = pj[(g % 2) * 2 + dl]
                        for hh in range(HC):
                            last = t.matmul(bank[:, 0:512],
                                            ws[:, hh * 1024 + do * 128:hh * 1024 + (do + 1) * 128],
                                            gb[:, hh * 512:(hh + 1) * 512],
                                            start=(hh == 0), stop=(hh == HC - 1))
                    c.inc(last, "te")
                    d_te[g] = c.cnt["te"]

            @c.on("scalar")
            def _(s, ncx=ncx):
                for g in range(4):
                    c.wait(s, "te", d_te[g])
                    for dl in range(2):
                        do = 2 * g + dl
                        bank = pj[(g % 2) * 2 + dl]
                        last = s.copy(outst[:, do * 512:(do + 1) * 512], bank[:, 0:512])
                    c.inc(last, "se")

            for sub in range(4):
                tt = 4 * ncx + sub
                @c.on("tensor")
                def _(t, ncx=ncx, sub=sub):
                    c.wait(t, "se")
                    c.wait(t, "ve")
                    for di in range(DC):
                        last = t.transpose(ptr[:, di * 128:(di + 1) * 128],
                                           outst[:, di * 512 + sub * 128:di * 512 + (sub + 1) * 128],
                                           idb[:])
                    c.inc(last, "te")

                @c.on("vector")
                def _(v, ncx=ncx, sub=sub):
                    c.wait(v, "te")
                    c.inc(v.tensor_copy(atm[:], ptr[:, 0:1024]), "ve")

                @c.on("sync")
                def _(sy, tt=tt):
                    c.wait(sy, "se")
                    c.inc(sy.dma_start(out=xs[:], in_=x1_d[128 * tt:128 * (tt + 1), :]), "dma", 16)

                @c.on("vector")
                def _(v):
                    c.wait(v, "dma")

                ln_tile(c, atm[:], lnt[:], sq, st8, D, res=xs[:], res_out=x1t[:])

                @c.on("sync")
                def _(sy, tt=tt):
                    c.wait(sy, "ve")
                    c.inc(sy.dma_start(out=x2o[128 * tt:128 * (tt + 1), :], in_=x1t[:]), "st", 16)

        @c.on("sync")
        def _(sy):
            c.wait(sy, "st")
    return c


def build_l1pre():
    c = Ctx("l1pre")
    nc = c.nc
    xr = nc.dram_tensor("xr", [NTOK, D], F32, kind="ExternalInput").ap()
    Wdq = nc.dram_tensor("Wdq", [128, D], F32, kind="ExternalInput").ap()
    Wuq = nc.dram_tensor("Wuq", [D, 128], F32, kind="ExternalInput").ap()
    Wdkv = nc.dram_tensor("Wdkv", [128, D], F32, kind="ExternalInput").ap()
    Wuk = nc.dram_tensor("Wuk", [D, 128], F32, kind="ExternalInput").ap()
    ident = nc.dram_tensor("ident", [128, 128], BF16, kind="ExternalInput").ap()
    ql_o = nc.dram_tensor("ql_o", [128, 16 * NTOK], BF16, kind="ExternalOutput").ap()
    ckv_o = nc.dram_tensor("ckv_o", [NTOK, 128], BF16, kind="ExternalOutput").ap()
    with c:
        idb = c.sb("idb", [128, 128], BF16)
        xs = c.sb("xs", [128, D], F32)
        sq = c.sb("sq", [128, D], F32)
        st8 = c.sb("st8", [128, 8], F32)
        ytm = c.sb("ytm", [128, D], BF16)
        yfm = c.sb("yfm", [128, DC * NTOK], BF16)
        wdq = c.sb("wdq", [128, D], BF16)
        wdqT = c.sb("wdqT", [128, DC * 128], BF16)
        wuq = c.sb("wuq", [128, DC * 128], BF16)
        wuqT = c.sb("wuqT", [128, DC * 128], BF16)
        wdkv = c.sb("wdkv", [128, D], BF16)
        wdkvT = c.sb("wdkvT", [128, DC * 128], BF16)
        wuk = c.sb("wuk", [128, DC * 128], BF16)
        at = c.sb("at", [128, DC * D], BF16)
        keffs = c.sb("keffs", [128, DC * 128], BF16)
        t1 = c.sb("t1", [128, NTOK], BF16)
        qfm = c.sb("qfm", [128, DC * NTOK], BF16)
        qlb = c.sb("qlb", [128, 512], BF16)
        ckfm = c.sb("ckfm", [128, NTOK], BF16)
        outst = c.sb("outst", [128, 1024], BF16)
        ptr = c.ps("ptr", 1024, BF16)
        pj = [c.ps(f"pj{i}", 512) for i in range(4)]

        @c.on("gpsimd")
        def _(g):
            c.inc(g.dma_start(out=idb[:], in_=ident), "gd", 16)
            c.inc(g.dma_start(out=wdq[:], in_=Wdq), "gd", 16)
            c.inc(g.dma_start(out=wdkv[:], in_=Wdkv), "gd", 16)
            for cc in range(DC):
                c.inc(g.dma_start(out=wuq[:, cc * 128:(cc + 1) * 128],
                                  in_=Wuq[128 * cc:128 * (cc + 1), :]), "gd", 16)
                c.inc(g.dma_start(out=wuk[:, cc * 128:(cc + 1) * 128],
                                  in_=Wuk[128 * cc:128 * (cc + 1), :]), "gd", 16)

        @c.on("tensor")
        def _(t):
            c.wait(t, "gd")
            for cc in range(DC):
                last = t.transpose(ptr[:, cc * 128:(cc + 1) * 128],
                                   wdq[:, cc * 128:(cc + 1) * 128], idb[:])
            c.inc(last, "te")

        @c.on("vector")
        def _(v):
            c.wait(v, "te")
            c.inc(v.tensor_copy(wdqT[:], ptr[:, 0:1024]), "ve")

        @c.on("tensor")
        def _(t):
            c.wait(t, "ve")
            for cc in range(DC):
                last = t.transpose(ptr[:, cc * 128:(cc + 1) * 128],
                                   wuq[:, cc * 128:(cc + 1) * 128], idb[:])
            c.inc(last, "te")

        @c.on("vector")
        def _(v):
            c.wait(v, "te")
            c.inc(v.tensor_copy(wuqT[:], ptr[:, 0:1024]), "ve")

        @c.on("tensor")
        def _(t):
            c.wait(t, "ve")
            for cc in range(DC):
                last = t.transpose(ptr[:, cc * 128:(cc + 1) * 128],
                                   wdkv[:, cc * 128:(cc + 1) * 128], idb[:])
            c.inc(last, "te")

        @c.on("vector")
        def _(v):
            c.wait(v, "te")
            c.inc(v.tensor_copy(wdkvT[:], ptr[:, 0:1024]), "ve")

        for c2 in range(DC):
            @c.on("tensor")
            def _(t, c2=c2):
                c.wait(t, "ve")
                c.wait(t, "se")
                for half in range(2):
                    last = t.matmul(pj[half][:, 0:512], wuqT[:, c2 * 128:(c2 + 1) * 128],
                                    wdq[:, half * 512:(half + 1) * 512],
                                    start=True, stop=True)
                c.inc(last, "te")

            @c.on("scalar")
            def _(s, c2=c2):
                c.wait(s, "te")
                for half in range(2):
                    last = s.copy(at[:, c2 * D + half * 512:c2 * D + (half + 1) * 512],
                                  pj[half][:, 0:512])
                c.inc(last, "se")

        for cg in range(2):
            @c.on("tensor")
            def _(t, cg=cg):
                c.wait(t, "se")
                for cl in range(4):
                    cc = 4 * cg + cl
                    for c2 in range(DC):
                        last = t.matmul(pj[cl][0:128, 0:128],
                                        at[:, c2 * D + cc * 128:c2 * D + (cc + 1) * 128],
                                        wuk[:, c2 * 128:(c2 + 1) * 128],
                                        start=(c2 == 0), stop=(c2 == DC - 1))
                c.inc(last, "te")

            @c.on("scalar")
            def _(s, cg=cg):
                c.wait(s, "te")
                for cl in range(4):
                    cc = 4 * cg + cl
                    last = s.activation(keffs[:, cc * 128:(cc + 1) * 128],
                                        pj[cl][0:128, 0:128], AF.Copy, scale=0.125)
                c.inc(last, "se")

        for tt in range(NT):
            @c.on("sync")
            def _(sy, tt=tt):
                if tt > 0:
                    c.wait(sy, "se")
                c.inc(sy.dma_start(out=xs[:], in_=xr[128 * tt:128 * (tt + 1), :]), "dma", 16)

            @c.on("vector")
            def _(v):
                c.wait(v, "dma")

            ln_tile(c, xs[:], ytm[:], sq, st8, D)

            @c.on("tensor")
            def _(t, tt=tt):
                c.wait(t, "se")
                c.wait(t, "ve")
                for dc2 in range(DC):
                    last = t.transpose(ptr[:, dc2 * 128:(dc2 + 1) * 128],
                                       ytm[:, dc2 * 128:(dc2 + 1) * 128], idb[:])
                c.inc(last, "te")

            @c.on("vector")
            def _(v, tt=tt):
                c.wait(v, "te")
                for dc2 in range(DC):
                    last = v.tensor_copy(yfm[:, dc2 * NTOK + 128 * tt:dc2 * NTOK + 128 * (tt + 1)],
                                         ptr[:, dc2 * 128:(dc2 + 1) * 128])
                c.inc(last, "ve")

        for ncx in range(NC):
            @c.on("tensor")
            def _(t, ncx=ncx):
                c.wait(t, "ve")
                c.wait(t, "se")
                for di in range(DC):
                    t.matmul(pj[0][:, 0:512], wdqT[:, di * 128:(di + 1) * 128],
                             yfm[:, di * NTOK + 512 * ncx:di * NTOK + 512 * (ncx + 1)],
                             start=(di == 0), stop=(di == DC - 1))
                for di in range(DC):
                    last = t.matmul(pj[1][:, 0:512], wdkvT[:, di * 128:(di + 1) * 128],
                                    yfm[:, di * NTOK + 512 * ncx:di * NTOK + 512 * (ncx + 1)],
                                    start=(di == 0), stop=(di == DC - 1))
                c.inc(last, "te")

            @c.on("scalar")
            def _(s, ncx=ncx):
                c.wait(s, "te")
                s.copy(t1[:, 512 * ncx:512 * (ncx + 1)], pj[0][:, 0:512])
                c.inc(s.copy(ckfm[:, 512 * ncx:512 * (ncx + 1)], pj[1][:, 0:512]), "se")

            for cg in range(2):
                @c.on("tensor")
                def _(t, ncx=ncx, cg=cg):
                    c.wait(t, "se")
                    for cl in range(4):
                        cc = 4 * cg + cl
                        last = t.matmul(pj[cl][:, 0:512], wuqT[:, cc * 128:(cc + 1) * 128],
                                        t1[:, 512 * ncx:512 * (ncx + 1)],
                                        start=True, stop=True)
                    c.inc(last, "te")

                @c.on("scalar")
                def _(s, ncx=ncx, cg=cg):
                    c.wait(s, "te")
                    for cl in range(4):
                        cc = 4 * cg + cl
                        last = s.copy(qfm[:, cc * NTOK + 512 * ncx:cc * NTOK + 512 * (ncx + 1)],
                                      pj[cl][:, 0:512])
                    c.inc(last, "se")

        for h in range(16):
            for ncx in range(NC):
                @c.on("tensor")
                def _(t, h=h, ncx=ncx):
                    c.wait(t, "se")
                    c.wait(t, "st")
                    r0 = 64 * (h % 2)
                    cc = h // 2
                    last = t.matmul(pj[ncx % 4][:, 0:512],
                                    keffs[r0:r0 + 64, cc * 128:(cc + 1) * 128],
                                    qfm[r0:r0 + 64, cc * NTOK + 512 * ncx:cc * NTOK + 512 * (ncx + 1)],
                                    start=True, stop=True)
                    c.inc(last, "te")

                @c.on("scalar")
                def _(s, h=h, ncx=ncx):
                    c.wait(s, "te")
                    c.inc(s.copy(qlb[:, 0:512], pj[ncx % 4][:, 0:512]), "se")

                @c.on("sync")
                def _(sy, h=h, ncx=ncx):
                    c.wait(sy, "se")
                    c.inc(sy.dma_start(out=ql_o[0:128, h * NTOK + 512 * ncx:h * NTOK + 512 * (ncx + 1)],
                                       in_=qlb[:, 0:512]), "st", 16)

        for tt in range(NT):
            @c.on("tensor")
            def _(t, tt=tt):
                c.wait(t, "se")
                c.wait(t, "ve")
                c.inc(t.transpose(ptr[:, 0:128], ckfm[:, 128 * tt:128 * (tt + 1)], idb[:]), "te")

            @c.on("vector")
            def _(v, tt=tt):
                c.wait(v, "te")
                c.wait(v, "st")
                c.inc(v.tensor_copy(outst[:, 0:128], ptr[:, 0:128]), "ve")

            @c.on("sync")
            def _(sy, tt=tt):
                c.wait(sy, "ve")
                c.inc(sy.dma_start(out=ckv_o[128 * tt:128 * (tt + 1), :], in_=outst[:, 0:128]), "st", 16)

        @c.on("sync")
        def _(sy):
            c.wait(sy, "st")
    return c


def build_l1attn():
    c = Ctx("l1attn")
    nc = c.nc
    xr = nc.dram_tensor("xr", [NTOK, D], F32, kind="ExternalInput").ap()
    ql_i = nc.dram_tensor("ql_i", [128, 16 * NTOK], BF16, kind="ExternalInput").ap()
    ckvfm = nc.dram_tensor("ckvfm", [128, N], BF16, kind="ExternalInput").ap()
    ckvtm = nc.dram_tensor("ckvtm", [N, 128], BF16, kind="ExternalInput").ap()
    masks = nc.dram_tensor("masks", [NC * 128, 16 * 512], BF16, kind="ExternalInput").ap()
    Wuv = nc.dram_tensor("Wuv", [D, 128], F32, kind="ExternalInput").ap()
    Wo = nc.dram_tensor("Wo", [D, D], F32, kind="ExternalInput").ap()
    x3o = nc.dram_tensor("x3o", [NTOK, D], F32, kind="ExternalOutput").ap()
    wo_d = nc.dram_tensor("wo_d", [D, D], BF16)
    ctx_d = nc.dram_tensor("ctx_d", [NTOK, 16 * 128], BF16)
    y_d = nc.dram_tensor("y_d", [D, NTOK], BF16)
    NST = N // 128
    with c:
        qls = c.sb("qls", [128, 16 * NTOK], BF16)
        ckf = c.sb("ckf", [128, N], BF16)
        cke = c.sb("cke", [128, NST * 130], BF16)
        msk = c.sb("msk", [128, 16 * 512], BF16)
        wuv = c.sb("wuv", [128, DC * 128], BF16)
        wob = c.sb("wob", [128, D], BF16)
        vef = c.sb("vef", [128, D], BF16)
        pe = c.sb("pe", [128, 8 * 512], BF16)
        zc = c.sb("zc", [128, 1], F32)
        cstg = c.sb("cstg", [128, 128], BF16)
        ctxf = c.sb("ctxf", [128, NTOK], BF16)
        yst = c.sb("yst", [128, 512], BF16)
        xs = c.sb("xs", [128, D], F32)
        sq = c.sb("sq", [128, D], F32)
        st8 = c.sb("st8", [128, 8], F32)
        lnt = c.sb("lnt", [128, D], F32)
        x1t = c.sb("x1t", [128, D], F32)
        atm = c.sb("atm", [128, D], F32)
        pS = c.ps("pS", 1024)
        pC = [c.ps(f"pC{i}", 1024) for i in range(2)]

        def cOFF(ts):
            return (ts // 2) * 512 + (ts % 2) * 129

        @c.on("gpsimd")
        def _(g):
            for hq in range(4):
                c.inc(g.dma_start(out=qls[:, hq * 4 * NTOK:(hq + 1) * 4 * NTOK],
                                  in_=ql_i[0:128, hq * 4 * NTOK:(hq + 1) * 4 * NTOK]), "gd", 16)
            c.inc(g.dma_start(out=ckf[:], in_=ckvfm), "gd", 16)
            src = ckvtm.rearrange("(t p) k -> p t k", p=128)
            dst = cke[:].rearrange("p (t k) -> p t k", k=130)[:, :, 0:128]
            c.inc(g.dma_start(out=dst, in_=src), "gd", 16)
            for cc in range(DC):
                c.inc(g.dma_start(out=wuv[:, cc * 128:(cc + 1) * 128],
                                  in_=Wuv[128 * cc:128 * (cc + 1), :]), "gd", 16)

        @c.on("vector")
        def _(v):
            c.wait(v, "gd")
            ones_view = cke[:].rearrange("p (t k) -> p t k", k=130)[:, :, 128:129]
            c.inc(v.memset(ones_view, 1.0), "ve")

        for cc in range(DC):
            @c.on("gpsimd")
            def _(g, cc=cc):
                c.wait(g, "st")
                c.inc(g.dma_start(out=wob[:], in_=Wo[128 * cc:128 * (cc + 1), :]), "gd", 16)

            @c.on("sync")
            def _(sy, cc=cc):
                c.wait(sy, "gd")
                c.inc(sy.dma_start(out=wo_d[128 * cc:128 * (cc + 1), :], in_=wob[:]), "st", 16)

        for c2 in range(DC):
            @c.on("sync")
            def _(sy, c2=c2):
                c.wait(sy, "st")
                c.wait(sy, "te")
                c.inc(sy.dma_start(out=wob[:], in_=wo_d[:, 128 * c2:128 * (c2 + 1)],
                                   transpose=True), "dma", 16)

            @c.on("tensor")
            def _(t, c2=c2):
                c.wait(t, "dma")
                c.wait(t, "ve")
                for half in range(2):
                    last = t.matmul(pS[:, half * 512:(half + 1) * 512],
                                    wuv[:, c2 * 128:(c2 + 1) * 128],
                                    wob[:, half * 512:(half + 1) * 512],
                                    start=(c2 == 0), stop=(c2 == DC - 1))
                c.inc(last, "te")

        @c.on("vector")
        def _(v):
            c.wait(v, "te")
            c.inc(v.tensor_copy(vef[:], pS[:, 0:1024]), "ve")

        pS2 = c.ps("pS2", 1024)
        pe2off = 1024
        for k in range(NC):
            dma_k = None

            @c.on("sync")
            def _(sy, k=k):
                c.wait(sy, "ve")
                c.wait(sy, "te")
                c.inc(sy.dma_start(out=msk[:], in_=masks[k * 128:(k + 1) * 128, :]), "dma", 16)
            dma_k = c.cnt["dma"]

            for hg in range(8):
                JMAX = 16 * k + 16
                te_ck = {}
                se0 = c.cnt["se"]
                ve0 = c.cnt["ve"]

                def se_of(j):
                    return se0 + j + 1

                def ve_of(j, k=k):
                    return ve0 + (j - 16 * k) + 1

                def S_of(j):
                    return pS if j % 2 == 0 else pS2

                def pv_wait_emit(t, k, hg, j, JMAX):
                    if j == 0:
                        c.wait(t, "ve", ve0)
                    if j >= 16 * k:
                        c.wait(t, "ve", ve_of(j))
                    else:
                        c.wait(t, "se", se_of(j))
                    for i in range(2):
                        for ts in range(4):
                            last = t.matmul(pC[i][:, cOFF(ts):cOFF(ts) + 129],
                                            pe[:, (j % 4) * pe2off + i * 512 + ts * 128:
                                               (j % 4) * pe2off + i * 512 + (ts + 1) * 128],
                                            cke[:, 130 * j:130 * j + 129],
                                            start=(j == 0 and ts % 2 == 0),
                                            stop=(j == JMAX - 1 and ts % 2 == 1))
                    c.inc(last, "te")

                @c.on("tensor")
                def _(t, k=k, hg=hg, JMAX=JMAX):
                    for j in range(JMAX):
                        if j == 0:
                            c.wait(t, "dma", dma_k)
                        if j >= 2:
                            pv_wait_emit(t, k, hg, j - 2, JMAX)
                        if j >= 2:
                            c.wait(t, "se", se_of(j - 2))
                        for i in range(2):
                            h = 2 * hg + i
                            last = t.matmul(S_of(j)[:, i * 512:(i + 1) * 512],
                                            ckf[:, 128 * j:128 * (j + 1)],
                                            qls[:, h * NTOK + 512 * k:h * NTOK + 512 * (k + 1)],
                                            start=True, stop=True)
                        c.inc(last, "te")
                        te_ck[j] = c.cnt["te"]
                    if JMAX >= 2:
                        pv_wait_emit(t, k, hg, JMAX - 2, JMAX)
                    pv_wait_emit(t, k, hg, JMAX - 1, JMAX)

                @c.on("scalar")
                def _(s, k=k, hg=hg, JMAX=JMAX):
                    for j in range(JMAX):
                        c.wait(s, "te", te_ck[j])
                        c.inc(s.activation(pe[:, (j % 4) * pe2off:(j % 4) * pe2off + 1024],
                                           S_of(j)[:, 0:1024], AF.Exp), "se")
                assert c.cnt["se"] == se0 + JMAX

                @c.on("vector")
                def _(v, k=k, hg=hg, JMAX=JMAX):
                    for j in range(16 * k, JMAX):
                        jj = j - 16 * k
                        c.wait(v, "se", se_of(j))
                        for i in range(2):
                            last = v.tensor_tensor(pe[:, (j % 4) * pe2off + i * 512:
                                                      (j % 4) * pe2off + (i + 1) * 512],
                                                   pe[:, (j % 4) * pe2off + i * 512:
                                                      (j % 4) * pe2off + (i + 1) * 512],
                                                   msk[:, jj * 512:(jj + 1) * 512],
                                                   op=ALU.mult)
                        c.inc(last, "ve")
                assert c.cnt["ve"] == ve0 + 16

                for i in range(2):
                    h = 2 * hg + i
                    for ts in range(4):
                        @c.on("vector")
                        def _(v, k=k, i=i, ts=ts, h=h):
                            c.wait(v, "te")
                            c.wait(v, "st")
                            c.inc(v.reciprocal(zc[:], pC[i][:, cOFF(ts) + 128:cOFF(ts) + 129]), "ve")
                            c.wait(v, "ve")
                            c.inc(v.tensor_scalar(cstg[:], pC[i][:, cOFF(ts):cOFF(ts) + 128],
                                                  zc[:], None, op0=ALU.mult), "ve")

                        @c.on("sync")
                        def _(sy, k=k, i=i, ts=ts, h=h):
                            c.wait(sy, "ve")
                            tt = 4 * k + ts
                            c.inc(sy.dma_start(out=ctx_d[128 * tt:128 * (tt + 1), 128 * h:128 * (h + 1)],
                                               in_=cstg[:]), "st", 16)

        for h in range(16):
            @c.on("sync")
            def _(sy, h=h):
                c.wait(sy, "te")
                c.inc(sy.dma_start(out=ctxf[:], in_=ctx_d[:, 128 * h:128 * (h + 1)],
                                   transpose=True), "dma", 16)

            for ncx in range(NC):
                @c.on("tensor")
                def _(t, h=h, ncx=ncx):
                    c.wait(t, "dma")
                    c.wait(t, "se")
                    c.inc(t.matmul(pS[0:64, 0:512], vef[:, 64 * h:64 * (h + 1)],
                                   ctxf[:, 512 * ncx:512 * (ncx + 1)], start=True, stop=True), "te")

                @c.on("scalar")
                def _(s, h=h, ncx=ncx):
                    c.wait(s, "te")
                    c.wait(s, "st")
                    c.inc(s.copy(yst[:, 0:0 + 512][0:64, :], pS[0:64, 0:512]), "se")

                @c.on("sync")
                def _(sy, h=h, ncx=ncx):
                    c.wait(sy, "se")
                    c.inc(sy.dma_start(out=y_d[64 * h:64 * (h + 1), 512 * ncx:512 * (ncx + 1)],
                                       in_=yst[0:64, 0:512]), "st", 16)

        ybt = c.sb("ybt", [128, D], BF16)
        for tt in range(NT):
            @c.on("sync")
            def _(sy, tt=tt):
                c.wait(sy, "ve")
                c.inc(sy.dma_start(out=ybt[:], in_=y_d[:, 128 * tt:128 * (tt + 1)],
                                   transpose=True), "dma", 16)
                c.inc(sy.dma_start(out=xs[:], in_=xr[128 * tt:128 * (tt + 1), :]), "dma", 16)

            @c.on("vector")
            def _(v, tt=tt):
                c.wait(v, "dma")
                c.inc(v.tensor_copy(atm[:], ybt[:]), "ve")

            ln_tile(c, atm[:], lnt[:], sq, st8, D, res=xs[:], res_out=x1t[:])

            @c.on("sync")
            def _(sy, tt=tt):
                c.wait(sy, "ve")
                c.inc(sy.dma_start(out=x3o[128 * tt:128 * (tt + 1), :], in_=x1t[:]), "st", 16)

        @c.on("sync")
        def _(sy):
            c.wait(sy, "st")
    return c


def build_l1fpre():
    c = Ctx("l1fpre")
    nc = c.nc
    xr = nc.dram_tensor("xr", [NTOK, D], F32, kind="ExternalInput").ap()
    qw = nc.dram_tensor("qw", [D, D], F32, kind="ExternalInput").ap()
    qb = nc.dram_tensor("qb", [1, D], F32, kind="ExternalInput").ap()
    kw = nc.dram_tensor("kw", [D, 512], F32, kind="ExternalInput").ap()
    kbi = nc.dram_tensor("kbi", [1, 512], F32, kind="ExternalInput").ap()
    vw = nc.dram_tensor("vw", [D, 512], F32, kind="ExternalInput").ap()
    vbi = nc.dram_tensor("vbi", [1, 512], F32, kind="ExternalInput").ap()
    ident = nc.dram_tensor("ident", [128, 128], BF16, kind="ExternalInput").ap()
    q_o = nc.dram_tensor("q_o", [D, NTOK], BF16, kind="ExternalOutput").ap()
    k_o = nc.dram_tensor("k_o", [512, NTOK], BF16, kind="ExternalOutput").ap()
    vt_o = nc.dram_tensor("vt_o", [NTOK, 512], BF16, kind="ExternalOutput").ap()
    with c:
        idb = c.sb("idb", [128, 128], BF16)
        xs = c.sb("xs", [128, D], F32)
        sq = c.sb("sq", [128, D], F32)
        st8 = c.sb("st8", [128, 8], F32)
        ytm = c.sb("ytm", [128, D], BF16)
        yfm = c.sb("yfm", [128, DC * NTOK], BF16)
        qwb = c.sb("qwb", [128, DC * D], BF16)
        kwb = c.sb("kwb", [128, DC * 512], BF16)
        vwb = c.sb("vwb", [128, DC * 512], BF16)
        brow = c.sb("brow", [1, 2 * D], BF16)
        onesone = c.sb("onesone", [1, 1], BF16)
        bcol = c.sb("bcol", [128, 16], F32)
        outst = c.sb("outst", [128, 8 * 512], BF16)
        ptr = c.ps("ptr", 1024, BF16)
        pb = c.ps("pb", 512)
        pj = [c.ps(f"pj{i}", 512) for i in range(4)]

        @c.on("gpsimd")
        def _(g):
            c.inc(g.dma_start(out=idb[:], in_=ident), "gd", 16)
            for dc in range(DC):
                c.inc(g.dma_start(out=qwb[:, dc * D:(dc + 1) * D],
                                  in_=qw[128 * dc:128 * (dc + 1), :]), "gd", 16)
                c.inc(g.dma_start(out=kwb[:, dc * 512:(dc + 1) * 512],
                                  in_=kw[128 * dc:128 * (dc + 1), :]), "gd", 16)
                c.inc(g.dma_start(out=vwb[:, dc * 512:(dc + 1) * 512],
                                  in_=vw[128 * dc:128 * (dc + 1), :]), "gd", 16)
            c.inc(g.dma_start(out=brow[:, 0:D], in_=qb), "gd", 16)
            c.inc(g.dma_start(out=brow[:, D:D + 512], in_=kbi), "gd", 16)
            c.inc(g.dma_start(out=brow[:, D + 512:D + 1024], in_=vbi), "gd", 16)

        @c.on("vector")
        def _(v):
            c.inc(v.memset(onesone[:], 1.0), "ve")

        @c.on("tensor")
        def _(t):
            c.wait(t, "gd")
            c.wait(t, "ve")
            for i in range(16):
                last = t.matmul(pb[:, i:i + 1], brow[:, 128 * i:128 * (i + 1)],
                                onesone[:], start=(i == 0), stop=(i == 15))
            c.inc(last, "te")

        @c.on("vector")
        def _(v):
            c.wait(v, "te")
            c.inc(v.tensor_copy(bcol[:], pb[:, 0:16]), "ve")

        for tt in range(NT):
            @c.on("sync")
            def _(sy, tt=tt):
                if tt > 0:
                    c.wait(sy, "se")
                c.inc(sy.dma_start(out=xs[:], in_=xr[128 * tt:128 * (tt + 1), :]), "dma", 16)

            @c.on("vector")
            def _(v):
                c.wait(v, "dma")

            ln_tile(c, xs[:], ytm[:], sq, st8, D)

            @c.on("tensor")
            def _(t, tt=tt):
                c.wait(t, "se")
                c.wait(t, "ve")
                for dc2 in range(DC):
                    last = t.transpose(ptr[:, dc2 * 128:(dc2 + 1) * 128],
                                       ytm[:, dc2 * 128:(dc2 + 1) * 128], idb[:])
                c.inc(last, "te")

            @c.on("vector")
            def _(v, tt=tt):
                c.wait(v, "te")
                for dc2 in range(DC):
                    last = v.tensor_copy(yfm[:, dc2 * NTOK + 128 * tt:dc2 * NTOK + 128 * (tt + 1)],
                                         ptr[:, dc2 * 128:(dc2 + 1) * 128])
                c.inc(last, "ve")

        for ncx in range(NC):
            for dg in range(2):
                @c.on("tensor")
                def _(t, ncx=ncx, dg=dg):
                    c.wait(t, "ve")
                    c.wait(t, "se")
                    for dl in range(4):
                        do = 4 * dg + dl
                        for di in range(DC):
                            last = t.matmul(pj[dl][:, 0:512],
                                            qwb[:, di * D + do * 128:di * D + (do + 1) * 128],
                                            yfm[:, di * NTOK + 512 * ncx:di * NTOK + 512 * (ncx + 1)],
                                            start=(di == 0), stop=(di == DC - 1))
                    c.inc(last, "te")

                @c.on("scalar")
                def _(s, ncx=ncx, dg=dg):
                    c.wait(s, "te")
                    c.wait(s, "st")
                    for dl in range(4):
                        do = 4 * dg + dl
                        last = s.activation(outst[:, do * 512:(do + 1) * 512], pj[dl][:, 0:512],
                                            AF.Identity, bias=bcol[:, do:do + 1])
                    c.inc(last, "se")

                @c.on("sync")
                def _(sy, ncx=ncx, dg=dg):
                    c.wait(sy, "se")
                    for dl in range(4):
                        do = 4 * dg + dl
                        c.inc(sy.dma_start(out=q_o[do * 128:(do + 1) * 128, 512 * ncx:512 * (ncx + 1)],
                                           in_=outst[:, do * 512:(do + 1) * 512]), "st", 16)

        for ncx in range(NC):
            @c.on("tensor")
            def _(t, ncx=ncx):
                c.wait(t, "ve")
                c.wait(t, "se")
                for dl in range(4):
                    for di in range(DC):
                        last = t.matmul(pj[dl][:, 0:512],
                                        kwb[:, di * 512 + dl * 128:di * 512 + (dl + 1) * 128],
                                        yfm[:, di * NTOK + 512 * ncx:di * NTOK + 512 * (ncx + 1)],
                                        start=(di == 0), stop=(di == DC - 1))
                c.inc(last, "te")

            @c.on("scalar")
            def _(s, ncx=ncx):
                c.wait(s, "te")
                c.wait(s, "st")
                for dl in range(4):
                    last = s.activation(outst[:, dl * 512:(dl + 1) * 512], pj[dl][:, 0:512],
                                        AF.Identity, bias=bcol[:, 8 + dl:9 + dl])
                c.inc(last, "se")

            @c.on("sync")
            def _(sy, ncx=ncx):
                c.wait(sy, "se")
                for dl in range(4):
                    c.inc(sy.dma_start(out=k_o[dl * 128:(dl + 1) * 128, 512 * ncx:512 * (ncx + 1)],
                                       in_=outst[:, dl * 512:(dl + 1) * 512]), "st", 16)

            @c.on("tensor")
            def _(t, ncx=ncx):
                c.wait(t, "ve")
                c.wait(t, "se")
                for dl in range(4):
                    for di in range(DC):
                        last = t.matmul(pj[dl][:, 0:512],
                                        vwb[:, di * 512 + dl * 128:di * 512 + (dl + 1) * 128],
                                        yfm[:, di * NTOK + 512 * ncx:di * NTOK + 512 * (ncx + 1)],
                                        start=(di == 0), stop=(di == DC - 1))
                c.inc(last, "te")

            @c.on("scalar")
            def _(s, ncx=ncx):
                c.wait(s, "te")
                c.wait(s, "st")
                for dl in range(4):
                    last = s.activation(outst[:, (4 + dl) * 512:(5 + dl) * 512], pj[dl][:, 0:512],
                                        AF.Identity, bias=bcol[:, 12 + dl:13 + dl])
                c.inc(last, "se")

            for sub in range(4):
                @c.on("tensor")
                def _(t, ncx=ncx, sub=sub):
                    c.wait(t, "se")
                    c.wait(t, "ve")
                    for dl in range(4):
                        last = t.transpose(ptr[:, dl * 128:(dl + 1) * 128],
                                           outst[:, (4 + dl) * 512 + sub * 128:(4 + dl) * 512 + (sub + 1) * 128],
                                           idb[:])
                    c.inc(last, "te")

                @c.on("vector")
                def _(v, ncx=ncx, sub=sub):
                    c.wait(v, "te")
                    c.wait(v, "st")
                    c.inc(v.tensor_copy(ytm[:, 0:512], ptr[:, 0:512]), "ve")

                @c.on("sync")
                def _(sy, ncx=ncx, sub=sub):
                    c.wait(sy, "ve")
                    c.inc(sy.dma_start(out=vt_o[512 * ncx + 128 * sub:512 * ncx + 128 * (sub + 1), :],
                                       in_=ytm[:, 0:512]), "st", 16)

        @c.on("sync")
        def _(sy):
            c.wait(sy, "st")
    return c


def build_l1fmain():
    c = Ctx("l1fmain")
    nc = c.nc
    xr = nc.dram_tensor("xr", [NTOK, D], F32, kind="ExternalInput").ap()
    qfm = nc.dram_tensor("qfm", [D, NTOK], BF16, kind="ExternalInput").ap()
    kf = nc.dram_tensor("kf", [512, N], BF16, kind="ExternalInput").ap()
    vtf = nc.dram_tensor("vtf", [N, 512], BF16, kind="ExternalInput").ap()
    ow = nc.dram_tensor("ow", [D, D], F32, kind="ExternalInput").ap()
    ob = nc.dram_tensor("ob", [1, D], F32, kind="ExternalInput").ap()
    rw = nc.dram_tensor("rw", [D, 8], F32, kind="ExternalInput").ap()
    rb = nc.dram_tensor("rb", [1, 8], F32, kind="ExternalInput").ap()
    w1 = nc.dram_tensor("w1", [8 * D, 2 * D], F32, kind="ExternalInput").ap()
    b1 = nc.dram_tensor("b1", [8, 2 * D], F32, kind="ExternalInput").ap()
    w2 = nc.dram_tensor("w2", [8 * 2 * D, D], F32, kind="ExternalInput").ap()
    b2 = nc.dram_tensor("b2", [8, D], F32, kind="ExternalInput").ap()
    identf = nc.dram_tensor("identf", [128, 128], F32, kind="ExternalInput").ap()
    x4o = nc.dram_tensor("x4o", [NTOK, D], F32, kind="ExternalOutput").ap()
    ao_d = nc.dram_tensor("ao_d", [NTOK, 512], BF16)
    NST = N // 128
    SCS = float(32) ** -0.5
    with c:
        idf = c.sb("idf", [128, 128], F32)
        ar1 = c.sb("ar1", [128, 16384], BF16)
        ar2 = c.sb("ar2", [128, 8192], BF16)
        ws1 = c.sb("ws1", [128, 16384], BF16)
        mla = c.sb("mla", [128, DC * NTOK], BF16)
        macc = c.sb("macc", [128, DC * NTOK], F32)
        aost = c.sb("aost", [128, 32], BF16)
        zc = c.sb("zc", [128, 1], F32)
        rws = c.sb("rws", [128, 64], BF16)
        rbrow = c.sb("rbrow", [1, 8], BF16)
        b1row = c.sb("b1row", [1, 2 * D], BF16)
        b2row = c.sb("b2row", [1, D], BF16)
        bcol = c.sb("bcol", [128, 24], F32)
        onesone = c.sb("onesone", [1, 1], BF16)
        onesbf = c.sb("onesbf", [1, 128], BF16)
        lg = c.sb("lg", [128, 128], F32)
        p8 = c.sb("p8", [128, 8], F32)
        s1 = c.sb("s1", [128, 1], F32)
        s2 = c.sb("s2", [128, 1], F32)
        s3 = c.sb("s3", [128, 1], F32)
        m8 = c.sb("m8", [128, 8], F32)
        comb = c.sb("comb", [128, 128], F32)
        cfb = c.sb("cfb", [1, NTOK], BF16)
        bct = c.sb("bct", [128, 512], BF16)
        xs = c.sb("xs", [128, D], F32)
        sq = c.sb("sq", [128, D], F32)
        st8 = c.sb("st8", [128, 8], F32)
        ffnt = c.sb("ffnt", [128, D], F32)
        lnt = c.sb("lnt", [128, D], F32)
        pA = c.ps("pA", 512)
        pjS = c.ps("pjS", 2048)
        pj = [pjS[:, 512 * i:512 * (i + 1)] for i in range(4)]

        KHS = 0
        QHS = 8192
        VES = 10240
        PES = 12416

        @c.on("gpsimd")
        def _(g):
            c.inc(g.dma_start(out=idf[:], in_=identf), "gd", 16)
            c.inc(g.dma_start(out=rbrow[:], in_=rb), "gd", 16)
            for di in range(DC):
                c.inc(g.dma_start(out=rws[:, di * 8:(di + 1) * 8],
                                  in_=rw[128 * di:128 * (di + 1), :]), "gd", 16)

        @c.on("vector")
        def _(v):
            v.memset(onesone[:], 1.0)
            c.inc(v.memset(onesbf[:], 1.0), "ve")

        for h in range(16):
            @c.on("gpsimd")
            def _(g, h=h):
                c.wait(g, "te")
                c.inc(g.dma_start(out=ar1[0:32, KHS:KHS + N], in_=kf[32 * h:32 * (h + 1), :]), "gd", 16)
                c.inc(g.dma_start(out=ar1[0:32, QHS:QHS + NTOK],
                                  in_=qfm[64 * h:64 * h + 32, :]), "gd", 16)
                vsrc = vtf.rearrange("(t p) k -> p t k", p=128)[:, :, 32 * h:32 * (h + 1)]
                vdst = ar1[:, VES:VES + NST * 34].rearrange("p (t k) -> p t k", k=34)[:, :, 0:32]
                c.inc(g.dma_start(out=vdst, in_=vsrc), "gd", 16)

            @c.on("vector")
            def _(v, h=h):
                c.wait(v, "gd")
                ones_view = ar1[:, VES:VES + NST * 34].rearrange("p (t k) -> p t k", k=34)[:, :, 32:33]
                c.inc(v.memset(ones_view, 1.0), "ve")

            for sc in range(NC):
                se0 = c.cnt["se"]
                te_ck = {}

                def se_of(tt):
                    return se0 + tt // 2 + 1

                @c.on("tensor")
                def _(t, h=h, sc=sc):
                    for tt in range(NST):
                        if tt == 0:
                            c.wait(t, "gd")
                            c.wait(t, "ve")
                            c.wait(t, "se", se0)
                        if tt >= 2:
                            ttp = tt - 2
                            if ttp == 0:
                                c.wait(t, "ve")
                            c.wait(t, "se", se_of(ttp))
                            for ts in range(4):
                                last = t.matmul(pA[:, ts * 33:ts * 33 + 33],
                                                ar1[:, PES + (ttp % 4) * 512 + ts * 128:PES + (ttp % 4) * 512 + (ts + 1) * 128],
                                                ar1[:, VES + 34 * ttp:VES + 34 * ttp + 33],
                                                start=(ttp == 0 and ts == 0), stop=False)
                            c.inc(last, "te")
                        if tt >= 4:
                            c.wait(t, "se", se_of(tt - 4))
                        c.inc(t.matmul(pjS[:, (tt % 4) * 512:(tt % 4) * 512 + 512],
                                       ar1[0:32, KHS + 128 * tt:KHS + 128 * (tt + 1)],
                                       ar1[0:32, QHS + 512 * sc:QHS + 512 * (sc + 1)],
                                       start=True, stop=True), "te")
                        te_ck[tt] = c.cnt["te"]
                    for ttp in (NST - 2, NST - 1):
                        c.wait(t, "se", se_of(ttp))
                        for ts in range(4):
                            last = t.matmul(pA[:, ts * 33:ts * 33 + 33],
                                            ar1[:, PES + (ttp % 4) * 512 + ts * 128:PES + (ttp % 4) * 512 + (ts + 1) * 128],
                                            ar1[:, VES + 34 * ttp:VES + 34 * ttp + 33],
                                            start=False,
                                            stop=(ttp == NST - 1 and ts == 3))
                        c.inc(last, "te")
                    te_ck["end"] = c.cnt["te"]

                @c.on("scalar")
                def _(s, h=h, sc=sc):
                    for p in range(NST // 2):
                        tt = 2 * p
                        c.wait(s, "te", te_ck[tt + 1])
                        base = (tt % 4) * 512
                        c.inc(s.activation(ar1[:, PES + base:PES + base + 1024],
                                           pjS[:, base:base + 1024], AF.Exp, scale=SCS), "se")
                assert c.cnt["se"] == se0 + NST // 2

                for ts in range(4):
                    @c.on("vector")
                    def _(v, h=h, sc=sc, ts=ts):
                        c.wait(v, "te")
                        c.wait(v, "st")
                        c.inc(v.reciprocal(zc[:], pA[:, ts * 33 + 32:ts * 33 + 33]), "ve")
                        c.wait(v, "ve")
                        c.inc(v.tensor_scalar(aost[:], pA[:, ts * 33:ts * 33 + 32],
                                              zc[:], None, op0=ALU.mult), "ve")

                    @c.on("sync")
                    def _(sy, h=h, sc=sc, ts=ts):
                        c.wait(sy, "ve")
                        tix = 4 * sc + ts
                        c.inc(sy.dma_start(out=ao_d[128 * tix:128 * (tix + 1), 32 * h:32 * (h + 1)],
                                           in_=aost[:]), "st", 16)

        for cc2 in range(4):
            @c.on("sync")
            def _(sy, cc2=cc2):
                c.wait(sy, "st")
                c.wait(sy, "te")
                c.inc(sy.dma_start(out=ar2[:, cc2 * NTOK:(cc2 + 1) * NTOK],
                                   in_=ao_d[:, 128 * cc2:128 * (cc2 + 1)], transpose=True), "dma", 16)

        @c.on("gpsimd")
        def _(g):
            for cc2 in range(4):
                c.inc(g.dma_start(out=ws1[:, cc2 * 1024:(cc2 + 1) * 1024],
                                  in_=ow[128 * cc2:128 * (cc2 + 1), :]), "gd", 16)
            c.inc(g.dma_start(out=b1row[:, 0:D], in_=ob), "gd", 16)

        @c.on("tensor")
        def _(t):
            c.wait(t, "gd")
            c.wait(t, "ve")
            for i in range(8):
                last = t.matmul(pj[3][:, i:i + 1], b1row[:, 128 * i:128 * (i + 1)],
                                onesone[:], start=(i == 0), stop=(i == 7))
            c.inc(last, "te")

        @c.on("vector")
        def _(v):
            c.wait(v, "te")
            c.inc(v.tensor_copy(bcol[:, 0:8], pj[3][:, 0:8]), "ve")

        for ncx in range(NC):
            for dg in range(2):
                @c.on("tensor")
                def _(t, ncx=ncx, dg=dg):
                    c.wait(t, "dma")
                    c.wait(t, "ve")
                    c.wait(t, "se")
                    for dl in range(4):
                        do = 4 * dg + dl
                        for cc2 in range(4):
                            last = t.matmul(pj[dl][:, 0:512],
                                            ws1[:, cc2 * 1024 + do * 128:cc2 * 1024 + (do + 1) * 128],
                                            ar2[:, cc2 * NTOK + 512 * ncx:cc2 * NTOK + 512 * (ncx + 1)],
                                            start=(cc2 == 0), stop=(cc2 == 3))
                    c.inc(last, "te")

                @c.on("scalar")
                def _(s, ncx=ncx, dg=dg):
                    c.wait(s, "te")
                    for dl in range(4):
                        do = 4 * dg + dl
                        last = s.activation(mla[:, do * NTOK + 512 * ncx:do * NTOK + 512 * (ncx + 1)],
                                            pj[dl][:, 0:512], AF.Identity, bias=bcol[:, do:do + 1])
                    c.inc(last, "se")

        for nt in range(NT):
            @c.on("tensor")
            def _(t, nt=nt):
                c.wait(t, "se")
                c.wait(t, "ve")
                for di in range(DC):
                    t.matmul(pj[0][:, nt * 8:(nt + 1) * 8],
                             mla[:, di * NTOK + 128 * nt:di * NTOK + 128 * (nt + 1)],
                             rws[:, di * 8:(di + 1) * 8],
                             start=(nt == 0 and di == 0), stop=False)
                last = t.matmul(pj[0][:, nt * 8:(nt + 1) * 8], onesbf[:], rbrow[:],
                                start=False, stop=(nt == NT - 1))
                c.inc(last, "te")

        @c.on("vector")
        def _(v):
            c.wait(v, "te")
            c.inc(v.tensor_copy(lg[:], pj[0][:, 0:128]), "ve")

        for nt in range(NT):
            sl = slice(nt * 8, (nt + 1) * 8)
            @c.on("vector")
            def _(v, nt=nt, sl=sl):
                c.wait(v, "ve")
                c.inc(v.reduce_max(s1[:], lg[:, sl], axis=AX), "ve")
                c.wait(v, "ve")
                c.inc(v.tensor_scalar(p8[:], lg[:, sl], s1[:], None, op0=ALU.subtract), "ve")

            @c.on("scalar")
            def _(s, nt=nt):
                c.wait(s, "ve")
                c.inc(s.activation(p8[:], p8[:], AF.Exp, accum_out=s2[:]), "se")

            @c.on("vector")
            def _(v, nt=nt):
                c.wait(v, "se")
                c.inc(v.reciprocal(s2[:], s2[:]), "ve")
                c.wait(v, "ve")
                c.inc(v.tensor_scalar(p8[:], p8[:], s2[:], None, op0=ALU.mult), "ve")
                c.wait(v, "ve")
                c.inc(v.reduce_max(s1[:], p8[:], axis=AX), "ve")
                c.wait(v, "ve")
                c.inc(v.tensor_scalar(m8[:], p8[:], s1[:], None, op0=ALU.is_ge), "ve")
                c.wait(v, "ve")
                c.inc(v.scalar_tensor_tensor(m8[:], m8[:], -2.0, p8[:], op0=ALU.mult, op1=ALU.add), "ve")
                c.wait(v, "ve")
                c.inc(v.reduce_max(s2[:], m8[:], axis=AX), "ve")
                c.wait(v, "ve")
                c.inc(v.tensor_scalar(m8[:], p8[:], s2[:], None, op0=ALU.is_ge), "ve")
                c.wait(v, "ve")
                c.inc(v.tensor_tensor(s1[:], s1[:], s2[:], op=ALU.add), "ve")
                c.wait(v, "ve")
                c.inc(v.tensor_scalar_add(s1[:], s1[:], 1e-8), "ve")
                c.wait(v, "ve")
                c.inc(v.reciprocal(s3[:], s1[:]), "ve")
                c.wait(v, "ve")
                c.inc(v.tensor_scalar_mul(s3[:], s3[:], 0.3), "ve")
                c.wait(v, "ve")
                c.inc(v.tensor_tensor(m8[:], m8[:], p8[:], op=ALU.mult), "ve")
                c.wait(v, "ve")
                c.inc(v.tensor_scalar(comb[:, nt * 8:(nt + 1) * 8], m8[:], s3[:], None,
                                      op0=ALU.mult), "ve")

        moe_gd = {}
        for e in range(8):
            @c.on("gpsimd")
            def _(g, e=e):
                c.wait(g, "te")
                for di in range(DC):
                    c.inc(g.dma_start(out=ws1[:, di * 2048:(di + 1) * 2048],
                                      in_=w1[e * D + 128 * di:e * D + 128 * (di + 1), :]), "gd", 16)
                c.inc(g.dma_start(out=b1row[:], in_=b1[e:e + 1, :]), "gd", 16)
                c.inc(g.dma_start(out=b2row[:], in_=b2[e:e + 1, :]), "gd", 16)
                moe_gd[("w1", e)] = c.cnt["gd"]

            @c.on("tensor")
            def _(t, e=e):
                c.wait(t, "gd")
                c.wait(t, "ve")
                for i in range(16):
                    last = t.matmul(pj[3][:, 64 + i:65 + i], b1row[:, 128 * i:128 * (i + 1)],
                                    onesone[:], start=(i == 0), stop=(i == 15))
                c.inc(last, "te")

            @c.on("vector")
            def _(v, e=e):
                c.wait(v, "te")
                c.inc(v.tensor_copy(bcol[:, 8:24], pj[3][:, 64:80]), "ve")

            for q4 in range(4):
                @c.on("tensor")
                def _(t, e=e, q4=q4):
                    c.wait(t, "ve")
                    c.wait(t, "se")
                    for i in range(4):
                        nt = 4 * q4 + i
                        last = t.matmul(pj[3][0:1, 128 * i:128 * (i + 1)],
                                        comb[:, nt * 8 + e:nt * 8 + e + 1], idf[:],
                                        start=(i == 0), stop=(i == 3))
                    c.inc(last, "te")

                @c.on("vector")
                def _(v, e=e, q4=q4):
                    c.wait(v, "te")
                    c.inc(v.tensor_copy(cfb[:, 512 * q4:512 * (q4 + 1)], pj[3][0:1, 0:512]), "ve")

            for ncx in range(NC):
                @c.on("tensor")
                def _(t, e=e, ncx=ncx):
                    c.wait(t, "ve")
                    c.wait(t, "se")
                    c.inc(t.matmul(pj[3][:, 0:512], onesbf[:], cfb[:, 512 * ncx:512 * (ncx + 1)],
                                   start=True, stop=True), "te")

                @c.on("vector")
                def _(v, e=e, ncx=ncx):
                    c.wait(v, "te")
                    c.inc(v.tensor_copy(bct[:], pj[3][:, 0:512]), "ve")

                h1_te = {}
                h1_se = {}

                def h1_gelu(s, e, ncx, g):
                    c.wait(s, "te", h1_te[g])
                    for hl in range(2):
                        hh = 2 * g + hl
                        bank = pj[(g % 2) * 2 + hl]
                        last = s.activation(ar2[:, hh * 512:(hh + 1) * 512], bank[:, 0:512],
                                            AF.Gelu, bias=bcol[:, 8 + hh:9 + hh])
                    c.inc(last, "se")
                    h1_se[g] = c.cnt["se"]

                se0_h1 = c.cnt["se"]

                @c.on("tensor")
                def _(t, e=e, ncx=ncx, se0_h1=se0_h1):
                    for g in range(8):
                        if g == 0:
                            if ncx == 0:
                                c.wait(t, "gd", moe_gd[("w1", e)])
                            c.wait(t, "ve")
                            c.wait(t, "se", se0_h1)
                        if g >= 2:
                            c.wait(t, "se", se0_h1 + g - 1)
                        for hl in range(2):
                            hh = 2 * g + hl
                            bank = pj[(g % 2) * 2 + hl]
                            for di in range(DC):
                                last = t.matmul(bank[:, 0:512],
                                                ws1[:, di * 2048 + hh * 128:di * 2048 + (hh + 1) * 128],
                                                mla[:, di * NTOK + 512 * ncx:di * NTOK + 512 * (ncx + 1)],
                                                start=(di == 0), stop=(di == DC - 1))
                        c.inc(last, "te")
                        h1_te[g] = c.cnt["te"]

                @c.on("scalar")
                def _(s, e=e, ncx=ncx):
                    for g in range(8):
                        h1_gelu(s, e, ncx, g)

                if ncx == 0:
                    @c.on("gpsimd")
                    def _(g_, e=e):
                        c.wait(g_, "te", h1_te[0])
                        for hh in range(16):
                            c.inc(g_.dma_start(out=ar1[:, hh * 1024:(hh + 1) * 1024],
                                               in_=w2[e * 2048 + 128 * hh:e * 2048 + 128 * (hh + 1), :]),
                                  "gd", 16)
                        moe_gd[("w2", e)] = c.cnt["gd"]

                @c.on("vector")
                def _(v, e=e, ncx=ncx):
                    c.wait(v, "se")
                    for hh in range(16):
                        last = v.tensor_tensor(ar2[:, hh * 512:(hh + 1) * 512],
                                               ar2[:, hh * 512:(hh + 1) * 512], bct[:], op=ALU.mult)
                    c.inc(last, "ve")

                h2_te = {}
                h2_ve = {}

                def h2_acc(v, e, ncx, g):
                    c.wait(v, "te", h2_te[g])
                    for dl in range(2):
                        do = 2 * g + dl
                        bank = pj[(g % 2) * 2 + dl]
                        dst = macc[:, do * NTOK + 512 * ncx:do * NTOK + 512 * (ncx + 1)]
                        if e == 0:
                            last = v.tensor_copy(dst, bank[:, 0:512])
                        else:
                            last = v.tensor_tensor(dst, dst, bank[:, 0:512], op=ALU.add)
                    c.inc(last, "ve")
                    h2_ve[g] = c.cnt["ve"]

                ve0_h2 = c.cnt["ve"]

                @c.on("tensor")
                def _(t, e=e, ncx=ncx, ve0_h2=ve0_h2):
                    for g in range(4):
                        if g == 0:
                            c.wait(t, "gd", moe_gd[("w2", e)])
                            c.wait(t, "ve", ve0_h2)
                            c.wait(t, "se")
                        if g >= 2:
                            c.wait(t, "ve", ve0_h2 + g - 1)
                        for dl in range(2):
                            do = 2 * g + dl
                            bank = pj[(g % 2) * 2 + dl]
                            for hh in range(16):
                                t.matmul(bank[:, 0:512],
                                         ar1[:, hh * 1024 + do * 128:hh * 1024 + (do + 1) * 128],
                                         ar2[:, hh * 512:(hh + 1) * 512],
                                         start=(hh == 0), stop=False)
                            last = t.matmul(bank[:, 0:512], b2row[:, do * 128:(do + 1) * 128],
                                            cfb[:, 512 * ncx:512 * (ncx + 1)],
                                            start=False, stop=True)
                        c.inc(last, "te")
                        h2_te[g] = c.cnt["te"]

                @c.on("vector")
                def _(v, e=e, ncx=ncx):
                    for g in range(4):
                        h2_acc(v, e, ncx, g)

        for tt in range(NT):
            @c.on("sync")
            def _(sy, tt=tt):
                if tt > 0:
                    c.wait(sy, "ve")
                c.inc(sy.dma_start(out=xs[:], in_=xr[128 * tt:128 * (tt + 1), :]), "dma", 16)

            @c.on("vector")
            def _(v):
                c.wait(v, "dma")

            ln_tile(c, xs[:], lnt[:], sq, st8, D)

            for half in range(2):
                @c.on("tensor")
                def _(t, tt=tt, half=half):
                    c.wait(t, "ve")
                    c.wait(t, "se")
                    for q in range(4):
                        do = 4 * half + q
                        last = t.transpose(pj[q][:, 0:128],
                                           macc[:, do * NTOK + 128 * tt:do * NTOK + 128 * (tt + 1)],
                                           idf[:])
                    c.inc(last, "te")

                @c.on("vector")
                def _(v, tt=tt, half=half):
                    c.wait(v, "te")
                    for q in range(4):
                        do = 4 * half + q
                        last = v.tensor_copy(ffnt[:, do * 128:(do + 1) * 128], pj[q][:, 0:128])
                    c.inc(last, "ve")

            @c.on("vector")
            def _(v, tt=tt):
                c.wait(v, "se")
                c.wait(v, "ve")
                c.inc(v.tensor_tensor(ffnt[:], ffnt[:], lnt[:], op=ALU.add), "ve")

            ln_tile(c, ffnt[:], lnt[:], sq, st8, D, res=xs[:], res_out=sq[:])

            @c.on("sync")
            def _(sy, tt=tt):
                c.wait(sy, "ve")
                c.inc(sy.dma_start(out=x4o[128 * tt:128 * (tt + 1), :], in_=sq[:]), "st", 16)

        @c.on("sync")
        def _(sy):
            c.wait(sy, "st")
    return c


def rows_of_core(core):
    b, q = core // 4, core % 4
    idx = []
    for k in range(4):
        g = 4 * k + q
        idx.append((b, g * 512, (g + 1) * 512))
    return b, idx


def shard_rows(x):
    out = []
    for core in range(8):
        b, idx = rows_of_core(core)
        out.append(np.ascontiguousarray(
            np.concatenate([x[b, s:e] for (_, s, e) in idx], axis=0)))
    return out


def unshard_rows(shards, dtype=np.float32):
    x = np.empty((B, N, D), dtype)
    for core in range(8):
        b, idx = rows_of_core(core)
        for k, (_, s, e) in enumerate(idx):
            x[b, s:e] = shards[core][k * 512:(k + 1) * 512]
    return x


def assemble_full_cols(pieces):
    d = pieces[0].shape[0]
    full = [np.empty((d, N), pieces[0].dtype) for _ in range(B)]
    for core in range(8):
        b, idx = rows_of_core(core)
        for k, (_, s, e) in enumerate(idx):
            full[b][:, s:e] = pieces[core][:, k * 512:(k + 1) * 512]
    return full


def assemble_full_rows(pieces):
    d = pieces[0].shape[1]
    full = [np.empty((N, d), pieces[0].dtype) for _ in range(B)]
    for core in range(8):
        b, idx = rows_of_core(core)
        for k, (_, s, e) in enumerate(idx):
            full[b][s:e] = pieces[core][k * 512:(k + 1) * 512]
    return full


_IDENT = np.eye(128, dtype=np.float32).astype(BF)
_IDENTF = np.eye(128, dtype=np.float32)


def run_l0pre(x, wq, wk, wv):
    nc = build_l0pre()
    xs = shard_rows(x)
    maps = [{"xr": xs[i], "wq": wq, "wk": wk, "wv": wv, "ident": _IDENT,
             "identf": _IDENTF} for i in range(8)]
    res = run_bass_kernel_spmd(nc.nc, maps, core_ids=list(range(8)))
    return res.results


def my_cols(core):
    b, idx = rows_of_core(core)
    return b, [(s, e) for (_, s, e) in idx]


def run_l0attn(l1res):
    q_full = assemble_full_cols([np.asarray(r["q_fm"]) for r in l1res])
    nc = build_l0attn()
    maps = []
    for core in range(8):
        b = core // 4
        maps.append({"qf": np.ascontiguousarray(q_full[b]),
                     "kf": np.asarray(l1res[core]["k_fm"]),
                     "vt": np.asarray(l1res[core]["v_tm"])})
    res = run_bass_kernel_spmd(nc.nc, maps, core_ids=list(range(8)))
    return res.results


def run_l0red_ffn(x, l1res, l2res, wg, wu, wd):
    xs = shard_rows(x)
    nc = build_l0red_ffn()
    maps = []
    for core in range(8):
        b, cols = my_cols(core)
        ps = []
        for i in range(4):
            pfull = np.asarray(l2res[4 * b + i]["part"])
            ps.append(np.ascontiguousarray(
                np.concatenate([pfull[:, s:e] for (s, e) in cols], axis=1)))
        maps.append({"xr": xs[core], "p0": ps[0], "p1": ps[1], "p2": ps[2],
                     "p3": ps[3], "vfm": np.asarray(l1res[core]["v_fm"]),
                     "wg_i": wg, "wu_i": wu, "wd_i": wd, "ident": _IDENT})
    res = run_bass_kernel_spmd(nc.nc, maps, core_ids=list(range(8)))
    return res.results


def make_masks(core):
    q = core % 4
    out = np.zeros((NC * 128, 16 * 512), np.float32)
    for k in range(NC):
        t = (4 * k + q) * 512 + np.arange(512)[None, :]
        for jj in range(16):
            s = (16 * k + jj) * 128 + np.arange(128)[:, None]
            out[k * 128:(k + 1) * 128, jj * 512:(jj + 1) * 512] = (s <= t)
    return out.astype(BF)


_MASKS = None


def run_l1pre(x2, Wdq, Wuq, Wdkv, Wuk):
    xs = shard_rows(x2)
    nc = build_l1pre()
    maps = [{"xr": xs[i], "Wdq": Wdq, "Wuq": Wuq, "Wdkv": Wdkv, "Wuk": Wuk,
             "ident": _IDENT} for i in range(8)]
    res = run_bass_kernel_spmd(nc.nc, maps, core_ids=list(range(8)))
    return res.results


def run_l1attn(x2, l4res, Wuv, Wo):
    global _MASKS
    if _MASKS is None:
        _MASKS = [make_masks(i) for i in range(8)]
    xs = shard_rows(x2)
    ckv_full = assemble_full_rows([np.asarray(r["ckv_o"]) for r in l4res])
    nc = build_l1attn()
    maps = []
    for core in range(8):
        b = core // 4
        maps.append({"xr": xs[core], "ql_i": np.asarray(l4res[core]["ql_o"]),
                     "ckvfm": np.ascontiguousarray(ckv_full[b].T),
                     "ckvtm": ckv_full[b], "masks": _MASKS[core],
                     "Wuv": Wuv, "Wo": Wo})
    res = run_bass_kernel_spmd(nc.nc, maps, core_ids=list(range(8)))
    return res.results


def run_l1fpre(x3, qw, qb, kw, kb, vw, vb):
    xs = shard_rows(x3)
    nc = build_l1fpre()
    maps = [{"xr": xs[i], "qw": qw, "qb": qb.reshape(1, -1), "kw": kw,
             "kbi": kb.reshape(1, -1), "vw": vw, "vbi": vb.reshape(1, -1),
             "ident": _IDENT} for i in range(8)]
    res = run_bass_kernel_spmd(nc.nc, maps, core_ids=list(range(8)))
    return res.results


def run_l1fmain(x3, l6res, ow, ob, rw, rb, w1, b1, w2, b2):
    xs = shard_rows(x3)
    k_full = assemble_full_cols([np.asarray(r["k_o"]) for r in l6res])
    vt_full = assemble_full_rows([np.asarray(r["vt_o"]) for r in l6res])
    nc = build_l1fmain()
    maps = []
    for core in range(8):
        b = core // 4
        maps.append({"xr": xs[core], "qfm": np.asarray(l6res[core]["q_o"]),
                     "kf": np.ascontiguousarray(k_full[b]), "vtf": vt_full[b],
                     "ow": ow, "ob": ob.reshape(1, -1), "rw": rw, "rb": rb.reshape(1, -1),
                     "w1": w1.reshape(8 * D, 2 * D), "b1": b1,
                     "w2": w2.reshape(8 * 2 * D, D), "b2": b2, "identf": _IDENTF})
    res = run_bass_kernel_spmd(nc.nc, maps, core_ids=list(range(8)))
    return res.results


def kernel(**inputs):
    x = np.asarray(inputs["L0_attn_x"], np.float32)
    l1 = run_l0pre(x, inputs["L0_attn_wq"], inputs["L0_attn_wk"], inputs["L0_attn_wv"])
    l2 = run_l0attn(l1)
    l3 = run_l0red_ffn(x, l1, l2, inputs["L0_ffn_w_gate"], inputs["L0_ffn_w_up"],
                       inputs["L0_ffn_w_down"])
    x2 = unshard_rows([np.asarray(r["x2o"]) for r in l3])
    l4 = run_l1pre(x2, inputs["L1_attn_W_dq"], inputs["L1_attn_W_uq"],
                   inputs["L1_attn_W_dkv"], inputs["L1_attn_W_uk"])
    l5 = run_l1attn(x2, l4, inputs["L1_attn_W_uv"], inputs["L1_attn_W_o"])
    x3 = unshard_rows([np.asarray(r["x3o"]) for r in l5])
    l6 = run_l1fpre(x3, inputs["L1_ffn_q_w"], inputs["L1_ffn_q_b"], inputs["L1_ffn_k_w"],
                    inputs["L1_ffn_k_b"], inputs["L1_ffn_v_w"], inputs["L1_ffn_v_b"])
    l7 = run_l1fmain(x3, l6, inputs["L1_ffn_o_w"], inputs["L1_ffn_o_b"],
                     inputs["L1_ffn_router_w"], inputs["L1_ffn_router_b"],
                     inputs["L1_ffn_w1"], inputs["L1_ffn_b1"],
                     inputs["L1_ffn_w2"], inputs["L1_ffn_b2"])
    return unshard_rows([np.asarray(r["x4o"]) for r in l7])


# version 09
# speedup vs baseline: 1.2815x; 1.0577x over previous
import numpy as np
from contextlib import ExitStack
import concourse.bass as bass
import concourse.mybir as mybir
from concourse.bass_utils import run_bass_kernel_spmd
import ml_dtypes

BF = ml_dtypes.bfloat16
F32 = mybir.dt.float32
BF16 = mybir.dt.bfloat16
I32 = mybir.dt.int32
AF = mybir.ActivationFunctionType
ALU = mybir.AluOpType
AX = mybir.AxisListType.X

D = 1024
B = 2
N = 8192
NTOK = 2048
NT = NTOK // 128
NC = NTOK // 512
DC = D // 128


class Ctx:

    def __init__(self, name):
        self.name = name
        self.nc = bass.Bass("TRN2", target_bir_lowering=False, debug=False)
        self.es = ExitStack()
        self.sems = {}
        self.cnt = {}
        self.block = None

    def __enter__(self):
        self.es.__enter__()
        for s in ("dma", "gd", "te", "ve", "se", "st"):
            self.sems[s] = self.es.enter_context(self.nc.semaphore(s))
            self.cnt[s] = 0
        self.block = self.es.enter_context(self.nc.Block())
        return self

    def __exit__(self, *a):
        return self.es.__exit__(*a)

    def sb(self, name, shape, dtype):
        return self.es.enter_context(self.nc.sbuf_tensor(name, shape, dtype))

    def ps(self, name, cols=512, dtype=F32):
        return self.es.enter_context(self.nc.psum_tensor(name, [128, cols], dtype))

    def inc(self, inst, s, k=1):
        inst.then_inc(self.sems[s], k)
        self.cnt[s] += k

    def wait(self, eng, s, val=None):
        eng.wait_ge(self.sems[s], self.cnt[s] if val is None else val)

    def on(self, engname):
        def deco(fn):
            getattr(self.block, engname)(fn)
        return deco


def ln_tile(c, src, dst, sq, stats, Dfree, res=None, res_out=None):
    mu, m2, var, rstd, nmr = (stats[:, i:i + 1] for i in range(5))

    @c.on("vector")
    def _(v):
        c.inc(v.reduce_sum(mu, src, axis=AX), "ve")

    @c.on("scalar")
    def _(s):
        c.wait(s, "ve")
        c.inc(s.activation(sq[:, 0:src.shape[1]], src, AF.Square, accum_out=m2), "se")

    @c.on("vector")
    def _(v):
        c.wait(v, "se")
        v.tensor_scalar_mul(mu, mu, 1.0 / Dfree)
        c.inc(v.tensor_scalar_mul(m2, m2, 1.0 / Dfree), "ve")
        c.wait(v, "ve")
        c.inc(v.tensor_tensor(var, mu, mu, op=ALU.mult), "ve")
        c.wait(v, "ve")
        c.inc(v.tensor_tensor(var, m2, var, op=ALU.subtract), "ve")
        c.wait(v, "ve")
        c.inc(v.tensor_scalar_add(var, var, 1e-5), "ve")

    @c.on("scalar")
    def _(s):
        c.wait(s, "ve")
        c.inc(s.activation(rstd, var, AF.Sqrt), "se")

    @c.on("vector")
    def _(v):
        c.wait(v, "se")
        c.inc(v.reciprocal(rstd, rstd), "ve")
        c.wait(v, "ve")
        c.inc(v.scalar_tensor_tensor(nmr, mu, -1.0, rstd, op0=ALU.mult, op1=ALU.mult), "ve")

    @c.on("scalar")
    def _(s):
        c.wait(s, "ve")
        c.wait(s, "te")
        c.inc(s.activation(dst, src, AF.Identity, bias=nmr, scale=rstd), "se")

    if res is not None:
        @c.on("vector")
        def _(v):
            c.wait(v, "se")
            c.inc(v.tensor_tensor(res_out, res, dst, op=ALU.add), "ve")


def build_l0pre():
    c = Ctx("l0pre")
    nc = c.nc
    xr = nc.dram_tensor("xr", [NTOK, D], F32, kind="ExternalInput").ap()
    wq = nc.dram_tensor("wq", [D, D], F32, kind="ExternalInput").ap()
    wk = nc.dram_tensor("wk", [D, D], F32, kind="ExternalInput").ap()
    wv = nc.dram_tensor("wv", [D, D], F32, kind="ExternalInput").ap()
    ident = nc.dram_tensor("ident", [128, 128], BF16, kind="ExternalInput").ap()
    identf = nc.dram_tensor("identf", [128, 128], F32, kind="ExternalInput").ap()
    q_fm = nc.dram_tensor("q_fm", [D, NTOK], BF16, kind="ExternalOutput").ap()
    k_fm = nc.dram_tensor("k_fm", [D, NTOK], BF16, kind="ExternalOutput").ap()
    v_fm = nc.dram_tensor("v_fm", [D, NTOK], BF16, kind="ExternalOutput").ap()
    v_tm = nc.dram_tensor("v_tm", [NTOK, D], BF16, kind="ExternalOutput").ap()
    SCQ = float(N) ** -0.5
    with c:
        idb = c.sb("idb", [128, 128], BF16)
        idf = c.sb("idf", [128, 128], F32)
        xs = c.sb("xs", [128, D], F32)
        sq = c.sb("sq", [128, D], F32)
        st8 = c.sb("st8", [128, 8], F32)
        ytmf = c.sb("ytmf", [128, D], F32)
        ytm = c.sb("ytm", [128, D], BF16)
        yfm = c.sb("yfm", [128, DC * NTOK], BF16)
        ypfm = c.sb("ypfm", [128, DC * NTOK], BF16)
        wqb = c.sb("wqb", [128, DC * D], BF16)
        wkb = c.sb("wkb", [128, DC * D], BF16)
        ebf = c.sb("ebf", [128, DC * D], F32)
        eb = c.sb("eb", [128, DC * D], BF16)
        et = c.sb("et", [128, DC * D], BF16)
        csum = c.sb("csum", [128, 8], F32)
        rec = c.sb("rec", [128, 8], F32)
        recrow = c.sb("recrow", [1, D], F32)
        recbc = c.sb("recbc", [128, D], F32)
        srow = c.sb("srow", [1, NTOK], F32)
        srowb = c.sb("srowb", [1, NTOK], BF16)
        scol = c.sb("scol", [128, 1], F32)
        tmpf = c.sb("tmpf", [128, D], F32)
        negone = c.sb("negone", [128, 1], F32)
        onesb = c.sb("onesb", [1, 128], BF16)
        onesf = c.sb("onesf", [1, 128], F32)
        outst = c.sb("outst", [128, 8 * 512], BF16)
        ptr = c.ps("ptr", 1024, BF16)
        prow = c.ps("prow", 512)
        prb = c.ps("prb", 1024)
        pj = [c.ps(f"pj{i}", 512) for i in range(4)]

        @c.on("gpsimd")
        def _(g):
            c.inc(g.dma_start(out=idb[:], in_=ident), "gd", 16)
            c.inc(g.dma_start(out=idf[:], in_=identf), "gd", 16)
            for dc in range(DC):
                c.inc(g.dma_start(out=wqb[:, dc * D:(dc + 1) * D],
                                  in_=wq[128 * dc:128 * (dc + 1), :]), "gd", 16)
            for dc in range(DC):
                c.inc(g.dma_start(out=wkb[:, dc * D:(dc + 1) * D],
                                  in_=wk[128 * dc:128 * (dc + 1), :]), "gd", 16)
            for kb in range(DC):
                c.inc(g.dma_start(out=ebf[:, kb * D:(kb + 1) * D],
                                  in_=wv[128 * kb:128 * (kb + 1), :]), "gd", 16)

        @c.on("vector")
        def _(v):
            v.memset(onesb[:], 1.0)
            v.memset(onesf[:], 1.0)
            c.inc(v.memset(negone[:], -1.0), "ve")

        @c.on("scalar")
        def _(s):
            c.wait(s, "gd")
            c.wait(s, "ve")
            for kb in range(DC):
                c.inc(s.activation(ebf[:, kb * D:(kb + 1) * D],
                                   ebf[:, kb * D:(kb + 1) * D], AF.Exp), "se")
                c.wait(s, "se")
                c.inc(s.activation(eb[:, kb * D:(kb + 1) * D],
                                   ebf[:, kb * D:(kb + 1) * D], AF.Identity,
                                   bias=negone[:]), "se")

        for jb in range(8):
            @c.on("tensor")
            def _(t, jb=jb):
                c.wait(t, "se")
                c.wait(t, "ve")
                for kb in range(DC):
                    last = t.transpose(ptr[:, kb * 128:(kb + 1) * 128],
                                       eb[:, kb * D + jb * 128:kb * D + (jb + 1) * 128],
                                       idb[:])
                c.inc(last, "te")

            @c.on("vector")
            def _(v, jb=jb):
                c.wait(v, "te")
                c.inc(v.tensor_copy(et[:, jb * D:(jb + 1) * D], ptr[:, 0:1024]), "ve")

            @c.on("scalar")
            def _(s, jb=jb):
                c.wait(s, "ve")
                c.wait(s, "se")
                c.inc(s.activation(et[:, jb * D:(jb + 1) * D], et[:, jb * D:(jb + 1) * D],
                                   AF.Identity, accum_out=csum[:, jb:jb + 1]), "se")

        @c.on("vector")
        def _(v):
            c.wait(v, "se")
            c.inc(v.tensor_scalar_add(csum[:], csum[:], float(D)), "ve")
            c.wait(v, "ve")
            c.inc(v.reciprocal(rec[:], csum[:]), "ve")
            c.wait(v, "ve")
            c.inc(v.tensor_scalar_mul(rec[:], rec[:], 1.0 / D), "ve")

        for half in range(2):
            @c.on("tensor")
            def _(t, half=half):
                c.wait(t, "ve")
                for j in range(4):
                    jb = 4 * half + j
                    last = t.matmul(prow[0:1, 128 * j:128 * (j + 1)], rec[:, jb:jb + 1],
                                    idf[:], start=(j == 0), stop=(j == 3))
                c.inc(last, "te")

            @c.on("vector")
            def _(v, half=half):
                c.wait(v, "te")
                c.inc(v.tensor_copy(recrow[:, 512 * half:512 * (half + 1)],
                                    prow[0:1, 0:512]), "ve")

        @c.on("tensor")
        def _(t):
            c.wait(t, "ve")
            t.matmul(prb[:, 0:512], onesf[:], recrow[:, 0:512], start=True, stop=True)
            last = t.matmul(prb[:, 512:1024], onesf[:], recrow[:, 512:1024],
                            start=True, stop=True)
            c.inc(last, "te")

        @c.on("vector")
        def _(v):
            c.wait(v, "te")
            c.inc(v.tensor_copy(recbc[:], prb[:, 0:1024]), "ve")

        for tt in range(NT):
            @c.on("sync")
            def _(sy, tt=tt):
                if tt > 0:
                    c.wait(sy, "se")
                c.inc(sy.dma_start(out=xs[:], in_=xr[128 * tt:128 * (tt + 1), :]), "dma", 16)

            @c.on("vector")
            def _(v):
                c.wait(v, "dma")

            ln_tile(c, xs[:], ytmf[:], sq, st8, D)

            @c.on("vector")
            def _(v, tt=tt):
                c.wait(v, "se")
                c.inc(v.tensor_copy(ytm[:], ytmf[:]), "ve")
                c.inc(v.tensor_tensor(tmpf[:], ytmf[:], recbc[:], op=ALU.mult), "ve")
                c.wait(v, "ve")
                c.inc(v.reduce_sum(scol[:], tmpf[:], axis=AX), "ve")

            @c.on("tensor")
            def _(t, tt=tt):
                c.wait(t, "ve")
                for dc2 in range(DC):
                    last = t.transpose(ptr[:, dc2 * 128:(dc2 + 1) * 128],
                                       ytm[:, dc2 * 128:(dc2 + 1) * 128], idb[:])
                c.inc(last, "te")
                last = t.matmul(prow[0:1, 128 * (tt % 4):128 * (tt % 4 + 1)], scol[:],
                                idf[:], start=(tt % 4 == 0), stop=(tt % 4 == 3))
                c.inc(last, "te")

            @c.on("vector")
            def _(v, tt=tt):
                c.wait(v, "te")
                for dc2 in range(DC):
                    last = v.tensor_copy(yfm[:, dc2 * NTOK + 128 * tt:dc2 * NTOK + 128 * (tt + 1)],
                                         ptr[:, dc2 * 128:(dc2 + 1) * 128])
                c.inc(last, "ve")
                if tt % 4 == 3:
                    c.inc(v.tensor_copy(srow[:, 512 * (tt // 4):512 * (tt // 4 + 1)],
                                        prow[0:1, 0:512]), "ve")

        @c.on("vector")
        def _(v):
            c.wait(v, "ve")
            c.inc(v.tensor_copy(srowb[:], srow[:]), "ve")
            for jb in range(DC):
                last = v.tensor_scalar(ypfm[:, jb * NTOK:(jb + 1) * NTOK],
                                       yfm[:, jb * NTOK:(jb + 1) * NTOK],
                                       rec[:, jb:jb + 1], None, op0=ALU.mult)
            c.inc(last, "ve")

        def project(wtiles, rhs, out_dram, scale):
            for ncx in range(NC):
                for dg in range(2):
                    @c.on("tensor")
                    def _(t, ncx=ncx, dg=dg):
                        c.wait(t, "ve")
                        c.wait(t, "se")
                        for dl in range(4):
                            do = 4 * dg + dl
                            for di in range(DC):
                                last = t.matmul(pj[dl][:, 0:512], wtiles(di, do),
                                                rhs(di, ncx), start=(di == 0), stop=(di == DC - 1))
                        c.inc(last, "te")

                    @c.on("scalar")
                    def _(s, ncx=ncx, dg=dg):
                        c.wait(s, "te")
                        c.wait(s, "st")
                        for dl in range(4):
                            do = 4 * dg + dl
                            if scale == 1.0:
                                last = s.copy(outst[:, do * 512:(do + 1) * 512], pj[dl][:, 0:512])
                            else:
                                last = s.activation(outst[:, do * 512:(do + 1) * 512],
                                                    pj[dl][:, 0:512], AF.Copy, scale=scale)
                        c.inc(last, "se")

                    @c.on("sync")
                    def _(sy, ncx=ncx, dg=dg):
                        c.wait(sy, "se")
                        for dl in range(4):
                            do = 4 * dg + dl
                            c.inc(sy.dma_start(out=out_dram[do * 128:(do + 1) * 128, 512 * ncx:512 * (ncx + 1)],
                                               in_=outst[:, do * 512:(do + 1) * 512]), "st", 16)

        project(lambda di, do: wqb[:, di * D + do * 128:di * D + (do + 1) * 128],
                lambda di, ncx: yfm[:, di * NTOK + 512 * ncx:di * NTOK + 512 * (ncx + 1)],
                q_fm, SCQ)
        project(lambda di, do: wkb[:, di * D + do * 128:di * D + (do + 1) * 128],
                lambda di, ncx: yfm[:, di * NTOK + 512 * ncx:di * NTOK + 512 * (ncx + 1)],
                k_fm, 1.0)

        for ncx in range(NC):
            for dg in range(2):
                @c.on("tensor")
                def _(t, ncx=ncx, dg=dg):
                    c.wait(t, "ve")
                    c.wait(t, "se")
                    for dl in range(4):
                        do = 4 * dg + dl
                        for di in range(DC):
                            t.matmul(pj[dl][:, 0:512],
                                     et[:, di * D + do * 128:di * D + (do + 1) * 128],
                                     ypfm[:, di * NTOK + 512 * ncx:di * NTOK + 512 * (ncx + 1)],
                                     start=(di == 0), stop=False)
                        last = t.matmul(pj[dl][:, 0:512], onesb[:],
                                        srowb[:, 512 * ncx:512 * (ncx + 1)],
                                        start=False, stop=True)
                    c.inc(last, "te")

                @c.on("scalar")
                def _(s, ncx=ncx, dg=dg):
                    c.wait(s, "te")
                    c.wait(s, "st")
                    for dl in range(4):
                        do = 4 * dg + dl
                        last = s.copy(outst[:, do * 512:(do + 1) * 512], pj[dl][:, 0:512])
                    c.inc(last, "se")

                @c.on("sync")
                def _(sy, ncx=ncx, dg=dg):
                    c.wait(sy, "se")
                    for dl in range(4):
                        do = 4 * dg + dl
                        c.inc(sy.dma_start(out=v_fm[do * 128:(do + 1) * 128, 512 * ncx:512 * (ncx + 1)],
                                           in_=outst[:, do * 512:(do + 1) * 512]), "st", 16)

            for sub in range(4):
                @c.on("tensor")
                def _(t, ncx=ncx, sub=sub):
                    c.wait(t, "se")
                    c.wait(t, "ve")
                    for do in range(DC):
                        last = t.transpose(ptr[:, do * 128:(do + 1) * 128],
                                           outst[:, do * 512 + sub * 128:do * 512 + (sub + 1) * 128],
                                           idb[:])
                    c.inc(last, "te")

                @c.on("vector")
                def _(v, ncx=ncx, sub=sub):
                    c.wait(v, "te")
                    c.wait(v, "st")
                    c.inc(v.tensor_copy(ytm[:, 0:1024], ptr[:, 0:1024]), "ve")

                @c.on("sync")
                def _(sy, ncx=ncx, sub=sub):
                    c.wait(sy, "ve")
                    c.inc(sy.dma_start(out=v_tm[512 * ncx + 128 * sub:512 * ncx + 128 * (sub + 1), :],
                                       in_=ytm[:, 0:1024]), "st", 16)

        @c.on("sync")
        def _(sy):
            c.wait(sy, "st")
    return c


def _pstg2(pstg, stg, gi):
    buf = pstg if (gi // 2) % 2 == 0 else stg
    return buf[:, (gi % 2) * 512:(gi % 2 + 1) * 512]


def build_l0attn():
    c = Ctx("l0attn")
    nc = c.nc
    qf = nc.dram_tensor("qf", [D, N], BF16, kind="ExternalInput").ap()
    kf = nc.dram_tensor("kf", [D, NTOK], BF16, kind="ExternalInput").ap()
    vt = nc.dram_tensor("vt", [NTOK, D], BF16, kind="ExternalInput").ap()
    part = nc.dram_tensor("part", [D, N], BF16, kind="ExternalOutput").ap()
    pT = nc.dram_tensor("pT", [NTOK, N], BF16)
    NQ = N // 512
    with c:
        kb = c.sb("kb", [128, DC * NTOK], BF16)
        vtm = c.sb("vtm", [128, NT * D], BF16)
        vp = c.sb("vp", [128, NT * D], BF16)
        qs = c.sb("qs", [128, DC * 512], BF16)
        stg = c.sb("stg", [128, 2 * 512], BF16)
        pstg = c.sb("pstg", [128, 2 * 512], BF16)
        zparts = c.sb("zparts", [128, NT * NQ], F32)
        z1 = c.sb("z1", [128, 1], F32)
        outst = c.sb("outst", [128, 8 * 512], BF16)
        pj = [c.ps(f"pj{i}", 512) for i in range(8)]

        @c.on("gpsimd")
        def _(g):
            for di in range(DC):
                c.inc(g.dma_start(out=kb[:, di * NTOK:(di + 1) * NTOK],
                                  in_=kf[128 * di:128 * (di + 1), :]), "gd", 16)
            for mt in range(NT):
                c.inc(g.dma_start(out=vtm[:, mt * D:(mt + 1) * D],
                                  in_=vt[128 * mt:128 * (mt + 1), :]), "gd", 16)

        qs2 = c.sb("qs2", [128, DC * 512], BF16)
        stg2 = c.sb("stg2", [128, 2 * 512], BF16)
        sS_te = {}
        sS_se = {}
        sS_st = {}

        def qs_of(ncx):
            return qs if ncx % 2 == 0 else qs2

        def stgb(i):
            return stg[:, (i % 2) * 512:(i % 2 + 1) * 512] if (i // 2) % 2 == 0 \
                else stg2[:, (i % 2) * 512:(i % 2 + 1) * 512]

        stS0 = c.cnt["st"]
        for ncx in range(NQ):
            @c.on("sync")
            def _(sy, ncx=ncx):
                if ncx >= 2:
                    c.wait(sy, "te", sS_te[(ncx - 2, NT - 1)])
                for di in range(DC):
                    c.inc(sy.dma_start(out=qs_of(ncx)[:, di * 512:(di + 1) * 512],
                                       in_=qf[128 * di:128 * (di + 1), 512 * ncx:512 * (ncx + 1)]),
                          "dma", 16)
            sS_st[("qld", ncx)] = c.cnt["dma"]
            se0 = c.cnt["se"]

            def se_of(gi, se0=se0, ncx=ncx):
                return se0 + (gi - ncx * NT) + 1

            @c.on("tensor")
            def _(t, ncx=ncx, se0=se0):
                for mt in range(NT):
                    gi = ncx * NT + mt
                    if mt == 0:
                        c.wait(t, "dma", sS_st[("qld", ncx)])
                        if ncx == 0:
                            c.wait(t, "gd")
                    if gi >= 4:
                        if mt < 4:
                            c.wait(t, "se", sS_se[gi - 4])
                        else:
                            c.wait(t, "se", se_of(gi - 4))
                    for di in range(DC):
                        last = t.matmul(pj[gi % 4][:, 0:512],
                                        kb[:, di * NTOK + 128 * mt:di * NTOK + 128 * (mt + 1)],
                                        qs_of(ncx)[:, di * 512:(di + 1) * 512],
                                        start=(di == 0), stop=(di == DC - 1))
                    c.inc(last, "te")
                    sS_te[(ncx, mt)] = c.cnt["te"]

            @c.on("scalar")
            def _(s, ncx=ncx):
                for mt in range(NT):
                    gi = ncx * NT + mt
                    c.wait(s, "te", sS_te[(ncx, mt)])
                    if gi >= 4:
                        c.wait(s, "st", stS0 + 16 * (gi - 4 + 1))
                    c.inc(s.activation(stgb(gi),
                                       pj[gi % 4][:, 0:512], AF.Exp,
                                       accum_out=zparts[:, mt * NQ + ncx:mt * NQ + ncx + 1]), "se")
                    sS_se[gi] = c.cnt["se"]

            @c.on("sync")
            def _(sy, ncx=ncx):
                for mt in range(NT):
                    gi = ncx * NT + mt
                    c.wait(sy, "se", sS_se[gi])
                    c.inc(sy.dma_start(out=pT[128 * mt:128 * (mt + 1), 512 * ncx:512 * (ncx + 1)],
                                       in_=stgb(gi)), "st", 16)
                    sS_st[gi] = c.cnt["st"]

        for mt in range(NT):
            @c.on("vector")
            def _(v, mt=mt):
                c.wait(v, "se")
                c.inc(v.reduce_sum(z1[:], zparts[:, mt * NQ:(mt + 1) * NQ], axis=AX), "ve")
                c.wait(v, "ve")
                c.inc(v.reciprocal(z1[:], z1[:]), "ve")
                c.wait(v, "ve")
                c.inc(v.tensor_scalar(vp[:, mt * D:(mt + 1) * D], vtm[:, mt * D:(mt + 1) * D],
                                      z1[:], None, op0=ALU.mult), "ve")

        pB_dma = {}
        pB_te = {}
        pB_se = {}
        stB0 = None
        for half in range(2):
            for ncx in range(NQ):
                for mt in range(NT):
                    @c.on("sync")
                    def _(sy, half=half, ncx=ncx, mt=mt):
                        gi = (half * NQ + ncx) * NT + mt
                        if gi < 4:
                            c.wait(sy, "st")
                        else:
                            c.wait(sy, "te", pB_te[gi - 4])
                        c.inc(sy.dma_start(out=_pstg2(pstg, stg, gi),
                                           in_=pT[128 * mt:128 * (mt + 1), 512 * ncx:512 * (ncx + 1)]),
                              "dma", 16)
                        pB_dma[gi] = c.cnt["dma"]

                    @c.on("tensor")
                    def _(t, half=half, ncx=ncx, mt=mt):
                        gi = (half * NQ + ncx) * NT + mt
                        c.wait(t, "dma", pB_dma[gi])
                        if mt == 0:
                            if ncx == 0 and half == 0:
                                c.wait(t, "ve")
                            c.wait(t, "se")
                        for dl in range(4):
                            do = 4 * half + dl
                            last = t.matmul(pj[dl][:, 0:512],
                                            vp[:, mt * D + do * 128:mt * D + (do + 1) * 128],
                                            _pstg2(pstg, stg, gi),
                                            start=(mt == 0), stop=(mt == NT - 1))
                        c.inc(last, "te")
                        pB_te[gi] = c.cnt["te"]

                @c.on("scalar")
                def _(s, half=half, ncx=ncx):
                    c.wait(s, "te")
                    c.wait(s, "st", max(0, c.cnt["st"] - 16 * 64))
                    for dl in range(4):
                        do = 4 * half + dl
                        last = s.copy(outst[:, do * 512:(do + 1) * 512], pj[dl][:, 0:512])
                    c.inc(last, "se")

                @c.on("sync")
                def _(sy, half=half, ncx=ncx):
                    c.wait(sy, "se")
                    for dl in range(4):
                        do = 4 * half + dl
                        c.inc(sy.dma_start(out=part[do * 128:(do + 1) * 128, 512 * ncx:512 * (ncx + 1)],
                                           in_=outst[:, do * 512:(do + 1) * 512]), "st", 16)

        @c.on("sync")
        def _(sy):
            c.wait(sy, "st")
    return c


def build_l0red_ffn():
    c = Ctx("l0red")
    nc = c.nc
    xr = nc.dram_tensor("xr", [NTOK, D], F32, kind="ExternalInput").ap()
    p0 = nc.dram_tensor("p0", [D, NTOK], BF16, kind="ExternalInput").ap()
    p1 = nc.dram_tensor("p1", [D, NTOK], BF16, kind="ExternalInput").ap()
    p2 = nc.dram_tensor("p2", [D, NTOK], BF16, kind="ExternalInput").ap()
    p3 = nc.dram_tensor("p3", [D, NTOK], BF16, kind="ExternalInput").ap()
    vfm = nc.dram_tensor("vfm", [D, NTOK], BF16, kind="ExternalInput").ap()
    wg_i = nc.dram_tensor("wg_i", [3 * D, D], I32, kind="ExternalInput").ap()
    wu_i = nc.dram_tensor("wu_i", [3 * D, D], I32, kind="ExternalInput").ap()
    wd_i = nc.dram_tensor("wd_i", [D, 3 * D], I32, kind="ExternalInput").ap()
    ident = nc.dram_tensor("ident", [128, 128], BF16, kind="ExternalInput").ap()
    x2o = nc.dram_tensor("x2o", [NTOK, D], F32, kind="ExternalOutput").ap()
    x1_d = nc.dram_tensor("x1_d", [NTOK, D], F32)
    wgT_d = nc.dram_tensor("wgT_d", [D, 3 * D], BF16)
    wuT_d = nc.dram_tensor("wuT_d", [D, 3 * D], BF16)
    wdT_d = nc.dram_tensor("wdT_d", [3 * D, D], BF16)
    H3 = 3 * D
    HC = H3 // 128
    with c:
        idb = c.sb("idb", [128, 128], BF16)
        buf0 = c.sb("buf0", [128, DC * 512], BF16)
        buf1 = c.sb("buf1", [128, DC * 512], BF16)
        afm = c.sb("afm", [128, DC * 512], BF16)
        xs = c.sb("xs", [128, D], F32)
        sq = c.sb("sq", [128, D], F32)
        st8 = c.sb("st8", [128, 8], F32)
        lnt = c.sb("lnt", [128, D], F32)
        x1t = c.sb("x1t", [128, D], F32)
        atm = c.sb("atm", [128, D], F32)
        ytm = c.sb("ytm", [128, D], BF16)
        yfm = c.sb("yfm", [128, DC * NTOK], BF16)
        wi = c.sb("wi", [128, D], I32)
        wb = c.sb("wb", [128, D], BF16)
        ws = c.sb("ws", [128, 24 * 1024], BF16)
        gb = c.sb("gb", [128, HC * 512], BF16)
        ub = c.sb("ub", [128, HC * 512], BF16)
        outst = c.sb("outst", [128, 8 * 512], BF16)
        ptr = c.ps("ptr", 1024, BF16)
        pj = [c.ps(f"pj{i}", 512) for i in range(4)]

        @c.on("gpsimd")
        def _(g):
            c.inc(g.dma_start(out=idb[:], in_=ident), "gd", 16)

        for ncx in range(NC):
            @c.on("sync")
            def _(sy, ncx=ncx):
                c.wait(sy, "ve")
                for di in range(DC):
                    c.inc(sy.dma_start(out=buf0[:, di * 512:(di + 1) * 512],
                                       in_=p0[128 * di:128 * (di + 1), 512 * ncx:512 * (ncx + 1)]), "dma", 16)
                    c.inc(sy.dma_start(out=buf1[:, di * 512:(di + 1) * 512],
                                       in_=p1[128 * di:128 * (di + 1), 512 * ncx:512 * (ncx + 1)]), "dma", 16)

            @c.on("vector")
            def _(v, ncx=ncx):
                c.wait(v, "dma")
                c.inc(v.tensor_tensor(afm[:], buf0[:], buf1[:], op=ALU.add), "ve")

            @c.on("sync")
            def _(sy, ncx=ncx):
                c.wait(sy, "ve")
                for di in range(DC):
                    c.inc(sy.dma_start(out=buf0[:, di * 512:(di + 1) * 512],
                                       in_=p2[128 * di:128 * (di + 1), 512 * ncx:512 * (ncx + 1)]), "dma", 16)
                    c.inc(sy.dma_start(out=buf1[:, di * 512:(di + 1) * 512],
                                       in_=p3[128 * di:128 * (di + 1), 512 * ncx:512 * (ncx + 1)]), "dma", 16)

            @c.on("vector")
            def _(v, ncx=ncx):
                c.wait(v, "dma")
                c.inc(v.tensor_tensor(buf0[:], buf0[:], buf1[:], op=ALU.add), "ve")
                c.wait(v, "ve")
                c.inc(v.tensor_tensor(afm[:], afm[:], buf0[:], op=ALU.add), "ve")

            @c.on("sync")
            def _(sy, ncx=ncx):
                c.wait(sy, "ve")
                for di in range(DC):
                    c.inc(sy.dma_start(out=buf1[:, di * 512:(di + 1) * 512],
                                       in_=vfm[128 * di:128 * (di + 1), 512 * ncx:512 * (ncx + 1)]), "dma", 16)

            @c.on("vector")
            def _(v, ncx=ncx):
                c.wait(v, "dma")
                c.inc(v.tensor_tensor(afm[:], afm[:], buf1[:], op=ALU.subtract), "ve")

            for sub in range(4):
                tt = 4 * ncx + sub
                @c.on("tensor")
                def _(t, ncx=ncx, sub=sub):
                    c.wait(t, "ve")
                    for di in range(DC):
                        last = t.transpose(ptr[:, di * 128:(di + 1) * 128],
                                           afm[:, di * 512 + sub * 128:di * 512 + (sub + 1) * 128],
                                           idb[:])
                    c.inc(last, "te")

                @c.on("vector")
                def _(v, ncx=ncx, sub=sub):
                    c.wait(v, "te")
                    c.inc(v.tensor_copy(atm[:], ptr[:, 0:1024]), "ve")

                @c.on("sync")
                def _(sy, tt=tt):
                    c.wait(sy, "se")
                    c.inc(sy.dma_start(out=xs[:], in_=xr[128 * tt:128 * (tt + 1), :]), "dma", 16)

                @c.on("vector")
                def _(v):
                    c.wait(v, "dma")

                ln_tile(c, atm[:], lnt[:], sq, st8, D, res=xs[:], res_out=x1t[:])

                @c.on("sync")
                def _(sy, tt=tt):
                    c.wait(sy, "ve")
                    c.inc(sy.dma_start(out=x1_d[128 * tt:128 * (tt + 1), :], in_=x1t[:]), "st", 16)

        def prep_weight(w_i, rows, cols, wT_d):
            for rt in range(rows // 128):
                for ct in range(cols // 1024):
                    @c.on("sync")
                    def _(sy, rt=rt, ct=ct):
                        c.wait(sy, "ve")
                        c.inc(sy.dma_start(out=wi[:],
                                           in_=w_i[128 * rt:128 * (rt + 1), 1024 * ct:1024 * (ct + 1)]),
                              "dma", 16)

                    @c.on("vector")
                    def _(v):
                        c.wait(v, "dma")
                        c.wait(v, "te")
                        c.inc(v.tensor_scalar(wb[:], wi[:], 1.0 / 127.0, None, op0=ALU.mult), "ve")

                    @c.on("tensor")
                    def _(t):
                        c.wait(t, "ve")
                        c.wait(t, "st")
                        for dc2 in range(DC):
                            last = t.transpose(ptr[:, dc2 * 128:(dc2 + 1) * 128],
                                               wb[:, dc2 * 128:(dc2 + 1) * 128], idb[:])
                        c.inc(last, "te")

                    @c.on("vector")
                    def _(v):
                        c.wait(v, "te")
                        c.inc(v.tensor_copy(ytm[:, 0:1024], ptr[:, 0:1024]), "ve")

                    @c.on("sync")
                    def _(sy, rt=rt, ct=ct):
                        c.wait(sy, "ve")
                        for dc2 in range(DC):
                            c.inc(sy.dma_start(
                                out=wT_d[1024 * ct + 128 * dc2:1024 * ct + 128 * (dc2 + 1),
                                         128 * rt:128 * (rt + 1)],
                                in_=ytm[:, dc2 * 128:(dc2 + 1) * 128]), "st", 16)

        prep_weight(wg_i, H3, D, wgT_d)
        prep_weight(wu_i, H3, D, wuT_d)
        prep_weight(wd_i, D, H3, wdT_d)

        for tt in range(NT):
            @c.on("sync")
            def _(sy, tt=tt):
                if tt > 0:
                    c.wait(sy, "se")
                c.inc(sy.dma_start(out=xs[:], in_=x1_d[128 * tt:128 * (tt + 1), :]), "dma", 16)

            @c.on("vector")
            def _(v):
                c.wait(v, "dma")

            ln_tile(c, xs[:], ytm[:], sq, st8, D)

            @c.on("tensor")
            def _(t, tt=tt):
                c.wait(t, "se")
                c.wait(t, "ve")
                for dc2 in range(DC):
                    last = t.transpose(ptr[:, dc2 * 128:(dc2 + 1) * 128],
                                       ytm[:, dc2 * 128:(dc2 + 1) * 128], idb[:])
                c.inc(last, "te")

            @c.on("vector")
            def _(v, tt=tt):
                c.wait(v, "te")
                for dc2 in range(DC):
                    last = v.tensor_copy(yfm[:, dc2 * NTOK + 128 * tt:dc2 * NTOK + 128 * (tt + 1)],
                                         ptr[:, dc2 * 128:(dc2 + 1) * 128])
                c.inc(last, "ve")

        for ncx in range(NC):
            @c.on("sync")
            def _(sy, ncx=ncx):
                c.wait(sy, "te")
                for di in range(DC):
                    c.inc(sy.dma_start(out=ws[:, di * H3:(di + 1) * H3],
                                       in_=wgT_d[128 * di:128 * (di + 1), :]), "dma", 16)

            g_te = {}

            def gu_phase(ncx, dest, silu):
                se0 = c.cnt["se"]

                @c.on("tensor")
                def _(t, ncx=ncx, se0=se0, dest=dest):
                    for g in range(12):
                        if g == 0:
                            c.wait(t, "dma")
                            c.wait(t, "ve")
                            c.wait(t, "se", se0)
                        if g >= 2:
                            c.wait(t, "se", se0 + g - 1)
                        for hl in range(2):
                            hh = 2 * g + hl
                            bank = pj[(g % 2) * 2 + hl]
                            for di in range(DC):
                                last = t.matmul(bank[:, 0:512],
                                                ws[:, di * H3 + hh * 128:di * H3 + (hh + 1) * 128],
                                                yfm[:, di * NTOK + 512 * ncx:di * NTOK + 512 * (ncx + 1)],
                                                start=(di == 0), stop=(di == DC - 1))
                        c.inc(last, "te")
                        g_te[g] = c.cnt["te"]

                @c.on("scalar")
                def _(s, ncx=ncx, dest=dest, silu=silu):
                    for g in range(12):
                        c.wait(s, "te", g_te[g])
                        for hl in range(2):
                            hh = 2 * g + hl
                            bank = pj[(g % 2) * 2 + hl]
                            if silu:
                                last = s.activation(dest[:, hh * 512:(hh + 1) * 512],
                                                    bank[:, 0:512], AF.Silu)
                            else:
                                last = s.copy(dest[:, hh * 512:(hh + 1) * 512], bank[:, 0:512])
                        c.inc(last, "se")

            gu_phase(ncx, gb, True)

            @c.on("sync")
            def _(sy, ncx=ncx):
                c.wait(sy, "te")
                for di in range(DC):
                    c.inc(sy.dma_start(out=ws[:, di * H3:(di + 1) * H3],
                                       in_=wuT_d[128 * di:128 * (di + 1), :]), "dma", 16)

            gu_phase(ncx, ub, False)

            @c.on("vector")
            def _(v, ncx=ncx):
                c.wait(v, "se")
                for hh in range(HC):
                    last = v.tensor_tensor(gb[:, hh * 512:(hh + 1) * 512],
                                           gb[:, hh * 512:(hh + 1) * 512],
                                           ub[:, hh * 512:(hh + 1) * 512], op=ALU.mult)
                c.inc(last, "ve")

            @c.on("sync")
            def _(sy, ncx=ncx):
                c.wait(sy, "te")
                for hh in range(HC):
                    c.inc(sy.dma_start(out=ws[:, hh * 1024:(hh + 1) * 1024],
                                       in_=wdT_d[128 * hh:128 * (hh + 1), :]), "dma", 16)

            d_te = {}
            se0d = c.cnt["se"]

            @c.on("tensor")
            def _(t, ncx=ncx, se0d=se0d):
                for g in range(4):
                    if g == 0:
                        c.wait(t, "dma")
                        c.wait(t, "ve")
                        c.wait(t, "se", se0d)
                    if g >= 2:
                        c.wait(t, "se", se0d + g - 1)
                    for dl in range(2):
                        do = 2 * g + dl
                        bank = pj[(g % 2) * 2 + dl]
                        for hh in range(HC):
                            last = t.matmul(bank[:, 0:512],
                                            ws[:, hh * 1024 + do * 128:hh * 1024 + (do + 1) * 128],
                                            gb[:, hh * 512:(hh + 1) * 512],
                                            start=(hh == 0), stop=(hh == HC - 1))
                    c.inc(last, "te")
                    d_te[g] = c.cnt["te"]

            @c.on("scalar")
            def _(s, ncx=ncx):
                for g in range(4):
                    c.wait(s, "te", d_te[g])
                    for dl in range(2):
                        do = 2 * g + dl
                        bank = pj[(g % 2) * 2 + dl]
                        last = s.copy(outst[:, do * 512:(do + 1) * 512], bank[:, 0:512])
                    c.inc(last, "se")

            for sub in range(4):
                tt = 4 * ncx + sub
                @c.on("tensor")
                def _(t, ncx=ncx, sub=sub):
                    c.wait(t, "se")
                    c.wait(t, "ve")
                    for di in range(DC):
                        last = t.transpose(ptr[:, di * 128:(di + 1) * 128],
                                           outst[:, di * 512 + sub * 128:di * 512 + (sub + 1) * 128],
                                           idb[:])
                    c.inc(last, "te")

                @c.on("vector")
                def _(v, ncx=ncx, sub=sub):
                    c.wait(v, "te")
                    c.inc(v.tensor_copy(atm[:], ptr[:, 0:1024]), "ve")

                @c.on("sync")
                def _(sy, tt=tt):
                    c.wait(sy, "se")
                    c.inc(sy.dma_start(out=xs[:], in_=x1_d[128 * tt:128 * (tt + 1), :]), "dma", 16)

                @c.on("vector")
                def _(v):
                    c.wait(v, "dma")

                ln_tile(c, atm[:], lnt[:], sq, st8, D, res=xs[:], res_out=x1t[:])

                @c.on("sync")
                def _(sy, tt=tt):
                    c.wait(sy, "ve")
                    c.inc(sy.dma_start(out=x2o[128 * tt:128 * (tt + 1), :], in_=x1t[:]), "st", 16)

        @c.on("sync")
        def _(sy):
            c.wait(sy, "st")
    return c


def build_l1pre():
    c = Ctx("l1pre")
    nc = c.nc
    xr = nc.dram_tensor("xr", [NTOK, D], F32, kind="ExternalInput").ap()
    Wdq = nc.dram_tensor("Wdq", [128, D], F32, kind="ExternalInput").ap()
    Wuq = nc.dram_tensor("Wuq", [D, 128], F32, kind="ExternalInput").ap()
    Wdkv = nc.dram_tensor("Wdkv", [128, D], F32, kind="ExternalInput").ap()
    Wuk = nc.dram_tensor("Wuk", [D, 128], F32, kind="ExternalInput").ap()
    ident = nc.dram_tensor("ident", [128, 128], BF16, kind="ExternalInput").ap()
    ql_o = nc.dram_tensor("ql_o", [128, 16 * NTOK], BF16, kind="ExternalOutput").ap()
    ckv_o = nc.dram_tensor("ckv_o", [NTOK, 128], BF16, kind="ExternalOutput").ap()
    with c:
        idb = c.sb("idb", [128, 128], BF16)
        xs = c.sb("xs", [128, D], F32)
        sq = c.sb("sq", [128, D], F32)
        st8 = c.sb("st8", [128, 8], F32)
        ytm = c.sb("ytm", [128, D], BF16)
        yfm = c.sb("yfm", [128, DC * NTOK], BF16)
        wdq = c.sb("wdq", [128, D], BF16)
        wdqT = c.sb("wdqT", [128, DC * 128], BF16)
        wuq = c.sb("wuq", [128, DC * 128], BF16)
        wuqT = c.sb("wuqT", [128, DC * 128], BF16)
        wdkv = c.sb("wdkv", [128, D], BF16)
        wdkvT = c.sb("wdkvT", [128, DC * 128], BF16)
        wuk = c.sb("wuk", [128, DC * 128], BF16)
        at = c.sb("at", [128, DC * D], BF16)
        keffs = c.sb("keffs", [128, DC * 128], BF16)
        t1 = c.sb("t1", [128, NTOK], BF16)
        qfm = c.sb("qfm", [128, DC * NTOK], BF16)
        qlb = c.sb("qlb", [128, 512], BF16)
        ckfm = c.sb("ckfm", [128, NTOK], BF16)
        outst = c.sb("outst", [128, 1024], BF16)
        ptr = c.ps("ptr", 1024, BF16)
        pj = [c.ps(f"pj{i}", 512) for i in range(4)]

        @c.on("gpsimd")
        def _(g):
            c.inc(g.dma_start(out=idb[:], in_=ident), "gd", 16)
            c.inc(g.dma_start(out=wdq[:], in_=Wdq), "gd", 16)
            c.inc(g.dma_start(out=wdkv[:], in_=Wdkv), "gd", 16)
            for cc in range(DC):
                c.inc(g.dma_start(out=wuq[:, cc * 128:(cc + 1) * 128],
                                  in_=Wuq[128 * cc:128 * (cc + 1), :]), "gd", 16)
                c.inc(g.dma_start(out=wuk[:, cc * 128:(cc + 1) * 128],
                                  in_=Wuk[128 * cc:128 * (cc + 1), :]), "gd", 16)

        @c.on("tensor")
        def _(t):
            c.wait(t, "gd")
            for cc in range(DC):
                last = t.transpose(ptr[:, cc * 128:(cc + 1) * 128],
                                   wdq[:, cc * 128:(cc + 1) * 128], idb[:])
            c.inc(last, "te")

        @c.on("vector")
        def _(v):
            c.wait(v, "te")
            c.inc(v.tensor_copy(wdqT[:], ptr[:, 0:1024]), "ve")

        @c.on("tensor")
        def _(t):
            c.wait(t, "ve")
            for cc in range(DC):
                last = t.transpose(ptr[:, cc * 128:(cc + 1) * 128],
                                   wuq[:, cc * 128:(cc + 1) * 128], idb[:])
            c.inc(last, "te")

        @c.on("vector")
        def _(v):
            c.wait(v, "te")
            c.inc(v.tensor_copy(wuqT[:], ptr[:, 0:1024]), "ve")

        @c.on("tensor")
        def _(t):
            c.wait(t, "ve")
            for cc in range(DC):
                last = t.transpose(ptr[:, cc * 128:(cc + 1) * 128],
                                   wdkv[:, cc * 128:(cc + 1) * 128], idb[:])
            c.inc(last, "te")

        @c.on("vector")
        def _(v):
            c.wait(v, "te")
            c.inc(v.tensor_copy(wdkvT[:], ptr[:, 0:1024]), "ve")

        for c2 in range(DC):
            @c.on("tensor")
            def _(t, c2=c2):
                c.wait(t, "ve")
                c.wait(t, "se")
                for half in range(2):
                    last = t.matmul(pj[half][:, 0:512], wuqT[:, c2 * 128:(c2 + 1) * 128],
                                    wdq[:, half * 512:(half + 1) * 512],
                                    start=True, stop=True)
                c.inc(last, "te")

            @c.on("scalar")
            def _(s, c2=c2):
                c.wait(s, "te")
                for half in range(2):
                    last = s.copy(at[:, c2 * D + half * 512:c2 * D + (half + 1) * 512],
                                  pj[half][:, 0:512])
                c.inc(last, "se")

        for cg in range(2):
            @c.on("tensor")
            def _(t, cg=cg):
                c.wait(t, "se")
                for cl in range(4):
                    cc = 4 * cg + cl
                    for c2 in range(DC):
                        last = t.matmul(pj[cl][0:128, 0:128],
                                        at[:, c2 * D + cc * 128:c2 * D + (cc + 1) * 128],
                                        wuk[:, c2 * 128:(c2 + 1) * 128],
                                        start=(c2 == 0), stop=(c2 == DC - 1))
                c.inc(last, "te")

            @c.on("scalar")
            def _(s, cg=cg):
                c.wait(s, "te")
                for cl in range(4):
                    cc = 4 * cg + cl
                    last = s.activation(keffs[:, cc * 128:(cc + 1) * 128],
                                        pj[cl][0:128, 0:128], AF.Copy, scale=0.125)
                c.inc(last, "se")

        for tt in range(NT):
            @c.on("sync")
            def _(sy, tt=tt):
                if tt > 0:
                    c.wait(sy, "se")
                c.inc(sy.dma_start(out=xs[:], in_=xr[128 * tt:128 * (tt + 1), :]), "dma", 16)

            @c.on("vector")
            def _(v):
                c.wait(v, "dma")

            ln_tile(c, xs[:], ytm[:], sq, st8, D)

            @c.on("tensor")
            def _(t, tt=tt):
                c.wait(t, "se")
                c.wait(t, "ve")
                for dc2 in range(DC):
                    last = t.transpose(ptr[:, dc2 * 128:(dc2 + 1) * 128],
                                       ytm[:, dc2 * 128:(dc2 + 1) * 128], idb[:])
                c.inc(last, "te")

            @c.on("vector")
            def _(v, tt=tt):
                c.wait(v, "te")
                for dc2 in range(DC):
                    last = v.tensor_copy(yfm[:, dc2 * NTOK + 128 * tt:dc2 * NTOK + 128 * (tt + 1)],
                                         ptr[:, dc2 * 128:(dc2 + 1) * 128])
                c.inc(last, "ve")

        for ncx in range(NC):
            @c.on("tensor")
            def _(t, ncx=ncx):
                c.wait(t, "ve")
                c.wait(t, "se")
                for di in range(DC):
                    t.matmul(pj[0][:, 0:512], wdqT[:, di * 128:(di + 1) * 128],
                             yfm[:, di * NTOK + 512 * ncx:di * NTOK + 512 * (ncx + 1)],
                             start=(di == 0), stop=(di == DC - 1))
                for di in range(DC):
                    last = t.matmul(pj[1][:, 0:512], wdkvT[:, di * 128:(di + 1) * 128],
                                    yfm[:, di * NTOK + 512 * ncx:di * NTOK + 512 * (ncx + 1)],
                                    start=(di == 0), stop=(di == DC - 1))
                c.inc(last, "te")

            @c.on("scalar")
            def _(s, ncx=ncx):
                c.wait(s, "te")
                s.copy(t1[:, 512 * ncx:512 * (ncx + 1)], pj[0][:, 0:512])
                c.inc(s.copy(ckfm[:, 512 * ncx:512 * (ncx + 1)], pj[1][:, 0:512]), "se")

            for cg in range(2):
                @c.on("tensor")
                def _(t, ncx=ncx, cg=cg):
                    c.wait(t, "se")
                    for cl in range(4):
                        cc = 4 * cg + cl
                        last = t.matmul(pj[cl][:, 0:512], wuqT[:, cc * 128:(cc + 1) * 128],
                                        t1[:, 512 * ncx:512 * (ncx + 1)],
                                        start=True, stop=True)
                    c.inc(last, "te")

                @c.on("scalar")
                def _(s, ncx=ncx, cg=cg):
                    c.wait(s, "te")
                    for cl in range(4):
                        cc = 4 * cg + cl
                        last = s.copy(qfm[:, cc * NTOK + 512 * ncx:cc * NTOK + 512 * (ncx + 1)],
                                      pj[cl][:, 0:512])
                    c.inc(last, "se")

        for h in range(16):
            for ncx in range(NC):
                @c.on("tensor")
                def _(t, h=h, ncx=ncx):
                    c.wait(t, "se")
                    c.wait(t, "st")
                    r0 = 64 * (h % 2)
                    cc = h // 2
                    last = t.matmul(pj[ncx % 4][:, 0:512],
                                    keffs[r0:r0 + 64, cc * 128:(cc + 1) * 128],
                                    qfm[r0:r0 + 64, cc * NTOK + 512 * ncx:cc * NTOK + 512 * (ncx + 1)],
                                    start=True, stop=True)
                    c.inc(last, "te")

                @c.on("scalar")
                def _(s, h=h, ncx=ncx):
                    c.wait(s, "te")
                    c.inc(s.copy(qlb[:, 0:512], pj[ncx % 4][:, 0:512]), "se")

                @c.on("sync")
                def _(sy, h=h, ncx=ncx):
                    c.wait(sy, "se")
                    c.inc(sy.dma_start(out=ql_o[0:128, h * NTOK + 512 * ncx:h * NTOK + 512 * (ncx + 1)],
                                       in_=qlb[:, 0:512]), "st", 16)

        for tt in range(NT):
            @c.on("tensor")
            def _(t, tt=tt):
                c.wait(t, "se")
                c.wait(t, "ve")
                c.inc(t.transpose(ptr[:, 0:128], ckfm[:, 128 * tt:128 * (tt + 1)], idb[:]), "te")

            @c.on("vector")
            def _(v, tt=tt):
                c.wait(v, "te")
                c.wait(v, "st")
                c.inc(v.tensor_copy(outst[:, 0:128], ptr[:, 0:128]), "ve")

            @c.on("sync")
            def _(sy, tt=tt):
                c.wait(sy, "ve")
                c.inc(sy.dma_start(out=ckv_o[128 * tt:128 * (tt + 1), :], in_=outst[:, 0:128]), "st", 16)

        @c.on("sync")
        def _(sy):
            c.wait(sy, "st")
    return c


def build_l1attn():
    c = Ctx("l1attn")
    nc = c.nc
    xr = nc.dram_tensor("xr", [NTOK, D], F32, kind="ExternalInput").ap()
    ql_i = nc.dram_tensor("ql_i", [128, 16 * NTOK], BF16, kind="ExternalInput").ap()
    ckvfm = nc.dram_tensor("ckvfm", [128, N], BF16, kind="ExternalInput").ap()
    ckvtm = nc.dram_tensor("ckvtm", [N, 128], BF16, kind="ExternalInput").ap()
    masks = nc.dram_tensor("masks", [NC * 128, 16 * 512], BF16, kind="ExternalInput").ap()
    Wuv = nc.dram_tensor("Wuv", [D, 128], F32, kind="ExternalInput").ap()
    Wo = nc.dram_tensor("Wo", [D, D], F32, kind="ExternalInput").ap()
    x3o = nc.dram_tensor("x3o", [NTOK, D], F32, kind="ExternalOutput").ap()
    wo_d = nc.dram_tensor("wo_d", [D, D], BF16)
    ctx_d = nc.dram_tensor("ctx_d", [NTOK, 16 * 128], BF16)
    y_d = nc.dram_tensor("y_d", [D, NTOK], BF16)
    NST = N // 128
    with c:
        qls = c.sb("qls", [128, 16 * NTOK], BF16)
        ckf = c.sb("ckf", [128, N], BF16)
        cke = c.sb("cke", [128, NST * 130], BF16)
        msk = c.sb("msk", [128, 16 * 512], BF16)
        wuv = c.sb("wuv", [128, DC * 128], BF16)
        wob = c.sb("wob", [128, D], BF16)
        vef = c.sb("vef", [128, D], BF16)
        pe = c.sb("pe", [128, 8 * 512], BF16)
        zc = c.sb("zc", [128, 1], F32)
        cstg = c.sb("cstg", [128, 128], BF16)
        zc8 = c.sb("zc8", [128, 8], F32)
        cst8 = c.sb("cst8", [128, 8 * 128], BF16)
        ctxf = c.sb("ctxf", [128, NTOK], BF16)
        yst = c.sb("yst", [128, 512], BF16)
        xs = c.sb("xs", [128, D], F32)
        sq = c.sb("sq", [128, D], F32)
        st8 = c.sb("st8", [128, 8], F32)
        lnt = c.sb("lnt", [128, D], F32)
        x1t = c.sb("x1t", [128, D], F32)
        atm = c.sb("atm", [128, D], F32)
        pS = c.ps("pS", 1024)
        pC = [c.ps(f"pC{i}", 1024) for i in range(2)]

        def cOFF(ts):
            return (ts // 2) * 512 + (ts % 2) * 129

        @c.on("gpsimd")
        def _(g):
            for hq in range(4):
                c.inc(g.dma_start(out=qls[:, hq * 4 * NTOK:(hq + 1) * 4 * NTOK],
                                  in_=ql_i[0:128, hq * 4 * NTOK:(hq + 1) * 4 * NTOK]), "gd", 16)
            c.inc(g.dma_start(out=ckf[:], in_=ckvfm), "gd", 16)
            src = ckvtm.rearrange("(t p) k -> p t k", p=128)
            dst = cke[:].rearrange("p (t k) -> p t k", k=130)[:, :, 0:128]
            c.inc(g.dma_start(out=dst, in_=src), "gd", 16)
            for cc in range(DC):
                c.inc(g.dma_start(out=wuv[:, cc * 128:(cc + 1) * 128],
                                  in_=Wuv[128 * cc:128 * (cc + 1), :]), "gd", 16)

        @c.on("vector")
        def _(v):
            c.wait(v, "gd")
            ones_view = cke[:].rearrange("p (t k) -> p t k", k=130)[:, :, 128:129]
            c.inc(v.memset(ones_view, 1.0), "ve")

        for cc in range(DC):
            @c.on("gpsimd")
            def _(g, cc=cc):
                c.wait(g, "st")
                c.inc(g.dma_start(out=wob[:], in_=Wo[128 * cc:128 * (cc + 1), :]), "gd", 16)

            @c.on("sync")
            def _(sy, cc=cc):
                c.wait(sy, "gd")
                c.inc(sy.dma_start(out=wo_d[128 * cc:128 * (cc + 1), :], in_=wob[:]), "st", 16)

        for c2 in range(DC):
            @c.on("sync")
            def _(sy, c2=c2):
                c.wait(sy, "st")
                c.wait(sy, "te")
                c.inc(sy.dma_start(out=wob[:], in_=wo_d[:, 128 * c2:128 * (c2 + 1)],
                                   transpose=True), "dma", 16)

            @c.on("tensor")
            def _(t, c2=c2):
                c.wait(t, "dma")
                c.wait(t, "ve")
                for half in range(2):
                    last = t.matmul(pS[:, half * 512:(half + 1) * 512],
                                    wuv[:, c2 * 128:(c2 + 1) * 128],
                                    wob[:, half * 512:(half + 1) * 512],
                                    start=(c2 == 0), stop=(c2 == DC - 1))
                c.inc(last, "te")

        @c.on("vector")
        def _(v):
            c.wait(v, "te")
            c.inc(v.tensor_copy(vef[:], pS[:, 0:1024]), "ve")

        pS2 = c.ps("pS2", 1024)
        pe2off = 1024
        for k in range(NC):
            dma_k = None

            @c.on("sync")
            def _(sy, k=k):
                c.wait(sy, "ve")
                c.wait(sy, "te")
                c.inc(sy.dma_start(out=msk[:], in_=masks[k * 128:(k + 1) * 128, :]), "dma", 16)
            dma_k = c.cnt["dma"]

            for hg in range(8):
                JMAX = 16 * k + 16
                te_ck = {}
                se0 = c.cnt["se"]
                ve0 = c.cnt["ve"]

                def se_of(j):
                    return se0 + j + 1

                def ve_of(j, k=k):
                    return ve0 + (j - 16 * k) + 1

                def S_of(j):
                    return pS if j % 2 == 0 else pS2

                def pv_wait_emit(t, k, hg, j, JMAX):
                    if j == 0:
                        c.wait(t, "ve", ve0)
                    if j >= 16 * k:
                        c.wait(t, "ve", ve_of(j))
                    else:
                        c.wait(t, "se", se_of(j))
                    for i in range(2):
                        for ts in range(4):
                            last = t.matmul(pC[i][:, cOFF(ts):cOFF(ts) + 129],
                                            pe[:, (j % 4) * pe2off + i * 512 + ts * 128:
                                               (j % 4) * pe2off + i * 512 + (ts + 1) * 128],
                                            cke[:, 130 * j:130 * j + 129],
                                            start=(j == 0 and ts % 2 == 0),
                                            stop=(j == JMAX - 1 and ts % 2 == 1))
                    c.inc(last, "te")

                @c.on("tensor")
                def _(t, k=k, hg=hg, JMAX=JMAX):
                    for j in range(JMAX):
                        if j == 0:
                            c.wait(t, "dma", dma_k)
                        if j >= 2:
                            pv_wait_emit(t, k, hg, j - 2, JMAX)
                        if j >= 2:
                            c.wait(t, "se", se_of(j - 2))
                        for i in range(2):
                            h = 2 * hg + i
                            last = t.matmul(S_of(j)[:, i * 512:(i + 1) * 512],
                                            ckf[:, 128 * j:128 * (j + 1)],
                                            qls[:, h * NTOK + 512 * k:h * NTOK + 512 * (k + 1)],
                                            start=True, stop=True)
                        c.inc(last, "te")
                        te_ck[j] = c.cnt["te"]
                    if JMAX >= 2:
                        pv_wait_emit(t, k, hg, JMAX - 2, JMAX)
                    pv_wait_emit(t, k, hg, JMAX - 1, JMAX)

                @c.on("scalar")
                def _(s, k=k, hg=hg, JMAX=JMAX):
                    for j in range(JMAX):
                        c.wait(s, "te", te_ck[j])
                        c.inc(s.activation(pe[:, (j % 4) * pe2off:(j % 4) * pe2off + 1024],
                                           S_of(j)[:, 0:1024], AF.Exp), "se")
                assert c.cnt["se"] == se0 + JMAX

                @c.on("vector")
                def _(v, k=k, hg=hg, JMAX=JMAX):
                    for j in range(16 * k, JMAX):
                        jj = j - 16 * k
                        c.wait(v, "se", se_of(j))
                        for i in range(2):
                            last = v.tensor_tensor(pe[:, (j % 4) * pe2off + i * 512:
                                                      (j % 4) * pe2off + (i + 1) * 512],
                                                   pe[:, (j % 4) * pe2off + i * 512:
                                                      (j % 4) * pe2off + (i + 1) * 512],
                                                   msk[:, jj * 512:(jj + 1) * 512],
                                                   op=ALU.mult)
                        c.inc(last, "ve")
                assert c.cnt["ve"] == ve0 + 16

                ve0n = c.cnt["ve"]

                @c.on("vector")
                def _(v, k=k, hg=hg, ve0n=ve0n):
                    c.wait(v, "te")
                    c.wait(v, "st")
                    for i in range(2):
                        for ts in range(4):
                            inst = v.reciprocal(zc8[:, 4 * i + ts:4 * i + ts + 1],
                                                pC[i][:, cOFF(ts) + 128:cOFF(ts) + 129])
                    c.inc(inst, "ve")
                    c.wait(v, "ve")
                    for i in range(2):
                        for ts in range(4):
                            c.inc(v.tensor_scalar(cst8[:, (4 * i + ts) * 128:(4 * i + ts + 1) * 128],
                                                  pC[i][:, cOFF(ts):cOFF(ts) + 128],
                                                  zc8[:, 4 * i + ts:4 * i + ts + 1], None,
                                                  op0=ALU.mult), "ve")

                @c.on("sync")
                def _(sy, k=k, hg=hg, ve0n=ve0n):
                    for i in range(2):
                        h = 2 * hg + i
                        for ts in range(4):
                            u = 4 * i + ts
                            c.wait(sy, "ve", ve0n + 1 + u + 1)
                            tt = 4 * k + ts
                            c.inc(sy.dma_start(out=ctx_d[128 * tt:128 * (tt + 1), 128 * h:128 * (h + 1)],
                                               in_=cst8[:, u * 128:(u + 1) * 128]), "st", 16)

        for h in range(16):
            @c.on("sync")
            def _(sy, h=h):
                c.wait(sy, "te")
                c.inc(sy.dma_start(out=ctxf[:], in_=ctx_d[:, 128 * h:128 * (h + 1)],
                                   transpose=True), "dma", 16)

            for ncx in range(NC):
                @c.on("tensor")
                def _(t, h=h, ncx=ncx):
                    c.wait(t, "dma")
                    c.wait(t, "se")
                    c.inc(t.matmul(pS[0:64, 0:512], vef[:, 64 * h:64 * (h + 1)],
                                   ctxf[:, 512 * ncx:512 * (ncx + 1)], start=True, stop=True), "te")

                @c.on("scalar")
                def _(s, h=h, ncx=ncx):
                    c.wait(s, "te")
                    c.wait(s, "st")
                    c.inc(s.copy(yst[:, 0:0 + 512][0:64, :], pS[0:64, 0:512]), "se")

                @c.on("sync")
                def _(sy, h=h, ncx=ncx):
                    c.wait(sy, "se")
                    c.inc(sy.dma_start(out=y_d[64 * h:64 * (h + 1), 512 * ncx:512 * (ncx + 1)],
                                       in_=yst[0:64, 0:512]), "st", 16)

        ybt = c.sb("ybt", [128, D], BF16)
        for tt in range(NT):
            @c.on("sync")
            def _(sy, tt=tt):
                c.wait(sy, "ve")
                c.inc(sy.dma_start(out=ybt[:], in_=y_d[:, 128 * tt:128 * (tt + 1)],
                                   transpose=True), "dma", 16)
                c.inc(sy.dma_start(out=xs[:], in_=xr[128 * tt:128 * (tt + 1), :]), "dma", 16)

            @c.on("vector")
            def _(v, tt=tt):
                c.wait(v, "dma")
                c.inc(v.tensor_copy(atm[:], ybt[:]), "ve")

            ln_tile(c, atm[:], lnt[:], sq, st8, D, res=xs[:], res_out=x1t[:])

            @c.on("sync")
            def _(sy, tt=tt):
                c.wait(sy, "ve")
                c.inc(sy.dma_start(out=x3o[128 * tt:128 * (tt + 1), :], in_=x1t[:]), "st", 16)

        @c.on("sync")
        def _(sy):
            c.wait(sy, "st")
    return c


def build_l1fpre():
    c = Ctx("l1fpre")
    nc = c.nc
    xr = nc.dram_tensor("xr", [NTOK, D], F32, kind="ExternalInput").ap()
    qw = nc.dram_tensor("qw", [D, D], F32, kind="ExternalInput").ap()
    qb = nc.dram_tensor("qb", [1, D], F32, kind="ExternalInput").ap()
    kw = nc.dram_tensor("kw", [D, 512], F32, kind="ExternalInput").ap()
    kbi = nc.dram_tensor("kbi", [1, 512], F32, kind="ExternalInput").ap()
    vw = nc.dram_tensor("vw", [D, 512], F32, kind="ExternalInput").ap()
    vbi = nc.dram_tensor("vbi", [1, 512], F32, kind="ExternalInput").ap()
    ident = nc.dram_tensor("ident", [128, 128], BF16, kind="ExternalInput").ap()
    q_o = nc.dram_tensor("q_o", [D, NTOK], BF16, kind="ExternalOutput").ap()
    k_o = nc.dram_tensor("k_o", [512, NTOK], BF16, kind="ExternalOutput").ap()
    vt_o = nc.dram_tensor("vt_o", [NTOK, 512], BF16, kind="ExternalOutput").ap()
    with c:
        idb = c.sb("idb", [128, 128], BF16)
        xs = c.sb("xs", [128, D], F32)
        sq = c.sb("sq", [128, D], F32)
        st8 = c.sb("st8", [128, 8], F32)
        ytm = c.sb("ytm", [128, D], BF16)
        yfm = c.sb("yfm", [128, DC * NTOK], BF16)
        qwb = c.sb("qwb", [128, DC * D], BF16)
        kwb = c.sb("kwb", [128, DC * 512], BF16)
        vwb = c.sb("vwb", [128, DC * 512], BF16)
        brow = c.sb("brow", [1, 2 * D], BF16)
        onesone = c.sb("onesone", [1, 1], BF16)
        bcol = c.sb("bcol", [128, 16], F32)
        outst = c.sb("outst", [128, 8 * 512], BF16)
        ptr = c.ps("ptr", 1024, BF16)
        pb = c.ps("pb", 512)
        pj = [c.ps(f"pj{i}", 512) for i in range(4)]

        @c.on("gpsimd")
        def _(g):
            c.inc(g.dma_start(out=idb[:], in_=ident), "gd", 16)
            for dc in range(DC):
                c.inc(g.dma_start(out=qwb[:, dc * D:(dc + 1) * D],
                                  in_=qw[128 * dc:128 * (dc + 1), :]), "gd", 16)
                c.inc(g.dma_start(out=kwb[:, dc * 512:(dc + 1) * 512],
                                  in_=kw[128 * dc:128 * (dc + 1), :]), "gd", 16)
                c.inc(g.dma_start(out=vwb[:, dc * 512:(dc + 1) * 512],
                                  in_=vw[128 * dc:128 * (dc + 1), :]), "gd", 16)
            c.inc(g.dma_start(out=brow[:, 0:D], in_=qb), "gd", 16)
            c.inc(g.dma_start(out=brow[:, D:D + 512], in_=kbi), "gd", 16)
            c.inc(g.dma_start(out=brow[:, D + 512:D + 1024], in_=vbi), "gd", 16)

        @c.on("vector")
        def _(v):
            c.inc(v.memset(onesone[:], 1.0), "ve")

        @c.on("tensor")
        def _(t):
            c.wait(t, "gd")
            c.wait(t, "ve")
            for i in range(16):
                last = t.matmul(pb[:, i:i + 1], brow[:, 128 * i:128 * (i + 1)],
                                onesone[:], start=(i == 0), stop=(i == 15))
            c.inc(last, "te")

        @c.on("vector")
        def _(v):
            c.wait(v, "te")
            c.inc(v.tensor_copy(bcol[:], pb[:, 0:16]), "ve")

        for tt in range(NT):
            @c.on("sync")
            def _(sy, tt=tt):
                if tt > 0:
                    c.wait(sy, "se")
                c.inc(sy.dma_start(out=xs[:], in_=xr[128 * tt:128 * (tt + 1), :]), "dma", 16)

            @c.on("vector")
            def _(v):
                c.wait(v, "dma")

            ln_tile(c, xs[:], ytm[:], sq, st8, D)

            @c.on("tensor")
            def _(t, tt=tt):
                c.wait(t, "se")
                c.wait(t, "ve")
                for dc2 in range(DC):
                    last = t.transpose(ptr[:, dc2 * 128:(dc2 + 1) * 128],
                                       ytm[:, dc2 * 128:(dc2 + 1) * 128], idb[:])
                c.inc(last, "te")

            @c.on("vector")
            def _(v, tt=tt):
                c.wait(v, "te")
                for dc2 in range(DC):
                    last = v.tensor_copy(yfm[:, dc2 * NTOK + 128 * tt:dc2 * NTOK + 128 * (tt + 1)],
                                         ptr[:, dc2 * 128:(dc2 + 1) * 128])
                c.inc(last, "ve")

        for ncx in range(NC):
            for dg in range(2):
                @c.on("tensor")
                def _(t, ncx=ncx, dg=dg):
                    c.wait(t, "ve")
                    c.wait(t, "se")
                    for dl in range(4):
                        do = 4 * dg + dl
                        for di in range(DC):
                            last = t.matmul(pj[dl][:, 0:512],
                                            qwb[:, di * D + do * 128:di * D + (do + 1) * 128],
                                            yfm[:, di * NTOK + 512 * ncx:di * NTOK + 512 * (ncx + 1)],
                                            start=(di == 0), stop=(di == DC - 1))
                    c.inc(last, "te")

                @c.on("scalar")
                def _(s, ncx=ncx, dg=dg):
                    c.wait(s, "te")
                    c.wait(s, "st")
                    for dl in range(4):
                        do = 4 * dg + dl
                        last = s.activation(outst[:, do * 512:(do + 1) * 512], pj[dl][:, 0:512],
                                            AF.Identity, bias=bcol[:, do:do + 1])
                    c.inc(last, "se")

                @c.on("sync")
                def _(sy, ncx=ncx, dg=dg):
                    c.wait(sy, "se")
                    for dl in range(4):
                        do = 4 * dg + dl
                        c.inc(sy.dma_start(out=q_o[do * 128:(do + 1) * 128, 512 * ncx:512 * (ncx + 1)],
                                           in_=outst[:, do * 512:(do + 1) * 512]), "st", 16)

        for ncx in range(NC):
            @c.on("tensor")
            def _(t, ncx=ncx):
                c.wait(t, "ve")
                c.wait(t, "se")
                for dl in range(4):
                    for di in range(DC):
                        last = t.matmul(pj[dl][:, 0:512],
                                        kwb[:, di * 512 + dl * 128:di * 512 + (dl + 1) * 128],
                                        yfm[:, di * NTOK + 512 * ncx:di * NTOK + 512 * (ncx + 1)],
                                        start=(di == 0), stop=(di == DC - 1))
                c.inc(last, "te")

            @c.on("scalar")
            def _(s, ncx=ncx):
                c.wait(s, "te")
                c.wait(s, "st")
                for dl in range(4):
                    last = s.activation(outst[:, dl * 512:(dl + 1) * 512], pj[dl][:, 0:512],
                                        AF.Identity, bias=bcol[:, 8 + dl:9 + dl])
                c.inc(last, "se")

            @c.on("sync")
            def _(sy, ncx=ncx):
                c.wait(sy, "se")
                for dl in range(4):
                    c.inc(sy.dma_start(out=k_o[dl * 128:(dl + 1) * 128, 512 * ncx:512 * (ncx + 1)],
                                       in_=outst[:, dl * 512:(dl + 1) * 512]), "st", 16)

            @c.on("tensor")
            def _(t, ncx=ncx):
                c.wait(t, "ve")
                c.wait(t, "se")
                for dl in range(4):
                    for di in range(DC):
                        last = t.matmul(pj[dl][:, 0:512],
                                        vwb[:, di * 512 + dl * 128:di * 512 + (dl + 1) * 128],
                                        yfm[:, di * NTOK + 512 * ncx:di * NTOK + 512 * (ncx + 1)],
                                        start=(di == 0), stop=(di == DC - 1))
                c.inc(last, "te")

            @c.on("scalar")
            def _(s, ncx=ncx):
                c.wait(s, "te")
                c.wait(s, "st")
                for dl in range(4):
                    last = s.activation(outst[:, (4 + dl) * 512:(5 + dl) * 512], pj[dl][:, 0:512],
                                        AF.Identity, bias=bcol[:, 12 + dl:13 + dl])
                c.inc(last, "se")

            for sub in range(4):
                @c.on("tensor")
                def _(t, ncx=ncx, sub=sub):
                    c.wait(t, "se")
                    c.wait(t, "ve")
                    for dl in range(4):
                        last = t.transpose(ptr[:, dl * 128:(dl + 1) * 128],
                                           outst[:, (4 + dl) * 512 + sub * 128:(4 + dl) * 512 + (sub + 1) * 128],
                                           idb[:])
                    c.inc(last, "te")

                @c.on("vector")
                def _(v, ncx=ncx, sub=sub):
                    c.wait(v, "te")
                    c.wait(v, "st")
                    c.inc(v.tensor_copy(ytm[:, 0:512], ptr[:, 0:512]), "ve")

                @c.on("sync")
                def _(sy, ncx=ncx, sub=sub):
                    c.wait(sy, "ve")
                    c.inc(sy.dma_start(out=vt_o[512 * ncx + 128 * sub:512 * ncx + 128 * (sub + 1), :],
                                       in_=ytm[:, 0:512]), "st", 16)

        @c.on("sync")
        def _(sy):
            c.wait(sy, "st")
    return c


def build_l1fmain():
    c = Ctx("l1fmain")
    nc = c.nc
    xr = nc.dram_tensor("xr", [NTOK, D], F32, kind="ExternalInput").ap()
    qfm = nc.dram_tensor("qfm", [D, NTOK], BF16, kind="ExternalInput").ap()
    kf = nc.dram_tensor("kf", [512, N], BF16, kind="ExternalInput").ap()
    vtf = nc.dram_tensor("vtf", [N, 512], BF16, kind="ExternalInput").ap()
    ow = nc.dram_tensor("ow", [D, D], F32, kind="ExternalInput").ap()
    ob = nc.dram_tensor("ob", [1, D], F32, kind="ExternalInput").ap()
    rw = nc.dram_tensor("rw", [D, 8], F32, kind="ExternalInput").ap()
    rb = nc.dram_tensor("rb", [1, 8], F32, kind="ExternalInput").ap()
    w1 = nc.dram_tensor("w1", [8 * D, 2 * D], F32, kind="ExternalInput").ap()
    b1 = nc.dram_tensor("b1", [8, 2 * D], F32, kind="ExternalInput").ap()
    w2 = nc.dram_tensor("w2", [8 * 2 * D, D], F32, kind="ExternalInput").ap()
    b2 = nc.dram_tensor("b2", [8, D], F32, kind="ExternalInput").ap()
    identf = nc.dram_tensor("identf", [128, 128], F32, kind="ExternalInput").ap()
    x4o = nc.dram_tensor("x4o", [NTOK, D], F32, kind="ExternalOutput").ap()
    ao_d = nc.dram_tensor("ao_d", [NTOK, 512], BF16)
    NST = N // 128
    SCS = float(32) ** -0.5
    with c:
        idf = c.sb("idf", [128, 128], F32)
        ar1 = c.sb("ar1", [128, 16384], BF16)
        ar2 = c.sb("ar2", [128, 8192], BF16)
        ws1 = c.sb("ws1", [128, 16384], BF16)
        mla = c.sb("mla", [128, DC * NTOK], BF16)
        macc = c.sb("macc", [128, DC * NTOK], F32)
        aost = c.sb("aost", [128, 128], BF16)
        zc4 = c.sb("zc4", [128, 4], F32)
        zc = c.sb("zc", [128, 1], F32)
        rws = c.sb("rws", [128, 64], BF16)
        rbrow = c.sb("rbrow", [1, 8], BF16)
        b1row = c.sb("b1row", [1, 2 * D], BF16)
        b2row = c.sb("b2row", [1, D], BF16)
        bcol = c.sb("bcol", [128, 24], F32)
        onesone = c.sb("onesone", [1, 1], BF16)
        onesbf = c.sb("onesbf", [1, 128], BF16)
        lg = c.sb("lg", [128, 128], F32)
        p8 = c.sb("p8", [128, 8], F32)
        s1 = c.sb("s1", [128, 1], F32)
        s2 = c.sb("s2", [128, 1], F32)
        s3 = c.sb("s3", [128, 1], F32)
        m8 = c.sb("m8", [128, 8], F32)
        comb = c.sb("comb", [128, 128], F32)
        cfb = c.sb("cfb", [1, NTOK], BF16)
        bct = c.sb("bct", [128, 512], BF16)
        xs = c.sb("xs", [128, D], F32)
        sq = c.sb("sq", [128, D], F32)
        st8 = c.sb("st8", [128, 8], F32)
        ffnt = c.sb("ffnt", [128, D], F32)
        lnt = c.sb("lnt", [128, D], F32)
        pA = c.ps("pA", 512)
        pjS = c.ps("pjS", 2048)
        pj = [pjS[:, 512 * i:512 * (i + 1)] for i in range(4)]

        KHS = 0
        QHS = 8192
        VES = 10240
        PES = 12416

        @c.on("gpsimd")
        def _(g):
            c.inc(g.dma_start(out=idf[:], in_=identf), "gd", 16)
            c.inc(g.dma_start(out=rbrow[:], in_=rb), "gd", 16)
            for di in range(DC):
                c.inc(g.dma_start(out=rws[:, di * 8:(di + 1) * 8],
                                  in_=rw[128 * di:128 * (di + 1), :]), "gd", 16)

        @c.on("vector")
        def _(v):
            v.memset(onesone[:], 1.0)
            c.inc(v.memset(onesbf[:], 1.0), "ve")

        for h in range(16):
            @c.on("gpsimd")
            def _(g, h=h):
                c.wait(g, "te")
                c.inc(g.dma_start(out=ar1[0:32, KHS:KHS + N], in_=kf[32 * h:32 * (h + 1), :]), "gd", 16)
                c.inc(g.dma_start(out=ar1[0:32, QHS:QHS + NTOK],
                                  in_=qfm[64 * h:64 * h + 32, :]), "gd", 16)
                vsrc = vtf.rearrange("(t p) k -> p t k", p=128)[:, :, 32 * h:32 * (h + 1)]
                vdst = ar1[:, VES:VES + NST * 34].rearrange("p (t k) -> p t k", k=34)[:, :, 0:32]
                c.inc(g.dma_start(out=vdst, in_=vsrc), "gd", 16)

            @c.on("vector")
            def _(v, h=h):
                c.wait(v, "gd")
                ones_view = ar1[:, VES:VES + NST * 34].rearrange("p (t k) -> p t k", k=34)[:, :, 32:33]
                c.inc(v.memset(ones_view, 1.0), "ve")

            for sc in range(NC):
                se0 = c.cnt["se"]
                te_ck = {}

                def se_of(tt):
                    return se0 + tt // 2 + 1

                @c.on("tensor")
                def _(t, h=h, sc=sc):
                    for tt in range(NST):
                        if tt == 0:
                            c.wait(t, "gd")
                            c.wait(t, "ve")
                            c.wait(t, "se", se0)
                        if tt >= 2:
                            ttp = tt - 2
                            if ttp == 0:
                                c.wait(t, "ve")
                            c.wait(t, "se", se_of(ttp))
                            for ts in range(4):
                                last = t.matmul(pA[:, ts * 33:ts * 33 + 33],
                                                ar1[:, PES + (ttp % 4) * 512 + ts * 128:PES + (ttp % 4) * 512 + (ts + 1) * 128],
                                                ar1[:, VES + 34 * ttp:VES + 34 * ttp + 33],
                                                start=(ttp == 0 and ts == 0), stop=False)
                            c.inc(last, "te")
                        if tt >= 4:
                            c.wait(t, "se", se_of(tt - 4))
                        c.inc(t.matmul(pjS[:, (tt % 4) * 512:(tt % 4) * 512 + 512],
                                       ar1[0:32, KHS + 128 * tt:KHS + 128 * (tt + 1)],
                                       ar1[0:32, QHS + 512 * sc:QHS + 512 * (sc + 1)],
                                       start=True, stop=True), "te")
                        te_ck[tt] = c.cnt["te"]
                    for ttp in (NST - 2, NST - 1):
                        c.wait(t, "se", se_of(ttp))
                        for ts in range(4):
                            last = t.matmul(pA[:, ts * 33:ts * 33 + 33],
                                            ar1[:, PES + (ttp % 4) * 512 + ts * 128:PES + (ttp % 4) * 512 + (ts + 1) * 128],
                                            ar1[:, VES + 34 * ttp:VES + 34 * ttp + 33],
                                            start=False,
                                            stop=(ttp == NST - 1 and ts == 3))
                        c.inc(last, "te")
                    te_ck["end"] = c.cnt["te"]

                @c.on("scalar")
                def _(s, h=h, sc=sc):
                    for p in range(NST // 2):
                        tt = 2 * p
                        c.wait(s, "te", te_ck[tt + 1])
                        base = (tt % 4) * 512
                        c.inc(s.activation(ar1[:, PES + base:PES + base + 1024],
                                           pjS[:, base:base + 1024], AF.Exp, scale=SCS), "se")
                assert c.cnt["se"] == se0 + NST // 2

                ve0n = c.cnt["ve"]
                st0n = c.cnt["st"]

                @c.on("vector")
                def _(v, h=h, sc=sc, ve0n=ve0n, st0n=st0n):
                    c.wait(v, "te")
                    c.wait(v, "st")
                    for ts in range(4):
                        v.reciprocal(zc4[:, ts:ts + 1], pA[:, ts * 33 + 32:ts * 33 + 33])
                    c.inc(v.nop() if False else v.reciprocal(zc4[:, 3:4], pA[:, 3 * 33 + 32:3 * 33 + 33]), "ve")
                    c.wait(v, "ve")
                    for ts in range(4):
                        c.inc(v.tensor_scalar(aost[:, 32 * ts:32 * (ts + 1)],
                                              pA[:, ts * 33:ts * 33 + 32],
                                              zc4[:, ts:ts + 1], None, op0=ALU.mult), "ve")

                @c.on("sync")
                def _(sy, h=h, sc=sc, ve0n=ve0n):
                    for ts in range(4):
                        c.wait(sy, "ve", ve0n + 2 + ts)
                        tix = 4 * sc + ts
                        c.inc(sy.dma_start(out=ao_d[128 * tix:128 * (tix + 1), 32 * h:32 * (h + 1)],
                                           in_=aost[:, 32 * ts:32 * (ts + 1)]), "st", 16)

        for cc2 in range(4):
            @c.on("sync")
            def _(sy, cc2=cc2):
                c.wait(sy, "st")
                c.wait(sy, "te")
                c.inc(sy.dma_start(out=ar2[:, cc2 * NTOK:(cc2 + 1) * NTOK],
                                   in_=ao_d[:, 128 * cc2:128 * (cc2 + 1)], transpose=True), "dma", 16)

        @c.on("gpsimd")
        def _(g):
            for cc2 in range(4):
                c.inc(g.dma_start(out=ws1[:, cc2 * 1024:(cc2 + 1) * 1024],
                                  in_=ow[128 * cc2:128 * (cc2 + 1), :]), "gd", 16)
            c.inc(g.dma_start(out=b1row[:, 0:D], in_=ob), "gd", 16)

        @c.on("tensor")
        def _(t):
            c.wait(t, "gd")
            c.wait(t, "ve")
            for i in range(8):
                last = t.matmul(pj[3][:, i:i + 1], b1row[:, 128 * i:128 * (i + 1)],
                                onesone[:], start=(i == 0), stop=(i == 7))
            c.inc(last, "te")

        @c.on("vector")
        def _(v):
            c.wait(v, "te")
            c.inc(v.tensor_copy(bcol[:, 0:8], pj[3][:, 0:8]), "ve")

        for ncx in range(NC):
            for dg in range(2):
                @c.on("tensor")
                def _(t, ncx=ncx, dg=dg):
                    c.wait(t, "dma")
                    c.wait(t, "ve")
                    c.wait(t, "se")
                    for dl in range(4):
                        do = 4 * dg + dl
                        for cc2 in range(4):
                            last = t.matmul(pj[dl][:, 0:512],
                                            ws1[:, cc2 * 1024 + do * 128:cc2 * 1024 + (do + 1) * 128],
                                            ar2[:, cc2 * NTOK + 512 * ncx:cc2 * NTOK + 512 * (ncx + 1)],
                                            start=(cc2 == 0), stop=(cc2 == 3))
                    c.inc(last, "te")

                @c.on("scalar")
                def _(s, ncx=ncx, dg=dg):
                    c.wait(s, "te")
                    for dl in range(4):
                        do = 4 * dg + dl
                        last = s.activation(mla[:, do * NTOK + 512 * ncx:do * NTOK + 512 * (ncx + 1)],
                                            pj[dl][:, 0:512], AF.Identity, bias=bcol[:, do:do + 1])
                    c.inc(last, "se")

        for nt in range(NT):
            @c.on("tensor")
            def _(t, nt=nt):
                c.wait(t, "se")
                c.wait(t, "ve")
                for di in range(DC):
                    t.matmul(pj[0][:, nt * 8:(nt + 1) * 8],
                             mla[:, di * NTOK + 128 * nt:di * NTOK + 128 * (nt + 1)],
                             rws[:, di * 8:(di + 1) * 8],
                             start=(nt == 0 and di == 0), stop=False)
                last = t.matmul(pj[0][:, nt * 8:(nt + 1) * 8], onesbf[:], rbrow[:],
                                start=False, stop=(nt == NT - 1))
                c.inc(last, "te")

        @c.on("vector")
        def _(v):
            c.wait(v, "te")
            c.inc(v.tensor_copy(lg[:], pj[0][:, 0:128]), "ve")

        for nt in range(NT):
            sl = slice(nt * 8, (nt + 1) * 8)
            @c.on("vector")
            def _(v, nt=nt, sl=sl):
                c.wait(v, "ve")
                c.inc(v.reduce_max(s1[:], lg[:, sl], axis=AX), "ve")
                c.wait(v, "ve")
                c.inc(v.tensor_scalar(p8[:], lg[:, sl], s1[:], None, op0=ALU.subtract), "ve")

            @c.on("scalar")
            def _(s, nt=nt):
                c.wait(s, "ve")
                c.inc(s.activation(p8[:], p8[:], AF.Exp, accum_out=s2[:]), "se")

            @c.on("vector")
            def _(v, nt=nt):
                c.wait(v, "se")
                c.inc(v.reciprocal(s2[:], s2[:]), "ve")
                c.wait(v, "ve")
                c.inc(v.tensor_scalar(p8[:], p8[:], s2[:], None, op0=ALU.mult), "ve")
                c.wait(v, "ve")
                c.inc(v.reduce_max(s1[:], p8[:], axis=AX), "ve")
                c.wait(v, "ve")
                c.inc(v.tensor_scalar(m8[:], p8[:], s1[:], None, op0=ALU.is_ge), "ve")
                c.wait(v, "ve")
                c.inc(v.scalar_tensor_tensor(m8[:], m8[:], -2.0, p8[:], op0=ALU.mult, op1=ALU.add), "ve")
                c.wait(v, "ve")
                c.inc(v.reduce_max(s2[:], m8[:], axis=AX), "ve")
                c.wait(v, "ve")
                c.inc(v.tensor_scalar(m8[:], p8[:], s2[:], None, op0=ALU.is_ge), "ve")
                c.wait(v, "ve")
                c.inc(v.tensor_tensor(s1[:], s1[:], s2[:], op=ALU.add), "ve")
                c.wait(v, "ve")
                c.inc(v.tensor_scalar_add(s1[:], s1[:], 1e-8), "ve")
                c.wait(v, "ve")
                c.inc(v.reciprocal(s3[:], s1[:]), "ve")
                c.wait(v, "ve")
                c.inc(v.tensor_scalar_mul(s3[:], s3[:], 0.3), "ve")
                c.wait(v, "ve")
                c.inc(v.tensor_tensor(m8[:], m8[:], p8[:], op=ALU.mult), "ve")
                c.wait(v, "ve")
                c.inc(v.tensor_scalar(comb[:, nt * 8:(nt + 1) * 8], m8[:], s3[:], None,
                                      op0=ALU.mult), "ve")

        moe_gd = {}
        moe_h1done = {}
        for e in range(8):
            @c.on("gpsimd")
            def _(g, e=e):
                if e == 0:
                    c.wait(g, "te")
                else:
                    c.wait(g, "te", moe_h1done[e - 1])
                for di in range(DC):
                    c.inc(g.dma_start(out=ws1[:, di * 2048:(di + 1) * 2048],
                                      in_=w1[e * D + 128 * di:e * D + 128 * (di + 1), :]), "gd", 16)
                c.inc(g.dma_start(out=b1row[:], in_=b1[e:e + 1, :]), "gd", 16)
                moe_gd[("w1", e)] = c.cnt["gd"]

            @c.on("tensor")
            def _(t, e=e):
                c.wait(t, "gd")
                c.wait(t, "ve")
                for i in range(16):
                    last = t.matmul(pj[3][:, 64 + i:65 + i], b1row[:, 128 * i:128 * (i + 1)],
                                    onesone[:], start=(i == 0), stop=(i == 15))
                c.inc(last, "te")

            @c.on("vector")
            def _(v, e=e):
                c.wait(v, "te")
                c.inc(v.tensor_copy(bcol[:, 8:24], pj[3][:, 64:80]), "ve")

            for q4 in range(4):
                @c.on("tensor")
                def _(t, e=e, q4=q4):
                    c.wait(t, "ve")
                    c.wait(t, "se")
                    for i in range(4):
                        nt = 4 * q4 + i
                        last = t.matmul(pj[3][0:1, 128 * i:128 * (i + 1)],
                                        comb[:, nt * 8 + e:nt * 8 + e + 1], idf[:],
                                        start=(i == 0), stop=(i == 3))
                    c.inc(last, "te")

                @c.on("vector")
                def _(v, e=e, q4=q4):
                    c.wait(v, "te")
                    c.inc(v.tensor_copy(cfb[:, 512 * q4:512 * (q4 + 1)], pj[3][0:1, 0:512]), "ve")

            for ncx in range(NC):
                @c.on("tensor")
                def _(t, e=e, ncx=ncx):
                    c.wait(t, "ve")
                    c.wait(t, "se")
                    c.inc(t.matmul(pj[3][:, 0:512], onesbf[:], cfb[:, 512 * ncx:512 * (ncx + 1)],
                                   start=True, stop=True), "te")

                @c.on("vector")
                def _(v, e=e, ncx=ncx):
                    c.wait(v, "te")
                    c.inc(v.tensor_copy(bct[:], pj[3][:, 0:512]), "ve")

                h1_te = {}
                h1_se = {}

                def h1_gelu(s, e, ncx, g):
                    c.wait(s, "te", h1_te[g])
                    for hl in range(2):
                        hh = 2 * g + hl
                        bank = pj[(g % 2) * 2 + hl]
                        last = s.activation(ar2[:, hh * 512:(hh + 1) * 512], bank[:, 0:512],
                                            AF.Gelu, bias=bcol[:, 8 + hh:9 + hh])
                    c.inc(last, "se")
                    h1_se[g] = c.cnt["se"]

                se0_h1 = c.cnt["se"]

                @c.on("tensor")
                def _(t, e=e, ncx=ncx, se0_h1=se0_h1):
                    for g in range(8):
                        if g == 0:
                            if ncx == 0:
                                c.wait(t, "gd", moe_gd[("w1", e)])
                            c.wait(t, "ve")
                            c.wait(t, "se", se0_h1)
                        if g >= 2:
                            c.wait(t, "se", se0_h1 + g - 1)
                        for hl in range(2):
                            hh = 2 * g + hl
                            bank = pj[(g % 2) * 2 + hl]
                            for di in range(DC):
                                last = t.matmul(bank[:, 0:512],
                                                ws1[:, di * 2048 + hh * 128:di * 2048 + (hh + 1) * 128],
                                                mla[:, di * NTOK + 512 * ncx:di * NTOK + 512 * (ncx + 1)],
                                                start=(di == 0), stop=(di == DC - 1))
                        c.inc(last, "te")
                        h1_te[g] = c.cnt["te"]

                if ncx == NC - 1:
                    moe_h1done[e] = c.cnt["te"]

                @c.on("scalar")
                def _(s, e=e, ncx=ncx):
                    for g in range(8):
                        h1_gelu(s, e, ncx, g)

                if ncx == 0:
                    @c.on("gpsimd")
                    def _(g_, e=e):
                        c.wait(g_, "te", h1_te[0])
                        for hh in range(16):
                            c.inc(g_.dma_start(out=ar1[:, hh * 1024:(hh + 1) * 1024],
                                               in_=w2[e * 2048 + 128 * hh:e * 2048 + 128 * (hh + 1), :]),
                                  "gd", 16)
                        c.inc(g_.dma_start(out=b2row[:], in_=b2[e:e + 1, :]), "gd", 16)
                        moe_gd[("w2", e)] = c.cnt["gd"]

                @c.on("vector")
                def _(v, e=e, ncx=ncx):
                    c.wait(v, "se")
                    for hh in range(16):
                        last = v.tensor_tensor(ar2[:, hh * 512:(hh + 1) * 512],
                                               ar2[:, hh * 512:(hh + 1) * 512], bct[:], op=ALU.mult)
                    c.inc(last, "ve")

                h2_te = {}
                h2_ve = {}

                def h2_acc(v, e, ncx, g):
                    c.wait(v, "te", h2_te[g])
                    for dl in range(2):
                        do = 2 * g + dl
                        bank = pj[(g % 2) * 2 + dl]
                        dst = macc[:, do * NTOK + 512 * ncx:do * NTOK + 512 * (ncx + 1)]
                        if e == 0:
                            last = v.tensor_copy(dst, bank[:, 0:512])
                        else:
                            last = v.tensor_tensor(dst, dst, bank[:, 0:512], op=ALU.add)
                    c.inc(last, "ve")
                    h2_ve[g] = c.cnt["ve"]

                ve0_h2 = c.cnt["ve"]

                @c.on("tensor")
                def _(t, e=e, ncx=ncx, ve0_h2=ve0_h2):
                    for g in range(4):
                        if g == 0:
                            c.wait(t, "gd", moe_gd[("w2", e)])
                            c.wait(t, "ve", ve0_h2)
                            c.wait(t, "se")
                        if g >= 2:
                            c.wait(t, "ve", ve0_h2 + g - 1)
                        for dl in range(2):
                            do = 2 * g + dl
                            bank = pj[(g % 2) * 2 + dl]
                            for hh in range(16):
                                t.matmul(bank[:, 0:512],
                                         ar1[:, hh * 1024 + do * 128:hh * 1024 + (do + 1) * 128],
                                         ar2[:, hh * 512:(hh + 1) * 512],
                                         start=(hh == 0), stop=False)
                            last = t.matmul(bank[:, 0:512], b2row[:, do * 128:(do + 1) * 128],
                                            cfb[:, 512 * ncx:512 * (ncx + 1)],
                                            start=False, stop=True)
                        c.inc(last, "te")
                        h2_te[g] = c.cnt["te"]

                @c.on("vector")
                def _(v, e=e, ncx=ncx):
                    for g in range(4):
                        h2_acc(v, e, ncx, g)

        for tt in range(NT):
            @c.on("sync")
            def _(sy, tt=tt):
                if tt > 0:
                    c.wait(sy, "ve")
                c.inc(sy.dma_start(out=xs[:], in_=xr[128 * tt:128 * (tt + 1), :]), "dma", 16)

            @c.on("vector")
            def _(v):
                c.wait(v, "dma")

            ln_tile(c, xs[:], lnt[:], sq, st8, D)

            for half in range(2):
                @c.on("tensor")
                def _(t, tt=tt, half=half):
                    c.wait(t, "ve")
                    c.wait(t, "se")
                    for q in range(4):
                        do = 4 * half + q
                        last = t.transpose(pj[q][:, 0:128],
                                           macc[:, do * NTOK + 128 * tt:do * NTOK + 128 * (tt + 1)],
                                           idf[:])
                    c.inc(last, "te")

                @c.on("vector")
                def _(v, tt=tt, half=half):
                    c.wait(v, "te")
                    for q in range(4):
                        do = 4 * half + q
                        last = v.tensor_copy(ffnt[:, do * 128:(do + 1) * 128], pj[q][:, 0:128])
                    c.inc(last, "ve")

            @c.on("vector")
            def _(v, tt=tt):
                c.wait(v, "se")
                c.wait(v, "ve")
                c.inc(v.tensor_tensor(ffnt[:], ffnt[:], lnt[:], op=ALU.add), "ve")

            ln_tile(c, ffnt[:], lnt[:], sq, st8, D, res=xs[:], res_out=sq[:])

            @c.on("sync")
            def _(sy, tt=tt):
                c.wait(sy, "ve")
                c.inc(sy.dma_start(out=x4o[128 * tt:128 * (tt + 1), :], in_=sq[:]), "st", 16)

        @c.on("sync")
        def _(sy):
            c.wait(sy, "st")
    return c


def rows_of_core(core):
    b, q = core // 4, core % 4
    idx = []
    for k in range(4):
        g = 4 * k + q
        idx.append((b, g * 512, (g + 1) * 512))
    return b, idx


def shard_rows(x):
    out = []
    for core in range(8):
        b, idx = rows_of_core(core)
        out.append(np.ascontiguousarray(
            np.concatenate([x[b, s:e] for (_, s, e) in idx], axis=0)))
    return out


def unshard_rows(shards, dtype=np.float32):
    x = np.empty((B, N, D), dtype)
    for core in range(8):
        b, idx = rows_of_core(core)
        for k, (_, s, e) in enumerate(idx):
            x[b, s:e] = shards[core][k * 512:(k + 1) * 512]
    return x


def assemble_full_cols(pieces):
    d = pieces[0].shape[0]
    full = [np.empty((d, N), pieces[0].dtype) for _ in range(B)]
    for core in range(8):
        b, idx = rows_of_core(core)
        for k, (_, s, e) in enumerate(idx):
            full[b][:, s:e] = pieces[core][:, k * 512:(k + 1) * 512]
    return full


def assemble_full_rows(pieces):
    d = pieces[0].shape[1]
    full = [np.empty((N, d), pieces[0].dtype) for _ in range(B)]
    for core in range(8):
        b, idx = rows_of_core(core)
        for k, (_, s, e) in enumerate(idx):
            full[b][s:e] = pieces[core][k * 512:(k + 1) * 512]
    return full


_IDENT = np.eye(128, dtype=np.float32).astype(BF)
_IDENTF = np.eye(128, dtype=np.float32)


def run_l0pre(x, wq, wk, wv):
    nc = build_l0pre()
    xs = shard_rows(x)
    maps = [{"xr": xs[i], "wq": wq, "wk": wk, "wv": wv, "ident": _IDENT,
             "identf": _IDENTF} for i in range(8)]
    res = run_bass_kernel_spmd(nc.nc, maps, core_ids=list(range(8)))
    return res.results


def my_cols(core):
    b, idx = rows_of_core(core)
    return b, [(s, e) for (_, s, e) in idx]


def run_l0attn(l1res):
    q_full = assemble_full_cols([np.asarray(r["q_fm"]) for r in l1res])
    nc = build_l0attn()
    maps = []
    for core in range(8):
        b = core // 4
        maps.append({"qf": np.ascontiguousarray(q_full[b]),
                     "kf": np.asarray(l1res[core]["k_fm"]),
                     "vt": np.asarray(l1res[core]["v_tm"])})
    res = run_bass_kernel_spmd(nc.nc, maps, core_ids=list(range(8)))
    return res.results


def run_l0red_ffn(x, l1res, l2res, wg, wu, wd):
    xs = shard_rows(x)
    nc = build_l0red_ffn()
    maps = []
    for core in range(8):
        b, cols = my_cols(core)
        ps = []
        for i in range(4):
            pfull = np.asarray(l2res[4 * b + i]["part"])
            ps.append(np.ascontiguousarray(
                np.concatenate([pfull[:, s:e] for (s, e) in cols], axis=1)))
        maps.append({"xr": xs[core], "p0": ps[0], "p1": ps[1], "p2": ps[2],
                     "p3": ps[3], "vfm": np.asarray(l1res[core]["v_fm"]),
                     "wg_i": wg, "wu_i": wu, "wd_i": wd, "ident": _IDENT})
    res = run_bass_kernel_spmd(nc.nc, maps, core_ids=list(range(8)))
    return res.results


def make_masks(core):
    q = core % 4
    out = np.zeros((NC * 128, 16 * 512), np.float32)
    for k in range(NC):
        t = (4 * k + q) * 512 + np.arange(512)[None, :]
        for jj in range(16):
            s = (16 * k + jj) * 128 + np.arange(128)[:, None]
            out[k * 128:(k + 1) * 128, jj * 512:(jj + 1) * 512] = (s <= t)
    return out.astype(BF)


_MASKS = None


def run_l1pre(x2, Wdq, Wuq, Wdkv, Wuk):
    xs = shard_rows(x2)
    nc = build_l1pre()
    maps = [{"xr": xs[i], "Wdq": Wdq, "Wuq": Wuq, "Wdkv": Wdkv, "Wuk": Wuk,
             "ident": _IDENT} for i in range(8)]
    res = run_bass_kernel_spmd(nc.nc, maps, core_ids=list(range(8)))
    return res.results


def run_l1attn(x2, l4res, Wuv, Wo):
    global _MASKS
    if _MASKS is None:
        _MASKS = [make_masks(i) for i in range(8)]
    xs = shard_rows(x2)
    ckv_full = assemble_full_rows([np.asarray(r["ckv_o"]) for r in l4res])
    nc = build_l1attn()
    maps = []
    for core in range(8):
        b = core // 4
        maps.append({"xr": xs[core], "ql_i": np.asarray(l4res[core]["ql_o"]),
                     "ckvfm": np.ascontiguousarray(ckv_full[b].T),
                     "ckvtm": ckv_full[b], "masks": _MASKS[core],
                     "Wuv": Wuv, "Wo": Wo})
    res = run_bass_kernel_spmd(nc.nc, maps, core_ids=list(range(8)))
    return res.results


def run_l1fpre(x3, qw, qb, kw, kb, vw, vb):
    xs = shard_rows(x3)
    nc = build_l1fpre()
    maps = [{"xr": xs[i], "qw": qw, "qb": qb.reshape(1, -1), "kw": kw,
             "kbi": kb.reshape(1, -1), "vw": vw, "vbi": vb.reshape(1, -1),
             "ident": _IDENT} for i in range(8)]
    res = run_bass_kernel_spmd(nc.nc, maps, core_ids=list(range(8)))
    return res.results


def run_l1fmain(x3, l6res, ow, ob, rw, rb, w1, b1, w2, b2):
    xs = shard_rows(x3)
    k_full = assemble_full_cols([np.asarray(r["k_o"]) for r in l6res])
    vt_full = assemble_full_rows([np.asarray(r["vt_o"]) for r in l6res])
    nc = build_l1fmain()
    maps = []
    for core in range(8):
        b = core // 4
        maps.append({"xr": xs[core], "qfm": np.asarray(l6res[core]["q_o"]),
                     "kf": np.ascontiguousarray(k_full[b]), "vtf": vt_full[b],
                     "ow": ow, "ob": ob.reshape(1, -1), "rw": rw, "rb": rb.reshape(1, -1),
                     "w1": w1.reshape(8 * D, 2 * D), "b1": b1,
                     "w2": w2.reshape(8 * 2 * D, D), "b2": b2, "identf": _IDENTF})
    res = run_bass_kernel_spmd(nc.nc, maps, core_ids=list(range(8)))
    return res.results


def kernel(**inputs):
    x = np.asarray(inputs["L0_attn_x"], np.float32)
    l1 = run_l0pre(x, inputs["L0_attn_wq"], inputs["L0_attn_wk"], inputs["L0_attn_wv"])
    l2 = run_l0attn(l1)
    l3 = run_l0red_ffn(x, l1, l2, inputs["L0_ffn_w_gate"], inputs["L0_ffn_w_up"],
                       inputs["L0_ffn_w_down"])
    x2 = unshard_rows([np.asarray(r["x2o"]) for r in l3])
    l4 = run_l1pre(x2, inputs["L1_attn_W_dq"], inputs["L1_attn_W_uq"],
                   inputs["L1_attn_W_dkv"], inputs["L1_attn_W_uk"])
    l5 = run_l1attn(x2, l4, inputs["L1_attn_W_uv"], inputs["L1_attn_W_o"])
    x3 = unshard_rows([np.asarray(r["x3o"]) for r in l5])
    l6 = run_l1fpre(x3, inputs["L1_ffn_q_w"], inputs["L1_ffn_q_b"], inputs["L1_ffn_k_w"],
                    inputs["L1_ffn_k_b"], inputs["L1_ffn_v_w"], inputs["L1_ffn_v_b"])
    l7 = run_l1fmain(x3, l6, inputs["L1_ffn_o_w"], inputs["L1_ffn_o_b"],
                     inputs["L1_ffn_router_w"], inputs["L1_ffn_router_b"],
                     inputs["L1_ffn_w1"], inputs["L1_ffn_b1"],
                     inputs["L1_ffn_w2"], inputs["L1_ffn_b2"])
    return unshard_rows([np.asarray(r["x4o"]) for r in l7])


# version 10
# speedup vs baseline: 1.5512x; 1.2105x over previous
import numpy as np
from contextlib import ExitStack
import concourse.bass as bass
import concourse.mybir as mybir
from concourse.bass_utils import run_bass_kernel_spmd
import ml_dtypes

BF = ml_dtypes.bfloat16
F32 = mybir.dt.float32
BF16 = mybir.dt.bfloat16
I32 = mybir.dt.int32
AF = mybir.ActivationFunctionType
ALU = mybir.AluOpType
AX = mybir.AxisListType.X

D = 1024
B = 2
N = 8192
NTOK = 2048
NT = NTOK // 128
NC = NTOK // 512
DC = D // 128


class Ctx:

    def __init__(self, name):
        self.name = name
        self.nc = bass.Bass("TRN2", target_bir_lowering=False, debug=False)
        self.es = ExitStack()
        self.sems = {}
        self.cnt = {}
        self.block = None

    def __enter__(self):
        self.es.__enter__()
        for s in ("dma", "gd", "te", "ve", "se", "st"):
            self.sems[s] = self.es.enter_context(self.nc.semaphore(s))
            self.cnt[s] = 0
        self.block = self.es.enter_context(self.nc.Block())
        return self

    def __exit__(self, *a):
        return self.es.__exit__(*a)

    def sb(self, name, shape, dtype):
        return self.es.enter_context(self.nc.sbuf_tensor(name, shape, dtype))

    def ps(self, name, cols=512, dtype=F32):
        return self.es.enter_context(self.nc.psum_tensor(name, [128, cols], dtype))

    def inc(self, inst, s, k=1):
        inst.then_inc(self.sems[s], k)
        self.cnt[s] += k

    def wait(self, eng, s, val=None):
        eng.wait_ge(self.sems[s], self.cnt[s] if val is None else val)

    def on(self, engname):
        def deco(fn):
            getattr(self.block, engname)(fn)
        return deco


def ln_tile(c, src, dst, sq, stats, Dfree, res=None, res_out=None):
    mu, m2, var, rstd, nmr = (stats[:, i:i + 1] for i in range(5))

    @c.on("vector")
    def _(v):
        c.inc(v.reduce_sum(mu, src, axis=AX), "ve")

    @c.on("scalar")
    def _(s):
        c.wait(s, "ve")
        c.inc(s.activation(sq[:, 0:src.shape[1]], src, AF.Square, accum_out=m2), "se")

    @c.on("vector")
    def _(v):
        c.wait(v, "se")
        v.tensor_scalar_mul(mu, mu, 1.0 / Dfree)
        c.inc(v.tensor_scalar_mul(m2, m2, 1.0 / Dfree), "ve")
        c.wait(v, "ve")
        c.inc(v.tensor_tensor(var, mu, mu, op=ALU.mult), "ve")
        c.wait(v, "ve")
        c.inc(v.tensor_tensor(var, m2, var, op=ALU.subtract), "ve")
        c.wait(v, "ve")
        c.inc(v.tensor_scalar_add(var, var, 1e-5), "ve")

    @c.on("scalar")
    def _(s):
        c.wait(s, "ve")
        c.inc(s.activation(rstd, var, AF.Sqrt), "se")

    @c.on("vector")
    def _(v):
        c.wait(v, "se")
        c.inc(v.reciprocal(rstd, rstd), "ve")
        c.wait(v, "ve")
        c.inc(v.scalar_tensor_tensor(nmr, mu, -1.0, rstd, op0=ALU.mult, op1=ALU.mult), "ve")

    @c.on("scalar")
    def _(s):
        c.wait(s, "ve")
        c.wait(s, "te")
        c.inc(s.activation(dst, src, AF.Identity, bias=nmr, scale=rstd), "se")

    if res is not None:
        @c.on("vector")
        def _(v):
            c.wait(v, "se")
            c.inc(v.tensor_tensor(res_out, res, dst, op=ALU.add), "ve")


def build_l0pre():
    c = Ctx("l0pre")
    nc = c.nc
    xr = nc.dram_tensor("xr", [NTOK, D], F32, kind="ExternalInput").ap()
    wq = nc.dram_tensor("wq", [D, D], F32, kind="ExternalInput").ap()
    wk = nc.dram_tensor("wk", [D, D], F32, kind="ExternalInput").ap()
    wv = nc.dram_tensor("wv", [D, D], F32, kind="ExternalInput").ap()
    ident = nc.dram_tensor("ident", [128, 128], BF16, kind="ExternalInput").ap()
    identf = nc.dram_tensor("identf", [128, 128], F32, kind="ExternalInput").ap()
    q_fm = nc.dram_tensor("q_fm", [D, NTOK], BF16, kind="ExternalOutput").ap()
    k_fm = nc.dram_tensor("k_fm", [D, NTOK], BF16, kind="ExternalOutput").ap()
    v_fm = nc.dram_tensor("v_fm", [D, NTOK], BF16, kind="ExternalOutput").ap()
    v_tm = nc.dram_tensor("v_tm", [NTOK, D], BF16, kind="ExternalOutput").ap()
    SCQ = float(N) ** -0.5
    with c:
        idb = c.sb("idb", [128, 128], BF16)
        idf = c.sb("idf", [128, 128], F32)
        xs = c.sb("xs", [128, D], F32)
        sq = c.sb("sq", [128, D], F32)
        st8 = c.sb("st8", [128, 8], F32)
        ytmf = c.sb("ytmf", [128, D], F32)
        ytm = c.sb("ytm", [128, D], BF16)
        yfm = c.sb("yfm", [128, DC * NTOK], BF16)
        ypfm = c.sb("ypfm", [128, DC * NTOK], BF16)
        wqb = c.sb("wqb", [128, DC * D], BF16)
        wkb = c.sb("wkb", [128, DC * D], BF16)
        ebf = c.sb("ebf", [128, DC * D], F32)
        eb = c.sb("eb", [128, DC * D], BF16)
        et = c.sb("et", [128, DC * D], BF16)
        csum = c.sb("csum", [128, 8], F32)
        rec = c.sb("rec", [128, 8], F32)
        recrow = c.sb("recrow", [1, D], F32)
        recbc = c.sb("recbc", [128, D], F32)
        srow = c.sb("srow", [1, NTOK], F32)
        srowb = c.sb("srowb", [1, NTOK], BF16)
        scol = c.sb("scol", [128, 1], F32)
        tmpf = c.sb("tmpf", [128, D], F32)
        negone = c.sb("negone", [128, 1], F32)
        onesb = c.sb("onesb", [1, 128], BF16)
        onesf = c.sb("onesf", [1, 128], F32)
        outst = c.sb("outst", [128, 8 * 512], BF16)
        ptr = c.ps("ptr", 1024, BF16)
        prow = c.ps("prow", 512)
        prb = c.ps("prb", 1024)
        pj = [c.ps(f"pj{i}", 512) for i in range(4)]

        @c.on("gpsimd")
        def _(g):
            c.inc(g.dma_start(out=idb[:], in_=ident), "gd", 16)
            c.inc(g.dma_start(out=idf[:], in_=identf), "gd", 16)
            for dc in range(DC):
                c.inc(g.dma_start(out=wqb[:, dc * D:(dc + 1) * D],
                                  in_=wq[128 * dc:128 * (dc + 1), :]), "gd", 16)
            for dc in range(DC):
                c.inc(g.dma_start(out=wkb[:, dc * D:(dc + 1) * D],
                                  in_=wk[128 * dc:128 * (dc + 1), :]), "gd", 16)
            for kb in range(DC):
                c.inc(g.dma_start(out=ebf[:, kb * D:(kb + 1) * D],
                                  in_=wv[128 * kb:128 * (kb + 1), :]), "gd", 16)

        @c.on("vector")
        def _(v):
            v.memset(onesb[:], 1.0)
            v.memset(onesf[:], 1.0)
            c.inc(v.memset(negone[:], -1.0), "ve")

        @c.on("scalar")
        def _(s):
            c.wait(s, "gd")
            c.wait(s, "ve")
            for kb in range(DC):
                c.inc(s.activation(ebf[:, kb * D:(kb + 1) * D],
                                   ebf[:, kb * D:(kb + 1) * D], AF.Exp), "se")
                c.wait(s, "se")
                c.inc(s.activation(eb[:, kb * D:(kb + 1) * D],
                                   ebf[:, kb * D:(kb + 1) * D], AF.Identity,
                                   bias=negone[:]), "se")

        for jb in range(8):
            @c.on("tensor")
            def _(t, jb=jb):
                c.wait(t, "se")
                c.wait(t, "ve")
                for kb in range(DC):
                    last = t.transpose(ptr[:, kb * 128:(kb + 1) * 128],
                                       eb[:, kb * D + jb * 128:kb * D + (jb + 1) * 128],
                                       idb[:])
                c.inc(last, "te")

            @c.on("vector")
            def _(v, jb=jb):
                c.wait(v, "te")
                c.inc(v.tensor_copy(et[:, jb * D:(jb + 1) * D], ptr[:, 0:1024]), "ve")

            @c.on("scalar")
            def _(s, jb=jb):
                c.wait(s, "ve")
                c.wait(s, "se")
                c.inc(s.activation(et[:, jb * D:(jb + 1) * D], et[:, jb * D:(jb + 1) * D],
                                   AF.Identity, accum_out=csum[:, jb:jb + 1]), "se")

        @c.on("vector")
        def _(v):
            c.wait(v, "se")
            c.inc(v.tensor_scalar_add(csum[:], csum[:], float(D)), "ve")
            c.wait(v, "ve")
            c.inc(v.reciprocal(rec[:], csum[:]), "ve")
            c.wait(v, "ve")
            c.inc(v.tensor_scalar_mul(rec[:], rec[:], 1.0 / D), "ve")

        for half in range(2):
            @c.on("tensor")
            def _(t, half=half):
                c.wait(t, "ve")
                for j in range(4):
                    jb = 4 * half + j
                    last = t.matmul(prow[0:1, 128 * j:128 * (j + 1)], rec[:, jb:jb + 1],
                                    idf[:], start=(j == 0), stop=(j == 3))
                c.inc(last, "te")

            @c.on("vector")
            def _(v, half=half):
                c.wait(v, "te")
                c.inc(v.tensor_copy(recrow[:, 512 * half:512 * (half + 1)],
                                    prow[0:1, 0:512]), "ve")

        @c.on("tensor")
        def _(t):
            c.wait(t, "ve")
            t.matmul(prb[:, 0:512], onesf[:], recrow[:, 0:512], start=True, stop=True)
            last = t.matmul(prb[:, 512:1024], onesf[:], recrow[:, 512:1024],
                            start=True, stop=True)
            c.inc(last, "te")

        @c.on("vector")
        def _(v):
            c.wait(v, "te")
            c.inc(v.tensor_copy(recbc[:], prb[:, 0:1024]), "ve")

        for tt in range(NT):
            @c.on("sync")
            def _(sy, tt=tt):
                if tt > 0:
                    c.wait(sy, "se")
                c.inc(sy.dma_start(out=xs[:], in_=xr[128 * tt:128 * (tt + 1), :]), "dma", 16)

            @c.on("vector")
            def _(v):
                c.wait(v, "dma")

            ln_tile(c, xs[:], ytmf[:], sq, st8, D)

            @c.on("vector")
            def _(v, tt=tt):
                c.wait(v, "se")
                c.inc(v.tensor_copy(ytm[:], ytmf[:]), "ve")
                c.inc(v.tensor_tensor(tmpf[:], ytmf[:], recbc[:], op=ALU.mult), "ve")
                c.wait(v, "ve")
                c.inc(v.reduce_sum(scol[:], tmpf[:], axis=AX), "ve")

            @c.on("tensor")
            def _(t, tt=tt):
                c.wait(t, "ve")
                for dc2 in range(DC):
                    last = t.transpose(ptr[:, dc2 * 128:(dc2 + 1) * 128],
                                       ytm[:, dc2 * 128:(dc2 + 1) * 128], idb[:])
                c.inc(last, "te")
                last = t.matmul(prow[0:1, 128 * (tt % 4):128 * (tt % 4 + 1)], scol[:],
                                idf[:], start=(tt % 4 == 0), stop=(tt % 4 == 3))
                c.inc(last, "te")

            @c.on("vector")
            def _(v, tt=tt):
                c.wait(v, "te")
                for dc2 in range(DC):
                    last = v.tensor_copy(yfm[:, dc2 * NTOK + 128 * tt:dc2 * NTOK + 128 * (tt + 1)],
                                         ptr[:, dc2 * 128:(dc2 + 1) * 128])
                c.inc(last, "ve")
                if tt % 4 == 3:
                    c.inc(v.tensor_copy(srow[:, 512 * (tt // 4):512 * (tt // 4 + 1)],
                                        prow[0:1, 0:512]), "ve")

        @c.on("vector")
        def _(v):
            c.wait(v, "ve")
            c.inc(v.tensor_copy(srowb[:], srow[:]), "ve")
            for jb in range(DC):
                last = v.tensor_scalar(ypfm[:, jb * NTOK:(jb + 1) * NTOK],
                                       yfm[:, jb * NTOK:(jb + 1) * NTOK],
                                       rec[:, jb:jb + 1], None, op0=ALU.mult)
            c.inc(last, "ve")

        def project(wtiles, rhs, out_dram, scale):
            for ncx in range(NC):
                for dg in range(2):
                    @c.on("tensor")
                    def _(t, ncx=ncx, dg=dg):
                        c.wait(t, "ve")
                        c.wait(t, "se")
                        for dl in range(4):
                            do = 4 * dg + dl
                            for di in range(DC):
                                last = t.matmul(pj[dl][:, 0:512], wtiles(di, do),
                                                rhs(di, ncx), start=(di == 0), stop=(di == DC - 1))
                        c.inc(last, "te")

                    @c.on("scalar")
                    def _(s, ncx=ncx, dg=dg):
                        c.wait(s, "te")
                        c.wait(s, "st")
                        for dl in range(4):
                            do = 4 * dg + dl
                            if scale == 1.0:
                                last = s.copy(outst[:, do * 512:(do + 1) * 512], pj[dl][:, 0:512])
                            else:
                                last = s.activation(outst[:, do * 512:(do + 1) * 512],
                                                    pj[dl][:, 0:512], AF.Copy, scale=scale)
                        c.inc(last, "se")

                    @c.on("sync")
                    def _(sy, ncx=ncx, dg=dg):
                        c.wait(sy, "se")
                        for dl in range(4):
                            do = 4 * dg + dl
                            c.inc(sy.dma_start(out=out_dram[do * 128:(do + 1) * 128, 512 * ncx:512 * (ncx + 1)],
                                               in_=outst[:, do * 512:(do + 1) * 512]), "st", 16)

        project(lambda di, do: wqb[:, di * D + do * 128:di * D + (do + 1) * 128],
                lambda di, ncx: yfm[:, di * NTOK + 512 * ncx:di * NTOK + 512 * (ncx + 1)],
                q_fm, SCQ)
        project(lambda di, do: wkb[:, di * D + do * 128:di * D + (do + 1) * 128],
                lambda di, ncx: yfm[:, di * NTOK + 512 * ncx:di * NTOK + 512 * (ncx + 1)],
                k_fm, 1.0)

        for ncx in range(NC):
            for dg in range(2):
                @c.on("tensor")
                def _(t, ncx=ncx, dg=dg):
                    c.wait(t, "ve")
                    c.wait(t, "se")
                    for dl in range(4):
                        do = 4 * dg + dl
                        for di in range(DC):
                            t.matmul(pj[dl][:, 0:512],
                                     et[:, di * D + do * 128:di * D + (do + 1) * 128],
                                     ypfm[:, di * NTOK + 512 * ncx:di * NTOK + 512 * (ncx + 1)],
                                     start=(di == 0), stop=False)
                        last = t.matmul(pj[dl][:, 0:512], onesb[:],
                                        srowb[:, 512 * ncx:512 * (ncx + 1)],
                                        start=False, stop=True)
                    c.inc(last, "te")

                @c.on("scalar")
                def _(s, ncx=ncx, dg=dg):
                    c.wait(s, "te")
                    c.wait(s, "st")
                    for dl in range(4):
                        do = 4 * dg + dl
                        last = s.copy(outst[:, do * 512:(do + 1) * 512], pj[dl][:, 0:512])
                    c.inc(last, "se")

                @c.on("sync")
                def _(sy, ncx=ncx, dg=dg):
                    c.wait(sy, "se")
                    for dl in range(4):
                        do = 4 * dg + dl
                        c.inc(sy.dma_start(out=v_fm[do * 128:(do + 1) * 128, 512 * ncx:512 * (ncx + 1)],
                                           in_=outst[:, do * 512:(do + 1) * 512]), "st", 16)

            for sub in range(4):
                @c.on("tensor")
                def _(t, ncx=ncx, sub=sub):
                    c.wait(t, "se")
                    c.wait(t, "ve")
                    for do in range(DC):
                        last = t.transpose(ptr[:, do * 128:(do + 1) * 128],
                                           outst[:, do * 512 + sub * 128:do * 512 + (sub + 1) * 128],
                                           idb[:])
                    c.inc(last, "te")

                @c.on("vector")
                def _(v, ncx=ncx, sub=sub):
                    c.wait(v, "te")
                    c.wait(v, "st")
                    c.inc(v.tensor_copy(ytm[:, 0:1024], ptr[:, 0:1024]), "ve")

                @c.on("sync")
                def _(sy, ncx=ncx, sub=sub):
                    c.wait(sy, "ve")
                    c.inc(sy.dma_start(out=v_tm[512 * ncx + 128 * sub:512 * ncx + 128 * (sub + 1), :],
                                       in_=ytm[:, 0:1024]), "st", 16)

        @c.on("sync")
        def _(sy):
            c.wait(sy, "st")
    return c


def _pstg2(pstg, stg, gi):
    buf = pstg if (gi // 2) % 2 == 0 else stg
    return buf[:, (gi % 2) * 512:(gi % 2 + 1) * 512]


def build_l0attn():
    c = Ctx("l0attn")
    nc = c.nc
    qf = nc.dram_tensor("qf", [D, N], BF16, kind="ExternalInput").ap()
    kf = nc.dram_tensor("kf", [D, NTOK], BF16, kind="ExternalInput").ap()
    vt = nc.dram_tensor("vt", [NTOK, D], BF16, kind="ExternalInput").ap()
    part = nc.dram_tensor("part", [D, N], BF16, kind="ExternalOutput").ap()
    pT = nc.dram_tensor("pT", [NTOK, N], BF16)
    NQ = N // 512
    with c:
        kb = c.sb("kb", [128, DC * NTOK], BF16)
        vtm = c.sb("vtm", [128, NT * D], BF16)
        vp = c.sb("vp", [128, NT * D], BF16)
        qs = c.sb("qs", [128, DC * 512], BF16)
        stg = c.sb("stg", [128, 2 * 512], BF16)
        pstg = c.sb("pstg", [128, 2 * 512], BF16)
        zparts = c.sb("zparts", [128, NT * NQ], F32)
        z1 = c.sb("z1", [128, 1], F32)
        outst = c.sb("outst", [128, 8 * 512], BF16)
        pj = [c.ps(f"pj{i}", 512) for i in range(8)]

        @c.on("gpsimd")
        def _(g):
            for di in range(DC):
                c.inc(g.dma_start(out=kb[:, di * NTOK:(di + 1) * NTOK],
                                  in_=kf[128 * di:128 * (di + 1), :]), "gd", 16)
            for mt in range(NT):
                c.inc(g.dma_start(out=vtm[:, mt * D:(mt + 1) * D],
                                  in_=vt[128 * mt:128 * (mt + 1), :]), "gd", 16)

        qs2 = c.sb("qs2", [128, DC * 512], BF16)
        stg2 = c.sb("stg2", [128, 2 * 512], BF16)
        sS_te = {}
        sS_se = {}
        sS_st = {}

        def qs_of(ncx):
            return qs if ncx % 2 == 0 else qs2

        def stgb(i):
            return stg[:, (i % 2) * 512:(i % 2 + 1) * 512] if (i // 2) % 2 == 0 \
                else stg2[:, (i % 2) * 512:(i % 2 + 1) * 512]

        stS0 = c.cnt["st"]
        for ncx in range(NQ):
            @c.on("sync")
            def _(sy, ncx=ncx):
                if ncx >= 2:
                    c.wait(sy, "te", sS_te[(ncx - 2, NT - 1)])
                for di in range(DC):
                    c.inc(sy.dma_start(out=qs_of(ncx)[:, di * 512:(di + 1) * 512],
                                       in_=qf[128 * di:128 * (di + 1), 512 * ncx:512 * (ncx + 1)]),
                          "dma", 16)
            sS_st[("qld", ncx)] = c.cnt["dma"]
            se0 = c.cnt["se"]

            def se_of(gi, se0=se0, ncx=ncx):
                return se0 + (gi - ncx * NT) + 1

            @c.on("tensor")
            def _(t, ncx=ncx, se0=se0):
                for mt in range(NT):
                    gi = ncx * NT + mt
                    if mt == 0:
                        c.wait(t, "dma", sS_st[("qld", ncx)])
                        if ncx == 0:
                            c.wait(t, "gd")
                    if gi >= 4:
                        if mt < 4:
                            c.wait(t, "se", sS_se[gi - 4])
                        else:
                            c.wait(t, "se", se_of(gi - 4))
                    for di in range(DC):
                        last = t.matmul(pj[gi % 4][:, 0:512],
                                        kb[:, di * NTOK + 128 * mt:di * NTOK + 128 * (mt + 1)],
                                        qs_of(ncx)[:, di * 512:(di + 1) * 512],
                                        start=(di == 0), stop=(di == DC - 1))
                    c.inc(last, "te")
                    sS_te[(ncx, mt)] = c.cnt["te"]

            @c.on("scalar")
            def _(s, ncx=ncx):
                for mt in range(NT):
                    gi = ncx * NT + mt
                    c.wait(s, "te", sS_te[(ncx, mt)])
                    if gi >= 4:
                        c.wait(s, "st", stS0 + 16 * (gi - 4 + 1))
                    c.inc(s.activation(stgb(gi),
                                       pj[gi % 4][:, 0:512], AF.Exp,
                                       accum_out=zparts[:, mt * NQ + ncx:mt * NQ + ncx + 1]), "se")
                    sS_se[gi] = c.cnt["se"]

            @c.on("sync")
            def _(sy, ncx=ncx):
                for mt in range(NT):
                    gi = ncx * NT + mt
                    c.wait(sy, "se", sS_se[gi])
                    c.inc(sy.dma_start(out=pT[128 * mt:128 * (mt + 1), 512 * ncx:512 * (ncx + 1)],
                                       in_=stgb(gi)), "st", 16)
                    sS_st[gi] = c.cnt["st"]

        for mt in range(NT):
            @c.on("vector")
            def _(v, mt=mt):
                c.wait(v, "se")
                c.inc(v.reduce_sum(z1[:], zparts[:, mt * NQ:(mt + 1) * NQ], axis=AX), "ve")
                c.wait(v, "ve")
                c.inc(v.reciprocal(z1[:], z1[:]), "ve")
                c.wait(v, "ve")
                c.inc(v.tensor_scalar(vp[:, mt * D:(mt + 1) * D], vtm[:, mt * D:(mt + 1) * D],
                                      z1[:], None, op0=ALU.mult), "ve")

        pB_dma = {}
        pB_te = {}
        pB_se = {}
        stB0 = None
        for half in range(2):
            for ncx in range(NQ):
                for mt in range(NT):
                    @c.on("sync")
                    def _(sy, half=half, ncx=ncx, mt=mt):
                        gi = (half * NQ + ncx) * NT + mt
                        if gi < 4:
                            c.wait(sy, "st")
                        else:
                            c.wait(sy, "te", pB_te[gi - 4])
                        c.inc(sy.dma_start(out=_pstg2(pstg, stg, gi),
                                           in_=pT[128 * mt:128 * (mt + 1), 512 * ncx:512 * (ncx + 1)]),
                              "dma", 16)
                        pB_dma[gi] = c.cnt["dma"]

                    @c.on("tensor")
                    def _(t, half=half, ncx=ncx, mt=mt):
                        gi = (half * NQ + ncx) * NT + mt
                        c.wait(t, "dma", pB_dma[gi])
                        if mt == 0:
                            if ncx == 0 and half == 0:
                                c.wait(t, "ve")
                            c.wait(t, "se")
                        for dl in range(4):
                            do = 4 * half + dl
                            last = t.matmul(pj[dl][:, 0:512],
                                            vp[:, mt * D + do * 128:mt * D + (do + 1) * 128],
                                            _pstg2(pstg, stg, gi),
                                            start=(mt == 0), stop=(mt == NT - 1))
                        c.inc(last, "te")
                        pB_te[gi] = c.cnt["te"]

                @c.on("scalar")
                def _(s, half=half, ncx=ncx):
                    c.wait(s, "te")
                    c.wait(s, "st", max(0, c.cnt["st"] - 16 * 64))
                    for dl in range(4):
                        do = 4 * half + dl
                        last = s.copy(outst[:, do * 512:(do + 1) * 512], pj[dl][:, 0:512])
                    c.inc(last, "se")

                @c.on("sync")
                def _(sy, half=half, ncx=ncx):
                    c.wait(sy, "se")
                    for dl in range(4):
                        do = 4 * half + dl
                        c.inc(sy.dma_start(out=part[do * 128:(do + 1) * 128, 512 * ncx:512 * (ncx + 1)],
                                           in_=outst[:, do * 512:(do + 1) * 512]), "st", 16)

        @c.on("sync")
        def _(sy):
            c.wait(sy, "st")
    return c


def build_l0red_ffn():
    c = Ctx("l0red")
    nc = c.nc
    xr = nc.dram_tensor("xr", [NTOK, D], F32, kind="ExternalInput").ap()
    p0 = nc.dram_tensor("p0", [D, NTOK], BF16, kind="ExternalInput").ap()
    p1 = nc.dram_tensor("p1", [D, NTOK], BF16, kind="ExternalInput").ap()
    p2 = nc.dram_tensor("p2", [D, NTOK], BF16, kind="ExternalInput").ap()
    p3 = nc.dram_tensor("p3", [D, NTOK], BF16, kind="ExternalInput").ap()
    vfm = nc.dram_tensor("vfm", [D, NTOK], BF16, kind="ExternalInput").ap()
    wg_i = nc.dram_tensor("wg_i", [3 * D, D], I32, kind="ExternalInput").ap()
    wu_i = nc.dram_tensor("wu_i", [3 * D, D], I32, kind="ExternalInput").ap()
    wd_i = nc.dram_tensor("wd_i", [D, 3 * D], I32, kind="ExternalInput").ap()
    ident = nc.dram_tensor("ident", [128, 128], BF16, kind="ExternalInput").ap()
    x2o = nc.dram_tensor("x2o", [NTOK, D], F32, kind="ExternalOutput").ap()
    x1_d = nc.dram_tensor("x1_d", [NTOK, D], F32)
    wgT_d = nc.dram_tensor("wgT_d", [D, 3 * D], BF16)
    wuT_d = nc.dram_tensor("wuT_d", [D, 3 * D], BF16)
    wdT_d = nc.dram_tensor("wdT_d", [3 * D, D], BF16)
    H3 = 3 * D
    HC = H3 // 128
    with c:
        idb = c.sb("idb", [128, 128], BF16)
        buf0 = c.sb("buf0", [128, DC * 512], BF16)
        buf1 = c.sb("buf1", [128, DC * 512], BF16)
        afm = c.sb("afm", [128, DC * 512], BF16)
        xs = c.sb("xs", [128, D], F32)
        sq = c.sb("sq", [128, D], F32)
        st8 = c.sb("st8", [128, 8], F32)
        lnt = c.sb("lnt", [128, D], F32)
        x1t = c.sb("x1t", [128, D], F32)
        atm = c.sb("atm", [128, D], F32)
        ytm = c.sb("ytm", [128, D], BF16)
        yfm = c.sb("yfm", [128, DC * NTOK], BF16)
        wi = c.sb("wi", [128, D], I32)
        wb = c.sb("wb", [128, D], BF16)
        ws = c.sb("ws", [128, 24 * 1024], BF16)
        gb = c.sb("gb", [128, HC * 512], BF16)
        ub = c.sb("ub", [128, HC * 512], BF16)
        outst = c.sb("outst", [128, 8 * 512], BF16)
        ptr = c.ps("ptr", 1024, BF16)
        pj = [c.ps(f"pj{i}", 512) for i in range(4)]

        @c.on("gpsimd")
        def _(g):
            c.inc(g.dma_start(out=idb[:], in_=ident), "gd", 16)

        for ncx in range(NC):
            @c.on("sync")
            def _(sy, ncx=ncx):
                c.wait(sy, "ve")
                for di in range(DC):
                    c.inc(sy.dma_start(out=buf0[:, di * 512:(di + 1) * 512],
                                       in_=p0[128 * di:128 * (di + 1), 512 * ncx:512 * (ncx + 1)]), "dma", 16)
                    c.inc(sy.dma_start(out=buf1[:, di * 512:(di + 1) * 512],
                                       in_=p1[128 * di:128 * (di + 1), 512 * ncx:512 * (ncx + 1)]), "dma", 16)

            @c.on("vector")
            def _(v, ncx=ncx):
                c.wait(v, "dma")
                c.inc(v.tensor_tensor(afm[:], buf0[:], buf1[:], op=ALU.add), "ve")

            @c.on("sync")
            def _(sy, ncx=ncx):
                c.wait(sy, "ve")
                for di in range(DC):
                    c.inc(sy.dma_start(out=buf0[:, di * 512:(di + 1) * 512],
                                       in_=p2[128 * di:128 * (di + 1), 512 * ncx:512 * (ncx + 1)]), "dma", 16)
                    c.inc(sy.dma_start(out=buf1[:, di * 512:(di + 1) * 512],
                                       in_=p3[128 * di:128 * (di + 1), 512 * ncx:512 * (ncx + 1)]), "dma", 16)

            @c.on("vector")
            def _(v, ncx=ncx):
                c.wait(v, "dma")
                c.inc(v.tensor_tensor(buf0[:], buf0[:], buf1[:], op=ALU.add), "ve")
                c.wait(v, "ve")
                c.inc(v.tensor_tensor(afm[:], afm[:], buf0[:], op=ALU.add), "ve")

            @c.on("sync")
            def _(sy, ncx=ncx):
                c.wait(sy, "ve")
                for di in range(DC):
                    c.inc(sy.dma_start(out=buf1[:, di * 512:(di + 1) * 512],
                                       in_=vfm[128 * di:128 * (di + 1), 512 * ncx:512 * (ncx + 1)]), "dma", 16)

            @c.on("vector")
            def _(v, ncx=ncx):
                c.wait(v, "dma")
                c.inc(v.tensor_tensor(afm[:], afm[:], buf1[:], op=ALU.subtract), "ve")

            for sub in range(4):
                tt = 4 * ncx + sub
                @c.on("tensor")
                def _(t, ncx=ncx, sub=sub):
                    c.wait(t, "ve")
                    for di in range(DC):
                        last = t.transpose(ptr[:, di * 128:(di + 1) * 128],
                                           afm[:, di * 512 + sub * 128:di * 512 + (sub + 1) * 128],
                                           idb[:])
                    c.inc(last, "te")

                @c.on("vector")
                def _(v, ncx=ncx, sub=sub):
                    c.wait(v, "te")
                    c.inc(v.tensor_copy(atm[:], ptr[:, 0:1024]), "ve")

                @c.on("sync")
                def _(sy, tt=tt):
                    c.wait(sy, "se")
                    c.inc(sy.dma_start(out=xs[:], in_=xr[128 * tt:128 * (tt + 1), :]), "dma", 16)

                @c.on("vector")
                def _(v):
                    c.wait(v, "dma")

                ln_tile(c, atm[:], lnt[:], sq, st8, D, res=xs[:], res_out=x1t[:])

                @c.on("sync")
                def _(sy, tt=tt):
                    c.wait(sy, "ve")
                    c.inc(sy.dma_start(out=x1_d[128 * tt:128 * (tt + 1), :], in_=x1t[:]), "st", 16)

        def prep_weight(w_i, rows, cols, wT_d):
            for rt in range(rows // 128):
                for ct in range(cols // 1024):
                    @c.on("sync")
                    def _(sy, rt=rt, ct=ct):
                        c.wait(sy, "ve")
                        c.inc(sy.dma_start(out=wi[:],
                                           in_=w_i[128 * rt:128 * (rt + 1), 1024 * ct:1024 * (ct + 1)]),
                              "dma", 16)

                    @c.on("vector")
                    def _(v):
                        c.wait(v, "dma")
                        c.wait(v, "te")
                        c.inc(v.tensor_scalar(wb[:], wi[:], 1.0 / 127.0, None, op0=ALU.mult), "ve")

                    @c.on("tensor")
                    def _(t):
                        c.wait(t, "ve")
                        c.wait(t, "st")
                        for dc2 in range(DC):
                            last = t.transpose(ptr[:, dc2 * 128:(dc2 + 1) * 128],
                                               wb[:, dc2 * 128:(dc2 + 1) * 128], idb[:])
                        c.inc(last, "te")

                    @c.on("vector")
                    def _(v):
                        c.wait(v, "te")
                        c.inc(v.tensor_copy(ytm[:, 0:1024], ptr[:, 0:1024]), "ve")

                    @c.on("sync")
                    def _(sy, rt=rt, ct=ct):
                        c.wait(sy, "ve")
                        for dc2 in range(DC):
                            c.inc(sy.dma_start(
                                out=wT_d[1024 * ct + 128 * dc2:1024 * ct + 128 * (dc2 + 1),
                                         128 * rt:128 * (rt + 1)],
                                in_=ytm[:, dc2 * 128:(dc2 + 1) * 128]), "st", 16)

        prep_weight(wg_i, H3, D, wgT_d)
        prep_weight(wu_i, H3, D, wuT_d)
        prep_weight(wd_i, D, H3, wdT_d)

        for tt in range(NT):
            @c.on("sync")
            def _(sy, tt=tt):
                if tt > 0:
                    c.wait(sy, "se")
                c.inc(sy.dma_start(out=xs[:], in_=x1_d[128 * tt:128 * (tt + 1), :]), "dma", 16)

            @c.on("vector")
            def _(v):
                c.wait(v, "dma")

            ln_tile(c, xs[:], ytm[:], sq, st8, D)

            @c.on("tensor")
            def _(t, tt=tt):
                c.wait(t, "se")
                c.wait(t, "ve")
                for dc2 in range(DC):
                    last = t.transpose(ptr[:, dc2 * 128:(dc2 + 1) * 128],
                                       ytm[:, dc2 * 128:(dc2 + 1) * 128], idb[:])
                c.inc(last, "te")

            @c.on("vector")
            def _(v, tt=tt):
                c.wait(v, "te")
                for dc2 in range(DC):
                    last = v.tensor_copy(yfm[:, dc2 * NTOK + 128 * tt:dc2 * NTOK + 128 * (tt + 1)],
                                         ptr[:, dc2 * 128:(dc2 + 1) * 128])
                c.inc(last, "ve")

        for ncx in range(NC):
            @c.on("sync")
            def _(sy, ncx=ncx):
                c.wait(sy, "te")
                for di in range(DC):
                    c.inc(sy.dma_start(out=ws[:, di * H3:(di + 1) * H3],
                                       in_=wgT_d[128 * di:128 * (di + 1), :]), "dma", 16)

            g_te = {}

            def gu_phase(ncx, dest, silu):
                se0 = c.cnt["se"]

                @c.on("tensor")
                def _(t, ncx=ncx, se0=se0, dest=dest):
                    for g in range(12):
                        if g == 0:
                            c.wait(t, "dma")
                            c.wait(t, "ve")
                            c.wait(t, "se", se0)
                        if g >= 2:
                            c.wait(t, "se", se0 + g - 1)
                        for hl in range(2):
                            hh = 2 * g + hl
                            bank = pj[(g % 2) * 2 + hl]
                            for di in range(DC):
                                last = t.matmul(bank[:, 0:512],
                                                ws[:, di * H3 + hh * 128:di * H3 + (hh + 1) * 128],
                                                yfm[:, di * NTOK + 512 * ncx:di * NTOK + 512 * (ncx + 1)],
                                                start=(di == 0), stop=(di == DC - 1))
                        c.inc(last, "te")
                        g_te[g] = c.cnt["te"]

                @c.on("scalar")
                def _(s, ncx=ncx, dest=dest, silu=silu):
                    for g in range(12):
                        c.wait(s, "te", g_te[g])
                        for hl in range(2):
                            hh = 2 * g + hl
                            bank = pj[(g % 2) * 2 + hl]
                            if silu:
                                last = s.activation(dest[:, hh * 512:(hh + 1) * 512],
                                                    bank[:, 0:512], AF.Silu)
                            else:
                                last = s.copy(dest[:, hh * 512:(hh + 1) * 512], bank[:, 0:512])
                        c.inc(last, "se")

            gu_phase(ncx, gb, True)

            @c.on("sync")
            def _(sy, ncx=ncx):
                c.wait(sy, "te")
                for di in range(DC):
                    c.inc(sy.dma_start(out=ws[:, di * H3:(di + 1) * H3],
                                       in_=wuT_d[128 * di:128 * (di + 1), :]), "dma", 16)

            gu_phase(ncx, ub, False)

            @c.on("vector")
            def _(v, ncx=ncx):
                c.wait(v, "se")
                for hh in range(HC):
                    last = v.tensor_tensor(gb[:, hh * 512:(hh + 1) * 512],
                                           gb[:, hh * 512:(hh + 1) * 512],
                                           ub[:, hh * 512:(hh + 1) * 512], op=ALU.mult)
                c.inc(last, "ve")

            @c.on("sync")
            def _(sy, ncx=ncx):
                c.wait(sy, "te")
                for hh in range(HC):
                    c.inc(sy.dma_start(out=ws[:, hh * 1024:(hh + 1) * 1024],
                                       in_=wdT_d[128 * hh:128 * (hh + 1), :]), "dma", 16)

            d_te = {}
            se0d = c.cnt["se"]

            @c.on("tensor")
            def _(t, ncx=ncx, se0d=se0d):
                for g in range(4):
                    if g == 0:
                        c.wait(t, "dma")
                        c.wait(t, "ve")
                        c.wait(t, "se", se0d)
                    if g >= 2:
                        c.wait(t, "se", se0d + g - 1)
                    for dl in range(2):
                        do = 2 * g + dl
                        bank = pj[(g % 2) * 2 + dl]
                        for hh in range(HC):
                            last = t.matmul(bank[:, 0:512],
                                            ws[:, hh * 1024 + do * 128:hh * 1024 + (do + 1) * 128],
                                            gb[:, hh * 512:(hh + 1) * 512],
                                            start=(hh == 0), stop=(hh == HC - 1))
                    c.inc(last, "te")
                    d_te[g] = c.cnt["te"]

            @c.on("scalar")
            def _(s, ncx=ncx):
                for g in range(4):
                    c.wait(s, "te", d_te[g])
                    for dl in range(2):
                        do = 2 * g + dl
                        bank = pj[(g % 2) * 2 + dl]
                        last = s.copy(outst[:, do * 512:(do + 1) * 512], bank[:, 0:512])
                    c.inc(last, "se")

            for sub in range(4):
                tt = 4 * ncx + sub
                @c.on("tensor")
                def _(t, ncx=ncx, sub=sub):
                    c.wait(t, "se")
                    c.wait(t, "ve")
                    for di in range(DC):
                        last = t.transpose(ptr[:, di * 128:(di + 1) * 128],
                                           outst[:, di * 512 + sub * 128:di * 512 + (sub + 1) * 128],
                                           idb[:])
                    c.inc(last, "te")

                @c.on("vector")
                def _(v, ncx=ncx, sub=sub):
                    c.wait(v, "te")
                    c.inc(v.tensor_copy(atm[:], ptr[:, 0:1024]), "ve")

                @c.on("sync")
                def _(sy, tt=tt):
                    c.wait(sy, "se")
                    c.inc(sy.dma_start(out=xs[:], in_=x1_d[128 * tt:128 * (tt + 1), :]), "dma", 16)

                @c.on("vector")
                def _(v):
                    c.wait(v, "dma")

                ln_tile(c, atm[:], lnt[:], sq, st8, D, res=xs[:], res_out=x1t[:])

                @c.on("sync")
                def _(sy, tt=tt):
                    c.wait(sy, "ve")
                    c.inc(sy.dma_start(out=x2o[128 * tt:128 * (tt + 1), :], in_=x1t[:]), "st", 16)

        @c.on("sync")
        def _(sy):
            c.wait(sy, "st")
    return c


def build_l1pre():
    c = Ctx("l1pre")
    nc = c.nc
    xr = nc.dram_tensor("xr", [NTOK, D], F32, kind="ExternalInput").ap()
    Wdq = nc.dram_tensor("Wdq", [128, D], F32, kind="ExternalInput").ap()
    Wuq = nc.dram_tensor("Wuq", [D, 128], F32, kind="ExternalInput").ap()
    Wdkv = nc.dram_tensor("Wdkv", [128, D], F32, kind="ExternalInput").ap()
    Wuk = nc.dram_tensor("Wuk", [D, 128], F32, kind="ExternalInput").ap()
    ident = nc.dram_tensor("ident", [128, 128], BF16, kind="ExternalInput").ap()
    ql_o = nc.dram_tensor("ql_o", [128, 16 * NTOK], BF16, kind="ExternalOutput").ap()
    ckv_o = nc.dram_tensor("ckv_o", [NTOK, 128], BF16, kind="ExternalOutput").ap()
    with c:
        idb = c.sb("idb", [128, 128], BF16)
        xs = c.sb("xs", [128, D], F32)
        sq = c.sb("sq", [128, D], F32)
        st8 = c.sb("st8", [128, 8], F32)
        ytm = c.sb("ytm", [128, D], BF16)
        yfm = c.sb("yfm", [128, DC * NTOK], BF16)
        wdq = c.sb("wdq", [128, D], BF16)
        wdqT = c.sb("wdqT", [128, DC * 128], BF16)
        wuq = c.sb("wuq", [128, DC * 128], BF16)
        wuqT = c.sb("wuqT", [128, DC * 128], BF16)
        wdkv = c.sb("wdkv", [128, D], BF16)
        wdkvT = c.sb("wdkvT", [128, DC * 128], BF16)
        wuk = c.sb("wuk", [128, DC * 128], BF16)
        at = c.sb("at", [128, DC * D], BF16)
        keffs = c.sb("keffs", [128, DC * 128], BF16)
        t1 = c.sb("t1", [128, NTOK], BF16)
        qfm = c.sb("qfm", [128, DC * NTOK], BF16)
        qlb = c.sb("qlb", [128, 512], BF16)
        ckfm = c.sb("ckfm", [128, NTOK], BF16)
        outst = c.sb("outst", [128, 1024], BF16)
        ptr = c.ps("ptr", 1024, BF16)
        pj = [c.ps(f"pj{i}", 512) for i in range(4)]

        @c.on("gpsimd")
        def _(g):
            c.inc(g.dma_start(out=idb[:], in_=ident), "gd", 16)
            c.inc(g.dma_start(out=wdq[:], in_=Wdq), "gd", 16)
            c.inc(g.dma_start(out=wdkv[:], in_=Wdkv), "gd", 16)
            for cc in range(DC):
                c.inc(g.dma_start(out=wuq[:, cc * 128:(cc + 1) * 128],
                                  in_=Wuq[128 * cc:128 * (cc + 1), :]), "gd", 16)
                c.inc(g.dma_start(out=wuk[:, cc * 128:(cc + 1) * 128],
                                  in_=Wuk[128 * cc:128 * (cc + 1), :]), "gd", 16)

        @c.on("tensor")
        def _(t):
            c.wait(t, "gd")
            for cc in range(DC):
                last = t.transpose(ptr[:, cc * 128:(cc + 1) * 128],
                                   wdq[:, cc * 128:(cc + 1) * 128], idb[:])
            c.inc(last, "te")

        @c.on("vector")
        def _(v):
            c.wait(v, "te")
            c.inc(v.tensor_copy(wdqT[:], ptr[:, 0:1024]), "ve")

        @c.on("tensor")
        def _(t):
            c.wait(t, "ve")
            for cc in range(DC):
                last = t.transpose(ptr[:, cc * 128:(cc + 1) * 128],
                                   wuq[:, cc * 128:(cc + 1) * 128], idb[:])
            c.inc(last, "te")

        @c.on("vector")
        def _(v):
            c.wait(v, "te")
            c.inc(v.tensor_copy(wuqT[:], ptr[:, 0:1024]), "ve")

        @c.on("tensor")
        def _(t):
            c.wait(t, "ve")
            for cc in range(DC):
                last = t.transpose(ptr[:, cc * 128:(cc + 1) * 128],
                                   wdkv[:, cc * 128:(cc + 1) * 128], idb[:])
            c.inc(last, "te")

        @c.on("vector")
        def _(v):
            c.wait(v, "te")
            c.inc(v.tensor_copy(wdkvT[:], ptr[:, 0:1024]), "ve")

        for c2 in range(DC):
            @c.on("tensor")
            def _(t, c2=c2):
                c.wait(t, "ve")
                c.wait(t, "se")
                for half in range(2):
                    last = t.matmul(pj[half][:, 0:512], wuqT[:, c2 * 128:(c2 + 1) * 128],
                                    wdq[:, half * 512:(half + 1) * 512],
                                    start=True, stop=True)
                c.inc(last, "te")

            @c.on("scalar")
            def _(s, c2=c2):
                c.wait(s, "te")
                for half in range(2):
                    last = s.copy(at[:, c2 * D + half * 512:c2 * D + (half + 1) * 512],
                                  pj[half][:, 0:512])
                c.inc(last, "se")

        for cg in range(2):
            @c.on("tensor")
            def _(t, cg=cg):
                c.wait(t, "se")
                for cl in range(4):
                    cc = 4 * cg + cl
                    for c2 in range(DC):
                        last = t.matmul(pj[cl][0:128, 0:128],
                                        at[:, c2 * D + cc * 128:c2 * D + (cc + 1) * 128],
                                        wuk[:, c2 * 128:(c2 + 1) * 128],
                                        start=(c2 == 0), stop=(c2 == DC - 1))
                c.inc(last, "te")

            @c.on("scalar")
            def _(s, cg=cg):
                c.wait(s, "te")
                for cl in range(4):
                    cc = 4 * cg + cl
                    last = s.activation(keffs[:, cc * 128:(cc + 1) * 128],
                                        pj[cl][0:128, 0:128], AF.Copy, scale=0.125)
                c.inc(last, "se")

        for tt in range(NT):
            @c.on("sync")
            def _(sy, tt=tt):
                if tt > 0:
                    c.wait(sy, "se")
                c.inc(sy.dma_start(out=xs[:], in_=xr[128 * tt:128 * (tt + 1), :]), "dma", 16)

            @c.on("vector")
            def _(v):
                c.wait(v, "dma")

            ln_tile(c, xs[:], ytm[:], sq, st8, D)

            @c.on("tensor")
            def _(t, tt=tt):
                c.wait(t, "se")
                c.wait(t, "ve")
                for dc2 in range(DC):
                    last = t.transpose(ptr[:, dc2 * 128:(dc2 + 1) * 128],
                                       ytm[:, dc2 * 128:(dc2 + 1) * 128], idb[:])
                c.inc(last, "te")

            @c.on("vector")
            def _(v, tt=tt):
                c.wait(v, "te")
                for dc2 in range(DC):
                    last = v.tensor_copy(yfm[:, dc2 * NTOK + 128 * tt:dc2 * NTOK + 128 * (tt + 1)],
                                         ptr[:, dc2 * 128:(dc2 + 1) * 128])
                c.inc(last, "ve")

        for ncx in range(NC):
            @c.on("tensor")
            def _(t, ncx=ncx):
                c.wait(t, "ve")
                c.wait(t, "se")
                for di in range(DC):
                    t.matmul(pj[0][:, 0:512], wdqT[:, di * 128:(di + 1) * 128],
                             yfm[:, di * NTOK + 512 * ncx:di * NTOK + 512 * (ncx + 1)],
                             start=(di == 0), stop=(di == DC - 1))
                for di in range(DC):
                    last = t.matmul(pj[1][:, 0:512], wdkvT[:, di * 128:(di + 1) * 128],
                                    yfm[:, di * NTOK + 512 * ncx:di * NTOK + 512 * (ncx + 1)],
                                    start=(di == 0), stop=(di == DC - 1))
                c.inc(last, "te")

            @c.on("scalar")
            def _(s, ncx=ncx):
                c.wait(s, "te")
                s.copy(t1[:, 512 * ncx:512 * (ncx + 1)], pj[0][:, 0:512])
                c.inc(s.copy(ckfm[:, 512 * ncx:512 * (ncx + 1)], pj[1][:, 0:512]), "se")

            for cg in range(2):
                @c.on("tensor")
                def _(t, ncx=ncx, cg=cg):
                    c.wait(t, "se")
                    for cl in range(4):
                        cc = 4 * cg + cl
                        last = t.matmul(pj[cl][:, 0:512], wuqT[:, cc * 128:(cc + 1) * 128],
                                        t1[:, 512 * ncx:512 * (ncx + 1)],
                                        start=True, stop=True)
                    c.inc(last, "te")

                @c.on("scalar")
                def _(s, ncx=ncx, cg=cg):
                    c.wait(s, "te")
                    for cl in range(4):
                        cc = 4 * cg + cl
                        last = s.copy(qfm[:, cc * NTOK + 512 * ncx:cc * NTOK + 512 * (ncx + 1)],
                                      pj[cl][:, 0:512])
                    c.inc(last, "se")

        for h in range(16):
            for ncx in range(NC):
                @c.on("tensor")
                def _(t, h=h, ncx=ncx):
                    c.wait(t, "se")
                    c.wait(t, "st")
                    r0 = 64 * (h % 2)
                    cc = h // 2
                    last = t.matmul(pj[ncx % 4][:, 0:512],
                                    keffs[r0:r0 + 64, cc * 128:(cc + 1) * 128],
                                    qfm[r0:r0 + 64, cc * NTOK + 512 * ncx:cc * NTOK + 512 * (ncx + 1)],
                                    start=True, stop=True)
                    c.inc(last, "te")

                @c.on("scalar")
                def _(s, h=h, ncx=ncx):
                    c.wait(s, "te")
                    c.inc(s.copy(qlb[:, 0:512], pj[ncx % 4][:, 0:512]), "se")

                @c.on("sync")
                def _(sy, h=h, ncx=ncx):
                    c.wait(sy, "se")
                    c.inc(sy.dma_start(out=ql_o[0:128, h * NTOK + 512 * ncx:h * NTOK + 512 * (ncx + 1)],
                                       in_=qlb[:, 0:512]), "st", 16)

        for tt in range(NT):
            @c.on("tensor")
            def _(t, tt=tt):
                c.wait(t, "se")
                c.wait(t, "ve")
                c.inc(t.transpose(ptr[:, 0:128], ckfm[:, 128 * tt:128 * (tt + 1)], idb[:]), "te")

            @c.on("vector")
            def _(v, tt=tt):
                c.wait(v, "te")
                c.wait(v, "st")
                c.inc(v.tensor_copy(outst[:, 0:128], ptr[:, 0:128]), "ve")

            @c.on("sync")
            def _(sy, tt=tt):
                c.wait(sy, "ve")
                c.inc(sy.dma_start(out=ckv_o[128 * tt:128 * (tt + 1), :], in_=outst[:, 0:128]), "st", 16)

        @c.on("sync")
        def _(sy):
            c.wait(sy, "st")
    return c


def build_l1attn():
    c = Ctx("l1attn")
    nc = c.nc
    xr = nc.dram_tensor("xr", [NTOK, D], F32, kind="ExternalInput").ap()
    ql_i = nc.dram_tensor("ql_i", [128, 16 * NTOK], BF16, kind="ExternalInput").ap()
    ckvfm = nc.dram_tensor("ckvfm", [128, N], BF16, kind="ExternalInput").ap()
    ckvtm = nc.dram_tensor("ckvtm", [N, 128], BF16, kind="ExternalInput").ap()
    masks = nc.dram_tensor("masks", [NC * 128, 16 * 512], BF16, kind="ExternalInput").ap()
    Wuv = nc.dram_tensor("Wuv", [D, 128], F32, kind="ExternalInput").ap()
    Wo = nc.dram_tensor("Wo", [D, D], F32, kind="ExternalInput").ap()
    x3o = nc.dram_tensor("x3o", [NTOK, D], F32, kind="ExternalOutput").ap()
    wo_d = nc.dram_tensor("wo_d", [D, D], BF16)
    ctx_d = nc.dram_tensor("ctx_d", [NTOK, 16 * 128], BF16)
    y_d = nc.dram_tensor("y_d", [D, NTOK], BF16)
    NST = N // 128
    with c:
        qls = c.sb("qls", [128, 16 * NTOK], BF16)
        ckf = c.sb("ckf", [128, N], BF16)
        cke = c.sb("cke", [128, NST * 130], BF16)
        msk = c.sb("msk", [128, 16 * 512], BF16)
        wuv = c.sb("wuv", [128, DC * 128], BF16)
        wob = c.sb("wob", [128, D], BF16)
        vef = c.sb("vef", [128, D], BF16)
        pe = c.sb("pe", [128, 8 * 512], BF16)
        zc = c.sb("zc", [128, 1], F32)
        cstg = c.sb("cstg", [128, 128], BF16)
        zc8 = c.sb("zc8", [128, 8], F32)
        cst8 = c.sb("cst8", [128, 8 * 128], BF16)
        ctxf = c.sb("ctxf", [128, NTOK], BF16)
        yst = c.sb("yst", [128, 512], BF16)
        xs = c.sb("xs", [128, D], F32)
        sq = c.sb("sq", [128, D], F32)
        st8 = c.sb("st8", [128, 8], F32)
        lnt = c.sb("lnt", [128, D], F32)
        x1t = c.sb("x1t", [128, D], F32)
        atm = c.sb("atm", [128, D], F32)
        pS = c.ps("pS", 1024)
        pC = [c.ps(f"pC{i}", 1024) for i in range(2)]

        def cOFF(ts):
            return (ts // 2) * 512 + (ts % 2) * 129

        @c.on("gpsimd")
        def _(g):
            for hq in range(4):
                c.inc(g.dma_start(out=qls[:, hq * 4 * NTOK:(hq + 1) * 4 * NTOK],
                                  in_=ql_i[0:128, hq * 4 * NTOK:(hq + 1) * 4 * NTOK]), "gd", 16)
            c.inc(g.dma_start(out=ckf[:], in_=ckvfm), "gd", 16)
            src = ckvtm.rearrange("(t p) k -> p t k", p=128)
            dst = cke[:].rearrange("p (t k) -> p t k", k=130)[:, :, 0:128]
            c.inc(g.dma_start(out=dst, in_=src), "gd", 16)
            for cc in range(DC):
                c.inc(g.dma_start(out=wuv[:, cc * 128:(cc + 1) * 128],
                                  in_=Wuv[128 * cc:128 * (cc + 1), :]), "gd", 16)

        @c.on("vector")
        def _(v):
            c.wait(v, "gd")
            ones_view = cke[:].rearrange("p (t k) -> p t k", k=130)[:, :, 128:129]
            c.inc(v.memset(ones_view, 1.0), "ve")

        for cc in range(DC):
            @c.on("gpsimd")
            def _(g, cc=cc):
                c.wait(g, "st")
                c.inc(g.dma_start(out=wob[:], in_=Wo[128 * cc:128 * (cc + 1), :]), "gd", 16)

            @c.on("sync")
            def _(sy, cc=cc):
                c.wait(sy, "gd")
                c.inc(sy.dma_start(out=wo_d[128 * cc:128 * (cc + 1), :], in_=wob[:]), "st", 16)

        for c2 in range(DC):
            @c.on("sync")
            def _(sy, c2=c2):
                c.wait(sy, "st")
                c.wait(sy, "te")
                c.inc(sy.dma_start(out=wob[:], in_=wo_d[:, 128 * c2:128 * (c2 + 1)],
                                   transpose=True), "dma", 16)

            @c.on("tensor")
            def _(t, c2=c2):
                c.wait(t, "dma")
                c.wait(t, "ve")
                for half in range(2):
                    last = t.matmul(pS[:, half * 512:(half + 1) * 512],
                                    wuv[:, c2 * 128:(c2 + 1) * 128],
                                    wob[:, half * 512:(half + 1) * 512],
                                    start=(c2 == 0), stop=(c2 == DC - 1))
                c.inc(last, "te")

        @c.on("vector")
        def _(v):
            c.wait(v, "te")
            c.inc(v.tensor_copy(vef[:], pS[:, 0:1024]), "ve")

        pS2 = c.ps("pS2", 1024)
        pe2off = 1024
        for k in range(NC):
            dma_k = None

            @c.on("sync")
            def _(sy, k=k):
                c.wait(sy, "ve")
                c.wait(sy, "te")
                c.inc(sy.dma_start(out=msk[:], in_=masks[k * 128:(k + 1) * 128, :]), "dma", 16)
            dma_k = c.cnt["dma"]

            for hg in range(8):
                JMAX = 16 * k + 16
                te_ck = {}
                se0 = c.cnt["se"]
                ve0 = c.cnt["ve"]

                def se_of(j):
                    return se0 + j + 1

                def ve_of(j, k=k):
                    return ve0 + (j - 16 * k) + 1

                def S_of(j):
                    return pS if j % 2 == 0 else pS2

                def pv_wait_emit(t, k, hg, j, JMAX):
                    if j == 0:
                        c.wait(t, "ve", ve0)
                    if j >= 16 * k:
                        c.wait(t, "ve", ve_of(j))
                    else:
                        c.wait(t, "se", se_of(j))
                    for i in range(2):
                        for ts in range(4):
                            last = t.matmul(pC[i][:, cOFF(ts):cOFF(ts) + 129],
                                            pe[:, (j % 4) * pe2off + i * 512 + ts * 128:
                                               (j % 4) * pe2off + i * 512 + (ts + 1) * 128],
                                            cke[:, 130 * j:130 * j + 129],
                                            start=(j == 0 and ts % 2 == 0),
                                            stop=(j == JMAX - 1 and ts % 2 == 1))
                    c.inc(last, "te")

                @c.on("tensor")
                def _(t, k=k, hg=hg, JMAX=JMAX):
                    for j in range(JMAX):
                        if j == 0:
                            c.wait(t, "dma", dma_k)
                        if j >= 3:
                            pv_wait_emit(t, k, hg, j - 3, JMAX)
                        if j >= 2:
                            c.wait(t, "se", se_of(j - 2))
                        for i in range(2):
                            h = 2 * hg + i
                            last = t.matmul(S_of(j)[:, i * 512:(i + 1) * 512],
                                            ckf[:, 128 * j:128 * (j + 1)],
                                            qls[:, h * NTOK + 512 * k:h * NTOK + 512 * (k + 1)],
                                            start=True, stop=True)
                        c.inc(last, "te")
                        te_ck[j] = c.cnt["te"]
                    for jp in range(max(0, JMAX - 3), JMAX):
                        pv_wait_emit(t, k, hg, jp, JMAX)

                @c.on("scalar")
                def _(s, k=k, hg=hg, JMAX=JMAX):
                    for j in range(JMAX):
                        c.wait(s, "te", te_ck[j])
                        c.inc(s.activation(pe[:, (j % 4) * pe2off:(j % 4) * pe2off + 1024],
                                           S_of(j)[:, 0:1024], AF.Exp), "se")
                assert c.cnt["se"] == se0 + JMAX

                @c.on("vector")
                def _(v, k=k, hg=hg, JMAX=JMAX):
                    for j in range(16 * k, JMAX):
                        jj = j - 16 * k
                        c.wait(v, "se", se_of(j))
                        for i in range(2):
                            last = v.tensor_tensor(pe[:, (j % 4) * pe2off + i * 512:
                                                      (j % 4) * pe2off + (i + 1) * 512],
                                                   pe[:, (j % 4) * pe2off + i * 512:
                                                      (j % 4) * pe2off + (i + 1) * 512],
                                                   msk[:, jj * 512:(jj + 1) * 512],
                                                   op=ALU.mult)
                        c.inc(last, "ve")
                assert c.cnt["ve"] == ve0 + 16

                ve0n = c.cnt["ve"]

                @c.on("vector")
                def _(v, k=k, hg=hg, ve0n=ve0n):
                    c.wait(v, "te")
                    c.wait(v, "st")
                    for i in range(2):
                        for ts in range(4):
                            inst = v.reciprocal(zc8[:, 4 * i + ts:4 * i + ts + 1],
                                                pC[i][:, cOFF(ts) + 128:cOFF(ts) + 129])
                    c.inc(inst, "ve")
                    c.wait(v, "ve")
                    for i in range(2):
                        for ts in range(4):
                            c.inc(v.tensor_scalar(cst8[:, (4 * i + ts) * 128:(4 * i + ts + 1) * 128],
                                                  pC[i][:, cOFF(ts):cOFF(ts) + 128],
                                                  zc8[:, 4 * i + ts:4 * i + ts + 1], None,
                                                  op0=ALU.mult), "ve")

                @c.on("sync")
                def _(sy, k=k, hg=hg, ve0n=ve0n):
                    for i in range(2):
                        h = 2 * hg + i
                        for ts in range(4):
                            u = 4 * i + ts
                            c.wait(sy, "ve", ve0n + 1 + u + 1)
                            tt = 4 * k + ts
                            c.inc(sy.dma_start(out=ctx_d[128 * tt:128 * (tt + 1), 128 * h:128 * (h + 1)],
                                               in_=cst8[:, u * 128:(u + 1) * 128]), "st", 16)

        for h in range(16):
            @c.on("sync")
            def _(sy, h=h):
                c.wait(sy, "te")
                c.inc(sy.dma_start(out=ctxf[:], in_=ctx_d[:, 128 * h:128 * (h + 1)],
                                   transpose=True), "dma", 16)

            for ncx in range(NC):
                @c.on("tensor")
                def _(t, h=h, ncx=ncx):
                    c.wait(t, "dma")
                    c.wait(t, "se")
                    c.inc(t.matmul(pS[0:64, 0:512], vef[:, 64 * h:64 * (h + 1)],
                                   ctxf[:, 512 * ncx:512 * (ncx + 1)], start=True, stop=True), "te")

                @c.on("scalar")
                def _(s, h=h, ncx=ncx):
                    c.wait(s, "te")
                    c.wait(s, "st")
                    c.inc(s.copy(yst[:, 0:0 + 512][0:64, :], pS[0:64, 0:512]), "se")

                @c.on("sync")
                def _(sy, h=h, ncx=ncx):
                    c.wait(sy, "se")
                    c.inc(sy.dma_start(out=y_d[64 * h:64 * (h + 1), 512 * ncx:512 * (ncx + 1)],
                                       in_=yst[0:64, 0:512]), "st", 16)

        ybt = c.sb("ybt", [128, D], BF16)
        for tt in range(NT):
            @c.on("sync")
            def _(sy, tt=tt):
                c.wait(sy, "ve")
                c.inc(sy.dma_start(out=ybt[:], in_=y_d[:, 128 * tt:128 * (tt + 1)],
                                   transpose=True), "dma", 16)
                c.inc(sy.dma_start(out=xs[:], in_=xr[128 * tt:128 * (tt + 1), :]), "dma", 16)

            @c.on("vector")
            def _(v, tt=tt):
                c.wait(v, "dma")
                c.inc(v.tensor_copy(atm[:], ybt[:]), "ve")

            ln_tile(c, atm[:], lnt[:], sq, st8, D, res=xs[:], res_out=x1t[:])

            @c.on("sync")
            def _(sy, tt=tt):
                c.wait(sy, "ve")
                c.inc(sy.dma_start(out=x3o[128 * tt:128 * (tt + 1), :], in_=x1t[:]), "st", 16)

        @c.on("sync")
        def _(sy):
            c.wait(sy, "st")
    return c


def build_l1fpre():
    c = Ctx("l1fpre")
    nc = c.nc
    xr = nc.dram_tensor("xr", [NTOK, D], F32, kind="ExternalInput").ap()
    qw = nc.dram_tensor("qw", [D, D], F32, kind="ExternalInput").ap()
    qb = nc.dram_tensor("qb", [1, D], F32, kind="ExternalInput").ap()
    kw = nc.dram_tensor("kw", [D, 512], F32, kind="ExternalInput").ap()
    kbi = nc.dram_tensor("kbi", [1, 512], F32, kind="ExternalInput").ap()
    vw = nc.dram_tensor("vw", [D, 512], F32, kind="ExternalInput").ap()
    vbi = nc.dram_tensor("vbi", [1, 512], F32, kind="ExternalInput").ap()
    ident = nc.dram_tensor("ident", [128, 128], BF16, kind="ExternalInput").ap()
    q_o = nc.dram_tensor("q_o", [D, NTOK], BF16, kind="ExternalOutput").ap()
    k_o = nc.dram_tensor("k_o", [512, NTOK], BF16, kind="ExternalOutput").ap()
    vt_o = nc.dram_tensor("vt_o", [NTOK, 512], BF16, kind="ExternalOutput").ap()
    with c:
        idb = c.sb("idb", [128, 128], BF16)
        xs = c.sb("xs", [128, D], F32)
        sq = c.sb("sq", [128, D], F32)
        st8 = c.sb("st8", [128, 8], F32)
        ytm = c.sb("ytm", [128, D], BF16)
        yfm = c.sb("yfm", [128, DC * NTOK], BF16)
        qwb = c.sb("qwb", [128, DC * D], BF16)
        kwb = c.sb("kwb", [128, DC * 512], BF16)
        vwb = c.sb("vwb", [128, DC * 512], BF16)
        brow = c.sb("brow", [1, 2 * D], BF16)
        onesone = c.sb("onesone", [1, 1], BF16)
        bcol = c.sb("bcol", [128, 16], F32)
        outst = c.sb("outst", [128, 8 * 512], BF16)
        ptr = c.ps("ptr", 1024, BF16)
        pb = c.ps("pb", 512)
        pj = [c.ps(f"pj{i}", 512) for i in range(4)]

        @c.on("gpsimd")
        def _(g):
            c.inc(g.dma_start(out=idb[:], in_=ident), "gd", 16)
            for dc in range(DC):
                c.inc(g.dma_start(out=qwb[:, dc * D:(dc + 1) * D],
                                  in_=qw[128 * dc:128 * (dc + 1), :]), "gd", 16)
                c.inc(g.dma_start(out=kwb[:, dc * 512:(dc + 1) * 512],
                                  in_=kw[128 * dc:128 * (dc + 1), :]), "gd", 16)
                c.inc(g.dma_start(out=vwb[:, dc * 512:(dc + 1) * 512],
                                  in_=vw[128 * dc:128 * (dc + 1), :]), "gd", 16)
            c.inc(g.dma_start(out=brow[:, 0:D], in_=qb), "gd", 16)
            c.inc(g.dma_start(out=brow[:, D:D + 512], in_=kbi), "gd", 16)
            c.inc(g.dma_start(out=brow[:, D + 512:D + 1024], in_=vbi), "gd", 16)

        @c.on("vector")
        def _(v):
            c.inc(v.memset(onesone[:], 1.0), "ve")

        @c.on("tensor")
        def _(t):
            c.wait(t, "gd")
            c.wait(t, "ve")
            for i in range(16):
                last = t.matmul(pb[:, i:i + 1], brow[:, 128 * i:128 * (i + 1)],
                                onesone[:], start=(i == 0), stop=(i == 15))
            c.inc(last, "te")

        @c.on("vector")
        def _(v):
            c.wait(v, "te")
            c.inc(v.tensor_copy(bcol[:], pb[:, 0:16]), "ve")

        for tt in range(NT):
            @c.on("sync")
            def _(sy, tt=tt):
                if tt > 0:
                    c.wait(sy, "se")
                c.inc(sy.dma_start(out=xs[:], in_=xr[128 * tt:128 * (tt + 1), :]), "dma", 16)

            @c.on("vector")
            def _(v):
                c.wait(v, "dma")

            ln_tile(c, xs[:], ytm[:], sq, st8, D)

            @c.on("tensor")
            def _(t, tt=tt):
                c.wait(t, "se")
                c.wait(t, "ve")
                for dc2 in range(DC):
                    last = t.transpose(ptr[:, dc2 * 128:(dc2 + 1) * 128],
                                       ytm[:, dc2 * 128:(dc2 + 1) * 128], idb[:])
                c.inc(last, "te")

            @c.on("vector")
            def _(v, tt=tt):
                c.wait(v, "te")
                for dc2 in range(DC):
                    last = v.tensor_copy(yfm[:, dc2 * NTOK + 128 * tt:dc2 * NTOK + 128 * (tt + 1)],
                                         ptr[:, dc2 * 128:(dc2 + 1) * 128])
                c.inc(last, "ve")

        for ncx in range(NC):
            for dg in range(2):
                @c.on("tensor")
                def _(t, ncx=ncx, dg=dg):
                    c.wait(t, "ve")
                    c.wait(t, "se")
                    for dl in range(4):
                        do = 4 * dg + dl
                        for di in range(DC):
                            last = t.matmul(pj[dl][:, 0:512],
                                            qwb[:, di * D + do * 128:di * D + (do + 1) * 128],
                                            yfm[:, di * NTOK + 512 * ncx:di * NTOK + 512 * (ncx + 1)],
                                            start=(di == 0), stop=(di == DC - 1))
                    c.inc(last, "te")

                @c.on("scalar")
                def _(s, ncx=ncx, dg=dg):
                    c.wait(s, "te")
                    c.wait(s, "st")
                    for dl in range(4):
                        do = 4 * dg + dl
                        last = s.activation(outst[:, do * 512:(do + 1) * 512], pj[dl][:, 0:512],
                                            AF.Identity, bias=bcol[:, do:do + 1])
                    c.inc(last, "se")

                @c.on("sync")
                def _(sy, ncx=ncx, dg=dg):
                    c.wait(sy, "se")
                    for dl in range(4):
                        do = 4 * dg + dl
                        c.inc(sy.dma_start(out=q_o[do * 128:(do + 1) * 128, 512 * ncx:512 * (ncx + 1)],
                                           in_=outst[:, do * 512:(do + 1) * 512]), "st", 16)

        for ncx in range(NC):
            @c.on("tensor")
            def _(t, ncx=ncx):
                c.wait(t, "ve")
                c.wait(t, "se")
                for dl in range(4):
                    for di in range(DC):
                        last = t.matmul(pj[dl][:, 0:512],
                                        kwb[:, di * 512 + dl * 128:di * 512 + (dl + 1) * 128],
                                        yfm[:, di * NTOK + 512 * ncx:di * NTOK + 512 * (ncx + 1)],
                                        start=(di == 0), stop=(di == DC - 1))
                c.inc(last, "te")

            @c.on("scalar")
            def _(s, ncx=ncx):
                c.wait(s, "te")
                c.wait(s, "st")
                for dl in range(4):
                    last = s.activation(outst[:, dl * 512:(dl + 1) * 512], pj[dl][:, 0:512],
                                        AF.Identity, bias=bcol[:, 8 + dl:9 + dl])
                c.inc(last, "se")

            @c.on("sync")
            def _(sy, ncx=ncx):
                c.wait(sy, "se")
                for dl in range(4):
                    c.inc(sy.dma_start(out=k_o[dl * 128:(dl + 1) * 128, 512 * ncx:512 * (ncx + 1)],
                                       in_=outst[:, dl * 512:(dl + 1) * 512]), "st", 16)

            @c.on("tensor")
            def _(t, ncx=ncx):
                c.wait(t, "ve")
                c.wait(t, "se")
                for dl in range(4):
                    for di in range(DC):
                        last = t.matmul(pj[dl][:, 0:512],
                                        vwb[:, di * 512 + dl * 128:di * 512 + (dl + 1) * 128],
                                        yfm[:, di * NTOK + 512 * ncx:di * NTOK + 512 * (ncx + 1)],
                                        start=(di == 0), stop=(di == DC - 1))
                c.inc(last, "te")

            @c.on("scalar")
            def _(s, ncx=ncx):
                c.wait(s, "te")
                c.wait(s, "st")
                for dl in range(4):
                    last = s.activation(outst[:, (4 + dl) * 512:(5 + dl) * 512], pj[dl][:, 0:512],
                                        AF.Identity, bias=bcol[:, 12 + dl:13 + dl])
                c.inc(last, "se")

            for sub in range(4):
                @c.on("tensor")
                def _(t, ncx=ncx, sub=sub):
                    c.wait(t, "se")
                    c.wait(t, "ve")
                    for dl in range(4):
                        last = t.transpose(ptr[:, dl * 128:(dl + 1) * 128],
                                           outst[:, (4 + dl) * 512 + sub * 128:(4 + dl) * 512 + (sub + 1) * 128],
                                           idb[:])
                    c.inc(last, "te")

                @c.on("vector")
                def _(v, ncx=ncx, sub=sub):
                    c.wait(v, "te")
                    c.wait(v, "st")
                    c.inc(v.tensor_copy(ytm[:, 0:512], ptr[:, 0:512]), "ve")

                @c.on("sync")
                def _(sy, ncx=ncx, sub=sub):
                    c.wait(sy, "ve")
                    c.inc(sy.dma_start(out=vt_o[512 * ncx + 128 * sub:512 * ncx + 128 * (sub + 1), :],
                                       in_=ytm[:, 0:512]), "st", 16)

        @c.on("sync")
        def _(sy):
            c.wait(sy, "st")
    return c


def build_l1fmain():
    c = Ctx("l1fmain")
    nc = c.nc
    xr = nc.dram_tensor("xr", [NTOK, D], F32, kind="ExternalInput").ap()
    qfm = nc.dram_tensor("qfm", [D, NTOK], BF16, kind="ExternalInput").ap()
    kf = nc.dram_tensor("kf", [512, N], BF16, kind="ExternalInput").ap()
    vtf = nc.dram_tensor("vtf", [N, 512], BF16, kind="ExternalInput").ap()
    ow = nc.dram_tensor("ow", [D, D], F32, kind="ExternalInput").ap()
    ob = nc.dram_tensor("ob", [1, D], F32, kind="ExternalInput").ap()
    rw = nc.dram_tensor("rw", [D, 8], F32, kind="ExternalInput").ap()
    rb = nc.dram_tensor("rb", [1, 8], F32, kind="ExternalInput").ap()
    w1 = nc.dram_tensor("w1", [8 * D, 2 * D], F32, kind="ExternalInput").ap()
    b1 = nc.dram_tensor("b1", [8, 2 * D], F32, kind="ExternalInput").ap()
    w2 = nc.dram_tensor("w2", [8 * 2 * D, D], F32, kind="ExternalInput").ap()
    b2 = nc.dram_tensor("b2", [8, D], F32, kind="ExternalInput").ap()
    identf = nc.dram_tensor("identf", [128, 128], F32, kind="ExternalInput").ap()
    x4o = nc.dram_tensor("x4o", [NTOK, D], F32, kind="ExternalOutput").ap()
    ao_d = nc.dram_tensor("ao_d", [NTOK, 512], BF16)
    NST = N // 128
    SCS = float(32) ** -0.5
    with c:
        idf = c.sb("idf", [128, 128], F32)
        ar1 = c.sb("ar1", [128, 16512], BF16)
        ar2 = c.sb("ar2", [128, 8192], BF16)
        ws1 = c.sb("ws1", [128, 16384], BF16)
        mla = c.sb("mla", [128, DC * NTOK], BF16)
        macc = c.sb("macc", [128, DC * NTOK], F32)
        aost = c.sb("aost", [128, 128], BF16)
        zc4 = c.sb("zc4", [128, 4], F32)
        zc = c.sb("zc", [128, 1], F32)
        rws = c.sb("rws", [128, 64], BF16)
        rbrow = c.sb("rbrow", [1, 8], BF16)
        b1row = c.sb("b1row", [1, 2 * D], BF16)
        b2row = c.sb("b2row", [1, D], BF16)
        bcol = c.sb("bcol", [128, 24], F32)
        onesone = c.sb("onesone", [1, 1], BF16)
        onesbf = c.sb("onesbf", [1, 128], BF16)
        lg = c.sb("lg", [128, 128], F32)
        p8 = c.sb("p8", [128, 8], F32)
        s1 = c.sb("s1", [128, 1], F32)
        s2 = c.sb("s2", [128, 1], F32)
        s3 = c.sb("s3", [128, 1], F32)
        m8 = c.sb("m8", [128, 8], F32)
        comb = c.sb("comb", [128, 128], F32)
        cfb = c.sb("cfb", [1, NTOK], BF16)
        bct = c.sb("bct", [128, 512], BF16)
        xs = c.sb("xs", [128, D], F32)
        sq = c.sb("sq", [128, D], F32)
        st8 = c.sb("st8", [128, 8], F32)
        ffnt = c.sb("ffnt", [128, D], F32)
        lnt = c.sb("lnt", [128, D], F32)
        pA = c.ps("pA", 512)
        pjS = c.ps("pjS", 2048)
        pj = [pjS[:, 512 * i:512 * (i + 1)] for i in range(4)]

        KHS = 0
        QHS = 8192
        VES = 10240
        PES = 12416

        @c.on("gpsimd")
        def _(g):
            c.inc(g.dma_start(out=idf[:], in_=identf), "gd", 16)
            c.inc(g.dma_start(out=rbrow[:], in_=rb), "gd", 16)
            for di in range(DC):
                c.inc(g.dma_start(out=rws[:, di * 8:(di + 1) * 8],
                                  in_=rw[128 * di:128 * (di + 1), :]), "gd", 16)

        @c.on("vector")
        def _(v):
            v.memset(onesone[:], 1.0)
            c.inc(v.memset(onesbf[:], 1.0), "ve")

        for h in range(16):
            @c.on("gpsimd")
            def _(g, h=h):
                c.wait(g, "te")
                c.inc(g.dma_start(out=ar1[0:32, KHS:KHS + N], in_=kf[32 * h:32 * (h + 1), :]), "gd", 16)
                c.inc(g.dma_start(out=ar1[0:32, QHS:QHS + NTOK],
                                  in_=qfm[64 * h:64 * h + 32, :]), "gd", 16)
                vsrc = vtf.rearrange("(t p) k -> p t k", p=128)[:, :, 32 * h:32 * (h + 1)]
                vdst = ar1[:, VES:VES + NST * 34].rearrange("p (t k) -> p t k", k=34)[:, :, 0:32]
                c.inc(g.dma_start(out=vdst, in_=vsrc), "gd", 16)

            @c.on("vector")
            def _(v, h=h):
                c.wait(v, "gd")
                ones_view = ar1[:, VES:VES + NST * 34].rearrange("p (t k) -> p t k", k=34)[:, :, 32:33]
                c.inc(v.memset(ones_view, 1.0), "ve")

            for sc in range(NC):
                se0 = c.cnt["se"]
                te_ck = {}

                def se_of(tt):
                    return se0 + tt // 2 + 1

                def pv_emit(t, ttp):
                    if ttp == 0:
                        c.wait(t, "ve")
                    c.wait(t, "se", se_of(ttp))
                    for ts in range(4):
                        last = t.matmul(pA[:, ts * 33:ts * 33 + 33],
                                        ar1[:, PES + (ttp % 8) * 512 + ts * 128:PES + (ttp % 8) * 512 + (ts + 1) * 128],
                                        ar1[:, VES + 34 * ttp:VES + 34 * ttp + 33],
                                        start=(ttp == 0 and ts == 0),
                                        stop=(ttp == NST - 1 and ts == 3))
                    c.inc(last, "te")

                @c.on("tensor")
                def _(t, h=h, sc=sc):
                    for tt in range(NST):
                        if tt == 0:
                            c.wait(t, "gd")
                            c.wait(t, "ve")
                            c.wait(t, "se", se0)
                        if tt >= 4:
                            pv_emit(t, tt - 4)
                        if tt >= 4:
                            c.wait(t, "se", se_of(tt - 4))
                        c.inc(t.matmul(pjS[:, (tt % 4) * 512:(tt % 4) * 512 + 512],
                                       ar1[0:32, KHS + 128 * tt:KHS + 128 * (tt + 1)],
                                       ar1[0:32, QHS + 512 * sc:QHS + 512 * (sc + 1)],
                                       start=True, stop=True), "te")
                        te_ck[tt] = c.cnt["te"]
                    for ttp in (NST - 4, NST - 3, NST - 2, NST - 1):
                        pv_emit(t, ttp)
                    te_ck["end"] = c.cnt["te"]

                @c.on("scalar")
                def _(s, h=h, sc=sc):
                    for p in range(NST // 2):
                        tt = 2 * p
                        c.wait(s, "te", te_ck[tt + 1])
                        base = (tt % 8) * 512
                        c.inc(s.activation(ar1[:, PES + base:PES + base + 1024],
                                           pjS[:, (tt % 4) * 512:(tt % 4) * 512 + 1024],
                                           AF.Exp, scale=SCS), "se")
                assert c.cnt["se"] == se0 + NST // 2

                ve0n = c.cnt["ve"]
                st0n = c.cnt["st"]

                @c.on("vector")
                def _(v, h=h, sc=sc, ve0n=ve0n, st0n=st0n):
                    c.wait(v, "te")
                    c.wait(v, "st")
                    for ts in range(4):
                        v.reciprocal(zc4[:, ts:ts + 1], pA[:, ts * 33 + 32:ts * 33 + 33])
                    c.inc(v.nop() if False else v.reciprocal(zc4[:, 3:4], pA[:, 3 * 33 + 32:3 * 33 + 33]), "ve")
                    c.wait(v, "ve")
                    for ts in range(4):
                        c.inc(v.tensor_scalar(aost[:, 32 * ts:32 * (ts + 1)],
                                              pA[:, ts * 33:ts * 33 + 32],
                                              zc4[:, ts:ts + 1], None, op0=ALU.mult), "ve")

                @c.on("sync")
                def _(sy, h=h, sc=sc, ve0n=ve0n):
                    for ts in range(4):
                        c.wait(sy, "ve", ve0n + 2 + ts)
                        tix = 4 * sc + ts
                        c.inc(sy.dma_start(out=ao_d[128 * tix:128 * (tix + 1), 32 * h:32 * (h + 1)],
                                           in_=aost[:, 32 * ts:32 * (ts + 1)]), "st", 16)

        for cc2 in range(4):
            @c.on("sync")
            def _(sy, cc2=cc2):
                c.wait(sy, "st")
                c.wait(sy, "te")
                c.inc(sy.dma_start(out=ar2[:, cc2 * NTOK:(cc2 + 1) * NTOK],
                                   in_=ao_d[:, 128 * cc2:128 * (cc2 + 1)], transpose=True), "dma", 16)

        @c.on("gpsimd")
        def _(g):
            for cc2 in range(4):
                c.inc(g.dma_start(out=ws1[:, cc2 * 1024:(cc2 + 1) * 1024],
                                  in_=ow[128 * cc2:128 * (cc2 + 1), :]), "gd", 16)
            c.inc(g.dma_start(out=b1row[:, 0:D], in_=ob), "gd", 16)

        @c.on("tensor")
        def _(t):
            c.wait(t, "gd")
            c.wait(t, "ve")
            for i in range(8):
                last = t.matmul(pj[3][:, i:i + 1], b1row[:, 128 * i:128 * (i + 1)],
                                onesone[:], start=(i == 0), stop=(i == 7))
            c.inc(last, "te")

        @c.on("vector")
        def _(v):
            c.wait(v, "te")
            c.inc(v.tensor_copy(bcol[:, 0:8], pj[3][:, 0:8]), "ve")

        for ncx in range(NC):
            for dg in range(2):
                @c.on("tensor")
                def _(t, ncx=ncx, dg=dg):
                    c.wait(t, "dma")
                    c.wait(t, "ve")
                    c.wait(t, "se")
                    for dl in range(4):
                        do = 4 * dg + dl
                        for cc2 in range(4):
                            last = t.matmul(pj[dl][:, 0:512],
                                            ws1[:, cc2 * 1024 + do * 128:cc2 * 1024 + (do + 1) * 128],
                                            ar2[:, cc2 * NTOK + 512 * ncx:cc2 * NTOK + 512 * (ncx + 1)],
                                            start=(cc2 == 0), stop=(cc2 == 3))
                    c.inc(last, "te")

                @c.on("scalar")
                def _(s, ncx=ncx, dg=dg):
                    c.wait(s, "te")
                    for dl in range(4):
                        do = 4 * dg + dl
                        last = s.activation(mla[:, do * NTOK + 512 * ncx:do * NTOK + 512 * (ncx + 1)],
                                            pj[dl][:, 0:512], AF.Identity, bias=bcol[:, do:do + 1])
                    c.inc(last, "se")

        for nt in range(NT):
            @c.on("tensor")
            def _(t, nt=nt):
                c.wait(t, "se")
                c.wait(t, "ve")
                for di in range(DC):
                    t.matmul(pj[0][:, nt * 8:(nt + 1) * 8],
                             mla[:, di * NTOK + 128 * nt:di * NTOK + 128 * (nt + 1)],
                             rws[:, di * 8:(di + 1) * 8],
                             start=(nt == 0 and di == 0), stop=False)
                last = t.matmul(pj[0][:, nt * 8:(nt + 1) * 8], onesbf[:], rbrow[:],
                                start=False, stop=(nt == NT - 1))
                c.inc(last, "te")

        @c.on("vector")
        def _(v):
            c.wait(v, "te")
            c.inc(v.tensor_copy(lg[:], pj[0][:, 0:128]), "ve")

        for nt in range(NT):
            sl = slice(nt * 8, (nt + 1) * 8)
            @c.on("vector")
            def _(v, nt=nt, sl=sl):
                c.wait(v, "ve")
                c.inc(v.reduce_max(s1[:], lg[:, sl], axis=AX), "ve")
                c.wait(v, "ve")
                c.inc(v.tensor_scalar(p8[:], lg[:, sl], s1[:], None, op0=ALU.subtract), "ve")

            @c.on("scalar")
            def _(s, nt=nt):
                c.wait(s, "ve")
                c.inc(s.activation(p8[:], p8[:], AF.Exp, accum_out=s2[:]), "se")

            @c.on("vector")
            def _(v, nt=nt):
                c.wait(v, "se")
                c.inc(v.reciprocal(s2[:], s2[:]), "ve")
                c.wait(v, "ve")
                c.inc(v.tensor_scalar(p8[:], p8[:], s2[:], None, op0=ALU.mult), "ve")
                c.wait(v, "ve")
                c.inc(v.reduce_max(s1[:], p8[:], axis=AX), "ve")
                c.wait(v, "ve")
                c.inc(v.tensor_scalar(m8[:], p8[:], s1[:], None, op0=ALU.is_ge), "ve")
                c.wait(v, "ve")
                c.inc(v.scalar_tensor_tensor(m8[:], m8[:], -2.0, p8[:], op0=ALU.mult, op1=ALU.add), "ve")
                c.wait(v, "ve")
                c.inc(v.reduce_max(s2[:], m8[:], axis=AX), "ve")
                c.wait(v, "ve")
                c.inc(v.tensor_scalar(m8[:], p8[:], s2[:], None, op0=ALU.is_ge), "ve")
                c.wait(v, "ve")
                c.inc(v.tensor_tensor(s1[:], s1[:], s2[:], op=ALU.add), "ve")
                c.wait(v, "ve")
                c.inc(v.tensor_scalar_add(s1[:], s1[:], 1e-8), "ve")
                c.wait(v, "ve")
                c.inc(v.reciprocal(s3[:], s1[:]), "ve")
                c.wait(v, "ve")
                c.inc(v.tensor_scalar_mul(s3[:], s3[:], 0.3), "ve")
                c.wait(v, "ve")
                c.inc(v.tensor_tensor(m8[:], m8[:], p8[:], op=ALU.mult), "ve")
                c.wait(v, "ve")
                c.inc(v.tensor_scalar(comb[:, nt * 8:(nt + 1) * 8], m8[:], s3[:], None,
                                      op0=ALU.mult), "ve")

        moe_gd = {}
        moe_h1done = {}
        for e in range(8):
            @c.on("gpsimd")
            def _(g, e=e):
                if e == 0:
                    c.wait(g, "te")
                else:
                    c.wait(g, "te", moe_h1done[e - 1])
                for di in range(DC):
                    c.inc(g.dma_start(out=ws1[:, di * 2048:(di + 1) * 2048],
                                      in_=w1[e * D + 128 * di:e * D + 128 * (di + 1), :]), "gd", 16)
                c.inc(g.dma_start(out=b1row[:], in_=b1[e:e + 1, :]), "gd", 16)
                moe_gd[("w1", e)] = c.cnt["gd"]

            @c.on("tensor")
            def _(t, e=e):
                c.wait(t, "gd")
                c.wait(t, "ve")
                for i in range(16):
                    last = t.matmul(pj[3][:, 64 + i:65 + i], b1row[:, 128 * i:128 * (i + 1)],
                                    onesone[:], start=(i == 0), stop=(i == 15))
                c.inc(last, "te")

            @c.on("vector")
            def _(v, e=e):
                c.wait(v, "te")
                c.inc(v.tensor_copy(bcol[:, 8:24], pj[3][:, 64:80]), "ve")

            for q4 in range(4):
                @c.on("tensor")
                def _(t, e=e, q4=q4):
                    c.wait(t, "ve")
                    c.wait(t, "se")
                    for i in range(4):
                        nt = 4 * q4 + i
                        last = t.matmul(pj[3][0:1, 128 * i:128 * (i + 1)],
                                        comb[:, nt * 8 + e:nt * 8 + e + 1], idf[:],
                                        start=(i == 0), stop=(i == 3))
                    c.inc(last, "te")

                @c.on("vector")
                def _(v, e=e, q4=q4):
                    c.wait(v, "te")
                    c.inc(v.tensor_copy(cfb[:, 512 * q4:512 * (q4 + 1)], pj[3][0:1, 0:512]), "ve")

            for ncx in range(NC):
                @c.on("tensor")
                def _(t, e=e, ncx=ncx):
                    c.wait(t, "ve")
                    c.wait(t, "se")
                    c.inc(t.matmul(pj[3][:, 0:512], onesbf[:], cfb[:, 512 * ncx:512 * (ncx + 1)],
                                   start=True, stop=True), "te")

                @c.on("vector")
                def _(v, e=e, ncx=ncx):
                    c.wait(v, "te")
                    c.inc(v.tensor_copy(bct[:], pj[3][:, 0:512]), "ve")

                h1_te = {}
                h1_se = {}

                def h1_gelu(s, e, ncx, g):
                    c.wait(s, "te", h1_te[g])
                    for hl in range(2):
                        hh = 2 * g + hl
                        bank = pj[(g % 2) * 2 + hl]
                        last = s.activation(ar2[:, hh * 512:(hh + 1) * 512], bank[:, 0:512],
                                            AF.Gelu, bias=bcol[:, 8 + hh:9 + hh])
                    c.inc(last, "se")
                    h1_se[g] = c.cnt["se"]

                se0_h1 = c.cnt["se"]

                @c.on("tensor")
                def _(t, e=e, ncx=ncx, se0_h1=se0_h1):
                    for g in range(8):
                        if g == 0:
                            if ncx == 0:
                                c.wait(t, "gd", moe_gd[("w1", e)])
                            c.wait(t, "ve")
                            c.wait(t, "se", se0_h1)
                        if g >= 2:
                            c.wait(t, "se", se0_h1 + g - 1)
                        for hl in range(2):
                            hh = 2 * g + hl
                            bank = pj[(g % 2) * 2 + hl]
                            for di in range(DC):
                                last = t.matmul(bank[:, 0:512],
                                                ws1[:, di * 2048 + hh * 128:di * 2048 + (hh + 1) * 128],
                                                mla[:, di * NTOK + 512 * ncx:di * NTOK + 512 * (ncx + 1)],
                                                start=(di == 0), stop=(di == DC - 1))
                        c.inc(last, "te")
                        h1_te[g] = c.cnt["te"]

                if ncx == NC - 1:
                    moe_h1done[e] = c.cnt["te"]

                @c.on("scalar")
                def _(s, e=e, ncx=ncx):
                    for g in range(8):
                        h1_gelu(s, e, ncx, g)

                if ncx == 0:
                    @c.on("gpsimd")
                    def _(g_, e=e):
                        c.wait(g_, "te", h1_te[0])
                        for hh in range(16):
                            c.inc(g_.dma_start(out=ar1[:, hh * 1024:(hh + 1) * 1024],
                                               in_=w2[e * 2048 + 128 * hh:e * 2048 + 128 * (hh + 1), :]),
                                  "gd", 16)
                        c.inc(g_.dma_start(out=b2row[:], in_=b2[e:e + 1, :]), "gd", 16)
                        moe_gd[("w2", e)] = c.cnt["gd"]

                @c.on("vector")
                def _(v, e=e, ncx=ncx):
                    c.wait(v, "se")
                    for hh in range(16):
                        last = v.tensor_tensor(ar2[:, hh * 512:(hh + 1) * 512],
                                               ar2[:, hh * 512:(hh + 1) * 512], bct[:], op=ALU.mult)
                    c.inc(last, "ve")

                h2_te = {}
                h2_ve = {}

                def h2_acc(v, e, ncx, g):
                    c.wait(v, "te", h2_te[g])
                    for dl in range(2):
                        do = 2 * g + dl
                        bank = pj[(g % 2) * 2 + dl]
                        dst = macc[:, do * NTOK + 512 * ncx:do * NTOK + 512 * (ncx + 1)]
                        if e == 0:
                            last = v.tensor_copy(dst, bank[:, 0:512])
                        else:
                            last = v.tensor_tensor(dst, dst, bank[:, 0:512], op=ALU.add)
                    c.inc(last, "ve")
                    h2_ve[g] = c.cnt["ve"]

                ve0_h2 = c.cnt["ve"]

                @c.on("tensor")
                def _(t, e=e, ncx=ncx, ve0_h2=ve0_h2):
                    for g in range(4):
                        if g == 0:
                            c.wait(t, "gd", moe_gd[("w2", e)])
                            c.wait(t, "ve", ve0_h2)
                            c.wait(t, "se")
                        if g >= 2:
                            c.wait(t, "ve", ve0_h2 + g - 1)
                        for dl in range(2):
                            do = 2 * g + dl
                            bank = pj[(g % 2) * 2 + dl]
                            for hh in range(16):
                                t.matmul(bank[:, 0:512],
                                         ar1[:, hh * 1024 + do * 128:hh * 1024 + (do + 1) * 128],
                                         ar2[:, hh * 512:(hh + 1) * 512],
                                         start=(hh == 0), stop=False)
                            last = t.matmul(bank[:, 0:512], b2row[:, do * 128:(do + 1) * 128],
                                            cfb[:, 512 * ncx:512 * (ncx + 1)],
                                            start=False, stop=True)
                        c.inc(last, "te")
                        h2_te[g] = c.cnt["te"]

                @c.on("vector")
                def _(v, e=e, ncx=ncx):
                    for g in range(4):
                        h2_acc(v, e, ncx, g)

        for tt in range(NT):
            @c.on("sync")
            def _(sy, tt=tt):
                if tt > 0:
                    c.wait(sy, "ve")
                c.inc(sy.dma_start(out=xs[:], in_=xr[128 * tt:128 * (tt + 1), :]), "dma", 16)

            @c.on("vector")
            def _(v):
                c.wait(v, "dma")

            ln_tile(c, xs[:], lnt[:], sq, st8, D)

            for half in range(2):
                @c.on("tensor")
                def _(t, tt=tt, half=half):
                    c.wait(t, "ve")
                    c.wait(t, "se")
                    for q in range(4):
                        do = 4 * half + q
                        last = t.transpose(pj[q][:, 0:128],
                                           macc[:, do * NTOK + 128 * tt:do * NTOK + 128 * (tt + 1)],
                                           idf[:])
                    c.inc(last, "te")

                @c.on("vector")
                def _(v, tt=tt, half=half):
                    c.wait(v, "te")
                    for q in range(4):
                        do = 4 * half + q
                        last = v.tensor_copy(ffnt[:, do * 128:(do + 1) * 128], pj[q][:, 0:128])
                    c.inc(last, "ve")

            @c.on("vector")
            def _(v, tt=tt):
                c.wait(v, "se")
                c.wait(v, "ve")
                c.inc(v.tensor_tensor(ffnt[:], ffnt[:], lnt[:], op=ALU.add), "ve")

            ln_tile(c, ffnt[:], lnt[:], sq, st8, D, res=xs[:], res_out=sq[:])

            @c.on("sync")
            def _(sy, tt=tt):
                c.wait(sy, "ve")
                c.inc(sy.dma_start(out=x4o[128 * tt:128 * (tt + 1), :], in_=sq[:]), "st", 16)

        @c.on("sync")
        def _(sy):
            c.wait(sy, "st")
    return c


def rows_of_core(core):
    b, q = core // 4, core % 4
    idx = []
    for k in range(4):
        g = 4 * k + q
        idx.append((b, g * 512, (g + 1) * 512))
    return b, idx


def shard_rows(x):
    out = []
    for core in range(8):
        b, idx = rows_of_core(core)
        out.append(np.ascontiguousarray(
            np.concatenate([x[b, s:e] for (_, s, e) in idx], axis=0)))
    return out


def unshard_rows(shards, dtype=np.float32):
    x = np.empty((B, N, D), dtype)
    for core in range(8):
        b, idx = rows_of_core(core)
        for k, (_, s, e) in enumerate(idx):
            x[b, s:e] = shards[core][k * 512:(k + 1) * 512]
    return x


def assemble_full_cols(pieces):
    d = pieces[0].shape[0]
    full = [np.empty((d, N), pieces[0].dtype) for _ in range(B)]
    for core in range(8):
        b, idx = rows_of_core(core)
        for k, (_, s, e) in enumerate(idx):
            full[b][:, s:e] = pieces[core][:, k * 512:(k + 1) * 512]
    return full


def assemble_full_rows(pieces):
    d = pieces[0].shape[1]
    full = [np.empty((N, d), pieces[0].dtype) for _ in range(B)]
    for core in range(8):
        b, idx = rows_of_core(core)
        for k, (_, s, e) in enumerate(idx):
            full[b][s:e] = pieces[core][k * 512:(k + 1) * 512]
    return full


_IDENT = np.eye(128, dtype=np.float32).astype(BF)
_IDENTF = np.eye(128, dtype=np.float32)


def run_l0pre(x, wq, wk, wv):
    nc = build_l0pre()
    xs = shard_rows(x)
    maps = [{"xr": xs[i], "wq": wq, "wk": wk, "wv": wv, "ident": _IDENT,
             "identf": _IDENTF} for i in range(8)]
    res = run_bass_kernel_spmd(nc.nc, maps, core_ids=list(range(8)))
    return res.results


def my_cols(core):
    b, idx = rows_of_core(core)
    return b, [(s, e) for (_, s, e) in idx]


def run_l0attn(l1res):
    q_full = assemble_full_cols([np.asarray(r["q_fm"]) for r in l1res])
    nc = build_l0attn()
    maps = []
    for core in range(8):
        b = core // 4
        maps.append({"qf": np.ascontiguousarray(q_full[b]),
                     "kf": np.asarray(l1res[core]["k_fm"]),
                     "vt": np.asarray(l1res[core]["v_tm"])})
    res = run_bass_kernel_spmd(nc.nc, maps, core_ids=list(range(8)))
    return res.results


def run_l0red_ffn(x, l1res, l2res, wg, wu, wd):
    xs = shard_rows(x)
    nc = build_l0red_ffn()
    maps = []
    for core in range(8):
        b, cols = my_cols(core)
        ps = []
        for i in range(4):
            pfull = np.asarray(l2res[4 * b + i]["part"])
            ps.append(np.ascontiguousarray(
                np.concatenate([pfull[:, s:e] for (s, e) in cols], axis=1)))
        maps.append({"xr": xs[core], "p0": ps[0], "p1": ps[1], "p2": ps[2],
                     "p3": ps[3], "vfm": np.asarray(l1res[core]["v_fm"]),
                     "wg_i": wg, "wu_i": wu, "wd_i": wd, "ident": _IDENT})
    res = run_bass_kernel_spmd(nc.nc, maps, core_ids=list(range(8)))
    return res.results


def make_masks(core):
    q = core % 4
    out = np.zeros((NC * 128, 16 * 512), np.float32)
    for k in range(NC):
        t = (4 * k + q) * 512 + np.arange(512)[None, :]
        for jj in range(16):
            s = (16 * k + jj) * 128 + np.arange(128)[:, None]
            out[k * 128:(k + 1) * 128, jj * 512:(jj + 1) * 512] = (s <= t)
    return out.astype(BF)


_MASKS = None


def run_l1pre(x2, Wdq, Wuq, Wdkv, Wuk):
    xs = shard_rows(x2)
    nc = build_l1pre()
    maps = [{"xr": xs[i], "Wdq": Wdq, "Wuq": Wuq, "Wdkv": Wdkv, "Wuk": Wuk,
             "ident": _IDENT} for i in range(8)]
    res = run_bass_kernel_spmd(nc.nc, maps, core_ids=list(range(8)))
    return res.results


def run_l1attn(x2, l4res, Wuv, Wo):
    global _MASKS
    if _MASKS is None:
        _MASKS = [make_masks(i) for i in range(8)]
    xs = shard_rows(x2)
    ckv_full = assemble_full_rows([np.asarray(r["ckv_o"]) for r in l4res])
    nc = build_l1attn()
    maps = []
    for core in range(8):
        b = core // 4
        maps.append({"xr": xs[core], "ql_i": np.asarray(l4res[core]["ql_o"]),
                     "ckvfm": np.ascontiguousarray(ckv_full[b].T),
                     "ckvtm": ckv_full[b], "masks": _MASKS[core],
                     "Wuv": Wuv, "Wo": Wo})
    res = run_bass_kernel_spmd(nc.nc, maps, core_ids=list(range(8)))
    return res.results


def run_l1fpre(x3, qw, qb, kw, kb, vw, vb):
    xs = shard_rows(x3)
    nc = build_l1fpre()
    maps = [{"xr": xs[i], "qw": qw, "qb": qb.reshape(1, -1), "kw": kw,
             "kbi": kb.reshape(1, -1), "vw": vw, "vbi": vb.reshape(1, -1),
             "ident": _IDENT} for i in range(8)]
    res = run_bass_kernel_spmd(nc.nc, maps, core_ids=list(range(8)))
    return res.results


def run_l1fmain(x3, l6res, ow, ob, rw, rb, w1, b1, w2, b2):
    xs = shard_rows(x3)
    k_full = assemble_full_cols([np.asarray(r["k_o"]) for r in l6res])
    vt_full = assemble_full_rows([np.asarray(r["vt_o"]) for r in l6res])
    nc = build_l1fmain()
    maps = []
    for core in range(8):
        b = core // 4
        maps.append({"xr": xs[core], "qfm": np.asarray(l6res[core]["q_o"]),
                     "kf": np.ascontiguousarray(k_full[b]), "vtf": vt_full[b],
                     "ow": ow, "ob": ob.reshape(1, -1), "rw": rw, "rb": rb.reshape(1, -1),
                     "w1": w1.reshape(8 * D, 2 * D), "b1": b1,
                     "w2": w2.reshape(8 * 2 * D, D), "b2": b2, "identf": _IDENTF})
    res = run_bass_kernel_spmd(nc.nc, maps, core_ids=list(range(8)))
    return res.results


def kernel(**inputs):
    x = np.asarray(inputs["L0_attn_x"], np.float32)
    l1 = run_l0pre(x, inputs["L0_attn_wq"], inputs["L0_attn_wk"], inputs["L0_attn_wv"])
    l2 = run_l0attn(l1)
    l3 = run_l0red_ffn(x, l1, l2, inputs["L0_ffn_w_gate"], inputs["L0_ffn_w_up"],
                       inputs["L0_ffn_w_down"])
    x2 = unshard_rows([np.asarray(r["x2o"]) for r in l3])
    l4 = run_l1pre(x2, inputs["L1_attn_W_dq"], inputs["L1_attn_W_uq"],
                   inputs["L1_attn_W_dkv"], inputs["L1_attn_W_uk"])
    l5 = run_l1attn(x2, l4, inputs["L1_attn_W_uv"], inputs["L1_attn_W_o"])
    x3 = unshard_rows([np.asarray(r["x3o"]) for r in l5])
    l6 = run_l1fpre(x3, inputs["L1_ffn_q_w"], inputs["L1_ffn_q_b"], inputs["L1_ffn_k_w"],
                    inputs["L1_ffn_k_b"], inputs["L1_ffn_v_w"], inputs["L1_ffn_v_b"])
    l7 = run_l1fmain(x3, l6, inputs["L1_ffn_o_w"], inputs["L1_ffn_o_b"],
                     inputs["L1_ffn_router_w"], inputs["L1_ffn_router_b"],
                     inputs["L1_ffn_w1"], inputs["L1_ffn_b1"],
                     inputs["L1_ffn_w2"], inputs["L1_ffn_b2"])
    return unshard_rows([np.asarray(r["x4o"]) for r in l7])
